# version 03
# baseline (speedup 1.0000x reference)
import numpy as np
from contextlib import ExitStack

import concourse.bass as bass
import concourse.mybir as mybir
from concourse.bass_utils import run_bass_kernel_spmd

F32 = mybir.dt.float32
BF16 = mybir.dt.bfloat16
AF = mybir.ActivationFunctionType
ALU = mybir.AluOpType
AX = mybir.AxisListType

NCORES = 8
D = 1024
B = 4
S = 8192
OWN = 4096
NEG = -1.0e30


class Ev:
    __slots__ = ("sem", "val")

    def __init__(self, sem, val):
        self.sem = sem
        self.val = val


class Builder:
    ENGINES = ("sync", "scalar", "vector", "gpsimd", "tensor")

    def __init__(self):
        self.nc = bass.Bass("TRN2", target_bir_lowering=False, debug=False)
        self.es = ExitStack()
        self.plan = []
        self.sems = {}
        self.cnt = {}
        self.hwm = {}

    def sem(self, name):
        if name not in self.sems:
            self.sems[name] = self.es.enter_context(self.nc.semaphore(name))
            self.cnt[name] = 0
        return self.sems[name]

    def sbuf(self, name, shape, dt):
        h = self.es.enter_context(self.nc.sbuf_tensor(name, shape, dt))
        return h[tuple(slice(None) for _ in shape)]

    def psum(self, name, shape, dt):
        h = self.es.enter_context(self.nc.psum_tensor(name, shape, dt))
        return h[tuple(slice(None) for _ in shape)]

    def dram(self, name, shape, dt, kind="Internal"):
        return self.nc.dram_tensor(name, shape, dt, kind=kind)

    def op(self, engine, fn, sem=None, k=1):
        if sem is not None:
            h = self.sem(sem)
            self.cnt[sem] += k
            ev = Ev(sem, self.cnt[sem])

            def thunk(eng, fn=fn, h=h, k=k):
                fn(eng).then_inc(h, k)
            self.plan.append((engine, thunk))
            return ev
        else:
            def thunk(eng, fn=fn):
                fn(eng)
            self.plan.append((engine, thunk))
            return None

    def wait(self, engine, *evs):
        for ev in evs:
            if ev is None:
                continue
            key = (engine, ev.sem)
            if self.hwm.get(key, 0) >= ev.val:
                continue
            self.hwm[key] = ev.val
            h = self.sem(ev.sem)

            def thunk(eng, h=h, v=ev.val):
                eng.wait_ge(h, v)
            self.plan.append((engine, thunk))

    def dma(self, engine, sem, out, in_, **kw):
        return self.op(engine, lambda e: e.dma_start(out=out, in_=in_, **kw), sem, 16)

    def dma_excl(self, engine, sem, out, in_, **kw):
        if self.cnt.get(sem, 0) > 0:
            self.wait(engine, Ev(sem, self.cnt[sem]))
        return self.dma(engine, sem, out, in_, **kw)

    def total(self, sem):
        return Ev(sem, self.cnt.get(sem, 0))

    def barrier(self, engines=None):
        evs = [Ev(s, self.cnt[s]) for s in self.cnt if self.cnt[s] > 0]
        for eng in (engines or self.ENGINES):
            self.wait(eng, *evs)

    def finalize(self):
        with self.nc.Block() as block:
            for eng_name in self.ENGINES:
                thunks = [t for (e, t) in self.plan if e == eng_name]
                if not thunks:
                    continue
                dec = getattr(block, eng_name)

                def section(eng, thunks=thunks):
                    for t in thunks:
                        t(eng)
                dec(section)
        self.es.close()
        return self.nc


def ln_tile(b, x_ap, out_ap, scr, D_, ev_in, eps=1e-5, out16_ap=None,
            g_bc=None, be_bc=None, ev_out_free=(), ev_out16_free=()):
    V, SC = "vector", "scalar"
    b.wait(V, *([ev_in] if isinstance(ev_in, Ev) else list(ev_in or [])))
    e = b.op(V, lambda en: en.reduce_sum(scr["m"], x_ap, axis=AX.X), "eV")
    b.wait(V, e)
    e = b.op(V, lambda en: en.tensor_scalar_mul(scr["m"], scr["m"], 1.0 / D_), "eV")
    b.wait(V, e)
    e = b.op(V, lambda en: en.tensor_scalar(scr["sq"], x_ap, scr["m"], None, ALU.subtract), "eV")
    ev_xread = e
    b.wait(SC, e)
    e = b.op(SC, lambda en: en.activation(scr["sq2"], scr["sq"], AF.Square, accum_out=scr["ss"]), "eS")
    b.wait(V, e)
    e = b.op(V, lambda en: en.tensor_scalar(scr["ss"], scr["ss"], 1.0 / D_, eps, ALU.mult, ALU.add), "eV")
    b.wait(SC, e)
    e = b.op(SC, lambda en: en.activation(scr["rstd"], scr["ss"], AF.Sqrt), "eS")
    b.wait(V, e)
    e = b.op(V, lambda en: en.reciprocal(scr["rstd"], scr["rstd"]), "eV")
    b.wait(V, e)
    b.wait(V, *ev_out_free)
    if g_bc is None:
        ev_out = b.op(V, lambda en: en.tensor_scalar_mul(out_ap, scr["sq"], scr["rstd"]), "eV")
    else:
        e = b.op(V, lambda en: en.tensor_scalar_mul(scr["sq"], scr["sq"], scr["rstd"]), "eV")
        b.wait(V, e)
        e = b.op(V, lambda en: en.tensor_mul(scr["sq"], scr["sq"], g_bc), "eV")
        b.wait(V, e)
        ev_out = b.op(V, lambda en: en.tensor_add(out_ap, scr["sq"], be_bc), "eV")
    ev16 = None
    if out16_ap is not None:
        b.wait(V, ev_out)
        b.wait(V, *ev_out16_free)
        ev16 = b.op(V, lambda en: en.tensor_copy(out16_ap, out_ap), "eV")
    return ev_out, ev16, ev_xread



def build_launch4(dbg=False):
    b = Builder()

    xL1 = b.dram("xL1", [S, D], F32, kind="ExternalInput").ap()
    Wq = b.dram("Wq", [D, 64], F32, kind="ExternalInput").ap()
    Wk = b.dram("Wk", [D, 64], F32, kind="ExternalInput").ap()
    Wv = b.dram("Wv", [D, D], F32, kind="ExternalInput").ap()
    identb_in = b.dram("identb_i", [128, 128], BF16, kind="ExternalInput").ap()
    dmask_in = b.dram("dmask_i", [4, 128, 512], F32, kind="ExternalInput").ap()
    prefmask_in = b.dram("prefmask", [128, 4096], BF16, kind="ExternalInput").ap()

    s1_out = b.dram("s1", [8, 128, OWN], F32, kind="ExternalOutput").ap()
    if dbg:
        dbg_xn = b.dram("dbg_xn", [128, D], F32, kind="ExternalOutput").ap()
        dbg_qk = b.dram("dbg_qk", [64, 256], F32, kind="ExternalOutput").ap()
        dbg_v = b.dram("dbg_v", [128, D], F32, kind="ExternalOutput").ap()
        dbg_s = b.dram("dbg_s", [128, 4608], F32, kind="ExternalOutput").ap()
        dbg_st = b.dram("dbg_st", [128, 3], F32, kind="ExternalOutput").ap()
        dbg_pt = b.dram("dbg_pt", [128, 512], F32, kind="ExternalOutput").ap()
        dbg_qf = b.sbuf("dbg_qf", [64, 256], F32)
        dbg_ptf = b.sbuf("dbg_ptf", [128, 512], F32)
        dbg_stf = b.sbuf("dbg_stf", [128, 3], F32)

    v_dram = b.dram("v_dram", [S // 128, 128, D], BF16).ap()
    rs_dram = b.dram("rs_dram", [128, 4], F32).ap()

    NT = S // 128
    NQB = OWN // 512

    identb = b.sbuf("identb", [128, 128], BF16)
    x_t = [b.sbuf("x_t0", [128, D], F32), b.sbuf("x_t1", [128, D], F32)]
    scr = {"m": b.sbuf("scr_m", [128, 1], F32)[:, :],
           "ss": b.sbuf("scr_ss", [128, 1], F32)[:, :],
           "rstd": b.sbuf("scr_rstd", [128, 1], F32)[:, :],
           "sq": b.sbuf("scr_sq", [128, D], F32)[:, :]}
    xn_t16 = b.sbuf("xn_t16", [128, D], BF16)
    xn_fm = b.sbuf("xn_fm", [128, 8 * 128], BF16)
    wq_f = b.sbuf("wq_f", [128, 8 * 64], F32)
    wk_f = b.sbuf("wk_f", [128, 8 * 64], F32)
    wq_s = b.sbuf("wq_s", [128, 8 * 64], BF16)
    wk_s = b.sbuf("wk_s", [128, 8 * 64], BF16)
    ar1 = b.sbuf("ar1", [128, 8192], F32)
    ar4 = b.sbuf("ar4", [128, 8192], BF16)
    q_fm = b.sbuf("q_fm", [64, S], BF16)
    k_fm = b.sbuf("k_fm", [64, S], BF16)
    v_t16 = b.sbuf("v_t16", [128, D], BF16)
    pt_buf = b.sbuf("pt_buf", [128, NT * 512], BF16)
    ar3 = b.sbuf("ar3", [128, 8 * 512], F32)
    prefm = b.sbuf("prefm", [128, 4096], BF16)
    dmask = b.sbuf("dmask", [128, 4 * 512], F32)
    vstream = b.sbuf("vstream", [128, 2 * 512], BF16)
    nmax = b.sbuf("nmax", [128, 1], F32)
    sumr = b.sbuf("sumr", [128, 1], F32)
    rsum4 = b.sbuf("rsum4", [128, 4], F32)
    rs_bc = b.sbuf("rs_bc", [128, 512], F32)

    scr["sq2"] = ar3[:, 0:D]
    ps_s = b.psum("ps_s", [128, 512], F32)
    ps_tr = b.psum("ps_tr", [128, 1024], BF16)
    ps_o = [b.psum(f"ps_{n}", [128, 512], F32) for n in "abcd"]

    e_id = b.dma("sync", "dW", identb[:, :], identb_in)
    e_wq = b.dma("sync", "dW", wq_f[:, :].rearrange("p (c n) -> p c n", n=64),
                 Wq.rearrange("(c p) n -> p c n", p=128))
    e_wk = b.dma("sync", "dW", wk_f[:, :].rearrange("p (c n) -> p c n", n=64),
                 Wk.rearrange("(c p) n -> p c n", p=128))
    e_wv = b.dma("sync", "dW", ar1[:, :].rearrange("p (c n) -> p c n", n=D),
                 Wv.rearrange("(c p) n -> p c n", p=128))
    e_dm = b.dma("sync", "dW", dmask[:, :].rearrange("p (c n) -> p c n", n=512),
                 dmask_in.rearrange("c p n -> p c n"))
    e_pm = b.dma("sync", "dW", prefm[:, :], prefmask_in)
    e_id = e_wq = e_wk = e_wv = e_dm = e_pm = b.total("dW")
    b.wait("vector", e_wq, e_wk, e_wv)
    e_cwq = b.op("vector", lambda e: e.tensor_copy(wq_s[:, :], wq_f[:, :]), "eV")
    e_cwk = b.op("vector", lambda e: e.tensor_copy(wk_s[:, :], wk_f[:, :]), "eV")
    e_cwv = b.op("vector", lambda e: e.tensor_copy(ar4[:, :], ar1[:, :]), "eV")

    ev_q_ev = ev_k_ev = None
    ev_vdma = None
    ev_fm_used = None
    ev_fm_prev = None
    ev_xread = [None, None]
    e_tr_prev = None
    for i in range(NT):
        bi_ = i % 2
        b.wait("sync", ev_xread[bi_])
        e_x = b.dma("sync", "dA%d" % bi_, x_t[bi_][:, :], xL1[i * 128:(i + 1) * 128, :])
        ev_ln, ev16, ev_xread[bi_] = ln_tile(b, x_t[bi_][:, :], scr["sq"], scr, D, e_x,
                                             out16_ap=xn_t16[:, :],
                                             ev_out16_free=[e_tr_prev])
        if dbg and i == 32:
            b.wait("gpsimd", ev_ln)
            e_tap = b.dma("gpsimd", "dD", dbg_xn, scr["sq"])
            b.wait("vector", e_tap)
        b.wait("tensor", ev16, e_id, ev_fm_prev)
        e_tr = None
        for c in range(8):
            e_tr = b.op("tensor", lambda e, c=c: e.transpose(
                ps_tr[:, c * 128:(c + 1) * 128],
                xn_t16[:, c * 128:(c + 1) * 128], identb[:, :]), "eT")
        e_tr_prev = e_tr
        b.wait("vector", e_tr, ev_fm_used)
        e_fm = b.op("vector", lambda e: e.tensor_copy(xn_fm[:, :], ps_tr[:, 0:1024]), "eV")
        ev_fm_prev = e_fm
        b.wait("tensor", e_fm, e_cwq, e_cwk, e_cwv, ev_q_ev, ev_k_ev)
        eq = ek = None
        for c in range(8):
            eq = b.op("tensor", lambda e, c=c: e.matmul(
                ps_o[0][0:64, 0:128], wq_s[:, c * 64:(c + 1) * 64],
                xn_fm[:, c * 128:(c + 1) * 128],
                start=(c == 0), stop=(c == 7)), "eT")
        for c in range(8):
            ek = b.op("tensor", lambda e, c=c: e.matmul(
                ps_o[1][0:64, 0:128], wk_s[:, c * 64:(c + 1) * 64],
                xn_fm[:, c * 128:(c + 1) * 128],
                start=(c == 0), stop=(c == 7)), "eT")
        if i > 0:
            b.wait("tensor", ev_vdma)
        ev_half = []
        for h in range(2):
            ph = ps_o[2 + h]
            evv = None
            for c in range(8):
                evv = b.op("tensor", lambda e, c=c, h=h, ph=ph: e.matmul(
                    ph[:, 0:512], xn_fm[:, c * 128:(c + 1) * 128],
                    ar4[:, c * 1024 + h * 512: c * 1024 + (h + 1) * 512],
                    start=(c == 0), stop=(c == 7)), "eT")
            ev_half.append(evv)
        ev_fm_used = ev_half[-1]
        b.wait("scalar", eq)
        ev_q_ev = b.op("scalar", lambda e, i=i: e.activation(
            q_fm[0:64, i * 128:(i + 1) * 128], ps_o[0][0:64, 0:128], AF.Copy,
            scale=0.125), "eS")
        b.wait("scalar", ek)
        ev_k_ev = b.op("scalar", lambda e, i=i: e.copy(
            k_fm[0:64, i * 128:(i + 1) * 128], ps_o[1][0:64, 0:128]), "eS")
        b.wait("vector", ev_half[0], ev_half[1])
        if i > 0:
            b.wait("vector", ev_vdma)
        e_v0 = b.op("vector", lambda e: e.tensor_copy(v_t16[:, 0:512], ps_o[2][:, 0:512]), "eV")
        e_v1 = b.op("vector", lambda e: e.tensor_copy(v_t16[:, 512:1024], ps_o[3][:, 0:512]), "eV")
        b.wait("gpsimd", e_v0, e_v1)
        ev_vdma = b.dma("gpsimd", "dV", v_dram[i], v_t16[:, :])

    if dbg:
        b.wait("vector", ev_q_ev, ev_k_ev)
        e_d1 = b.op("vector", lambda e: e.tensor_copy(dbg_qf[0:64, 0:128], q_fm[0:64, 4096:4224]), "eV")
        e_d2 = b.op("vector", lambda e: e.tensor_copy(dbg_qf[0:64, 128:256], k_fm[0:64, 4096:4224]), "eV")
        b.wait("gpsimd", e_d1, e_d2)
        b.dma("gpsimd", "dD", dbg_qk, dbg_qf[0:64, :])
        b.dma("gpsimd", "dD", dbg_v, v_dram[32])

    b.barrier()

    ev_pt_used = None
    ev_o_ev = [None] * 4
    ev_sb = None
    e_exp_prev = None
    e_trp_prev = None
    e_rc_prev = None
    ev_ptq_last = None
    e_rbc = None
    ev_rsd_prev = None
    evm_hist = []
    ev_dO_prev = None
    for qb in range(NQB):
        q0 = OWN + qb * 512
        kchunks = q0 // 512 + 1
        klen = kchunks * 512
        ktiles = klen // 128
        for qt in range(4):
            qi = qb * 4 + qt
            qrow = OWN + qi * 128
            diag = qrow // 512
            for kc in range(kchunks):
                b.wait("tensor", ev_sb)
                em = b.op("tensor", lambda e, kc=kc, qrow=qrow: e.matmul(
                    ps_s[:, 0:512], q_fm[0:64, qrow:qrow + 128],
                    k_fm[0:64, kc * 512:(kc + 1) * 512],
                    start=True, stop=True), "eT")
                b.wait("vector", em, e_exp_prev)
                ev_sb = b.op("vector", lambda e, kc=kc: e.tensor_copy(
                    ar1[:, kc * 512:(kc + 1) * 512], ps_s[:, 0:512]), "eV")
            b.wait("gpsimd", ev_sb, e_pm, e_dm)
            e_msk = None
            for kc in range(min(8, kchunks)):
                e_msk = b.op("gpsimd", lambda e, kc=kc: e.tensor_tensor(
                    ar1[:, kc * 512:(kc + 1) * 512], ar1[:, kc * 512:(kc + 1) * 512],
                    prefm[:, kc * 512:(kc + 1) * 512], ALU.add), "eP")
            slot = qi % 4
            e_msk2 = b.op("gpsimd", lambda e, diag=diag, slot=slot: e.tensor_tensor(
                ar1[:, diag * 512:(diag + 1) * 512], ar1[:, diag * 512:(diag + 1) * 512],
                dmask[:, slot * 512:(slot + 1) * 512], ALU.add), "eP")
            b.wait("vector", e_msk, e_msk2)
            if dbg and qb == 0 and qt == 0:
                b.wait("gpsimd", e_msk, e_msk2, ev_sb)
                e_tap = b.dma("gpsimd", "dD", dbg_s, ar1[:, 0:4608])
                b.wait("vector", e_tap)
                b.wait("scalar", e_tap)
            e_mx = b.op("vector", lambda e, klen=klen: e.reduce_max(
                nmax[:, :], ar1[:, 0:klen], axis=AX.X, negate=True), "eV")
            b.wait("scalar", e_mx, e_trp_prev, e_rc_prev)
            e_exp = b.op("scalar", lambda e, klen=klen: e.activation(
                ar4[:, 0:klen], ar1[:, 0:klen], AF.Exp, bias=nmax[:, :],
                accum_out=sumr[:, :]), "eS")
            e_exp_prev = e_exp
            b.wait("vector", e_exp, ev_rsd_prev)
            e_rc_prev = b.op("vector", lambda e, qt=qt: e.reciprocal(
                rsum4[:, qt:qt + 1], sumr[:, :]), "eV")
            if dbg and qb == 0 and qt == 0:
                b.wait("vector", e_mx, e_rc_prev)
                ed1 = b.op("vector", lambda e: e.tensor_copy(dbg_stf[:, 0:1], nmax[:, :]), "eV")
                ed2 = b.op("vector", lambda e: e.tensor_copy(dbg_stf[:, 1:2], sumr[:, :]), "eV")
                ed3 = b.op("vector", lambda e: e.tensor_copy(dbg_stf[:, 2:3], rsum4[:, 0:1]), "eV")
                b.wait("gpsimd", ed1, ed2, ed3)
                b.dma("gpsimd", "dD", dbg_st, dbg_stf[:, :])
            for g0 in range(0, ktiles, 8):
                gn = min(8, ktiles - g0)
                b.wait("tensor", e_exp, ev_pt_used, ev_ptq_last)
                e_trp = None
                for g in range(gn):
                    e_trp = b.op("tensor", lambda e, g=g, g0=g0: e.transpose(
                        ps_tr[:, g * 128:(g + 1) * 128],
                        ar4[:, (g0 + g) * 128:(g0 + g + 1) * 128], identb[:, :]), "eT")
                e_trp_prev = e_trp
                b.wait("vector", e_trp)
                ev_ptq_last = b.op("vector", lambda e, g0=g0, gn=gn, qt=qt: e.tensor_copy(
                    pt_buf[:, :].rearrange("p (g x) -> p g x", x=512)[
                        :, g0:g0 + gn, qt * 128:(qt + 1) * 128],
                    ps_tr[:, 0:gn * 128].rearrange("p (g x) -> p g x", x=128)), "eV")
        if dbg and qb == 0:
            b.wait("vector", ev_ptq_last)
            edp = b.op("vector", lambda e: e.tensor_copy(dbg_ptf[:, :], pt_buf[:, 32 * 512:33 * 512]), "eV")
            b.wait("gpsimd", edp)
            b.dma("gpsimd", "dD", dbg_pt, dbg_ptf[:, :])
        b.wait("gpsimd", e_rc_prev)
        e_rsd = b.dma_excl("gpsimd", "dB", rs_dram, rsum4[:, :])
        ev_rsd_prev = e_rsd
        b.wait("sync", e_rsd, e_rbc)
        for cc in range(4):
            b.dma("sync", "dBc", rs_bc[:, cc * 128:(cc + 1) * 128],
                  rs_dram[:, cc:cc + 1].rearrange("p o -> o p").broadcast_to((128, 128)),
                  allow_slow_non_contiguous=True)
        e_rbc = b.total("dBc")
        for half in range(2):
            evm = [None] * 4
            for kt in range(ktiles):
                buf = kt % 2
                if len(evm_hist) >= 2:
                    b.wait("sync", evm_hist[-2])
                e_vl = b.dma("sync", "dV2%d" % buf, vstream[:, buf * 512:(buf + 1) * 512],
                             v_dram[kt, :, half * 512:(half + 1) * 512])
                b.wait("tensor", e_vl, e_rbc, ev_ptq_last)
                if kt == 0:
                    for dc in range(4):
                        b.wait("tensor", ev_o_ev[dc])
                for dc in range(4):
                    evm[dc] = b.op("tensor", lambda e, kt=kt, dc=dc, buf=buf, ktiles=ktiles: e.matmul(
                        ps_o[dc][:, 0:512],
                        vstream[:, buf * 512 + dc * 128: buf * 512 + (dc + 1) * 128],
                        pt_buf[:, kt * 512:(kt + 1) * 512],
                        start=(kt == 0), stop=(kt == ktiles - 1)), "eT")
                evm_hist.append(evm[3])
            ev_pt_used = evm[3]
            for dc in range(4):
                gdc = half * 4 + dc
                b.wait("vector", evm[dc], e_rbc, ev_dO_prev)
                e_oev = b.op("vector", lambda e, dc=dc, gdc=gdc: e.tensor_mul(
                    ar3[:, gdc * 512:(gdc + 1) * 512], ps_o[dc][:, 0:512], rs_bc[:, :]), "eV")
                ev_o_ev[dc] = e_oev
                b.wait("gpsimd", e_oev)
                ev_dO_prev = b.dma("gpsimd", "dO", s1_out[gdc, :, qb * 512:(qb + 1) * 512],
                                   ar3[:, gdc * 512:(gdc + 1) * 512])
        b.wait("sync", *ev_o_ev)

    b.wait("gpsimd", Ev("dO", b.cnt.get("dO", 0)), Ev("dD", b.cnt.get("dD", 0)))
    return b



def _identity_bf16():
    import ml_dtypes
    return np.eye(128, dtype=ml_dtypes.bfloat16)


def _l4_dmask():
    m = np.zeros((4, 128, 512), dtype=np.float32)
    r = np.arange(128)[:, None]
    k = np.arange(512)[None, :]
    for slot in range(4):
        m[slot][k > slot * 128 + r] = NEG
    return m


def _l4_prefmask(first_half):
    import ml_dtypes
    m = np.zeros((128, 4096), dtype=np.float32)
    if first_half:
        m[:, :] = NEG
    return m.astype(ml_dtypes.bfloat16)


def run_launch4(xL1_full, dbg=False):
    nc = None
    b = build_launch4(dbg)
    nc = b.finalize()
    in_maps = []
    for c in range(NCORES):
        bi, h = c // 2, c % 2
        if h == 1:
            x = xL1_full[bi]
        else:
            x = np.concatenate([np.zeros((OWN, D), np.float32), xL1_full[bi, :OWN]], 0)
        in_maps.append({
            "xL1": np.ascontiguousarray(x),
            "Wq": _G["L1_attn_Wq"], "Wk": _G["L1_attn_Wk"], "Wv": _G["L1_attn_Wv"],
            "identb_i": _identity_bf16(),
            "dmask_i": _l4_dmask(),
            "prefmask": _l4_prefmask(h == 0),
        })
    res = run_bass_kernel_spmd(nc, in_maps, core_ids=list(range(NCORES)))
    run_launch4.res = res
    out = np.zeros((B, S, D), np.float32)
    for c in range(NCORES):
        bi, h = c // 2, c % 2
        s1 = res.results[c]["s1"]
        rows = np.zeros((OWN, D), np.float32)
        for dc in range(8):
            rows[:, dc * 128:(dc + 1) * 128] = s1[dc].T
        out[bi, h * OWN:(h + 1) * OWN] = rows
    return out





def build_launch5(dbg=False):
    b = Builder()

    xL1 = b.dram("xL1", [OWN, D], F32, kind="ExternalInput").ap()
    xL1f = b.dram("xL1f", [8, 128, OWN], F32, kind="ExternalInput").ap()
    s1f = b.dram("s1f", [8, 128, OWN], F32, kind="ExternalInput").ap()
    Wg = b.dram("Wg", [D, 8], F32, kind="ExternalInput").ap()
    bg = b.dram("bg", [1, 8], F32, kind="ExternalInput").ap()
    b1 = b.dram("b1", [8, 4096], F32, kind="ExternalInput").ap()
    b2 = b.dram("b2", [8, D], F32, kind="ExternalInput").ap()
    w1b = b.dram("w1b", [8, 32, 8, 128, 128], BF16, kind="ExternalInput").ap()
    w2b = b.dram("w2b", [8, 8, 32, 128, 128], BF16, kind="ExternalInput").ap()
    identb_in = b.dram("identb_i", [128, 128], BF16, kind="ExternalInput").ap()

    out_fm = b.dram("outf", [8, 128, OWN], F32, kind="ExternalOutput").ap()
    if dbg:
        dbg_w = b.dram("dbg_w", [128, 8], F32, kind="ExternalOutput").ap()
        dbg_g = b.dram("dbg_g", [128, 1024], F32, kind="ExternalOutput").ap()
        dbg_acc = b.dram("dbg_acc", [128, 1024], F32, kind="ExternalOutput").ap()
        dbg_wb = b.dram("dbg_wb", [128, 1024], F32, kind="ExternalOutput").ap()
        dbg_xf = b.dram("dbg_xf", [128, 1024], F32, kind="ExternalOutput").ap()
        dbg_w1 = b.dram("dbg_w1", [128, 1024], F32, kind="ExternalOutput").ap()

        dbg_g0 = b.dram("dbg_g0", [128, 1024], F32, kind="ExternalOutput").ap()
        dbg_a0 = b.dram("dbg_a0", [128, 1024], F32, kind="ExternalOutput").ap()

    w_dram = b.dram("w_dram", [8, OWN], F32).ap()

    NCH = 4
    identb = b.sbuf("identb", [128, 128], BF16)
    x_t = [b.sbuf("x_t0", [128, D], F32), b.sbuf("x_t1", [128, D], F32)]
    scr = {"m": b.sbuf("scr_m", [128, 1], F32)[:, :],
           "ss": b.sbuf("scr_ss", [128, 1], F32)[:, :],
           "rstd": b.sbuf("scr_rstd", [128, 1], F32)[:, :],
           "sq": b.sbuf("scr_sq", [128, D], F32)[:, :]}
    xn_t16 = b.sbuf("xn_t16", [128, D], BF16)
    xn_fm = b.sbuf("xn_fm", [128, 8 * 1024], BF16)
    wg_f = b.sbuf("wg_f", [128, 8 * 8], F32)
    wg_s = b.sbuf("wg_s", [128, 8 * 8], BF16)
    bg_bc = b.sbuf("bg_bc", [128, 8], F32)
    b1_sb = b.sbuf("b1_sb", [128, 8 * 32], F32)
    b2_sb = b.sbuf("b2_sb", [128, 8 * 8], F32)
    gt = b.sbuf("gt", [128, 8], F32)
    m1 = b.sbuf("m1", [128, 1], F32)
    m2 = b.sbuf("m2", [128, 1], F32)
    eqm = b.sbuf("eqm", [128, 8], F32)
    g2s = b.sbuf("g2s", [128, 8], F32)
    msk = b.sbuf("msk", [128, 8], F32)
    gm = b.sbuf("gm", [128, 8], F32)
    den = b.sbuf("den", [128, 1], F32)
    wts = b.sbuf("wts", [128, 8], F32)
    w1_sb = [b.sbuf("w1_sb0", [128, 8 * 128], BF16), b.sbuf("w1_sb1", [128, 8 * 128], BF16)]
    w2_sb = [b.sbuf("w2_sb0", [128, 32 * 128], BF16), b.sbuf("w2_sb1", [128, 32 * 128], BF16)]
    g_fm = b.sbuf("g_fm", [128, 32 * 1024], BF16)
    acc = b.sbuf("acc", [128, 8 * 1024], F32)
    tmp = b.sbuf("tmp", [128, 8 * 1024], F32)
    wb = [b.sbuf("wb0", [128, 1024], F32), b.sbuf("wb1", [128, 1024], F32)]
    s1_st = b.sbuf("s1_st", [128, 1024], F32)
    xf_st = b.sbuf("xf_st", [128, 1024], F32)
    out_st = b.sbuf("out_st", [128, 1024], F32)
    scr["sq2"] = None
    ps_tr = b.psum("ps_tr", [128, 1024], BF16)
    ps_g = b.psum("ps_g", [128, 512], F32)
    ps_h = [b.psum("ps_h0", [128, 512], F32), b.psum("ps_h1", [128, 512], F32)]
    ps_e = [b.psum("ps_e0", [128, 512], F32), b.psum("ps_e1", [128, 512], F32)]

    scr["sq2"] = tmp[:, 0:D]
    e_id = b.dma("sync", "dW", identb[:, :], identb_in)
    e_wg = b.dma("sync", "dW", wg_f[:, :].rearrange("p (c n) -> p c n", n=8),
                 Wg.rearrange("(c p) n -> p c n", p=128))
    e_bg = b.dma("sync", "dW", bg_bc[:, :], bg.broadcast_to((128, 8)))
    e_b1 = b.dma("sync", "dW", b1_sb[:, :].rearrange("p (e h) -> p e h", h=32),
                 b1.rearrange("e (h p) -> p e h", p=128),
                 allow_slow_non_contiguous=True)
    e_b2 = b.dma("sync", "dW", b2_sb[:, :].rearrange("p (e c) -> p e c", c=8),
                 b2.rearrange("e (c p) -> p e c", p=128),
                 allow_slow_non_contiguous=True)
    e_id = e_wg = e_bg = e_b1 = e_b2 = b.total("dW")
    b.wait("vector", e_wg)
    e_cwg = b.op("vector", lambda e: e.tensor_copy(wg_s[:, :], wg_f[:, :]), "eV")

    ev_xread = [None, None]
    e_tr_prev = None
    ev_fm_used = None
    e_wdma_prev = None
    ev_g_ev = None
    e_sm_prev = None
    ev_w1l = [None, None]
    ev_w2l = [None, None]
    ev_wbl = [None, None]
    ev_gelu = {}
    ev_h_used = [None, None]
    ev_e_used = [None, None]
    ev_acc_rd = None
    ev_tmp_rd = None
    e_out_dma = None
    ev_gfm_rd = None

    for cc in range(NCH):
        for j in range(8):
            i = cc * 8 + j
            bi_ = i % 2
            b.wait("sync", ev_xread[bi_])
            e_x = b.dma("sync", "dA%d" % bi_, x_t[bi_], xL1[i * 128:(i + 1) * 128, :])
            ev_ln, ev16, ev_xread[bi_] = ln_tile(
                b, x_t[bi_], scr["sq"], scr, D, e_x,
                out16_ap=xn_t16, ev_out16_free=[e_tr_prev])
            b.wait("tensor", ev16, e_id)
            e_tr = None
            for c in range(8):
                e_tr = b.op("tensor", lambda e, c=c: e.transpose(
                    ps_tr[:, c * 128:(c + 1) * 128],
                    xn_t16[:, c * 128:(c + 1) * 128], identb[:, :]), "eT")
            e_tr_prev = e_tr
            b.wait("vector", e_tr, ev_fm_used)
            e_fm = b.op("vector", lambda e, j=j: e.tensor_copy(
                xn_fm[:, :].rearrange("p (c t) -> p c t", t=1024)[:, :, j * 128:(j + 1) * 128],
                ps_tr[:, 0:1024].rearrange("p (c t) -> p c t", t=128)), "eV")
            b.wait("tensor", e_fm, e_cwg, ev_g_ev)
            eg = None
            for c in range(8):
                eg = b.op("tensor", lambda e, c=c, j=j: e.matmul(
                    ps_g[:, 0:8],
                    xn_fm[:, c * 1024 + j * 128: c * 1024 + (j + 1) * 128],
                    wg_s[:, c * 8:(c + 1) * 8],
                    start=(c == 0), stop=(c == 7)), "eT")
            V = "vector"
            b.wait(V, eg, e_bg, e_sm_prev)
            e1 = b.op(V, lambda e: e.tensor_add(gt[:, :], ps_g[:, 0:8], bg_bc[:, :]), "eV")
            ev_g_ev = e1
            b.wait(V, e1)
            e2 = b.op(V, lambda e: e.reduce_max(m1[:, :], gt[:, :], axis=AX.X, negate=True), "eV")
            b.wait("scalar", e1, e2, e_sm_prev)
            e3 = b.op("scalar", lambda e: e.activation(gt[:, :], gt[:, :], AF.Exp,
                                                       bias=m1[:, :], accum_out=den[:, :]), "eS")
            b.wait(V, e3)
            e4 = b.op(V, lambda e: e.reciprocal(den[:, :], den[:, :]), "eV")
            b.wait(V, e4)
            e5 = b.op(V, lambda e: e.tensor_scalar_mul(gt[:, :], gt[:, :], den[:, :]), "eV")
            b.wait(V, e5)
            e6 = b.op(V, lambda e: e.reduce_max(m1[:, :], gt[:, :], axis=AX.X), "eV")
            b.wait(V, e6)
            e7 = b.op(V, lambda e: e.tensor_scalar(eqm[:, :], gt[:, :], m1[:, :], None, ALU.is_ge), "eV")
            b.wait(V, e7)
            e8 = b.op(V, lambda e: e.scalar_tensor_tensor(g2s[:, :], eqm[:, :], NEG, gt[:, :],
                                                          ALU.mult, ALU.add), "eV")
            b.wait(V, e8)
            e9 = b.op(V, lambda e: e.reduce_max(m2[:, :], g2s[:, :], axis=AX.X), "eV")
            b.wait(V, e9)
            e10 = b.op(V, lambda e: e.tensor_scalar(msk[:, :], gt[:, :], m2[:, :], None, ALU.is_ge), "eV")
            b.wait(V, e10)
            e11 = b.op(V, lambda e: e.tensor_mul(gm[:, :], gt[:, :], msk[:, :]), "eV")
            b.wait(V, e11)
            e12 = b.op(V, lambda e: e.reduce_sum(den[:, :], gm[:, :], axis=AX.X), "eV")
            b.wait(V, e12)
            e13 = b.op(V, lambda e: e.tensor_scalar_max(den[:, :], den[:, :], 1e-12), "eV")
            b.wait(V, e13)
            e14 = b.op(V, lambda e: e.reciprocal(den[:, :], den[:, :]), "eV")
            b.wait(V, e14)
            b.wait(V, e_wdma_prev)
            e_sm_prev = b.op(V, lambda e: e.tensor_scalar_mul(wts[:, :], gm[:, :], den[:, :]), "eV")
            if dbg and i == 0:
                b.wait("gpsimd", e_sm_prev)
                e_tap = b.dma("gpsimd", "dD", dbg_w, wts[:, :])
                b.wait("vector", e_tap)
            b.wait("gpsimd", e_sm_prev)
            e_wdma_prev = b.dma_excl("gpsimd", "dB", w_dram[:, i * 128:(i + 1) * 128].rearrange("e t -> t e"),
                                     wts[:, :], allow_slow_non_contiguous=True)
        for e_i in range(8):
            pb = e_i % 2
            b.wait("sync", e_wdma_prev, ev_wbl[pb])
            ev_wb = b.dma("sync", "dB2", wb[pb][:, :],
                          w_dram[e_i:e_i + 1, cc * 1024:(cc + 1) * 1024].broadcast_to((128, 1024)))
            if dbg and cc == 0 and e_i == 0:
                b.wait("gpsimd", ev_wb)
                e_tap = b.dma("gpsimd", "dD", dbg_wb, wb[pb][:, :])
                b.wait("sync", e_tap)
            for hc in range(32):
                wbuf = hc % 2
                b.wait("sync", ev_w1l[wbuf])
                e_w1 = b.dma("sync", "dWs%d" % wbuf, w1_sb[wbuf][:, :].rearrange("p (c m) -> p c m", m=128),
                             w1b[e_i, hc].rearrange("c p m -> p c m"))
                if dbg and cc == 0 and e_i == 0 and hc == 0:
                    b.wait("vector", e_w1)
                    et1 = b.op("vector", lambda e: e.tensor_copy(s1_st[:, :], xn_fm[:, 0:1024]), "eV")
                    b.wait("gpsimd", et1)
                    e_tap = b.dma("gpsimd", "dD", dbg_xf, s1_st[:, :])
                    b.wait("vector", e_tap)
                    et2 = b.op("vector", lambda e: e.tensor_copy(s1_st[:, :], w1_sb[0][:, :]), "eV")
                    b.wait("gpsimd", et2)
                    e_tap = b.dma("gpsimd", "dD", dbg_w1, s1_st[:, :])
                    b.wait("vector", e_tap)
                for st in range(2):
                    b.wait("tensor", e_w1, ev_h_used[st])
                    em = None
                    for c in range(8):
                        em = b.op("tensor", lambda e, c=c, st=st, wbuf=wbuf: e.matmul(
                            ps_h[st][:, 0:512],
                            w1_sb[wbuf][:, c * 128:(c + 1) * 128],
                            xn_fm[:, c * 1024 + st * 512: c * 1024 + st * 512 + 512],
                            start=(c == 0), stop=(c == 7)), "eT")
                    ev_w1l[wbuf] = em
                    b.wait("scalar", em, ev_gfm_rd)
                    ev_h_used[st] = b.op("scalar", lambda e, st=st, hc=hc, e_i=e_i: e.activation(
                        g_fm[:, hc * 1024 + st * 512: hc * 1024 + st * 512 + 512],
                        ps_h[st][:, 0:512], AF.Gelu,
                        bias=b1_sb[:, e_i * 32 + hc: e_i * 32 + hc + 1]), "eS")
            if dbg and cc == 0 and e_i == 0:
                b.wait("gpsimd", ev_h_used[0], ev_h_used[1])
                e_tap = b.dma("gpsimd", "dD", dbg_g0, g_fm[:, 0:1024])
                b.wait("scalar", e_tap)
            ev_gelu_all = ev_h_used[1]
            for dc in range(8):
                wbuf = dc % 2
                b.wait("sync", ev_w2l[wbuf])
                e_w2 = b.dma("sync", "dWs2%d" % wbuf, w2_sb[wbuf][:, :].rearrange("p (c m) -> p c m", m=128),
                             w2b[e_i, dc].rearrange("c p m -> p c m"))
                for st in range(2):
                    b.wait("tensor", e_w2, ev_h_used[0], ev_h_used[1], ev_e_used[st])
                    em = None
                    for hc in range(32):
                        em = b.op("tensor", lambda e, hc=hc, st=st, wbuf=wbuf: e.matmul(
                            ps_e[st][:, 0:512],
                            w2_sb[wbuf][:, hc * 128:(hc + 1) * 128],
                            g_fm[:, hc * 1024 + st * 512: hc * 1024 + st * 512 + 512],
                            start=(hc == 0), stop=(hc == 31)), "eT")
                    ev_w2l[wbuf] = em
                    ev_gfm_rd = em
                    dst = acc if e_i == 0 else tmp
                    b.wait("vector", em, ev_wb, ev_acc_rd if e_i == 0 else ev_tmp_rd)
                    e_ev = b.op("vector", lambda e, dc=dc, st=st, e_i=e_i, dst=dst, pb=pb: e.scalar_tensor_tensor(
                        dst[:, dc * 1024 + st * 512: dc * 1024 + st * 512 + 512],
                        ps_e[st][:, 0:512],
                        b2_sb[:, e_i * 8 + dc: e_i * 8 + dc + 1],
                        wb[pb][:, st * 512:(st + 1) * 512],
                        ALU.add, ALU.mult), "eV")
                    ev_e_used[st] = e_ev
                    if e_i > 0:
                        b.wait("gpsimd", e_ev)
                        ev_acc_rd = b.op("gpsimd", lambda e, dc=dc, st=st: e.tensor_tensor(
                            acc[:, dc * 1024 + st * 512: dc * 1024 + st * 512 + 512],
                            acc[:, dc * 1024 + st * 512: dc * 1024 + st * 512 + 512],
                            tmp[:, dc * 1024 + st * 512: dc * 1024 + st * 512 + 512],
                            ALU.add), "eP")
                        ev_tmp_rd = ev_acc_rd
            if dbg and cc == 0 and e_i == 0:
                b.wait("gpsimd", ev_e_used[0], ev_e_used[1])
                e_tap = b.dma("gpsimd", "dD", dbg_a0, acc[:, 0:1024])
                b.wait("vector", e_tap)
            ev_wbl[pb] = ev_e_used[1]
            ev_fm_used = ev_w1l[1] or ev_w1l[0]
        if dbg and cc == 0:
            b.wait("gpsimd", ev_acc_rd)
            e_tap = b.dma("gpsimd", "dD", dbg_g, g_fm[:, 0:1024])
            e_tap2 = b.dma("gpsimd", "dD", dbg_acc, acc[:, 0:1024])
            b.wait("scalar", e_tap2)
            b.wait("vector", e_tap2)
        for dc in range(8):
            b.wait("scalar", ev_acc_rd, ev_tmp_rd)
            e_si = b.op("scalar", lambda e, dc=dc: e.activation(
                tmp[:, dc * 1024:(dc + 1) * 1024], acc[:, dc * 1024:(dc + 1) * 1024],
                AF.Silu), "eS")
            e_s1 = b.dma("sync", "dA2", s1_st[:, :], s1f[dc, :, cc * 1024:(cc + 1) * 1024])
            e_xf = b.dma("sync", "dA3", xf_st[:, :], xL1f[dc, :, cc * 1024:(cc + 1) * 1024])
            b.wait("vector", e_si, e_s1, e_out_dma)
            e_m = b.op("vector", lambda e, dc=dc: e.tensor_mul(
                out_st[:, :], tmp[:, dc * 1024:(dc + 1) * 1024], s1_st[:, :]), "eV")
            b.wait("vector", e_m, e_xf)
            e_o = b.op("vector", lambda e: e.tensor_add(out_st[:, :], out_st[:, :], xf_st[:, :]), "eV")
            b.wait("gpsimd", e_o)
            e_out_dma = b.dma("gpsimd", "dO", out_fm[dc, :, cc * 1024:(cc + 1) * 1024], out_st[:, :])
            ev_tmp_rd = e_si if dc == 7 else ev_tmp_rd
            ev_acc_rd = e_si if dc == 7 else ev_acc_rd
            b.wait("sync", e_m)
            b.wait("sync", e_o)

    b.wait("gpsimd", Ev("dO", b.cnt.get("dO", 0)), Ev("dD", b.cnt.get("dD", 0)))
    return b


def run_launch5(xL1_own_list, s1f_list, w1b=None, w2b=None, dbg=False):
    import ml_dtypes
    b = build_launch5(dbg)
    nc = b.finalize()
    if w1b is None:
        W1 = _G["L1_ffn_W1"]; W2 = _G["L1_ffn_W2"]
        w1b = np.ascontiguousarray(
            W1.reshape(8, 8, 128, 32, 128).transpose(0, 3, 1, 2, 4)).astype(ml_dtypes.bfloat16)
        w2b = np.ascontiguousarray(
            W2.reshape(8, 32, 128, 8, 128).transpose(0, 3, 1, 2, 4)).astype(ml_dtypes.bfloat16)
    in_maps = []
    for c in range(NCORES):
        x = xL1_own_list[c]
        xf = np.ascontiguousarray(x.T.reshape(8, 128, OWN))
        in_maps.append({
            "xL1": x, "xL1f": xf, "s1f": s1f_list[c],
            "Wg": _G["L1_ffn_Wg"], "bg": _G["L1_ffn_bg"].reshape(1, 8),
            "b1": _G["L1_ffn_b1"], "b2": _G["L1_ffn_b2"],
            "w1b": w1b, "w2b": w2b,
            "identb_i": _identity_bf16(),
        })
    res = run_bass_kernel_spmd(nc, in_maps, core_ids=list(range(NCORES)))
    run_launch5.res = res
    outs = []
    for c in range(NCORES):
        of = res.results[c]["outf"]
        rows = np.zeros((OWN, D), np.float32)
        for dc in range(8):
            rows[:, dc * 128:(dc + 1) * 128] = of[dc].T
        outs.append(rows)
    return outs



def build_launch1(dbg=False):
    b = Builder()
    NT1 = 33
    KV = 4128

    x_in = b.dram("x", [NT1 * 128, D], F32, kind="ExternalInput").ap()
    P_in = b.dram("Pp", [32, D], F32, kind="ExternalInput").ap()
    mem_W = b.dram("mem_W", [D, D], F32, kind="ExternalInput").ap()
    Wi = b.dram("Wi", [D, D], F32, kind="ExternalInput").ap()
    Wl1 = b.dram("Wl1", [D, D], F32, kind="ExternalInput").ap()
    Wl2 = b.dram("Wl2", [D, D], F32, kind="ExternalInput").ap()
    bi_in = b.dram("bi", [1, D], F32, kind="ExternalInput").ap()
    bl1_in = b.dram("bl1", [1, D], F32, kind="ExternalInput").ap()
    bl2_in = b.dram("bl2", [1, D], F32, kind="ExternalInput").ap()
    memb_in = b.dram("memb", [1, D], F32, kind="ExternalInput").ap()
    g1_in = b.dram("g1", [1, D], F32, kind="ExternalInput").ap()
    be1_in = b.dram("be1", [1, D], F32, kind="ExternalInput").ap()
    wmask_in = b.dram("wmask_i", [2, 128, 160], F32, kind="ExternalInput").ap()
    identb_in = b.dram("identb_i", [128, 128], BF16, kind="ExternalInput").ap()

    x1_out = b.dram("x1", [OWN, D], F32, kind="ExternalOutput").ap()
    hdd2f_out = b.dram("hdd2f", [8, 128, KV], BF16, kind="ExternalOutput").ap()
    gradp_out = b.dram("gradp", [8, 128, D], F32, kind="ExternalOutput").ap()

    kn_dram = b.dram("kn_dram", [8, 128, KV], BF16).ap()
    if dbg:
        dbg_xn = b.dram("dbg_xn", [128, D], F32, kind="ExternalOutput").ap()
        dbg_sr = b.dram("dbg_sr", [128, 160], F32, kind="ExternalOutput").ap()
        dbg_at = b.dram("dbg_at", [128, D], F32, kind="ExternalOutput").ap()
        dbg_kn = b.dram("dbg_kn", [128, D], F32, kind="ExternalOutput").ap()
        dbg_fm = b.dram("dbg_fm", [128, 160], F32, kind="ExternalOutput").ap()
        dbg_knr = b.dram("dbg_knr", [128, 1024], F32, kind="ExternalOutput").ap()
        dbg_kp = b.dram("dbg_kp", [128, 1024], F32, kind="ExternalOutput").ap()
        dbg_h1 = b.dram("dbg_h1", [128, 512], F32, kind="ExternalOutput").ap()
        dbg_w1t = b.dram("dbg_w1t", [128, 1024], F32, kind="ExternalOutput").ap()

    identb = b.sbuf("identb", [128, 128], BF16)
    big = b.sbuf("big", [128, 8 * NT1 * 128], BF16)
    grad_acc = b.sbuf("grad_acc", [128, 8 * D], F32)
    kn_st = [b.sbuf("kn_st0", [128, 8 * 512], BF16), b.sbuf("kn_st1", [128, 8 * 512], BF16)]
    kp_f = b.sbuf("kp_f", [128, 8 * 512], F32)
    kp_16 = b.sbuf("kp_16", [128, 8 * 512], BF16)
    h1_16 = b.sbuf("h1_16", [128, 8 * 512], BF16)
    h2_16 = b.sbuf("h2_16", [128, 8 * 512], BF16)
    df_16 = b.sbuf("df_16", [128, 8 * 512], BF16)
    hT = b.sbuf("hT", [128, 4 * D], BF16)
    dT = b.sbuf("dT", [128, 4 * D], BF16)
    bias_c = b.sbuf("bias_c", [128, 4 * 8], F32)
    wmask = b.sbuf("wmask", [128, 2 * 160], F32)
    scr = {"m": b.sbuf("scr_m", [128, 1], F32)[:, :],
           "ss": b.sbuf("scr_ss", [128, 1], F32)[:, :],
           "rstd": b.sbuf("scr_rstd", [128, 1], F32)[:, :],
           "sq": b.sbuf("scr_sq", [128, D], F32)[:, :]}
    rk = b.sbuf("rk", [128, 1], F32)
    nmax = b.sbuf("nmax", [128, 1], F32)
    sumr = b.sbuf("sumr", [128, 1], F32)
    rsum = b.sbuf("rsum", [128, 1], F32)
    x_t = [grad_acc[:, 0:D], grad_acc[:, D:2 * D]]
    xn_roll = [grad_acc[:, 2 * D:3 * D], grad_acc[:, 3 * D:4 * D]]
    attn_f = grad_acc[:, 4 * D:5 * D]
    g1b = grad_acc[:, 5 * D:6 * D]
    be1b = grad_acc[:, 6 * D:7 * D]
    srow = grad_acc[:, 7 * D:7 * D + 160]
    xn_t16 = kn_st[0][:, 0:D]
    kn_t16 = kn_st[0][:, D:2 * D]
    s32a = kn_st[0][:, 2 * D:3 * D]
    s32b = kn_st[0][:, 3 * D:4 * D]
    kn_fmt = kn_st[1][:, 0:D]
    p16 = kn_st[1][:, D:D + 160]
    pt1 = kn_st[1][:, D + 160:D + 288]
    pt2 = kn_st[1][:, D + 288:D + 416]
    scr["sq2"] = kp_f[:, 0:D]

    ps_tr = b.psum("ps_tr", [128, 1024], BF16)
    ps_a = b.psum("ps_a", [128, 512], F32)
    ps_b = b.psum("ps_b", [128, 512], F32)
    ps_c = b.psum("ps_c", [128, 512], F32)
    ps_d = b.psum("ps_d", [128, 512], F32)

    def wT_ap(w, kc, mc, width):
        base = w * 8192 + kc * 1024 + mc * 128
        return big[:, base: base + width]

    def wT_wview(w, mc):
        return big[:, w * 8192:(w + 1) * 8192].rearrange(
            "p (k t) -> p k t", t=1024)[:, :, mc * 128:(mc + 1) * 128]

    b.dma("sync", "dW", identb[:, :], identb_in)
    b.dma("sync", "dW", g1b, g1_in.broadcast_to((128, D)))
    b.dma("sync", "dW", be1b, be1_in.broadcast_to((128, D)))
    b.dma("sync", "dW", wmask[:, :].rearrange("p (s n) -> p s n", n=160),
          wmask_in.rearrange("s p n -> p s n"))
    for w, src_ in enumerate([bi_in, bl1_in, bl2_in, memb_in]):
        b.dma("sync", "dW", bias_c[:, w * 8:(w + 1) * 8],
              src_.rearrange("o (c p) -> p (o c)", p=128), allow_slow_non_contiguous=True)
    ev_const = b.total("dW")

    ev_xread = [None, None]
    e_tr_prev = None
    ev_kn_dma = None
    e_fm_prev = None
    ev_sr = None
    e_exp_prev = None
    ev_ptrd = None
    ev_pv = [None, None]
    ev_x1dma = None
    e_attn_ev = None
    ev_kn16_rd = None
    ev_xnroll_rd = [None, None]
    e_rc = None
    ev_attnf_rd = None
    ev_kncopy = None
    fmv = big[:, :].rearrange("p (c t) -> p c t", t=NT1 * 128)
    for i in range(NT1):
        bi_ = i % 2
        b.wait("sync", ev_xread[bi_])
        e_x = b.dma("sync", "dA%d" % bi_, x_t[bi_], x_in[i * 128:(i + 1) * 128, :])
        ev_ln, ev16, ev_xread[bi_] = ln_tile(
            b, x_t[bi_], xn_roll[bi_], scr, D, e_x,
            out16_ap=xn_t16, ev_out16_free=[e_tr_prev],
            ev_out_free=[ev_xnroll_rd[bi_]])
        if dbg and i == 6:
            b.wait("gpsimd", ev_ln)
            e_tap = b.dma("gpsimd", "dD", dbg_xn, xn_roll[bi_])
            b.wait("vector", e_tap)
        if i >= 1:
            V = "vector"
            e1 = b.op(V, lambda e: e.tensor_scalar(rk[:, :], scr["ss"], 1e-5, float(D),
                                                   ALU.subtract, ALU.mult), "eV")
            b.wait("scalar", e1)
            e2 = b.op("scalar", lambda e: e.activation(rk[:, :], rk[:, :], AF.Sqrt), "eS")
            b.wait(V, e2)
            e3 = b.op(V, lambda e: e.tensor_scalar_max(rk[:, :], rk[:, :], 1e-12), "eV")
            b.wait(V, e3)
            e4 = b.op(V, lambda e: e.reciprocal(rk[:, :], rk[:, :]), "eV")
            b.wait(V, e4)
            e5 = b.op(V, lambda e: e.tensor_scalar_mul(rk[:, :], rk[:, :], scr["rstd"]), "eV")
            b.wait("scalar", e5, ev_kn16_rd)
            e_kn = b.op("scalar", lambda e: e.activation(kn_t16, scr["sq"], AF.Copy,
                                                         scale=rk[:, :]), "eS")
            ev_kncopy = e_kn
            if dbg and i == 6:
                b.wait("gpsimd", e_kn)
                e_tap = b.dma("gpsimd", "dD", dbg_kn, kn_t16)
                b.wait("tensor", e_tap)
                b.wait("scalar", e_tap)
            b.wait("tensor", e_kn, ev_const, e_fm_prev, ev_ptrd)
            e_trk = None
            for c in range(8):
                e_trk = b.op("tensor", lambda e, c=c: e.transpose(
                    ps_tr[:, c * 128:(c + 1) * 128], kn_t16[:, c * 128:(c + 1) * 128],
                    identb[:, :]), "eT")
            ev_kn16_rd = e_trk
            b.wait("vector", e_trk, ev_kn_dma)
            e_knc = b.op("vector", lambda e: e.tensor_copy(kn_fmt, ps_tr[:, 0:1024]), "eV")
            ev_ptrd = e_knc
            b.wait("gpsimd", e_knc)
            ev_kn_dma = b.dma("gpsimd", "dKn",
                              kn_dram[:, :, (i - 1) * 128: i * 128].rearrange("c p t -> p c t"),
                              kn_fmt.rearrange("p (c t) -> p c t", t=128))
        b.wait("tensor", ev16, ev_const, e_fm_prev, ev_ptrd)
        e_trx = None
        for c in range(8):
            e_trx = b.op("tensor", lambda e, c=c: e.transpose(
                ps_tr[:, c * 128:(c + 1) * 128], xn_t16[:, c * 128:(c + 1) * 128],
                identb[:, :]), "eT")
        e_tr_prev = e_trx
        b.wait("vector", e_trx)
        e_fm = b.op("vector", lambda e, i=i: e.tensor_copy(
            fmv[:, :, i * 128:(i + 1) * 128],
            ps_tr[:, 0:1024].rearrange("p (c t) -> p c t", t=128)), "eV")
        e_fm_prev = e_fm
        ev_ptrd = e_fm

        if i >= 1:
            j = i - 1
            qc0 = 128 + j * 128
            kc0 = j * 128 + 96
            b.wait("tensor", e_fm, e_attn_ev)
            em = None
            for c in range(8):
                em = b.op("tensor", lambda e, c=c, qc0=qc0, kc0=kc0: e.matmul(
                    ps_a[:, 0:160], fmv[:, c, qc0:qc0 + 128], fmv[:, c, kc0:kc0 + 160],
                    start=(c == 0), stop=(c == 7)), "eT")
            slot = 0 if j == 0 else 1
            b.wait("vector", em, ev_sr, e_exp_prev)
            e_sr = b.op("vector", lambda e, slot=slot: e.tensor_add(
                srow, ps_a[:, 0:160], wmask[:, slot * 160:(slot + 1) * 160]), "eV")
            if dbg and j == 5:
                b.wait("gpsimd", e_sr)
                e_tap = b.dma("gpsimd", "dD", dbg_sr, srow)
                b.wait("vector", e_tap)
                e_t2 = b.op("vector", lambda e, kc0=kc0: e.tensor_copy(
                    srow, fmv[0:128, 0, kc0:kc0 + 160]), "eV")
                b.wait("gpsimd", e_t2)
                e_tap = b.dma("gpsimd", "dD", dbg_fm, srow)
                b.wait("vector", e_tap)
                e_t3 = b.op("vector", lambda e: e.tensor_add(
                    srow, ps_a[:, 0:160], wmask[:, 160:320]), "eV")
                b.wait("vector", e_t3)
            ev_sr = e_sr
            e_attn_ev = e_sr
            b.wait("vector", e_sr)
            e_mx = b.op("vector", lambda e: e.reduce_max(nmax[:, :], srow,
                                                         axis=AX.X, negate=True), "eV")
            b.wait("vector", e_mx)
            e_mx2 = b.op("vector", lambda e: e.tensor_scalar_mul(nmax[:, :], nmax[:, :], 1.0 / 32.0), "eV")
            b.wait("scalar", e_mx2, e_rc)
            e_exp = b.op("scalar", lambda e: e.activation(
                p16, srow, AF.Exp, bias=nmax[:, :], scale=1.0 / 32.0,
                accum_out=sumr[:, :]), "eS")
            e_exp_prev = e_exp
            b.wait("vector", e_exp)
            e_rc = b.op("vector", lambda e: e.reciprocal(rsum[:, :], sumr[:, :]), "eV")
            b.wait("tensor", e_exp, ev_pv[0], ev_pv[1], ev_ptrd)
            et1 = b.op("tensor", lambda e: e.transpose(ps_tr[:, 0:128], p16[:, 0:128], identb[:, :]), "eT")
            et2 = b.op("tensor", lambda e: e.transpose(ps_tr[0:32, 128:256], p16[:, 128:160], identb[:, :]), "eT")
            eta = et2
            for c in range(6):
                eta = b.op("tensor", lambda e, c=c, kc0=kc0: e.transpose(
                    ps_tr[:, 256 + c * 128: 256 + (c + 1) * 128], fmv[:, c, kc0:kc0 + 128],
                    identb[:, :]), "eT")
            b.wait("vector", et1, eta)
            ec1 = b.op("vector", lambda e: e.tensor_copy(pt1, ps_tr[:, 0:128]), "eV")
            ec2 = b.op("vector", lambda e: e.tensor_copy(pt2[0:32, :], ps_tr[0:32, 128:256]), "eV")
            ec3 = b.op("vector", lambda e: e.tensor_copy(s32a[:, 0:768], ps_tr[:, 256:1024]), "eV")
            b.wait("tensor", ec1, ec2, ec3)
            etb = None
            for c in range(2):
                etb = b.op("tensor", lambda e, c=c, kc0=kc0: e.transpose(
                    ps_tr[:, c * 128:(c + 1) * 128], fmv[:, 6 + c, kc0:kc0 + 128],
                    identb[:, :]), "eT")
            for c in range(6):
                etb = b.op("tensor", lambda e, c=c, kc0=kc0: e.transpose(
                    ps_tr[0:32, 256 + c * 128: 256 + (c + 1) * 128],
                    fmv[:, c, kc0 + 128:kc0 + 160], identb[:, :]), "eT")
            b.wait("vector", etb)
            ec4 = b.op("vector", lambda e: e.tensor_copy(s32a[:, 768:1024], ps_tr[:, 0:256]), "eV")
            ec5 = b.op("vector", lambda e: e.tensor_copy(s32b[0:32, 0:768], ps_tr[0:32, 256:1024]), "eV")
            b.wait("tensor", ec4, ec5)
            etc = None
            for c in range(2):
                etc = b.op("tensor", lambda e, c=c, kc0=kc0: e.transpose(
                    ps_tr[0:32, c * 128:(c + 1) * 128],
                    fmv[:, 6 + c, kc0 + 128:kc0 + 160], identb[:, :]), "eT")
            b.wait("vector", etc)
            ec6 = b.op("vector", lambda e: e.tensor_copy(s32b[0:32, 768:1024], ps_tr[0:32, 0:256]), "eV")
            ev_ptrd = ec6
            b.wait("tensor", ec1, ec2, ec3, ec4, ec5, ec6)
            for h in range(2):
                ph = ps_b if h == 0 else ps_c
                b.op("tensor", lambda e, h=h, ph=ph: e.matmul(
                    ph[:, 0:512], pt1, s32a[:, h * 512:(h + 1) * 512],
                    start=True, stop=False), "eT")
                ev_pv[h] = b.op("tensor", lambda e, h=h, ph=ph: e.matmul(
                    ph[:, 0:512], pt2[0:32, :], s32b[0:32, h * 512:(h + 1) * 512],
                    start=False, stop=True), "eT")
            b.wait("scalar", ev_pv[0], ev_pv[1], e_rc, ev_attnf_rd, ev_x1dma)
            e_at = b.op("scalar", lambda e: e.activation(attn_f[:, 0:512], ps_b[:, 0:512],
                                                         AF.Copy, scale=rsum[:, :]), "eS")
            e_at2 = b.op("scalar", lambda e: e.activation(attn_f[:, 512:1024], ps_c[:, 0:512],
                                                          AF.Copy, scale=rsum[:, :]), "eS")
            if dbg and j == 5:
                b.wait("gpsimd", e_at, e_at2)
                e_tap = b.dma("gpsimd", "dD", dbg_at, attn_f)
                b.wait("vector", e_tap)
                b.wait("scalar", e_tap)
            b.wait("vector", e_at, e_at2)
            e_add = b.op("vector", lambda e, bj=(j + 1) % 2: e.tensor_add(
                attn_f, attn_f, xn_roll[bj]), "eV")
            ev_xnroll_rd[(j + 1) % 2] = e_add
            b.wait("vector", ev_kncopy)
            ev_x1, _, ev_attnf_rd = ln_tile(b, attn_f, attn_f, scr, D, e_add,
                                            g_bc=g1b, be_bc=be1b,
                                            ev_out_free=[ev_x1dma])
            b.wait("gpsimd", ev_x1)
            ev_x1dma = b.dma("gpsimd", "dX1", x1_out[j * 128:(j + 1) * 128, :], attn_f)

    b.barrier()
    e_p = b.dma("sync", "dA0", x_t[0][0:32, :], P_in)
    b.wait("scalar", e_p)
    e_sq = b.op("scalar", lambda e: e.activation(scr["sq2"][0:32, :], x_t[0][0:32, :],
                                                 AF.Square, accum_out=scr["ss"][0:32, :]), "eS")
    b.wait("scalar", e_sq)
    e_sqq = b.op("scalar", lambda e: e.activation(rk[0:32, :], scr["ss"][0:32, :], AF.Sqrt), "eS")
    b.wait("vector", e_sqq)
    e_m1 = b.op("vector", lambda e: e.tensor_scalar_max(rk[0:32, :], rk[0:32, :], 1e-12), "eV")
    b.wait("vector", e_m1)
    e_m2 = b.op("vector", lambda e: e.reciprocal(rk[0:32, :], rk[0:32, :]), "eV")
    b.wait("scalar", e_m2)
    e_knp = b.op("scalar", lambda e: e.activation(kn_t16[0:32, :], x_t[0][0:32, :], AF.Copy,
                                                  scale=rk[0:32, :]), "eS")
    b.wait("tensor", e_knp)
    e_trp = None
    for c in range(8):
        e_trp = b.op("tensor", lambda e, c=c: e.transpose(
            ps_tr[:, c * 32:(c + 1) * 32], kn_t16[0:32, c * 128:(c + 1) * 128],
            identb[0:32, 0:32]), "eT")
    b.wait("vector", e_trp)
    e_kc = b.op("vector", lambda e: e.tensor_copy(
        kn_fmt[:, 0:256].rearrange("p (c t) -> p c t", t=32),
        ps_tr[:, 0:256].rearrange("p (c t) -> p c t", t=32)), "eV")
    b.wait("gpsimd", e_kc)
    b.dma("gpsimd", "dKn", kn_dram[:, :, 4096:4128].rearrange("c p t -> p c t"),
          kn_fmt[:, 0:256].rearrange("p (c t) -> p c t", t=32))

    b.barrier()
    ev_wst = None
    ev_cst_prev = None
    for w, src_ in enumerate([Wi, Wl1, Wl2, mem_W]):
        b.wait("sync", ev_cst_prev)
        e_l = b.dma_excl("sync", "dWl", grad_acc[:, :].rearrange("p (c n) -> p c n", n=D),
                         src_.rearrange("(c p) n -> p c n", p=128))
        b.wait("vector", e_l, ev_wst)
        e_cst = b.op("vector", lambda e: e.tensor_copy(kn_st[0][:, 0:4096], grad_acc[:, 0:4096]), "eV")
        e_cst2 = b.op("vector", lambda e: e.tensor_copy(kn_st[1][:, 0:4096], grad_acc[:, 4096:8192]), "eV")
        ev_cst_prev = e_cst2
        evE = None
        for mc in range(8):
            buf = kn_st[mc // 4]
            off = (mc % 4) * D
            b.wait("tensor", e_cst if mc < 4 else e_cst2, evE)
            e_t = None
            for kc in range(8):
                e_t = b.op("tensor", lambda e, buf=buf, off=off, kc=kc: e.transpose(
                    ps_tr[:, kc * 128:(kc + 1) * 128],
                    buf[:, off + kc * 128: off + (kc + 1) * 128], identb[:, :]), "eT")
            b.wait("vector", e_t)
            evE = b.op("vector", lambda e, w=w, mc=mc: e.tensor_copy(
                wT_wview(w, mc), ps_tr[:, 0:1024].rearrange("p (k t) -> p k t", t=128)), "eV")
        ev_wst = evE

    b.barrier()
    ev_knrd = [None, None]
    e_kpe = None
    e_kp16 = None
    ev_h1 = None
    ev_h2 = None
    ev_hdma = None
    ev_df = None
    ev_grad_mm = None
    e_ga = None
    evcp_prev = None
    ev_h1m = ev_h2m = ev_pdm = None
    ev_htr = ev_dtr = None
    widths = [512] * 8 + [32]
    for tt in range(9):
        wd = widths[tt]
        t0 = tt * 512
        kb = tt % 2
        b.wait("sync", ev_knrd[kb])
        e_kn = b.dma("sync", "dKr%d" % kb,
                     kn_st[kb][:, 0:8 * wd].rearrange("p (c t) -> p c t", t=wd),
                     kn_dram[:, :, t0:t0 + wd].rearrange("c p t -> p c t"))
        if dbg and tt == 0:
            b.wait("vector", e_kn)
            e_t1 = b.op("vector", lambda e: e.tensor_copy(kp_f[:, 0:1024], kn_st[0][:, 0:1024]), "eV")
            b.wait("gpsimd", e_t1)
            e_tap = b.dma("gpsimd", "dD", dbg_knr, kp_f[:, 0:1024])
            b.wait("vector", e_tap)
            e_t2 = b.op("vector", lambda e: e.tensor_copy(kp_f[:, 0:1024], big[:, 8192:9216]), "eV")
            b.wait("gpsimd", e_t2)
            e_tap = b.dma("gpsimd", "dD", dbg_w1t, kp_f[:, 0:1024])
            b.wait("vector", e_tap)
        ev_kp_last = None
        for mc in range(8):
            b.wait("tensor", e_kn, e_kpe)
            em = None
            for kc in range(8):
                em = b.op("tensor", lambda e, mc=mc, kc=kc, kb=kb, wd=wd: e.matmul(
                    ps_a[:, 0:wd], wT_ap(0, kc, mc, 128), kn_st[kb][:, kc * wd:(kc + 1) * wd],
                    start=(kc == 0), stop=(kc == 7)), "eT")
            ev_knrd[kb] = em
            b.wait("vector", em, ev_df, e_kp16)
            e_kpe = b.op("vector", lambda e, mc=mc, wd=wd: e.tensor_scalar(
                kp_f[:, mc * wd:(mc + 1) * wd], ps_a[:, 0:wd],
                bias_c[:, 0 * 8 + mc: 0 * 8 + mc + 1], None, ALU.add), "eV")
            ev_kp_last = e_kpe
        if dbg and tt == 0:
            b.wait("gpsimd", ev_kp_last)
            e_tap = b.dma("gpsimd", "dD", dbg_kp, kp_f[:, 0:1024])
            b.wait("vector", e_tap)
            b.wait("scalar", e_tap)
        b.wait("vector", ev_kp_last, ev_h1m)
        e_kp16 = b.op("vector", lambda e, wd=wd: e.tensor_copy(kp_16[:, 0:8 * wd], kp_f[:, 0:8 * wd]), "eV")
        for mc in range(8):
            b.wait("tensor", e_kp16, ev_h1, ev_h2m)
            em = None
            for kc in range(8):
                em = b.op("tensor", lambda e, mc=mc, kc=kc, wd=wd: e.matmul(
                    ps_b[:, 0:wd], wT_ap(1, kc, mc, 128), kp_16[:, kc * wd:(kc + 1) * wd],
                    start=(kc == 0), stop=(kc == 7)), "eT")
            ev_h1m = em
            b.wait("scalar", em)
            ev_h1 = b.op("scalar", lambda e, mc=mc, wd=wd: e.activation(
                h1_16[:, mc * wd:(mc + 1) * wd], ps_b[:, 0:wd], AF.Silu,
                bias=bias_c[:, 1 * 8 + mc: 1 * 8 + mc + 1]), "eS")
        if dbg and tt == 0:
            b.wait("gpsimd", ev_h1)
            e_tap = b.dma("gpsimd", "dD", dbg_h1, h1_16[:, 0:512])
            b.wait("scalar", e_tap)
            b.wait("tensor", e_tap)
        for mc in range(8):
            b.wait("tensor", ev_h1, ev_h2, ev_pdm, ev_hdma, ev_htr)
            em = None
            for kc in range(8):
                em = b.op("tensor", lambda e, mc=mc, kc=kc, wd=wd: e.matmul(
                    ps_c[:, 0:wd], wT_ap(2, kc, mc, 128), h1_16[:, kc * wd:(kc + 1) * wd],
                    start=(kc == 0), stop=(kc == 7)), "eT")
            ev_h2m = em
            b.wait("scalar", em)
            ev_h2 = b.op("scalar", lambda e, mc=mc, wd=wd: e.activation(
                h2_16[:, mc * wd:(mc + 1) * wd], ps_c[:, 0:wd], AF.Silu,
                bias=bias_c[:, 2 * 8 + mc: 2 * 8 + mc + 1]), "eS")
        b.wait("gpsimd", ev_h2, ev_hdma)
        ev_hdma = b.dma("gpsimd", "dH", hdd2f_out[:, :, t0:t0 + wd].rearrange("c p t -> p c t"),
                        h2_16[:, 0:8 * wd].rearrange("p (c t) -> p c t", t=wd))
        for mc in range(8):
            b.wait("tensor", ev_h2, ev_df)
            em = None
            for kc in range(8):
                em = b.op("tensor", lambda e, mc=mc, kc=kc, wd=wd: e.matmul(
                    ps_d[:, 0:wd], wT_ap(3, kc, mc, 128), h2_16[:, kc * wd:(kc + 1) * wd],
                    start=(kc == 0), stop=(kc == 7)), "eT")
            ev_pdm = em
            b.wait("vector", em, e_kp16, ev_dtr)
            ev_df = b.op("vector", lambda e, mc=mc, wd=wd: e.scalar_tensor_tensor(
                df_16[:, mc * wd:(mc + 1) * wd], ps_d[:, 0:wd],
                bias_c[:, 3 * 8 + mc: 3 * 8 + mc + 1], kp_f[:, mc * wd:(mc + 1) * wd],
                ALU.add, ALU.subtract), "eV")
        if tt == 8:
            b.wait("vector", ev_df)
            ev_df = b.op("vector", lambda e, wd=wd: e.tensor_scalar_mul(
                df_16[:, 0:8 * wd], df_16[:, 0:8 * wd], 0.5), "eV")
        nts = max(1, wd // 128)
        prow = min(wd, 128)
        ev_hT = ev_dT = None
        for src16, dstT in ((h2_16, hT), (df_16, dT)):
            evcp = None
            for ts in range(nts):
                b.wait("tensor", ev_h2 if src16 is h2_16 else ev_df, ev_grad_mm, evcp_prev, ev_ptrd)
                ev_ptrd = None
                e_t = None
                for mc in range(8):
                    e_t = b.op("tensor", lambda e, mc=mc, ts=ts, src16=src16, wd=wd, prow=prow: e.transpose(
                        ps_tr[0:prow, mc * 128:(mc + 1) * 128],
                        src16[:, mc * wd + ts * 128: mc * wd + ts * 128 + prow],
                        identb[:, :]), "eT")
                if src16 is h2_16:
                    ev_htr = e_t
                else:
                    ev_dtr = e_t
                b.wait("vector", e_t)
                evcp = b.op("vector", lambda e, ts=ts, dstT=dstT, prow=prow: e.tensor_copy(
                    dstT[0:prow, ts * D:(ts + 1) * D], ps_tr[0:prow, 0:1024]), "eV")
                evcp_prev = evcp
            if src16 is h2_16:
                ev_hT = evcp
            else:
                ev_dT = evcp
        for hc in range(8):
            for ms in range(2):
                b.wait("tensor", ev_hT, ev_dT, e_ga)
                em = None
                for ts in range(nts):
                    em = b.op("tensor", lambda e, hc=hc, ms=ms, ts=ts, prow=prow: e.matmul(
                        ps_d[:, 0:512], hT[0:prow, ts * D + hc * 128: ts * D + (hc + 1) * 128],
                        dT[0:prow, ts * D + ms * 512: ts * D + (ms + 1) * 512],
                        start=(ts == 0), stop=(ts == nts - 1)), "eT")
                ev_grad_mm = em
                b.wait("vector", em)
                if tt == 0:
                    e_ga = b.op("vector", lambda e, hc=hc, ms=ms: e.tensor_copy(
                        grad_acc[:, hc * D + ms * 512: hc * D + (ms + 1) * 512], ps_d[:, 0:512]), "eV")
                else:
                    e_ga = b.op("vector", lambda e, hc=hc, ms=ms: e.tensor_add(
                        grad_acc[:, hc * D + ms * 512: hc * D + (ms + 1) * 512],
                        grad_acc[:, hc * D + ms * 512: hc * D + (ms + 1) * 512],
                        ps_d[:, 0:512]), "eV")
    b.wait("gpsimd", e_ga)
    b.dma("gpsimd", "dG", gradp_out.rearrange("c p t -> p c t"),
          grad_acc[:, :].rearrange("p (c t) -> p c t", t=D))
    b.wait("gpsimd", b.total("dG"), b.total("dX1"), b.total("dH"), b.total("dKn"),
           b.total("dD"))
    return b




def _l1_wmask(first_half):
    m = np.full((2, 128, 160), NEG, dtype=np.float32)
    r = np.arange(128)[:, None]
    k = np.arange(160)[None, :]
    band = (k >= r + 16) & (k <= r + 32)
    m[1][band] = 0.0
    if first_half:
        m[0][band & (k >= 32)] = 0.0
    else:
        m[0][band] = 0.0
    return m


def run_launch1(x_full, dbg=False):
    b = build_launch1(dbg)
    nc = b.finalize()
    Ppad = np.zeros((32, D), np.float32)
    Ppad[:10] = _G["L0_attn_P"]
    in_maps = []
    for c in range(NCORES):
        bi, h = c // 2, c % 2
        start = h * OWN
        if h == 0:
            xs = np.concatenate([np.zeros((128, D), np.float32), x_full[bi, :OWN]], 0)
        else:
            xs = x_full[bi, OWN - 128: 2 * OWN]
        in_maps.append({
            "x": np.ascontiguousarray(xs), "Pp": Ppad,
            "mem_W": _G["L0_attn_mem_W"], "Wi": _G["L0_attn_Wi"],
            "Wl1": _G["L0_attn_Wl1"], "Wl2": _G["L0_attn_Wl2"],
            "bi": _G["L0_attn_bi"].reshape(1, D), "bl1": _G["L0_attn_bl1"].reshape(1, D),
            "bl2": _G["L0_attn_bl2"].reshape(1, D), "memb": _G["L0_attn_mem_b"].reshape(1, D),
            "g1": _G["L0_attn_g1"].reshape(1, D), "be1": _G["L0_attn_be1"].reshape(1, D),
            "wmask_i": _l1_wmask(h == 0), "identb_i": _identity_bf16(),
        })
    res = run_bass_kernel_spmd(nc, in_maps, core_ids=list(range(NCORES)))
    run_launch1.res = res
    return res.results



def build_launch2(dbg=False):
    b = Builder()
    WT = 256
    NTI = OWN // WT
    KV = 4128

    gradp_all = b.dram("gradp_all", [8, 8, 128, D], F32, kind="ExternalInput").ap()
    fgate_in = b.dram("fgate", [1, 1], F32, kind="ExternalInput").ap()
    lrate_in = b.dram("lrate", [1, 1], F32, kind="ExternalInput").ap()
    mem_W = b.dram("mem_W", [D, D], F32, kind="ExternalInput").ap()
    memb_in = b.dram("memb", [1, D], F32, kind="ExternalInput").ap()
    hdd2f_in = b.dram("hdd2f", [8, 128, KV], BF16, kind="ExternalInput").ap()
    x1f_in = b.dram("x1f", [8, 128, OWN], F32, kind="ExternalInput").ap()
    Wo = b.dram("Wo", [D, D], F32, kind="ExternalInput").ap()
    bo_in = b.dram("bo", [1, D], F32, kind="ExternalInput").ap()
    Wg = b.dram("Wg", [D, 2 * D], F32, kind="ExternalInput").ap()
    bg_in = b.dram("bg", [1, D], F32, kind="ExternalInput").ap()
    W1 = b.dram("W1", [4 * D, D], F32, kind="ExternalInput").ap()
    b1_in = b.dram("b1", [1, 4 * D], F32, kind="ExternalInput").ap()
    W2 = b.dram("W2", [D, 4 * D], F32, kind="ExternalInput").ap()
    b2_in = b.dram("b2", [1, D], F32, kind="ExternalInput").ap()
    g2_in = b.dram("g2", [1, D], F32, kind="ExternalInput").ap()
    be2_in = b.dram("be2", [1, D], F32, kind="ExternalInput").ap()
    g3_in = b.dram("g3", [1, D], F32, kind="ExternalInput").ap()
    be3_in = b.dram("be3", [1, D], F32, kind="ExternalInput").ap()
    identb_in = b.dram("identb_i", [128, 128], BF16, kind="ExternalInput").ap()
    identf_in = b.dram("identf_i", [128, 128], F32, kind="ExternalInput").ap()

    s_out = b.dram("s", [8, 128, OWN], F32, kind="ExternalOutput").ap()

    sc_dram = b.dram("sc_dram", [1, 2], F32).ap()
    if dbg:
        dbg_wn = b.dram("dbg_wn", [128, 1024], F32, kind="ExternalOutput").ap()
        dbg_mo = b.dram("dbg_mo", [128, 4096], F32, kind="ExternalOutput").ap()
        dbg_gt = b.dram("dbg_gt", [128, 4096], F32, kind="ExternalOutput").ap()
        dbg_x2 = b.dram("dbg_x2", [128, 2048], F32, kind="ExternalOutput").ap()
        dbg_mu = b.dram("dbg_mu", [128, 2048], F32, kind="ExternalOutput").ap()
        dbg_wg = b.dram("dbg_wg", [128, 2048], F32, kind="ExternalOutput").ap()
    w1t_dram = b.dram("w1t_dram", [32, 8, 128, 128], BF16).ap()
    wgt_dram = b.dram("wgt_dram", [8, 16, 128, 128], BF16).ap()
    w2t_dram = b.dram("w2t_dram", [8, 32, 128, 128], BF16).ap()

    identb = b.sbuf("identb", [128, 128], BF16)
    identf = b.sbuf("identf", [128, 128], F32)
    wres = b.sbuf("wres", [128, (8 + 8) * D], BF16)

    def wnewT(kc, mc, width=128):
        return wres[:, kc * D + mc * 128: kc * D + mc * 128 + width]

    def woT(kc, mc, width=128):
        return wres[:, 8 * D + kc * D + mc * 128: 8 * D + kc * D + mc * 128 + width]



    ar = b.sbuf("ar", [128, 24576], F32)
    gsum = ar[:, 0:8192]
    stage = ar[:, 8192:16384]
    memWTf = ar[:, 16384:24576]
    x1f_t = ar[:, 0:2048]
    mo_f = ar[:, 2048:4096]
    gate_f = ar[:, 4096:6144]
    tmp_f = ar[:, 6144:8192]
    x2tm = ar[:, 8192:10240]
    x2fm_f = ar[:, 10240:12288]
    stm = ar[:, 12288:14336]
    sq_scr = ar[:, 14336:15360]
    spre_f = ar[:, 15360:17408]
    hdd_t = [b.sbuf("hdd_t0", [128, 8 * WT], BF16), b.sbuf("hdd_t1", [128, 8 * WT], BF16)]
    mout16 = b.sbuf("mout16", [128, 8 * WT], BF16)
    mo16 = b.sbuf("mo16", [128, 8 * WT], BF16)
    x1_16 = b.sbuf("x1_16", [128, 8 * WT], BF16)
    x2_16 = b.sbuf("x2_16", [128, 8 * WT], BF16)
    ff1_16 = b.sbuf("ff1_16", [128, 32 * WT], BF16)
    stage16 = ff1_16[:, 0:8192]
    x2tm16 = b.sbuf("x2tm16", [128, 1024], BF16)
    w1tb = [b.sbuf("w1tb0", [128, 8 * 128], BF16), b.sbuf("w1tb1", [128, 8 * 128], BF16)]
    w2tb = [b.sbuf("w2tb0", [128, 16 * 128], BF16), b.sbuf("w2tb1", [128, 16 * 128], BF16)]
    wgtb = [b.sbuf("wgtb0", [128, 16 * 128], BF16), b.sbuf("wgtb1", [128, 16 * 128], BF16)]
    biasc = b.sbuf("biasc", [128, 8 * 4 + 32], F32)
    gainc = b.sbuf("gainc", [128, 4 * 8], F32)
    s0b = b.sbuf("s0b", [128, 1], F32)
    s1b = b.sbuf("s1b", [128, 1], F32)
    sc01 = b.sbuf("sc01", [1, 2], F32)
    scr = {"m": b.sbuf("scr_m", [128, 1], F32)[:, :],
           "ss": b.sbuf("scr_ss", [128, 1], F32)[:, :],
           "rstd": b.sbuf("scr_rstd", [128, 1], F32)[:, :],
           "sq": sq_scr, "sq2": tmp_f[:, 0:1024]}

    ps_tr = b.psum("ps_tr", [128, 1024], BF16)
    ps_ftr = b.psum("ps_ftr", [128, 512], F32)
    ps_m = [b.psum(f"ps_m{i}", [128, 512], F32) for i in range(4)]

    b.dma("sync", "dW", identb[:, :], identb_in)
    b.dma("sync", "dW", identf[:, :], identf_in)
    for w, src_ in enumerate([g2_in, be2_in, g3_in, be3_in]):
        b.dma("sync", "dW", gainc[:, w * 8:(w + 1) * 8],
              src_.rearrange("o (c p) -> p (o c)", p=128), allow_slow_non_contiguous=True)
    for w, src_ in enumerate([memb_in, bo_in, bg_in, b2_in]):
        b.dma("sync", "dW", biasc[:, w * 8:(w + 1) * 8],
              src_.rearrange("o (c p) -> p (o c)", p=128), allow_slow_non_contiguous=True)
    b.dma("sync", "dW", biasc[:, 32:64],
          b1_in.rearrange("o (c p) -> p (o c)", p=128), allow_slow_non_contiguous=True)
    ev_const = b.total("dW")

    e_f = b.dma("sync", "dSc", sc01[:, 0:1], fgate_in)
    e_l = b.dma("sync", "dSc", sc01[:, 1:2], lrate_in)
    b.wait("vector", e_f, e_l)
    e_s0 = b.op("vector", lambda e: e.tensor_scalar(sc01[0:1, 0:1], sc01[0:1, 0:1], -1.0, 1.0,
                                                    ALU.mult, ALU.add), "eV")
    e_s1 = b.op("vector", lambda e: e.tensor_scalar_mul(sc01[0:1, 1:2], sc01[0:1, 1:2],
                                                        float(2.0 / (32808.0 * 1024.0))), "eV")
    b.wait("gpsimd", e_s0, e_s1)
    e_scd = b.dma("gpsimd", "dSc2", sc_dram, sc01[0:1, :])
    b.wait("sync", e_scd)
    e_s0b = b.dma("sync", "dSc3", s0b[:, :], sc_dram[0:1, 0:1].broadcast_to((128, 1)))
    e_s1b = b.dma("sync", "dSc3", s1b[:, :], sc_dram[0:1, 1:2].broadcast_to((128, 1)))
    ev_scb = b.total("dSc3")

    e_ga = None
    ev_stage_rd = [None]
    for c in range(8):
        b.wait("sync", ev_stage_rd[0])
        e_g = b.dma_excl("sync", "dGl", stage.rearrange("p (c n) -> p c n", n=D),
                         gradp_all[c].rearrange("c p n -> p c n"))
        b.wait("vector", e_g, e_ga)
        if c == 0:
            e_ga = b.op("vector", lambda e: e.tensor_copy(gsum, stage), "eV")
        else:
            e_ga = b.op("vector", lambda e: e.tensor_add(gsum, gsum, stage), "eV")
        ev_stage_rd[0] = e_ga

    b.wait("sync", ev_stage_rd[0])
    e_mw = b.dma_excl("sync", "dGl", stage.rearrange("p (c n) -> p c n", n=D),
                      mem_W.rearrange("(c p) n -> p c n", p=128))
    ev_ftr_rd = None
    for mc in range(8):
        e_t = None
        for kc2 in range(0, 8, 4):
            b.wait("tensor", e_mw, ev_const, ev_ftr_rd)
            for kk in range(4):
                kc = kc2 + kk
                e_t = b.op("tensor", lambda e, mc=mc, kc=kc, kk=kk: e.transpose(
                    ps_ftr[:, kk * 128:(kk + 1) * 128],
                    stage[:, mc * D + kc * 128: mc * D + (kc + 1) * 128], identf[:, :]), "eT")
            ev_stage_rd[0] = e_t
            b.wait("vector", e_t)
            ev_ftr_rd = b.op("vector", lambda e, mc=mc, kc2=kc2: e.tensor_copy(
                memWTf.rearrange("p (k t) -> p k t", t=1024)[:, kc2:kc2 + 4, mc * 128:(mc + 1) * 128],
                ps_ftr[:, 0:512].rearrange("p (k t) -> p k t", t=128)), "eV")
    b.wait("vector", ev_ftr_rd, e_ga, ev_scb)
    e_w1s = b.op("vector", lambda e: e.tensor_scalar_mul(gsum, gsum, s1b[:, :]), "eV")
    b.wait("vector", e_w1s)
    e_wn = b.op("vector", lambda e: e.scalar_tensor_tensor(
        wres[:, 0:8192], memWTf, s0b[:, :], gsum, ALU.mult, ALU.add), "eV")

    def prep_bf16(src_, n_in_chunks, n_out_chunks, dst_fn, dst_dram=None, dram_shape=None):
        evs = {"cast": None, "tr": None, "cp": None}
        w_in = n_in_chunks * 128
        mc_per_load = max(1, 8192 // w_in)
        n_row_loads = n_out_chunks // mc_per_load
        for rl in range(n_row_loads):
            rows = mc_per_load * 128
            b.wait("sync", ev_stage_rd[0])
            e_ld = b.dma_excl("sync", "dGl", stage[:, 0:mc_per_load * w_in].rearrange(
                "p (c n) -> p c n", n=w_in),
                src_[rl * rows:(rl + 1) * rows].rearrange("(c p) n -> p c n", p=128))
            b.wait("vector", e_ld, evs["tr"])
            e_c = b.op("vector", lambda e, w=w_in, m_=mc_per_load: e.tensor_copy(
                stage16[:, 0:m_ * w], stage[:, 0:m_ * w]), "eV")
            evs["cast"] = e_c
            ev_stage_rd[0] = e_c
            for mcl in range(mc_per_load):
                mc = rl * mc_per_load + mcl
                for kc2 in range(0, n_in_chunks, 8):
                    b.wait("tensor", e_c, evs["cp"])
                    e_t = None
                    for kk in range(min(8, n_in_chunks - kc2)):
                        kc = kc2 + kk
                        e_t = b.op("tensor", lambda e, mcl=mcl, kc=kc, kk=kk, w=w_in: e.transpose(
                            ps_tr[:, kk * 128:(kk + 1) * 128],
                            stage16[:, mcl * w + kc * 128: mcl * w + (kc + 1) * 128],
                            identb[:, :]), "eT")
                    evs["tr"] = e_t
                    b.wait("vector", e_t)
                    nk = min(8, n_in_chunks - kc2)
                    if dst_dram is None:
                        e_cp = b.op("vector", lambda e, mc=mc, kc2=kc2, nk=nk: e.tensor_copy(
                            dst_fn(kc2, mc, nk),
                            ps_tr[:, 0:nk * 128].rearrange("p (k t) -> p k t", t=128)), "eV")
                    else:
                        e_cp = b.op("vector", lambda e, mc=mc, kc2=kc2, nk=nk: e.tensor_copy(
                            stage16[:, 8000:8000] if False else
                            x2tm16[:, 0:nk * 128], ps_tr[:, 0:nk * 128]), "eV")
                        b.wait("gpsimd", e_cp)
                        e_dma = b.dma("gpsimd", "dWt", dst_dram(kc2, mc, nk).rearrange("k p t -> p k t"),
                                      x2tm16[:, 0:nk * 128].rearrange("p (k t) -> p k t", t=128))
                        b.wait("vector", e_dma)
                        e_cp = e_dma
                    evs["cp"] = e_cp
        return evs["cp"]

    def woT_dst(kc2, mc, nk):
        return wres[:, 8 * D:16 * D].rearrange("p (k t) -> p k t", t=1024)[
            :, kc2:kc2 + nk, mc * 128:(mc + 1) * 128]

    ev_wo = prep_bf16(Wo, 8, 8, woT_dst)
    ev_wg = prep_bf16(Wg, 16, 8, None,
                      dst_dram=lambda kc2, mc, nk: wgt_dram[mc, kc2:kc2 + nk],
                      dram_shape=None)
    ev_w1 = prep_bf16(W1, 8, 32, None,
                      dst_dram=lambda kc2, mc, nk: w1t_dram[mc, kc2:kc2 + nk],
                      dram_shape=None)
    ev_tr2 = None
    ev_cp2 = None
    for fl in range(4):
        b.wait("sync", ev_stage_rd[0])
        e_ld = b.dma_excl("sync", "dGl", stage.rearrange("p (c n) -> p c n", n=1024),
                          W2[:, fl * 1024:(fl + 1) * 1024].rearrange("(c p) n -> p c n", p=128))
        b.wait("vector", e_ld, ev_tr2)
        e_c = b.op("vector", lambda e: e.tensor_copy(stage16[:, :], stage), "eV")
        ev_stage_rd[0] = e_c
        for mc in range(8):
            b.wait("tensor", e_c, ev_cp2)
            e_t = None
            for kk in range(8):
                e_t = b.op("tensor", lambda e, mc=mc, kk=kk: e.transpose(
                    ps_tr[:, kk * 128:(kk + 1) * 128],
                    stage16[:, mc * 1024 + kk * 128: mc * 1024 + (kk + 1) * 128],
                    identb[:, :]), "eT")
            ev_tr2 = e_t
            b.wait("vector", e_t)
            e_cp = b.op("vector", lambda e: e.tensor_copy(x2tm16[:, 0:1024], ps_tr[:, 0:1024]), "eV")
            b.wait("gpsimd", e_cp)
            e_dma = b.dma("gpsimd", "dWt", w2t_dram[mc, fl * 8:(fl + 1) * 8].rearrange("k p t -> p k t"),
                          x2tm16[:, 0:1024].rearrange("p (k t) -> p k t", t=128))
            b.wait("vector", e_dma)
            ev_cp2 = e_dma

    if dbg:
        b.wait("vector", e_wn)
        e_tc = b.op("vector", lambda e: e.tensor_copy(x1f_t[:, 0:1024], wres[:, 0:1024]), "eV")
        b.wait("gpsimd", e_tc)
        e_tap = b.dma("gpsimd", "dD", dbg_wn, x1f_t[:, 0:1024])
        b.wait("vector", e_tap)
    b.barrier()

    ev_hrd = [None, None]
    e_mo_ev = None
    e_mout = None
    ev_mo16 = None
    e_gate = None
    ev_x1c = None
    e_x1l = None
    ev_w1l = [None, None]
    ev_w2l = [None, None]
    ev_ff1 = None
    ev_ff1_rd = None
    ev_sdma = None
    ev_d = None
    ev_cb = None
    ev_mof_rd = None
    ev_gate_rd = None
    ev_x1rd = None
    ev_x116_rd = None
    ev_spre_rd = None
    ev_sev = None
    ev_wgl = [None, None]
    ev_x2c = None
    ev_x2f = None
    e_stt = None
    ev_tr_busy = None
    e_x2c16 = None
    ev_mm_evs = [None] * 4
    for ti in range(NTI):
        t0 = ti * WT
        hb = ti % 2
        b.wait("sync", ev_hrd[hb])
        e_h = b.dma("sync", "dH%d" % hb, hdd_t[hb][:, :].rearrange("p (c t) -> p c t", t=WT),
                    hdd2f_in[:, :, t0:t0 + WT].rearrange("c p t -> p c t"))
        b.wait("sync", ev_x1rd, ev_x1c)
        e_x1l = b.dma("sync", "dX1l", x1f_t.rearrange("p (c t) -> p c t", t=WT),
                      x1f_in[:, :, t0:t0 + WT].rearrange("c p t -> p c t"))
        b.wait("vector", e_x1l, ev_x1c, ev_x116_rd)
        e_x1c = b.op("vector", lambda e: e.tensor_copy(x1_16[:, :], x1f_t), "eV")
        for mc in range(8):
            b.wait("tensor", e_h, e_wn, ev_mm_evs[0])
            em = None
            for kc in range(8):
                em = b.op("tensor", lambda e, mc=mc, kc=kc, hb=hb: e.matmul(
                    ps_m[0][:, 0:WT], wnewT(kc, mc), hdd_t[hb][:, kc * WT:(kc + 1) * WT],
                    start=(kc == 0), stop=(kc == 7)), "eT")
            ev_hrd[hb] = em
            b.wait("vector", em, e_mout if True else None, ev_mo16)
            e_mout = b.op("vector", lambda e, mc=mc: e.tensor_scalar(
                mout16[:, mc * WT:(mc + 1) * WT], ps_m[0][:, 0:WT],
                biasc[:, 0 * 8 + mc:0 * 8 + mc + 1], None, ALU.add), "eV")
            ev_mm_evs[0] = e_mout
        for mc in range(8):
            b.wait("tensor", e_mout, ev_wo, ev_mm_evs[1])
            em = None
            for kc in range(8):
                em = b.op("tensor", lambda e, mc=mc, kc=kc: e.matmul(
                    ps_m[1][:, 0:WT], woT(kc, mc), mout16[:, kc * WT:(kc + 1) * WT],
                    start=(kc == 0), stop=(kc == 7)), "eT")
            b.wait("vector", em, e_mo_ev, ev_mof_rd)
            e_mo_ev = b.op("vector", lambda e, mc=mc: e.tensor_scalar(
                mo_f[:, mc * WT:(mc + 1) * WT], ps_m[1][:, 0:WT],
                biasc[:, 1 * 8 + mc:1 * 8 + mc + 1], None, ALU.add), "eV")
            ev_mm_evs[1] = e_mo_ev
        if dbg and ti == 0:
            b.wait("gpsimd", e_mout, e_mo_ev)
            e_tap = b.dma("gpsimd", "dD", dbg_mu[:, 0:2048],
                          mout16[:, 0:2048], allow_slow_non_contiguous=True)
            e_tap = b.dma("gpsimd", "dD", dbg_mo[:, 0:2048], mo_f[:, 0:2048])
            b.wait("vector", e_tap)
        b.wait("vector", e_mo_ev, ev_mo16)
        ev_mo16 = b.op("vector", lambda e: e.tensor_copy(mo16[:, :], mo_f), "eV")
        for mc in range(8):
            gb_ = mc % 2
            b.wait("sync", ev_wgl[gb_])
            e_wgld = b.dma("sync", "dWg%d" % gb_, wgtb[gb_][:, :].rearrange("p (c m) -> p c m", m=128),
                           wgt_dram[mc].rearrange("c p m -> p c m"))
            if dbg and ti == 0 and mc == 0:
                b.wait("vector", e_wgld)
                e_tc = b.op("vector", lambda e: e.tensor_copy(x2tm[:, 0:2048], wgtb[0][:, :]), "eV")
                b.wait("gpsimd", e_tc)
                e_tap = b.dma("gpsimd", "dD", dbg_wg, x2tm[:, 0:2048])
                b.wait("vector", e_tap)
                b.wait("tensor", e_tap)
            b.wait("tensor", ev_mo16, e_x1c, e_wgld, ev_mm_evs[2])
            em = None
            for kc in range(16):
                rhs = x1_16 if kc < 8 else mo16
                kk = kc % 8
                em = b.op("tensor", lambda e, mc=mc, kc=kc, rhs=rhs, kk=kk, gb_=gb_: e.matmul(
                    ps_m[2][:, 0:WT], wgtb[gb_][:, kc * 128:(kc + 1) * 128],
                    rhs[:, kk * WT:(kk + 1) * WT],
                    start=(kc == 0), stop=(kc == 15)), "eT")
            ev_wgl[gb_] = em
            ev_x116_rd = em
            b.wait("scalar", em, e_gate, ev_gate_rd)
            e_gate = b.op("scalar", lambda e, mc=mc: e.activation(
                gate_f[:, mc * WT:(mc + 1) * WT], ps_m[2][:, 0:WT], AF.Sigmoid,
                bias=biasc[:, 2 * 8 + mc:2 * 8 + mc + 1]), "eS")
            ev_mm_evs[2] = e_gate
        if dbg and ti == 0:
            b.wait("gpsimd", e_gate)
            e_tap = b.dma("gpsimd", "dD", dbg_gt[:, 0:2048], gate_f[:, 0:2048])
            b.wait("vector", e_tap)
            b.wait("scalar", e_tap)
        b.wait("vector", e_gate, ev_d)
        ev_d = b.op("vector", lambda e: e.tensor_sub(tmp_f, x1f_t, mo_f), "eV")
        b.wait("vector", ev_d)
        e_cb = b.op("vector", lambda e: e.tensor_mul(gate_f, gate_f, tmp_f), "eV")
        b.wait("vector", e_cb)
        e_cb2 = b.op("vector", lambda e: e.tensor_add(gate_f, gate_f, mo_f), "eV")
        ev_mof_rd = e_cb2
        b.wait("vector", e_cb2)
        e_x2p = b.op("vector", lambda e: e.tensor_add(gate_f, gate_f, x1f_t), "eV")
        ev_cb = e_x2p
        ev_x1rd = e_x2p
        ev_x116_rd = None
        for st in range(2):
            e_t = None
            for g2_ in range(2):
                b.wait("tensor", e_x2p, ev_tr_busy)
                for kk in range(4):
                    c = g2_ * 4 + kk
                    e_t = b.op("tensor", lambda e, c=c, st=st, kk=kk: e.transpose(
                        ps_ftr[:, kk * 128:(kk + 1) * 128],
                        gate_f[:, c * WT + st * 128: c * WT + st * 128 + 128], identf[:, :]), "eT")
                ev_gate_rd = e_t
                b.wait("vector", e_t, ev_x2c)
                ev_x2c = b.op("vector", lambda e, g2_=g2_, st=st: e.tensor_copy(
                    x2tm[:, st * 1024 + g2_ * 512: st * 1024 + (g2_ + 1) * 512], ps_ftr[:, 0:512]), "eV")
                ev_tr_busy = ev_x2c
            ev_x2l, _, _ = ln_tile(b, x2tm[:, st * 1024:(st + 1) * 1024],
                                   x2tm[:, st * 1024:(st + 1) * 1024], scr, D, ev_x2c)
            for g2_ in range(2):
                b.wait("tensor", ev_x2l, ev_tr_busy)
                e_t = None
                for kk in range(4):
                    c = g2_ * 4 + kk
                    e_t = b.op("tensor", lambda e, c=c, st=st, kk=kk: e.transpose(
                        ps_ftr[:, kk * 128:(kk + 1) * 128],
                        x2tm[:, st * 1024 + c * 128: st * 1024 + (c + 1) * 128], identf[:, :]), "eT")
                b.wait("vector", e_t, ev_x2f, e_x2c16)
                for kk in range(4):
                    c = g2_ * 4 + kk
                    ev_x2f = b.op("vector", lambda e, c=c, st=st, kk=kk: e.tensor_scalar(
                        x2fm_f[:, c * WT + st * 128: c * WT + (st + 1) * 128 - (WT - 128) + (st * 0)] if False else
                        x2fm_f[:, c * WT + st * 128: c * WT + st * 128 + 128],
                        ps_ftr[:, kk * 128:(kk + 1) * 128],
                        gainc[:, 0 * 8 + c:0 * 8 + c + 1], gainc[:, 1 * 8 + c:1 * 8 + c + 1],
                        ALU.mult, ALU.add), "eV")
                ev_tr_busy = ev_x2f
        if dbg and ti == 0:
            b.wait("gpsimd", ev_x2f)
            e_tap = b.dma("gpsimd", "dD", dbg_x2, x2fm_f[:, 0:2048])
            b.wait("vector", e_tap)
        b.wait("vector", ev_x2f, e_x2c16)
        e_x2c16 = b.op("vector", lambda e: e.tensor_copy(x2_16[:, :], x2fm_f), "eV")
        ev_x2f = e_x2c16
        for fc in range(32):
            wb_ = fc % 2
            b.wait("sync", ev_w1l[wb_])
            e_wl = b.dma("sync", "dW1%d" % wb_, w1tb[wb_][:, :].rearrange("p (c m) -> p c m", m=128),
                         w1t_dram[fc].rearrange("c p m -> p c m"))
            b.wait("tensor", e_wl, ev_x2f, ev_ff1_rd, ev_mm_evs[3])
            em = None
            for kc in range(8):
                em = b.op("tensor", lambda e, fc=fc, kc=kc, wb_=wb_: e.matmul(
                    ps_m[3][:, 0:WT], w1tb[wb_][:, kc * 128:(kc + 1) * 128],
                    x2_16[:, kc * WT:(kc + 1) * WT],
                    start=(kc == 0), stop=(kc == 7)), "eT")
            ev_w1l[wb_] = em
            b.wait("scalar", em, ev_ff1)
            ev_ff1 = b.op("scalar", lambda e, fc=fc: e.activation(
                ff1_16[:, fc * WT:(fc + 1) * WT], ps_m[3][:, 0:WT], AF.Silu,
                bias=biasc[:, 32 + fc:32 + fc + 1]), "eS")
            ev_mm_evs[3] = ev_ff1
        for mc in range(8):
            em = None
            for fh in range(2):
                wb_ = (mc * 2 + fh) % 2
                b.wait("sync", ev_w2l[wb_])
                e_wl = b.dma("sync", "dW2%d" % wb_,
                             w2tb[wb_][:, :].rearrange("p (c m) -> p c m", m=128),
                             w2t_dram[mc, fh * 16:(fh + 1) * 16].rearrange("c p m -> p c m"))
                b.wait("tensor", e_wl, ev_ff1, e_stt)
                for fl in range(16):
                    fc = fh * 16 + fl
                    em = b.op("tensor", lambda e, fc=fc, fl=fl, wb_=wb_: e.matmul(
                        ps_m[0][:, 0:WT], w2tb[wb_][:, fl * 128:(fl + 1) * 128],
                        ff1_16[:, fc * WT:(fc + 1) * WT],
                        start=(fc == 0), stop=(fc == 31)), "eT")
                ev_w2l[wb_] = em
            ev_ff1_rd = em
            ev_mm_evs[0] = em
            b.wait("vector", em, e_stt, ev_spre_rd)
            e_stt = b.op("vector", lambda e, mc=mc: e.scalar_tensor_tensor(
                spre_f[:, mc * WT:(mc + 1) * WT], ps_m[0][:, 0:WT],
                biasc[:, 3 * 8 + mc:3 * 8 + mc + 1],
                x2fm_f[:, mc * WT:(mc + 1) * WT], ALU.add, ALU.add), "eV")
            ev_mm_evs[0] = e_stt
        for st in range(2):
            ev_stc = None
            for g2_ in range(2):
                b.wait("tensor", e_stt, ev_tr_busy)
                e_t = None
                for kk in range(4):
                    c = g2_ * 4 + kk
                    e_t = b.op("tensor", lambda e, c=c, st=st, kk=kk: e.transpose(
                        ps_ftr[:, kk * 128:(kk + 1) * 128],
                        spre_f[:, c * WT + st * 128: c * WT + st * 128 + 128], identf[:, :]), "eT")
                ev_spre_rd = e_t
                b.wait("vector", e_t, ev_stc)
                ev_stc = b.op("vector", lambda e, g2_=g2_, st=st: e.tensor_copy(
                    stm[:, st * 1024 + g2_ * 512: st * 1024 + (g2_ + 1) * 512], ps_ftr[:, 0:512]), "eV")
                ev_tr_busy = ev_stc
            scr_s = dict(scr)
            scr_s["sq2"] = x2tm[:, 0:1024]
            ev_sl, _, _ = ln_tile(b, stm[:, st * 1024:(st + 1) * 1024],
                                  stm[:, st * 1024:(st + 1) * 1024], scr_s, D, ev_stc)
            for g2_ in range(2):
                b.wait("tensor", ev_sl, ev_tr_busy)
                e_t = None
                for kk in range(4):
                    c = g2_ * 4 + kk
                    e_t = b.op("tensor", lambda e, c=c, st=st, kk=kk: e.transpose(
                        ps_ftr[:, kk * 128:(kk + 1) * 128],
                        stm[:, st * 1024 + c * 128: st * 1024 + (c + 1) * 128], identf[:, :]), "eT")
                b.wait("vector", e_t, ev_sev, ev_sdma)
                for kk in range(4):
                    c = g2_ * 4 + kk
                    ev_sev = b.op("vector", lambda e, c=c, st=st, kk=kk: e.tensor_scalar(
                        tmp_f[:, c * WT + st * 128: c * WT + st * 128 + 128],
                        ps_ftr[:, kk * 128:(kk + 1) * 128],
                        gainc[:, 2 * 8 + c:2 * 8 + c + 1], gainc[:, 3 * 8 + c:3 * 8 + c + 1],
                        ALU.mult, ALU.add), "eV")
                ev_tr_busy = ev_sev
        b.wait("gpsimd", ev_sev)
        ev_sdma = b.dma("gpsimd", "dS", s_out[:, :, t0:t0 + WT].rearrange("c p t -> p c t"),
                        tmp_f.rearrange("p (c t) -> p c t", t=WT))
        ev_d = ev_sev
    b.wait("gpsimd", b.total("dS"), b.total("dD"))
    return b


def run_launch2(x1_list, hdd2f_list, gradp_list, dbg=False):
    b = build_launch2(dbg)
    nc = b.finalize()
    gradp_all = np.stack(gradp_list)
    in_maps = []
    for c in range(NCORES):
        x1f = np.ascontiguousarray(x1_list[c].T.reshape(8, 128, OWN))
        in_maps.append({
            "gradp_all": gradp_all, "fgate": _G["L0_attn_fgate"].reshape(1, 1),
            "lrate": _G["L0_attn_lrate"].reshape(1, 1),
            "mem_W": _G["L0_attn_mem_W"], "memb": _G["L0_attn_mem_b"].reshape(1, D),
            "hdd2f": hdd2f_list[c], "x1f": x1f,
            "Wo": _G["L0_attn_Wo"], "bo": _G["L0_attn_bo"].reshape(1, D),
            "Wg": _G["L0_attn_Wg"], "bg": _G["L0_attn_bg"].reshape(1, D),
            "W1": _G["L0_attn_W1"], "b1": _G["L0_attn_b1"].reshape(1, 4 * D),
            "W2": _G["L0_attn_W2"], "b2": _G["L0_attn_b2"].reshape(1, D),
            "g2": _G["L0_attn_g2"].reshape(1, D), "be2": _G["L0_attn_be2"].reshape(1, D),
            "g3": _G["L0_attn_g3"].reshape(1, D), "be3": _G["L0_attn_be3"].reshape(1, D),
            "identb_i": _identity_bf16(), "identf_i": np.eye(128, dtype=np.float32),
        })
    res = run_bass_kernel_spmd(nc, in_maps, core_ids=list(range(NCORES)))
    run_launch2.res = res
    outs = []
    for c in range(NCORES):
        sf = res.results[c]["s"]
        rows = np.zeros((OWN, D), np.float32)
        for dc in range(8):
            rows[:, dc * 128:(dc + 1) * 128] = sf[dc].T
        outs.append(rows)
    return outs


NT3 = 45
QK0 = [4, 8]
QA0 = [8, 12]
FFG = [[(1024 + 512 * k, 512) for k in range(9)] + [(5632, 128)],
       [(1664 + 512 * k, 512) for k in range(8)]]
HID2 = 2730
FF2 = 5460
FFCH = [(c0, min(128, HID2 - c0)) for c0 in range(0, HID2, 128)]
SG = 512


R3 = 5760
NT3 = 45
QK0 = [4, 8]
QA0 = [8, 12]
FFG = [[(1024 + 512 * k, 512) for k in range(9)] + [(5632, 128)],
       [(1664 + 512 * k, 512) for k in range(8)]]
HID2 = 2730
FF2 = 5460
FFCH = [(c0, min(128, HID2 - c0)) for c0 in range(0, HID2, 128)]
SG = 512


def build_launch3(dbg=False):
    b = Builder()

    x_in = b.dram("x", [R3, D], F32, kind="ExternalInput").ap()
    xfm_in = b.dram("xfm", [8, 128, R3], F32, kind="ExternalInput").ap()
    s_in = b.dram("sin", [8, 128, OWN], F32, kind="ExternalInput").ap()
    wqkv_in = b.dram("wqkv", [2 * D, 1536], F32, kind="ExternalInput").ap()
    qs_in = b.dram("qs", [2, 64], F32, kind="ExternalInput").ap()
    ks_in = b.dram("ks", [2, 64], F32, kind="ExternalInput").ap()
    wo_in = b.dram("wo", [2 * 512, D], F32, kind="ExternalInput").ap()
    lng_in = b.dram("lng", [2, D], F32, kind="ExternalInput").ap()
    lnb_in = b.dram("lnb", [2, D], F32, kind="ExternalInput").ap()
    w1_in = b.dram("w1", [2 * D, FF2], F32, kind="ExternalInput").ap()
    b1p_in = b.dram("b1p", [2, 66, 128], F32, kind="ExternalInput").ap()
    w2_in = b.dram("w2", [2 * HID2, D], F32, kind="ExternalInput").ap()
    b2_in = b.dram("b2", [2, D], F32, kind="ExternalInput").ap()
    tabs_in = b.dram("tabs_i", [4, R3, 64], F32, kind="ExternalInput").ap()
    wm3_in = b.dram("wm3_i", [5, 128, 640], F32, kind="ExternalInput").ap()
    identb_in = b.dram("identb_i", [128, 128], BF16, kind="ExternalInput").ap()
    identf_in = b.dram("identf_i", [128, 128], F32, kind="ExternalInput").ap()
    colm_in = b.dram("colm_i", [128, 1], F32, kind="ExternalInput").ap()
    w1f_in = b.dram("w1f", [8 * D, 4096], F32, kind="ExternalInput").ap()
    w2f_in = b.dram("w2f", [8 * 4096, D], F32, kind="ExternalInput").ap()
    w1b_out = b.dram("w1b", [8, 32, 8, 128, 128], BF16, kind="ExternalOutput").ap()
    w2b_out = b.dram("w2b", [8, 8, 32, 128, 128], BF16, kind="ExternalOutput").ap()
    xL1_out = b.dram("xL1", [8, 128, OWN], F32, kind="ExternalOutput").ap()
    if dbg:
        dbg_xn = b.dram("dbg_xn", [8, 128, 128], F32, kind="ExternalOutput").ap()
        dbg_xa0 = b.dram("dbg_xa0", [8, 128, R3], F32, kind="ExternalOutput").ap()
        dbg_xl1 = b.dram("dbg_xl1", [8, 128, R3], F32, kind="ExternalOutput").ap()
        dbg_q0 = b.dram("dbg_q0", [8, 64, R3], BF16, kind="ExternalOutput").ap()
        dbg_k0 = b.dram("dbg_k0", [8, 64, R3], BF16, kind="ExternalOutput").ap()
        dbg_m0 = b.dram("dbg_m0", [2, R3], F32, kind="ExternalOutput").ap()
        dbg_st = b.dram("dbg_st", [4, R3], F32, kind="ExternalOutput").ap()
        dbg_hn = b.dram("dbg_hn", [10, 128, 4096], BF16, kind="ExternalOutput").ap()
        dbg_og = b.dram("dbg_og", [10, 128, 4096], F32, kind="ExternalOutput").ap()
        dbg_o16 = b.dram("dbg_o16", [10, 128, 4096], BF16, kind="ExternalOutput").ap()
        dbg_xa1 = b.dram("dbg_xa1", [8, 128, R3], F32, kind="ExternalOutput").ap()
        dbg_g = b.dram("dbg_g", [8, 128, R3], F32, kind="ExternalOutput").ap()
        dbg_vd = b.dram("dbg_vd", [NT3, 128, 512], BF16, kind="ExternalOutput").ap()
        dbg_od = b.dram("dbg_od", [NT3, 128, 512], F32, kind="ExternalOutput").ap()
        dbg_rs = b.dram("dbg_rs", [NT3, 128, 24], F32, kind="ExternalOutput").ap()
        dbg_sr = b.dram("dbg_sr", [128, 5120], F32, kind="ExternalOutput").ap()
        dbg_p16 = b.dram("dbg_p16", [128, 5120], BF16, kind="ExternalOutput").ap()
        dbg_ms = b.dram("dbg_ms", [128, 16], F32, kind="ExternalOutput").ap()

    xf32 = [b.dram("xf32a", [8, 128, R3], F32).ap(), b.dram("xf32b", [8, 128, R3], F32).ap()]
    x16d = [b.dram("x16a", [8, 128, R3], BF16).ap(), b.dram("x16b", [8, 128, R3], BF16).ap()]
    xaf_d = b.dram("xaf_d", [8, 128, R3], F32).ap()
    xa16_d = b.dram("xa16_d", [8, 128, R3], BF16).ap()
    qd = b.dram("qd", [8, 64, R3], BF16).ap()
    kd = b.dram("kd", [8, 64, R3], BF16).ap()
    vd = b.dram("vd", [NT3, 128, 512], BF16).ap()
    w1c_d = b.dram("w1c_d", [128, 8, FF2], BF16).ap()
    w2c_d = b.dram("w2c_d", [128, 22, D], BF16).ap()
    strow_d = b.dram("strow_d", [4, R3], F32).ap()
    mrow_d = b.dram("mrow_d", [2, R3], F32).ap()

    identb = b.sbuf("identb", [128, 128], BF16)
    identf = b.sbuf("identf", [128, 128], F32)
    x_t = [b.sbuf("x_t0", [128, D], F32), b.sbuf("x_t1", [128, D], F32)]
    scr = {"m": b.sbuf("scr_m", [128, 1], F32)[:, :],
           "ss": b.sbuf("scr_ss", [128, 1], F32)[:, :],
           "rstd": b.sbuf("scr_rstd", [128, 1], F32)[:, :],
           "sq": b.sbuf("scr_sq", [128, D], F32)[:, :],
           "sq2": b.sbuf("scr_sq2", [128, D], F32)[:, :]}
    tabs = b.sbuf("tabs", [128, 4 * NT3 * 64], BF16)
    tabs_f = b.sbuf("tabs_f", [128, 128], F32)
    wm3 = b.sbuf("wm3", [128, 5 * 640], F32)
    colm = b.sbuf("colm", [128, 1], F32)
    ARF = b.sbuf("ARF", [128, 8448], F32)
    ARB = b.sbuf("ARB", [128, 26624], BF16)
    qkv_f = b.sbuf("qkv_f", [128, 1024], F32)
    sqh = b.sbuf("sqh", [128, 512], F32)
    ssh = b.sbuf("ssh", [128, 16], F32)
    xa_f = b.sbuf("xa_f", [128, 1024], F32)
    xa_16 = b.sbuf("xa_16", [128, 1024], BF16)
    o_t = b.sbuf("o_t", [128, 512], F32)
    o16 = b.sbuf("o16", [128, 512], BF16)
    xt16 = b.sbuf("xt16", [128, 1024], BF16)
    v16 = b.sbuf("v16", [128, 512], BF16)
    qk16 = b.sbuf("qk16", [128, 512], BF16)
    fm_st = b.sbuf("fm_st", [128, 8 * 129], BF16)
    o_fm = b.sbuf("o_fm", [128, 1024], BF16)
    mx3 = b.sbuf("mx3", [128, 8], F32)
    sm3 = b.sbuf("sm3", [128, 8], F32)
    rs3 = b.sbuf("rs3", [128, 8], F32)
    stats = b.sbuf("stats", [128, NT3 * 4], F32)
    mbc = b.sbuf("mbc", [128, SG], F32)
    rbc = b.sbuf("rbc", [128, SG], F32)
    biasg = b.sbuf("biasg", [128, 66 + 8 + 8 + 8], F32)
    ac3 = b.sbuf("ac3", [128, SG], BF16)

    wqkv_f_v = ARF[:, 0:6144]
    wo_f_v = ARF[:, 0:4096]
    srow3 = ARF[:, 0:5120]
    xg_f = ARF[:, 0:8 * (SG + 1)]
    og_f = ARF[:, 4104:4104 + 2048] if SG == 256 else ARF[:, 4224:4224 + 4096]
    rowseg = [ARF[:, j * 1280:(j + 1) * 1280] for j in range(6)]
    wqkv_s = ARB[:, 0:12288]
    wo_s = ARB[:, 20480:24576]
    p163 = ARB[:, 0:5120]
    ptb = ARB[:, 5120:10240]
    kring = ARB[:, 10240:16384]
    vring = ARB[:, 16384:19456]
    qtile = ARB[:, 19456:20480]
    hn16 = ARB[:, 0:8 * SG]
    prod3 = ARB[:, 4096:4096 + 22 * SG]
    w1tb3 = [ARB[:, 15360:15360 + 2048], ARB[:, 17408:17408 + 2048]]
    w2tb3 = [ARB[:, 19456:19456 + 2816], ARB[:, 22272:22272 + 2816]]
    cb16 = ARB[:, 25088:25088 + 1024]

    ps_tr = b.psum("ps_tr", [128, 1024], BF16)
    ps_ftr = b.psum("ps_ftr", [128, 512], F32)
    ps_q = b.psum("ps_q", [128, 512], F32)
    ps_k = b.psum("ps_k", [128, 512], F32)
    ps_v = b.psum("ps_v", [128, 512], F32)
    ps_o = b.psum("ps_o", [128, 512], F32)

    b.dma("sync", "dW", identb[:, :], identb_in)
    b.dma("sync", "dW", identf[:, :], identf_in)
    b.dma("sync", "dW", wm3[:, :].rearrange("p (s n) -> p s n", n=640),
          wm3_in.rearrange("s p n -> p s n"))
    b.dma("sync", "dW", colm[:, :], colm_in)
    ev_const = b.total("dW")
    e_z = b.op("vector", lambda e: e.memset(xa_f[:, :], 0.0), "eV")
    e_z2 = b.op("vector", lambda e: e.memset(stats[:, :], 0.0), "eV")
    b.wait("gpsimd", e_z)
    b.dma("gpsimd", "dZi", xaf_d[:, :, 1023:1024].rearrange("c p t -> p c t"),
          xa_f[:, 0:8].rearrange("p (c t) -> p c t", t=1), allow_slow_non_contiguous=True)
    b.wait("vector", Ev("dZi", b.cnt["dZi"]))
    e_z3 = b.op("vector", lambda e: e.memset(xa_16[:, :], 0.0), "eV")
    b.wait("gpsimd", e_z3)
    b.dma("gpsimd", "dZi", x16d[1][:, :, 1023:1024].rearrange("c p t -> p c t"),
          xa_16[:, 0:8].rearrange("p (c t) -> p c t", t=1), allow_slow_non_contiguous=True)
    ev_zi = b.total("dZi")
    b.wait("sync", ev_zi)
    b.wait("tensor", Ev("eV", b.cnt["eV"]))

    for e_i in range(8):
        for hc2 in range(32):
            b.dma("gpsimd", "dMC", w1b_out[e_i, hc2].rearrange("c p m -> p c m"),
                  w1f_in.rearrange("(e c p) (h m) -> e h c p m", c=8, p=128, m=128)[
                      e_i, hc2].rearrange("c p m -> p c m"))
        for dc2 in range(8):
            b.dma("gpsimd", "dMC", w2b_out[e_i, dc2].rearrange("h p m -> p h m"),
                  w2f_in.rearrange("(e h p) (c m) -> e c h p m", h=32, p=128, m=128)[
                      e_i, dc2].rearrange("h p m -> p h m"))

    ev_xread = [None, None]
    ev_ftr_rd = None
    ev_fmw = None
    ev_f16 = None
    e_c16 = None
    for i in range(NT3):
        bi_ = i % 2
        b.wait("sync", ev_xread[bi_])
        e_x = b.dma("sync", "dA%d" % bi_, x_t[bi_][:, :], x_in[i * 128:(i + 1) * 128, :])
        ev_ln, _, ev_xread[bi_] = ln_tile(b, x_t[bi_][:, :], scr["sq2"], scr, D, e_x)
        for g_ in range(2):
            b.wait("tensor", ev_ln, ev_ftr_rd, ev_const)
            e_t = None
            for kk in range(4):
                c = g_ * 4 + kk
                e_t = b.op("tensor", lambda e, c=c, kk=kk: e.transpose(
                    ps_ftr[:, kk * 128:(kk + 1) * 128],
                    scr["sq2"][:, c * 128:(c + 1) * 128], identf[:, :]), "eT")
            b.wait("vector", e_t, ev_fmw, e_c16)
            ev_ftr_rd = b.op("vector", lambda e, g_=g_: e.tensor_copy(
                xa_f[:, g_ * 512:(g_ + 1) * 512], ps_ftr[:, 0:512]), "eV")
        b.wait("vector", ev_ftr_rd, ev_f16)
        e_c16 = b.op("vector", lambda e: e.tensor_copy(xa_16[:, :], xa_f[:, :]), "eV")
        b.wait("gpsimd", ev_ftr_rd, e_c16)
        ev_fmw = b.dma("gpsimd", "dXf", xf32[0][:, :, i * 128:(i + 1) * 128].rearrange("c p t -> p c t"),
                       xa_f[:, :].rearrange("p (c t) -> p c t", t=128))
        ev_f16 = b.dma("gpsimd", "dXs", x16d[0][:, :, i * 128:(i + 1) * 128].rearrange("c p t -> p c t"),
                       xa_16[:, :].rearrange("p (c t) -> p c t", t=128))
    b.barrier()
    if dbg:
        b.dma("gpsimd", "dDb", dbg_xn, xf32[0][:, :, 2560:2688])
        b.wait("gpsimd", b.total("dDb"))

    for l in range(2):
        cur16 = x16d[l % 2]
        curf = xf32[l % 2]
        nxt16 = x16d[(l + 1) % 2]
        nxtf = xf32[(l + 1) % 2]

        for hf in range(2):
            e_wq = b.dma_excl("sync", "dWl", ARF[:, 0:6144].rearrange("p (c n) -> p c n", n=1536),
                              wqkv_in[l * D:(l + 1) * D].rearrange("(c p) n -> p c n", p=128)[
                                  :, hf * 4:(hf + 1) * 4])
            b.wait("vector", e_wq)
            e_cw = b.op("vector", lambda e, hf=hf: e.tensor_copy(
                wqkv_s[:, hf * 6144:(hf + 1) * 6144], ARF[:, 0:6144]), "eV")
            b.wait("sync", e_cw)
        e_wo = b.dma_excl("sync", "dWl", wo_f_v.rearrange("p (c n) -> p c n", n=D),
                          wo_in[l * 512:(l + 1) * 512].rearrange("(c p) n -> p c n", p=128))
        b.wait("vector", e_wo)
        e_cwo = b.op("vector", lambda e: e.tensor_copy(wo_s[:, :], wo_f_v), "eV")
        b.wait("sync", e_cwo)
        b.dma("gpsimd", "dWc", w1c_d.rearrange("p c n -> p c n"),
              w1_in[l * D:(l + 1) * D].rearrange("(c p) n -> p c n", p=128))
        for k in range(22):
            rows = FFCH[k][1]
            b.dma("gpsimd", "dWc", w2c_d[0:rows, k],
                  w2_in[l * HID2 + k * 128: l * HID2 + k * 128 + rows])
        ev_wc = b.total("dWc")
        b.dma("sync", "dWb", biasg[:, 0:66],
              b1p_in[l].rearrange("c p -> p c"), allow_slow_non_contiguous=True)
        b.dma("sync", "dWb", biasg[:, 66:74],
              b2_in[l:l + 1].rearrange("o (c p) -> p (o c)", p=128), allow_slow_non_contiguous=True)
        b.dma("sync", "dWb", biasg[:, 74:82],
              lng_in[l:l + 1].rearrange("o (c p) -> p (o c)", p=128), allow_slow_non_contiguous=True)
        b.dma("sync", "dWb", biasg[:, 82:90],
              lnb_in[l:l + 1].rearrange("o (c p) -> p (o c)", p=128), allow_slow_non_contiguous=True)
        ev_bias = b.total("dWb")

        e_q1 = b.dma("sync", "dQs", tabs_f[:, 0:64], qs_in[l:l + 1, :].broadcast_to((128, 64)))
        e_q2 = b.dma("sync", "dQs", tabs_f[:, 64:96], qs_in[l:l + 1, 32:64].broadcast_to((128, 32)))
        e_q3 = b.dma("sync", "dQs", tabs_f[:, 96:128], qs_in[l:l + 1, 0:32].broadcast_to((128, 32)))
        ev_tt_prev = None
        for (w, qcol) in [(0, 0), (1, 64)]:
            for i in range(NT3):
                e_tl = b.dma_excl("sync", "dTl", sqh[:, 0:64], tabs_in[w, i * 128:(i + 1) * 128, :])
                b.wait("vector", e_tl, e_q1, e_q2, e_q3, ev_tt_prev)
                ev_tt_prev = b.op("vector", lambda e, w=w, i=i, qcol=qcol: e.tensor_mul(
                    tabs[:, (w * NT3 + i) * 64:(w * NT3 + i + 1) * 64],
                    sqh[:, 0:64], tabs_f[:, qcol:qcol + 64]), "eV")
        b.wait("sync", ev_tt_prev)
        e_q1 = b.dma("sync", "dQs", tabs_f[:, 0:64], ks_in[l:l + 1, :].broadcast_to((128, 64)))
        e_q2 = b.dma("sync", "dQs", tabs_f[:, 64:96], ks_in[l:l + 1, 32:64].broadcast_to((128, 32)))
        e_q3 = b.dma("sync", "dQs", tabs_f[:, 96:128], ks_in[l:l + 1, 0:32].broadcast_to((128, 32)))
        for (w, qcol) in [(2, 0), (3, 64)]:
            for i in range(NT3):
                e_tl = b.dma_excl("sync", "dTl", sqh[:, 0:64], tabs_in[w, i * 128:(i + 1) * 128, :])
                b.wait("vector", e_tl, e_q1, e_q2, e_q3, ev_tt_prev)
                ev_tt_prev = b.op("vector", lambda e, w=w, i=i, qcol=qcol: e.tensor_mul(
                    tabs[:, (w * NT3 + i) * 64:(w * NT3 + i + 1) * 64],
                    sqh[:, 0:64], tabs_f[:, qcol:qcol + 64]), "eV")
        b.barrier()

        ev_fm_rd = None
        e_qe = e_ke = e_ve = None
        e_qkd = e_kkd = e_vkd = None
        ev_rot = None
        ev_tr16_rd = None
        for i in range(QK0[l], NT3):
            b.wait("sync", ev_fm_rd)
            e_fm = b.dma("sync", "dFm", fm_st[:, :].rearrange("p (c t) -> p c t", t=129),
                         cur16[:, :, i * 128 - 1:(i + 1) * 128].rearrange("c p t -> p c t"))
            ems = {}
            for (ps, n0, nm) in ((ps_q, 0, "q"), (ps_k, 512, "k"), (ps_v, 1024, "v")):
                b.wait("tensor", e_fm,
                       e_qe if ps is ps_q else (e_ke if ps is ps_k else e_ve))
                em = None
                for c in range(8):
                    coff = 1 if c < 4 else 0
                    em = b.op("tensor", lambda e, c=c, coff=coff, n0=n0, ps=ps: e.matmul(
                        ps[:, 0:512],
                        fm_st[:, c * 129 + coff: c * 129 + coff + 128],
                        wqkv_s[:, c * 1536 + n0: c * 1536 + n0 + 512],
                        start=(c == 0), stop=(c == 7)), "eT")
                ems[nm] = em
            ev_fm_rd = ems["v"]
            b.wait("vector", ems["v"], e_vkd)
            e_ve = b.op("vector", lambda e: e.tensor_copy(v16[:, :], ps_v[:, 0:512]), "eV")
            b.wait("gpsimd", e_ve)
            e_vkd = b.dma("gpsimd", "dVd", vd[i], v16[:, :])
            for (ps, tw, in_q) in ((ps_q, 0, True), (ps_k, 2, False)):
                if in_q and i < QA0[l]:
                    e_qe = ems["q"]
                    continue
                b.wait("vector", ems["q" if in_q else "k"], ev_rot)
                e_qf = b.op("vector", lambda e, ps=ps: e.tensor_copy(qkv_f[:, 0:512], ps[:, 0:512]), "eV")
                if in_q:
                    e_qe = e_qf
                else:
                    e_ke = e_qf
                b.wait("scalar", e_qf)
                e_s3 = b.op("scalar", lambda e: e.activation(sqh[:, 0:512], qkv_f[:, 0:512], AF.Square), "eS")
                b.wait("vector", e_s3)
                e_rd = b.op("vector", lambda e: e.tensor_reduce(
                    ssh[:, 0:8].rearrange("p (h o) -> p h o", o=1),
                    sqh[:, 0:512].rearrange("p (h c) -> p h c", c=64), axis=AX.X, op=ALU.add), "eV")
                b.wait("scalar", e_rd)
                e_s4 = b.op("scalar", lambda e: e.activation(ssh[:, 8:16], ssh[:, 0:8], AF.Sqrt), "eS")
                b.wait("vector", e_s4)
                e_m4 = b.op("vector", lambda e: e.tensor_scalar_max(ssh[:, 8:16], ssh[:, 8:16], 1e-12), "eV")
                b.wait("vector", e_m4)
                e_r4 = b.op("vector", lambda e: e.reciprocal(ssh[:, 8:16], ssh[:, 8:16]), "eV")
                b.wait("vector", e_r4)
                e_qn = b.op("vector", lambda e: e.tensor_mul(
                    qkv_f[:, 0:512].rearrange("p (h c) -> p h c", c=64),
                    qkv_f[:, 0:512].rearrange("p (h c) -> p h c", c=64),
                    ssh[:, 8:16].rearrange("p (h o) -> p h o", o=1).broadcast_to((128, 8, 64))), "eV")
                tA = tabs[:, (tw * NT3 + i) * 64:(tw * NT3 + i + 1) * 64]
                tB = tabs[:, ((tw + 1) * NT3 + i) * 64:((tw + 1) * NT3 + i + 1) * 64]
                b.wait("vector", e_qn)
                e_r1 = b.op("vector", lambda e, tA=tA: e.tensor_mul(
                    sqh[:, 0:512].rearrange("p (h c) -> p h c", c=64),
                    qkv_f[:, 0:512].rearrange("p (h c) -> p h c", c=64),
                    tA.rearrange("p (o c) -> p o c", o=1).broadcast_to((128, 8, 64))), "eV")
                qv = qkv_f[:, 0:512].rearrange("p (h two c) -> p h two c", two=2, c=32)
                ov = o_t[:, 0:512].rearrange("p (h two c) -> p h two c", two=2, c=32)
                e_r2 = b.op("vector", lambda e, tB=tB, qv=qv, ov=ov: e.tensor_mul(
                    ov[:, :, 0, :], qv[:, :, 1, :],
                    tB.rearrange("p (o two c) -> p o two c", o=1, two=2)[:, :, 0, :]
                    .broadcast_to((128, 8, 32))), "eV")
                e_r3 = b.op("vector", lambda e, tB=tB, qv=qv, ov=ov: e.tensor_mul(
                    ov[:, :, 1, :], qv[:, :, 0, :],
                    tB.rearrange("p (o two c) -> p o two c", o=1, two=2)[:, :, 1, :]
                    .broadcast_to((128, 8, 32))), "eV")
                b.wait("vector", e_r1, e_r2, e_r3, ev_tr16_rd)
                ev_rot = b.op("vector", lambda e: e.tensor_add(qk16[:, :], sqh[:, 0:512], o_t[:, 0:512]), "eV")
                b.wait("tensor", ev_rot, e_qkd if in_q else e_kkd)
                e_t = None
                for h in range(8):
                    e_t = b.op("tensor", lambda e, h=h: e.transpose(
                        ps_tr[0:64, h * 128:(h + 1) * 128], qk16[:, h * 64:(h + 1) * 64],
                        identb[:, :]), "eT")
                ev_tr16_rd = e_t
                b.wait("vector", e_t)
                e_cpq = b.op("vector", lambda e: e.tensor_copy(o_fm[0:64, :], ps_tr[0:64, 0:1024]), "eV")
                b.wait("gpsimd", e_cpq)
                dst = qd if in_q else kd
                e_dd = b.dma("gpsimd", "dQd", dst[:, :, i * 128:(i + 1) * 128].rearrange("h d t -> d h t"),
                             o_fm[0:64, :].rearrange("d (h t) -> d h t", t=128))
                if in_q:
                    e_qkd = e_dd
                else:
                    e_kkd = e_dd
        b.barrier()
        _l3_attn(b, l, locals())
        b.barrier()
        _l3_stats_rows(b, l, locals())
        b.barrier()
        _l3_ffn(b, l, locals())
        b.barrier()
        if dbg and l == 0:
            b.dma("gpsimd", "dDb", dbg_xa0, xaf_d)
            b.dma("gpsimd", "dDb", dbg_xl1, nxtf)
            b.dma("gpsimd", "dDb", dbg_q0, qd)
            b.dma("gpsimd", "dDb", dbg_k0, kd)
            b.dma("gpsimd", "dDb", dbg_m0, mrow_d[:, :])
            b.dma("gpsimd", "dDb", dbg_st, strow_d)
            b.dma("gpsimd", "dDb", dbg_vd, vd)
            b.wait("gpsimd", b.total("dDb"))
        if dbg and l == 1:
            b.dma("gpsimd", "dDb", dbg_xa1, xaf_d)
            b.dma("gpsimd", "dDb", dbg_g, nxtf)
            b.wait("gpsimd", b.total("dDb"))
        if l == 1:
            _l3_combine(b, locals())

    b.wait("gpsimd", b.total("dXL"), b.total("dMC"))
    return b


def _l3_attn(b, l, g):
    V, SC, T, P = "vector", "scalar", "tensor", "gpsimd"
    identb = g["identb"]; identf = g["identf"]
    wm3 = g["wm3"]; kring = g["kring"]; vring = g["vring"]; qtile = g["qtile"]
    srow3 = g["srow3"]; p163 = g["p163"]; ptb = g["ptb"]
    mx3 = g["mx3"]; sm3 = g["sm3"]; rs3 = g["rs3"]
    o_t = g["o_t"]; o16 = g["o16"]; o_fm = g["o_fm"]
    xa_f = g["xa_f"]; xa_16 = g["xa_16"]; stats = g["stats"]
    wo_s = g["wo_s"]; sqh = g["sqh"]; qkv_f = g["qkv_f"]
    ps_tr = g["ps_tr"]; ps_ftr = g["ps_ftr"]
    ps_q = g["ps_q"]; ps_k = g["ps_k"]; ps_v = g["ps_v"]; ps_o = g["ps_o"]
    qd = g["qd"]; kd = g["kd"]; vd = g["vd"]
    curf = g["curf"]; xaf_d = g["xaf_d"]; xa16_d = g["xa16_d"]
    e_cwo = g["e_cwo"]
    A0 = QA0[l]

    ev_kr = {}
    ev_vr = {}
    ev_ring_rd = [None] * 6
    ev_sev = None
    e_exp3 = None
    ev_pt3 = None
    ev_pv3 = None
    e_oev = None
    ev_o16 = None
    ev_ofm = None
    e_xa = None
    ev_xa16 = None
    e_xad = e_xad2 = None
    ev_st_wr = None
    e_qload = None
    ev_qrd = None
    ev_strd = None
    for i in range(A0, NT3):
        need = list(range(i - 4, i + 1)) if i == A0 else [i]
        for kt in need:
            sl = kt % 6
            b.wait("sync", ev_ring_rd[sl])
            ev_kr[kt] = b.dma("sync", "dKr%d" % (sl % 2),
                              kring[0:64, sl * 1024:(sl + 1) * 1024].rearrange("d (h t) -> d h t", t=128),
                              kd[:, :, kt * 128:(kt + 1) * 128].rearrange("h d t -> d h t"))
            ev_vr[kt] = b.dma("sync", "dVr%d" % (sl % 2), vring[:, sl * 512:(sl + 1) * 512], vd[kt])
        b.wait("sync", ev_qrd)
        e_qload = b.dma("sync", "dQl", qtile[0:64, :].rearrange("d (h t) -> d h t", t=128),
                        qd[:, :, i * 128:(i + 1) * 128].rearrange("h d t -> d h t"))
        for h in range(8):
            b.wait(T, e_qload, ev_sev)
            em0 = em1 = None
            for j in range(4):
                kt = i - 4 + j
                sl = kt % 6
                b.wait(T, ev_kr[kt])
                em0 = b.op(T, lambda e, h=h, j=j, sl=sl: e.matmul(
                    ps_q[:, j * 128:(j + 1) * 128], qtile[0:64, h * 128:(h + 1) * 128],
                    kring[0:64, sl * 1024 + h * 128: sl * 1024 + (h + 1) * 128],
                    start=True, stop=True), "eT")
            sl = i % 6
            b.wait(T, ev_kr[i])
            em1 = b.op(T, lambda e, h=h, sl=sl: e.matmul(
                ps_k[:, 0:128], qtile[0:64, h * 128:(h + 1) * 128],
                kring[0:64, sl * 1024 + h * 128: sl * 1024 + (h + 1) * 128],
                start=True, stop=True), "eT")
            b.wait(V, em0, em1, e_exp3)
            ev_sev = b.op(V, lambda e, h=h: e.tensor_copy(
                srow3[:, h * 640:h * 640 + 512], ps_q[:, 0:512]), "eV")
            ev_sev = b.op(V, lambda e, h=h: e.tensor_copy(
                srow3[:, h * 640 + 512:h * 640 + 640], ps_k[:, 0:128]), "eV")
        slot = (i - 12) if 13 <= i <= 16 else 0
        b.wait(P, ev_sev)
        ev_msk3 = b.op(P, lambda e, slot=slot: e.tensor_add(
            srow3.rearrange("p (h n) -> p h n", n=640),
            srow3.rearrange("p (h n) -> p h n", n=640),
            wm3[:, slot * 640:(slot + 1) * 640].rearrange("p (o n) -> p o n", o=1)
            .broadcast_to((128, 8, 640))), "eP")
        if g.get("dbg") and l == 0 and i == 30:
            e_tap = b.dma(P, "dDs", g["dbg_sr"], srow3)
            b.wait(V, e_tap)
        b.wait(V, ev_msk3)
        e_mx3 = b.op(V, lambda e: e.tensor_reduce(
            mx3[:, :].rearrange("p (h o) -> p h o", o=1),
            srow3.rearrange("p (h n) -> p h n", n=640), axis=AX.X,
            op=ALU.max, negate=True), "eV")
        b.wait(SC, e_mx3, ev_pt3)
        for h in range(8):
            e_exp3 = b.op(SC, lambda e, h=h: e.activation(
                p163[:, h * 640:(h + 1) * 640], srow3[:, h * 640:(h + 1) * 640],
                AF.Exp, bias=mx3[:, h:h + 1], accum_out=sm3[:, h:h + 1]), "eS")
        if g.get("dbg") and l == 0 and i == 30:
            b.wait(P, e_exp3)
            e_tap2 = b.dma(P, "dDs", g["dbg_p16"], p163)
            b.dma(P, "dDs", g["dbg_ms"][:, 0:8], mx3[:, :])
            e_tap3 = b.dma(P, "dDs", g["dbg_ms"][:, 8:16], sm3[:, :])
            b.wait(SC, e_tap2, e_tap3)
        b.wait(V, e_exp3)
        e_rs3 = b.op(V, lambda e: e.reciprocal(rs3[:, :], sm3[:, :]), "eV")
        ev_ptrd3 = None
        for f in range(5):
            b.wait(T, e_exp3, ev_pv3 if f == 0 else None, ev_ptrd3)
            e_t = None
            for k8 in range(8):
                idx = f * 8 + k8
                hh, pp = idx // 5, idx % 5
                e_t = b.op(T, lambda e, hh=hh, pp=pp, k8=k8: e.transpose(
                    ps_tr[:, k8 * 128:(k8 + 1) * 128],
                    p163[:, hh * 640 + pp * 128: hh * 640 + (pp + 1) * 128], identb[:, :]), "eT")
            ev_pt3 = e_t
            b.wait(V, e_t)
            ev_ptrd3 = b.op(V, lambda e, f=f: e.tensor_copy(
                ptb[:, f * 1024:(f + 1) * 1024], ps_tr[:, 0:1024]), "eV")
        b.wait(T, ev_ptrd3, e_oev)
        for h in range(8):
            for j in range(5):
                kt = i - 4 + j
                sl = kt % 6
                idx = h * 5 + j
                b.wait(T, ev_vr[kt])
                ev_pv3 = b.op(T, lambda e, h=h, j=j, sl=sl, idx=idx: e.matmul(
                    ps_o[:, h * 64:(h + 1) * 64], ptb[:, idx * 128:(idx + 1) * 128],
                    vring[:, sl * 512 + h * 64: sl * 512 + (h + 1) * 64],
                    start=(j == 0), stop=(j == 4)), "eT")
        ev_ring_rd[(i - 4) % 6] = ev_pv3
        b.wait(V, ev_pv3, e_rs3, ev_o16,
               Ev("dDo", b.cnt.get("dDo", 0)) if g.get("dbg") and l == 0 else None)
        e_oev = b.op(V, lambda e: e.tensor_mul(
            o_t[:, :].rearrange("p (h c) -> p h c", c=64),
            ps_o[:, 0:512].rearrange("p (h c) -> p h c", c=64),
            rs3[:, :].rearrange("p (h o) -> p h o", o=1).broadcast_to((128, 8, 64))), "eV")
        if g.get("dbg") and l == 0:
            b.wait(P, e_oev)
            b.dma(P, "dDo", g["dbg_od"][i], o_t[:, :])
            b.dma(P, "dDo", g["dbg_rs"][i, :, 0:8], mx3[:, :])
            b.dma(P, "dDo", g["dbg_rs"][i, :, 8:16], sm3[:, :])
            b.dma(P, "dDo", g["dbg_rs"][i, :, 16:24], rs3[:, :])
        b.wait(V, e_oev, ev_ofm)
        ev_o16 = b.op(V, lambda e: e.tensor_copy(o16[:, :], o_t[:, :]), "eV")
        b.wait(T, ev_o16, ev_ptrd3)
        e_t = None
        for c in range(4):
            e_t = b.op(T, lambda e, c=c: e.transpose(
                ps_tr[:, c * 128:(c + 1) * 128], o16[:, c * 128:(c + 1) * 128], identb[:, :]), "eT")
        b.wait(V, e_t)
        ev_ofm = b.op(V, lambda e: e.tensor_copy(o_fm[:, 0:512], ps_tr[:, 0:512]), "eV")
        b.wait("sync", e_xa, ev_strd, e_xad)
        e_xcur = b.dma("sync", "dXc", xa_f[:, :].rearrange("p (c t) -> p c t", t=128),
                       curf[:, :, i * 128:(i + 1) * 128].rearrange("c p t -> p c t"))
        ev_wo_mm = None
        for mc in range(8):
            b.wait(T, ev_ofm, e_cwo, ev_wo_mm)
            em = None
            for kc in range(4):
                em = b.op(T, lambda e, mc=mc, kc=kc: e.matmul(
                    ps_v[:, 0:128], wo_s[:, kc * D + mc * 128: kc * D + (mc + 1) * 128],
                    o_fm[:, kc * 128:(kc + 1) * 128],
                    start=(kc == 0), stop=(kc == 3)), "eT")
            b.wait(V, em, e_xcur, ev_xa16, e_xa)
            e_xa = b.op(V, lambda e, mc=mc: e.tensor_add(
                xa_f[:, mc * 128:(mc + 1) * 128], xa_f[:, mc * 128:(mc + 1) * 128],
                ps_v[:, 0:128]), "eV")
            ev_wo_mm = e_xa
        if i == 12:
            b.wait(V, e_xa)
            for mc in range(8):
                e_xa = b.op(V, lambda e, mc=mc: e.tensor_mul(
                    xa_f[:, mc * 128 + 127: mc * 128 + 128],
                    xa_f[:, mc * 128 + 127: mc * 128 + 128], g["colm"][:, :]), "eV")
        b.wait(V, e_xa, ev_xa16)
        ev_xa16 = b.op(V, lambda e: e.tensor_copy(xa_16[:, :], xa_f[:, :]), "eV")
        b.wait(P, e_xa, ev_xa16, e_xad, e_xad2)
        e_xad = b.dma(P, "dXa", xaf_d[:, :, i * 128:(i + 1) * 128].rearrange("c p t -> p c t"),
                      xa_f[:, :].rearrange("p (c t) -> p c t", t=128))
        e_xad2 = b.dma(P, "dXa2", xa16_d[:, :, i * 128:(i + 1) * 128].rearrange("c p t -> p c t"),
                       xa_16[:, :].rearrange("p (c t) -> p c t", t=128))
        for g_ in range(2):
            b.wait(T, e_xa, ev_qrd)
            e_t = None
            for kk in range(4):
                c = g_ * 4 + kk
                e_t = b.op(T, lambda e, c=c, kk=kk: e.transpose(
                    ps_ftr[:, kk * 128:(kk + 1) * 128], xa_f[:, c * 128:(c + 1) * 128],
                    identf[:, :]), "eT")
            b.wait(V, e_t, ev_qrd, ev_st_wr)
            ev_qrd = b.op(V, lambda e, g_=g_: e.tensor_copy(
                qkv_f[:, g_ * 512:(g_ + 1) * 512], ps_ftr[:, 0:512]), "eV")
        b.wait(V, ev_qrd)
        e_a1 = b.op(V, lambda e, i=i: e.reduce_sum(stats[:, i * 4:i * 4 + 1], qkv_f[:, 0:512], axis=AX.X), "eV")
        e_b1 = b.op(V, lambda e, i=i: e.reduce_sum(stats[:, i * 4 + 1:i * 4 + 2], qkv_f[:, 512:1024], axis=AX.X), "eV")
        b.wait(SC, ev_qrd, e_a1, e_b1)
        e_q5 = b.op(SC, lambda e, i=i: e.activation(sqh[:, 0:512], qkv_f[:, 0:512], AF.Square,
                                                    accum_out=stats[:, i * 4 + 2:i * 4 + 3]), "eS")
        b.wait(SC, e_q5)
        e_q6 = b.op(SC, lambda e, i=i: e.activation(sqh[:, 0:512], qkv_f[:, 512:1024], AF.Square,
                                                    accum_out=stats[:, i * 4 + 3:i * 4 + 4]), "eS")
        ev_st_wr = e_q6
        ev_strd = e_q6
    g["_"] = None


def _l3_stats_rows(b, l, g):
    V, SC, P = "vector", "scalar", "gpsimd"
    stats = g["stats"]; strow_d = g["strow_d"]; mrow_d = g["mrow_d"]
    rowseg = g["rowseg"]
    r0 = FFG[l][0][0]
    for j in range(4):
        b.dma(P, "dSt", strow_d[j:j + 1, :].rearrange("o (t p) -> p t o", p=128),
              stats[:, :].rearrange("p (t j) -> p t j", j=4)[:, :, j:j + 1],
              allow_slow_non_contiguous=True)
    b.wait("sync", Ev("dSt", b.cnt["dSt"]))
    seglen = 1184
    e_md = None
    e6 = None
    for s0 in range(r0, R3, seglen):
        w = min(seglen, R3 - s0)
        b.wait("sync", e6)
        e_l0 = b.dma("sync", "dSr", rowseg[0][0:1, 0:w], strow_d[0:1, s0:s0 + w])
        e_l1 = b.dma("sync", "dSr2", rowseg[1][0:1, 0:w], strow_d[1:2, s0 - 1:s0 - 1 + w])
        e_l2 = b.dma("sync", "dSr3", rowseg[2][0:1, 0:w], strow_d[2:3, s0:s0 + w])
        e_l3 = b.dma("sync", "dSr4", rowseg[3][0:1, 0:w], strow_d[3:4, s0 - 1:s0 - 1 + w])
        b.wait(V, e_l0, e_l1, e_l2, e_l3, e_md)
        e1 = b.op(V, lambda e, w=w: e.tensor_add(rowseg[4][0:1, 0:w], rowseg[0][0:1, 0:w],
                                                 rowseg[1][0:1, 0:w]), "eV")
        b.wait(V, e1)
        e2 = b.op(V, lambda e, w=w: e.tensor_scalar_mul(rowseg[4][0:1, 0:w], rowseg[4][0:1, 0:w],
                                                        1.0 / D), "eV")
        e3 = b.op(V, lambda e, w=w: e.tensor_add(rowseg[5][0:1, 0:w], rowseg[2][0:1, 0:w],
                                                 rowseg[3][0:1, 0:w]), "eV")
        b.wait(V, e2, e3)
        e4 = b.op(V, lambda e, w=w: e.tensor_scalar_mul(rowseg[5][0:1, 0:w], rowseg[5][0:1, 0:w],
                                                        1.0 / D), "eV")
        e5 = b.op(V, lambda e, w=w: e.tensor_mul(rowseg[0][0:1, 0:w], rowseg[4][0:1, 0:w],
                                                 rowseg[4][0:1, 0:w]), "eV")
        b.wait(V, e4, e5)
        e6 = b.op(V, lambda e, w=w: e.tensor_sub(rowseg[5][0:1, 0:w], rowseg[5][0:1, 0:w],
                                                 rowseg[0][0:1, 0:w]), "eV")
        b.wait(V, e6)
        e7 = b.op(V, lambda e, w=w: e.tensor_scalar_add(rowseg[5][0:1, 0:w], rowseg[5][0:1, 0:w],
                                                        1e-5), "eV")
        b.wait(SC, e7)
        e8 = b.op(SC, lambda e, w=w: e.activation(rowseg[5][0:1, 0:w], rowseg[5][0:1, 0:w],
                                                  AF.Sqrt), "eS")
        b.wait(V, e8)
        e9 = b.op(V, lambda e, w=w: e.reciprocal(rowseg[5][0:1, 0:w], rowseg[5][0:1, 0:w]), "eV")
        b.wait(P, e2, e9)
        b.dma(P, "dMr", mrow_d[0:1, s0:s0 + w], rowseg[4][0:1, 0:w])
        e_md = b.dma(P, "dMr", mrow_d[1:2, s0:s0 + w], rowseg[5][0:1, 0:w])
    g["_"] = None


def _l3_ffn(b, l, g):
    V, SC, T, P = "vector", "scalar", "tensor", "gpsimd"
    xg_f = g["xg_f"]; og_f = g["og_f"]; hn16 = g["hn16"]; prod3 = g["prod3"]
    w1tb3 = g["w1tb3"]; w2tb3 = g["w2tb3"]; ac3 = g["ac3"]; cb16 = g["cb16"]
    sqh = g["sqh"]; mbc = g["mbc"]; rbc = g["rbc"]; biasg = g["biasg"]; colm = g["colm"]
    ps_q = g["ps_q"]; ps_k = g["ps_k"]; ps_v = g["ps_v"]
    xaf_d = g["xaf_d"]; xa16_d = g["xa16_d"]; mrow_d = g["mrow_d"]
    w1c_d = g["w1c_d"]; w2c_d = g["w2c_d"]
    nxtf = g["nxtf"]; nxt16 = g["nxt16"]
    ev_wc = g["ev_wc"]; ev_bias = g["ev_bias"]

    ev_w1l3 = [None, None]
    ev_w2l3 = [None, None]
    e_hn = None
    ev_hn_rd = None
    e_ac = None
    e_gg = None
    e_pr = None
    ev_pr_rd = None
    e_og = None
    ev_og16 = None
    e_ogd = e_ogd2 = None
    ev_xg_rd = None
    for gi, (t0, gw) in enumerate(FFG[l]):
        b.wait("sync", ev_xg_rd)
        e_xgl = b.dma("sync", "dXg", xg_f.rearrange("p (c t) -> p c t", t=SG + 1)[:, :, 0:gw + 1],
                      xaf_d[:, :, t0 - 1:t0 + gw].rearrange("c p t -> p c t"))
        e_mb = b.dma("sync", "dXg2", mbc[:, 0:gw], mrow_d[0:1, t0:t0 + gw].broadcast_to((128, gw)))
        e_rb = b.dma("sync", "dXg2", rbc[:, 0:gw], mrow_d[1:2, t0:t0 + gw].broadcast_to((128, gw)))
        for c in range(8):
            coff = 1 if c < 4 else 0
            b.wait(V, e_xgl, e_mb, e_rb,
                   ev_hn_rd if c == 0 else None, e_hn if c == 0 else None,
                   e_ogd if c == 0 else None)
            e1 = b.op(V, lambda e, c=c, coff=coff, gw=gw: e.tensor_sub(
                og_f[:, 0:gw], xg_f[:, c * (SG + 1) + coff: c * (SG + 1) + coff + gw], mbc[:, 0:gw]), "eV")
            b.wait(V, e1)
            e2 = b.op(V, lambda e, gw=gw: e.tensor_mul(og_f[:, 0:gw], og_f[:, 0:gw], rbc[:, 0:gw]), "eV")
            b.wait(V, e2)
            e_hn = b.op(V, lambda e, c=c, gw=gw: e.tensor_scalar(
                hn16[:, c * SG:c * SG + gw], og_f[:, 0:gw],
                biasg[:, 74 + c:74 + c + 1], biasg[:, 82 + c:82 + c + 1],
                ALU.mult, ALU.add), "eV")
        if g.get("dbg") and l == 0:
            b.wait(P, e_hn)
            e_tapH = b.dma(P, "dDg", g["dbg_hn"][gi], hn16[:, 0:4096])
            b.wait(V, e_tapH)
        for ci, (c0, w) in enumerate(FFCH):
            wb_ = ci % 2
            b.wait("sync", ev_w1l3[wb_])
            e_w1 = b.dma("sync", "dWf%d" % wb_,
                         w1tb3[wb_][:, 0:8 * w].rearrange("p (c m) -> p c m", m=w),
                         w1c_d[:, :, c0:c0 + w])
            e_w1g = b.dma("sync", "dWf%d" % wb_,
                          w1tb3[wb_][:, 1024:1024 + 8 * w].rearrange("p (c m) -> p c m", m=w),
                          w1c_d[:, :, HID2 + c0:HID2 + c0 + w])
            b.wait(T, e_w1, e_w1g, e_hn, ev_wc, e_ac, e_gg)
            ema = emg = None
            for kc in range(8):
                ema = b.op(T, lambda e, kc=kc, w=w, wb_=wb_, gw=gw: e.matmul(
                    ps_q[0:w, 0:gw], w1tb3[wb_][:, kc * w:(kc + 1) * w],
                    hn16[:, kc * SG:kc * SG + gw],
                    start=(kc == 0), stop=(kc == 7)), "eT")
            for kc in range(8):
                emg = b.op(T, lambda e, kc=kc, w=w, wb_=wb_, gw=gw: e.matmul(
                    ps_k[0:w, 0:gw], w1tb3[wb_][:, 1024 + kc * w: 1024 + (kc + 1) * w],
                    hn16[:, kc * SG:kc * SG + gw],
                    start=(kc == 0), stop=(kc == 7)), "eT")
            ev_w1l3[wb_] = emg
            b.wait(V, ema, e_pr, ev_bias)
            e_ac = b.op(V, lambda e, ci=ci, w=w, gw=gw: e.tensor_scalar(
                ac3[0:w, 0:gw], ps_q[0:w, 0:gw], biasg[0:w, ci:ci + 1], None, ALU.add), "eV")
            b.wait(SC, emg, e_pr)
            e_gg = b.op(SC, lambda e, ci=ci, w=w, gw=gw: e.activation(
                sqh[0:w, 0:gw], ps_k[0:w, 0:gw], AF.Gelu, bias=biasg[0:w, 44 + ci:44 + ci + 1]), "eS")
            b.wait(V, e_ac, e_gg, ev_pr_rd if ci == 0 else None)
            e_pr = b.op(V, lambda e, ci=ci, w=w, gw=gw: e.tensor_mul(
                prod3[0:w, ci * SG:ci * SG + gw], ac3[0:w, 0:gw], sqh[0:w, 0:gw]), "eV")
        for mc in range(8):
            wb_ = mc % 2
            b.wait("sync", ev_w2l3[wb_])
            e_w2 = b.dma("sync", "dWh%d" % wb_,
                         w2tb3[wb_][:, 0:2816].rearrange("p (c m) -> p c m", m=128),
                         w2c_d[:, :, mc * 128:(mc + 1) * 128])
            em = None
            b.wait(T, e_w2, e_pr, e_og)
            for ci, (c0, w) in enumerate(FFCH):
                em = b.op(T, lambda e, ci=ci, w=w, mc=mc, wb_=wb_, gw=gw: e.matmul(
                    ps_v[:, 0:gw], w2tb3[wb_][0:w, ci * 128:(ci + 1) * 128],
                    prod3[0:w, ci * SG:ci * SG + gw],
                    start=(ci == 0), stop=(ci == 21)), "eT")
            ev_w2l3[wb_] = em
            ev_pr_rd = em
            b.wait(V, em, e_og, ev_og16, e_ogd)
            e_og = b.op(V, lambda e, mc=mc, gw=gw: e.scalar_tensor_tensor(
                og_f[:, mc * SG:mc * SG + gw], ps_v[:, 0:gw],
                biasg[:, 66 + mc:66 + mc + 1],
                xg_f[:, mc * (SG + 1) + 1: mc * (SG + 1) + 1 + gw], ALU.add, ALU.add), "eV")
            ev_xg_rd = e_og
        if g.get("dbg") and l == 0:
            b.wait(P, e_og)
            e_tapG = b.dma(P, "dDg", g["dbg_og"][gi], og_f[:, 0:4096])
            b.wait(V, e_tapG)
        if l == 0 and t0 <= 1663 < t0 + gw:
            co = 1663 - t0
            b.wait(V, e_og)
            for mc in range(8):
                e_og = b.op(V, lambda e, mc=mc, co=co: e.tensor_mul(
                    og_f[:, mc * SG + co: mc * SG + co + 1],
                    og_f[:, mc * SG + co: mc * SG + co + 1], colm[:, :]), "eV")
        b.wait(V, e_og, ev_og16, Ev("dXo2", b.cnt.get("dXo2", 0)))
        ev_og16 = b.op(V, lambda e, gw=gw: e.tensor_copy(
            hn16[:, 0:8 * SG].rearrange("p (c t) -> p c t", t=SG)[:, :, 0:gw],
            og_f[:, 0:8 * SG].rearrange("p (c t) -> p c t", t=SG)[:, :, 0:gw]), "eV")
        b.wait(P, ev_og16, e_ogd2)
        if g.get("dbg") and l == 0:
            e_tapO = b.dma(P, "dDg", g["dbg_o16"][gi], hn16[:, 0:4096])
        e_ogd2 = b.dma(P, "dXo2", nxt16[:, :, t0:t0 + gw].rearrange("c p t -> p c t"),
                       hn16[:, 0:8 * SG].rearrange("p (c t) -> p c t", t=SG)[:, :, 0:gw])
        b.wait(P, e_og, e_ogd)
        e_ogd = b.dma(P, "dXo", nxtf[:, :, t0:t0 + gw].rearrange("c p t -> p c t"),
                      og_f.rearrange("p (c t) -> p c t", t=SG)[:, :, 0:gw])
        ev_hn_rd = e_ogd2
    g["_"] = None


def _l3_combine(b, g):
    V, SC, P = "vector", "scalar", "gpsimd"
    ARF = g["ARF"]
    nxtf = g["nxtf"]; s_in = g["s_in"]; xfm_in = g["xfm_in"]; xL1_out = g["xL1_out"]
    gbuf = ARF[:, 0:2048]
    sbuf_ = ARF[:, 2048:4096]
    xbuf = ARF[:, 4096:6144]
    SG2 = 256
    e_si = e_mu = e_ad = e_od = None
    for gi in range(16):
        t0 = 1664 + gi * SG2
        so = gi * SG2
        b.wait("sync", e_si, e_mu)
        e_gl = b.dma("sync", "dCg", gbuf.rearrange("p (c t) -> p c t", t=SG2),
                     nxtf[:, :, t0:t0 + SG2].rearrange("c p t -> p c t"))
        b.wait("sync", e_ad, e_od)
        e_sl = b.dma("sync", "dCs", sbuf_.rearrange("p (c t) -> p c t", t=SG2),
                     s_in[:, :, so:so + SG2].rearrange("c p t -> p c t"))
        b.wait("sync", e_ad, e_od)
        e_xl = b.dma("sync", "dCx", xbuf.rearrange("p (c t) -> p c t", t=SG2),
                     xfm_in[:, :, t0:t0 + SG2].rearrange("c p t -> p c t"))
        b.wait(SC, e_gl, e_mu)
        e_si = b.op(SC, lambda e: e.activation(gbuf, gbuf, AF.Silu), "eS")
        b.wait(V, e_si, e_sl)
        e_mu = b.op(V, lambda e: e.tensor_mul(sbuf_, sbuf_, gbuf), "eV")
        b.wait(V, e_mu, e_xl)
        e_ad = b.op(V, lambda e: e.tensor_add(sbuf_, sbuf_, xbuf), "eV")
        b.wait(P, e_ad)
        e_od = b.dma(P, "dXL", xL1_out[:, :, so:so + SG2].rearrange("c p t -> p c t"),
                     sbuf_.rearrange("p (c t) -> p c t", t=SG2))
    g["_"] = None



def _l3_tables(start_global):
    t = np.arange(start_global - 1664, start_global + OWN, dtype=np.float64)
    half_idx = np.arange(0, 64, 2, dtype=np.float64)
    inv_freq = 1.0 / 10000.0 ** (half_idx / 64.0)
    freqs = t[:, None] * inv_freq[None, :]
    freqs = np.concatenate([freqs, freqs], 1)
    base = (half_idx + 0.4 * 64.0) / (1.4 * 64.0)
    power = (t - 8192 // 2) / 256.0
    scale = base[None, :] ** power[:, None]
    scale = np.concatenate([scale, scale], 1)
    cos = np.cos(freqs)
    sin = np.sin(freqs)
    sgn = np.ones(64); sgn[:32] = -1.0
    cS8q = (cos * scale * 8.0).astype(np.float32)
    Bq8 = (np.concatenate([sin[:, :32] * scale[:, :32], sin[:, 32:] * scale[:, 32:]], 1)
           * sgn[None, :] * 8.0).astype(np.float32)
    cSk = (cos / scale).astype(np.float32)
    Bk = (np.concatenate([sin[:, :32] / scale[:, :32], sin[:, 32:] / scale[:, 32:]], 1)
          * sgn[None, :]).astype(np.float32)
    return np.stack([cS8q, Bq8, cSk, Bk]).astype(np.float32)


def _l3_masks(first_half):
    m = np.full((5, 128, 640), NEG, np.float32)
    r = np.arange(128)[:, None]
    k = np.arange(640)[None, :]
    band = (k >= r) & (k <= r + 512)
    m[0][band] = 0.0
    for s in range(1, 5):
        i = 12 + s
        if first_half:
            ok = band & (k >= 2176 - 128 * i)
        else:
            ok = band
        m[s][ok] = 0.0
    return m


def run_launch3(x_full, s_list, dbg=False):
    b = build_launch3(dbg)
    nc = b.finalize()
    G = _G
    wqkv = np.ascontiguousarray(G["L0_ffn_wqkv"].reshape(2 * D, 1536))
    wo = np.ascontiguousarray(G["L0_ffn_wo"].reshape(2 * 512, D))
    w1 = np.ascontiguousarray(G["L0_ffn_w1"].reshape(2 * D, FF2))
    w2 = np.ascontiguousarray(G["L0_ffn_w2"].reshape(2 * HID2, D))
    b1p = np.zeros((2, 66, 128), np.float32)
    for l in range(2):
        for ci, (c0, w) in enumerate(FFCH):
            b1p[l, ci, :w] = G["L0_ffn_b1"][l, c0:c0 + w]
            b1p[l, 44 + ci, :w] = G["L0_ffn_b1"][l, HID2 + c0:HID2 + c0 + w]
    in_maps = []
    for c in range(NCORES):
        bi, h = c // 2, c % 2
        start = h * OWN
        lo = start - 1664
        if lo < 0:
            xs = np.concatenate([np.zeros((-lo, D), np.float32), x_full[bi, :start + OWN]], 0)
        else:
            xs = x_full[bi, lo:start + OWN]
        xs = np.ascontiguousarray(xs)
        xfm = np.ascontiguousarray(xs.T.reshape(8, 128, R3))
        colm = np.full((128, 1), 0.0 if h == 0 else 1.0, np.float32)
        in_maps.append({
            "x": xs, "xfm": xfm, "sin": s_list[c],
            "wqkv": wqkv, "qs": G["L0_ffn_q_scale"], "ks": G["L0_ffn_k_scale"],
            "wo": wo, "lng": G["L0_ffn_ln_g"], "lnb": G["L0_ffn_ln_b"],
            "w1": w1, "b1p": b1p, "w2": w2, "b2": G["L0_ffn_b2"],
            "tabs_i": _l3_tables(start), "wm3_i": _l3_masks(h == 0),
            "identb_i": _identity_bf16(), "identf_i": np.eye(128, dtype=np.float32),
            "colm_i": colm,
            "w1f": np.ascontiguousarray(G["L1_ffn_W1"].reshape(8 * D, 4096)),
            "w2f": np.ascontiguousarray(G["L1_ffn_W2"].reshape(8 * 4096, D)),
        })
    res = run_bass_kernel_spmd(nc, in_maps, core_ids=list(range(NCORES)))
    run_launch3.res = res
    outs = []
    for c in range(NCORES):
        xf = res.results[c]["xL1"]
        rows = np.zeros((OWN, D), np.float32)
        for dc in range(8):
            rows[:, dc * 128:(dc + 1) * 128] = xf[dc].T
        outs.append(rows)
    return outs


_G = {}


def kernel(**inputs):
    _G.clear()
    for k, v in inputs.items():
        _G[k] = np.asarray(v)
    x_full = _G["L0_attn_x"]

    res1 = run_launch1(x_full)
    x1_list = [res1[c]["x1"] for c in range(NCORES)]
    hdd2f_list = [res1[c]["hdd2f"] for c in range(NCORES)]
    gradp_list = [res1[c]["gradp"] for c in range(NCORES)]
    s_list = run_launch2(x1_list, hdd2f_list, gradp_list)

    s_fm = [np.ascontiguousarray(s.T.reshape(8, 128, OWN)) for s in s_list]
    xL1_list = run_launch3(x_full, s_fm)
    r3 = run_launch3.res.results[0]
    w1b, w2b = r3["w1b"], r3["w2b"]

    xL1_full = np.zeros((B, S, D), np.float32)
    for c in range(NCORES):
        bi, h = c // 2, c % 2
        xL1_full[bi, h * OWN:(h + 1) * OWN] = xL1_list[c]
    run_launch4(xL1_full)
    s1f_list = [run_launch4.res.results[c]["s1"] for c in range(NCORES)]

    outs5 = run_launch5(xL1_list, s1f_list, w1b, w2b)
    out = np.zeros((B, S, D), np.float32)
    for c in range(NCORES):
        bi, h = c // 2, c % 2
        out[bi, h * OWN:(h + 1) * OWN] = outs5[c]
    return out


# version 04
# speedup vs baseline: 1.0211x; 1.0211x over previous
import numpy as np
from contextlib import ExitStack

import concourse.bass as bass
import concourse.mybir as mybir
from concourse.bass_utils import run_bass_kernel_spmd

F32 = mybir.dt.float32
BF16 = mybir.dt.bfloat16
AF = mybir.ActivationFunctionType
ALU = mybir.AluOpType
AX = mybir.AxisListType

NCORES = 8
D = 1024
B = 4
S = 8192
OWN = 4096
NEG = -1.0e30


class Ev:
    __slots__ = ("sem", "val")

    def __init__(self, sem, val):
        self.sem = sem
        self.val = val


class Builder:
    ENGINES = ("sync", "scalar", "vector", "gpsimd", "tensor")

    def __init__(self):
        self.nc = bass.Bass("TRN2", target_bir_lowering=False, debug=False)
        self.es = ExitStack()
        self.plan = []
        self.sems = {}
        self.cnt = {}
        self.hwm = {}

    def sem(self, name):
        if name not in self.sems:
            self.sems[name] = self.es.enter_context(self.nc.semaphore(name))
            self.cnt[name] = 0
        return self.sems[name]

    def sbuf(self, name, shape, dt):
        h = self.es.enter_context(self.nc.sbuf_tensor(name, shape, dt))
        return h[tuple(slice(None) for _ in shape)]

    def psum(self, name, shape, dt):
        return self.es.enter_context(self.nc.psum_tensor(name, shape, dt))

    def dram(self, name, shape, dt, kind="Internal"):
        return self.nc.dram_tensor(name, shape, dt, kind=kind)

    def op(self, engine, fn, sem=None, k=1):
        if sem is not None:
            h = self.sem(sem)
            self.cnt[sem] += k
            ev = Ev(sem, self.cnt[sem])

            def thunk(eng, fn=fn, h=h, k=k):
                fn(eng).then_inc(h, k)
            self.plan.append((engine, thunk))
            return ev
        else:
            def thunk(eng, fn=fn):
                fn(eng)
            self.plan.append((engine, thunk))
            return None

    def wait(self, engine, *evs):
        for ev in evs:
            if ev is None:
                continue
            key = (engine, ev.sem)
            if self.hwm.get(key, 0) >= ev.val:
                continue
            self.hwm[key] = ev.val
            h = self.sem(ev.sem)

            def thunk(eng, h=h, v=ev.val):
                eng.wait_ge(h, v)
            self.plan.append((engine, thunk))

    def dma(self, engine, sem, out, in_, **kw):
        return self.op(engine, lambda e: e.dma_start(out=out, in_=in_, **kw), sem, 16)

    def dma_excl(self, engine, sem, out, in_, **kw):
        if self.cnt.get(sem, 0) > 0:
            self.wait(engine, Ev(sem, self.cnt[sem]))
        return self.dma(engine, sem, out, in_, **kw)

    def total(self, sem):
        return Ev(sem, self.cnt.get(sem, 0))

    def barrier(self, engines=None):
        evs = [Ev(s, self.cnt[s]) for s in self.cnt if self.cnt[s] > 0]
        for eng in (engines or self.ENGINES):
            self.wait(eng, *evs)

    def finalize(self):
        with self.nc.Block() as block:
            for eng_name in self.ENGINES:
                thunks = [t for (e, t) in self.plan if e == eng_name]
                if not thunks:
                    continue
                dec = getattr(block, eng_name)

                def section(eng, thunks=thunks):
                    for t in thunks:
                        t(eng)
                dec(section)
        self.es.close()
        return self.nc


def ln_tile(b, x_ap, out_ap, scr, D_, ev_in, eps=1e-5, out16_ap=None,
            g_bc=None, be_bc=None, ev_out_free=(), ev_out16_free=()):
    V, SC = "vector", "scalar"
    b.wait(V, *([ev_in] if isinstance(ev_in, Ev) else list(ev_in or [])))
    e = b.op(V, lambda en: en.reduce_sum(scr["m"], x_ap, axis=AX.X), "eV")
    b.wait(V, e)
    e = b.op(V, lambda en: en.tensor_scalar_mul(scr["m"], scr["m"], 1.0 / D_), "eV")
    b.wait(V, e)
    e = b.op(V, lambda en: en.tensor_scalar(scr["sq"], x_ap, scr["m"], None, ALU.subtract), "eV")
    ev_xread = e
    b.wait(SC, e)
    e = b.op(SC, lambda en: en.activation(scr["sq2"], scr["sq"], AF.Square, accum_out=scr["ss"]), "eS")
    b.wait(V, e)
    e = b.op(V, lambda en: en.tensor_scalar(scr["ss"], scr["ss"], 1.0 / D_, eps, ALU.mult, ALU.add), "eV")
    b.wait(SC, e)
    e = b.op(SC, lambda en: en.activation(scr["rstd"], scr["ss"], AF.Sqrt), "eS")
    b.wait(V, e)
    e = b.op(V, lambda en: en.reciprocal(scr["rstd"], scr["rstd"]), "eV")
    b.wait(V, e)
    b.wait(V, *ev_out_free)
    if g_bc is None:
        ev_out = b.op(V, lambda en: en.tensor_scalar_mul(out_ap, scr["sq"], scr["rstd"]), "eV")
    else:
        e = b.op(V, lambda en: en.tensor_scalar_mul(scr["sq"], scr["sq"], scr["rstd"]), "eV")
        b.wait(V, e)
        e = b.op(V, lambda en: en.tensor_mul(scr["sq"], scr["sq"], g_bc), "eV")
        b.wait(V, e)
        ev_out = b.op(V, lambda en: en.tensor_add(out_ap, scr["sq"], be_bc), "eV")
    ev16 = None
    if out16_ap is not None:
        b.wait(V, ev_out)
        b.wait(V, *ev_out16_free)
        ev16 = b.op(V, lambda en: en.tensor_copy(out16_ap, out_ap), "eV")
    return ev_out, ev16, ev_xread



def build_launch4(dbg=False):
    b = Builder()

    xL1 = b.dram("xL1", [S, D], F32, kind="ExternalInput").ap()
    Wq = b.dram("Wq", [D, 64], F32, kind="ExternalInput").ap()
    Wk = b.dram("Wk", [D, 64], F32, kind="ExternalInput").ap()
    Wv = b.dram("Wv", [D, D], F32, kind="ExternalInput").ap()
    identb_in = b.dram("identb_i", [128, 128], BF16, kind="ExternalInput").ap()
    dmask_in = b.dram("dmask_i", [4, 128, 512], F32, kind="ExternalInput").ap()
    prefmask_in = b.dram("prefmask", [128, 4096], BF16, kind="ExternalInput").ap()

    s1_out = b.dram("s1", [8, 128, OWN], F32, kind="ExternalOutput").ap()
    if dbg:
        dbg_xn = b.dram("dbg_xn", [128, D], F32, kind="ExternalOutput").ap()
        dbg_qk = b.dram("dbg_qk", [64, 256], F32, kind="ExternalOutput").ap()
        dbg_v = b.dram("dbg_v", [128, D], F32, kind="ExternalOutput").ap()
        dbg_s = b.dram("dbg_s", [128, 4608], F32, kind="ExternalOutput").ap()
        dbg_st = b.dram("dbg_st", [128, 3], F32, kind="ExternalOutput").ap()
        dbg_pt = b.dram("dbg_pt", [128, 512], F32, kind="ExternalOutput").ap()
        dbg_qf = b.sbuf("dbg_qf", [64, 256], F32)
        dbg_ptf = b.sbuf("dbg_ptf", [128, 512], F32)
        dbg_stf = b.sbuf("dbg_stf", [128, 3], F32)

    v_dram = b.dram("v_dram", [S // 128, 128, D], BF16).ap()
    rs_dram = b.dram("rs_dram", [128, 4], F32).ap()

    NT = S // 128
    NQB = OWN // 512

    identb = b.sbuf("identb", [128, 128], BF16)
    x_t = [b.sbuf("x_t0", [128, D], F32), b.sbuf("x_t1", [128, D], F32)]
    scr = {"m": b.sbuf("scr_m", [128, 1], F32)[:, :],
           "ss": b.sbuf("scr_ss", [128, 1], F32)[:, :],
           "rstd": b.sbuf("scr_rstd", [128, 1], F32)[:, :],
           "sq": b.sbuf("scr_sq", [128, D], F32)[:, :]}
    xn_t16 = b.sbuf("xn_t16", [128, D], BF16)
    xn_fm = b.sbuf("xn_fm", [128, 8 * 128], BF16)
    wq_f = b.sbuf("wq_f", [128, 8 * 64], F32)
    wk_f = b.sbuf("wk_f", [128, 8 * 64], F32)
    wq_s = b.sbuf("wq_s", [128, 8 * 64], BF16)
    wk_s = b.sbuf("wk_s", [128, 8 * 64], BF16)
    ar1 = b.sbuf("ar1", [128, 8192], F32)
    ar4 = b.sbuf("ar4", [128, 8192], BF16)
    q_fm = b.sbuf("q_fm", [64, S], BF16)
    k_fm = b.sbuf("k_fm", [64, S], BF16)
    v_t16 = b.sbuf("v_t16", [128, D], BF16)
    pt_buf = b.sbuf("pt_buf", [128, NT * 512], BF16)
    ar3 = b.sbuf("ar3", [128, 8 * 512], F32)
    prefm = b.sbuf("prefm", [128, 4096], BF16)
    dmask = b.sbuf("dmask", [128, 4 * 512], F32)
    vstream = b.sbuf("vstream", [128, 2 * 512], BF16)
    nmax = b.sbuf("nmax", [128, 1], F32)
    sumr = b.sbuf("sumr", [128, 1], F32)
    rsum4 = b.sbuf("rsum4", [128, 4], F32)
    rs_bc = b.sbuf("rs_bc", [128, 512], F32)

    scr["sq2"] = ar3[:, 0:D]
    ps_s = b.psum("ps_s", [128, 512], F32)
    ps_tr = b.psum("ps_tr", [128, 1024], BF16)
    ps_o = [b.psum(f"ps_{n}", [128, 512], F32) for n in "abcd"]

    e_id = b.dma("sync", "dW", identb[:, :], identb_in)
    e_wq = b.dma("sync", "dW", wq_f[:, :].rearrange("p (c n) -> p c n", n=64),
                 Wq.rearrange("(c p) n -> p c n", p=128))
    e_wk = b.dma("sync", "dW", wk_f[:, :].rearrange("p (c n) -> p c n", n=64),
                 Wk.rearrange("(c p) n -> p c n", p=128))
    e_wv = b.dma("sync", "dW", ar1[:, :].rearrange("p (c n) -> p c n", n=D),
                 Wv.rearrange("(c p) n -> p c n", p=128))
    e_dm = b.dma("sync", "dW", dmask[:, :].rearrange("p (c n) -> p c n", n=512),
                 dmask_in.rearrange("c p n -> p c n"))
    e_pm = b.dma("sync", "dW", prefm[:, :], prefmask_in)
    e_id = e_wq = e_wk = e_wv = e_dm = e_pm = b.total("dW")
    b.wait("vector", e_wq, e_wk, e_wv)
    e_cwq = b.op("vector", lambda e: e.tensor_copy(wq_s[:, :], wq_f[:, :]), "eV")
    e_cwk = b.op("vector", lambda e: e.tensor_copy(wk_s[:, :], wk_f[:, :]), "eV")
    e_cwv = b.op("vector", lambda e: e.tensor_copy(ar4[:, :], ar1[:, :]), "eV")

    ev_q_ev = ev_k_ev = None
    ev_vdma = None
    ev_fm_used = None
    ev_fm_prev = None
    ev_xread = [None, None]
    e_tr_prev = None
    for i in range(NT):
        bi_ = i % 2
        b.wait("sync", ev_xread[bi_])
        e_x = b.dma("sync", "dA%d" % bi_, x_t[bi_][:, :], xL1[i * 128:(i + 1) * 128, :])
        ev_ln, ev16, ev_xread[bi_] = ln_tile(b, x_t[bi_][:, :], scr["sq"], scr, D, e_x,
                                             out16_ap=xn_t16[:, :],
                                             ev_out16_free=[e_tr_prev])
        if dbg and i == 32:
            b.wait("gpsimd", ev_ln)
            e_tap = b.dma("gpsimd", "dD", dbg_xn, scr["sq"])
            b.wait("vector", e_tap)
        b.wait("tensor", ev16, e_id, ev_fm_prev)
        e_tr = None
        for c in range(8):
            e_tr = b.op("tensor", lambda e, c=c: e.transpose(
                ps_tr[:, c * 128:(c + 1) * 128],
                xn_t16[:, c * 128:(c + 1) * 128], identb[:, :]), "eT")
        e_tr_prev = e_tr
        b.wait("vector", e_tr, ev_fm_used)
        e_fm = b.op("vector", lambda e: e.tensor_copy(xn_fm[:, :], ps_tr[:, 0:1024]), "eV")
        ev_fm_prev = e_fm
        b.wait("tensor", e_fm, e_cwq, e_cwk, e_cwv, ev_q_ev, ev_k_ev)
        eq = ek = None
        for c in range(8):
            eq = b.op("tensor", lambda e, c=c: e.matmul(
                ps_o[0][0:64, 0:128], wq_s[:, c * 64:(c + 1) * 64],
                xn_fm[:, c * 128:(c + 1) * 128],
                start=(c == 0), stop=(c == 7)), "eT")
        for c in range(8):
            ek = b.op("tensor", lambda e, c=c: e.matmul(
                ps_o[1][0:64, 0:128], wk_s[:, c * 64:(c + 1) * 64],
                xn_fm[:, c * 128:(c + 1) * 128],
                start=(c == 0), stop=(c == 7)), "eT")
        if i > 0:
            b.wait("tensor", ev_vdma)
        ev_half = []
        for h in range(2):
            ph = ps_o[2 + h]
            evv = None
            for c in range(8):
                evv = b.op("tensor", lambda e, c=c, h=h, ph=ph: e.matmul(
                    ph[:, 0:512], xn_fm[:, c * 128:(c + 1) * 128],
                    ar4[:, c * 1024 + h * 512: c * 1024 + (h + 1) * 512],
                    start=(c == 0), stop=(c == 7)), "eT")
            ev_half.append(evv)
        ev_fm_used = ev_half[-1]
        b.wait("scalar", eq)
        ev_q_ev = b.op("scalar", lambda e, i=i: e.activation(
            q_fm[0:64, i * 128:(i + 1) * 128], ps_o[0][0:64, 0:128], AF.Copy,
            scale=0.125), "eS")
        b.wait("scalar", ek)
        ev_k_ev = b.op("scalar", lambda e, i=i: e.copy(
            k_fm[0:64, i * 128:(i + 1) * 128], ps_o[1][0:64, 0:128]), "eS")
        b.wait("vector", ev_half[0], ev_half[1])
        if i > 0:
            b.wait("vector", ev_vdma)
        e_v0 = b.op("vector", lambda e: e.tensor_copy(v_t16[:, 0:512], ps_o[2][:, 0:512]), "eV")
        e_v1 = b.op("vector", lambda e: e.tensor_copy(v_t16[:, 512:1024], ps_o[3][:, 0:512]), "eV")
        b.wait("gpsimd", e_v0, e_v1)
        ev_vdma = b.dma("gpsimd", "dV", v_dram[i], v_t16[:, :])

    if dbg:
        b.wait("vector", ev_q_ev, ev_k_ev)
        e_d1 = b.op("vector", lambda e: e.tensor_copy(dbg_qf[0:64, 0:128], q_fm[0:64, 4096:4224]), "eV")
        e_d2 = b.op("vector", lambda e: e.tensor_copy(dbg_qf[0:64, 128:256], k_fm[0:64, 4096:4224]), "eV")
        b.wait("gpsimd", e_d1, e_d2)
        b.dma("gpsimd", "dD", dbg_qk, dbg_qf[0:64, :])
        b.dma("gpsimd", "dD", dbg_v, v_dram[32])

    b.barrier()

    ev_pt_used = None
    ev_o_ev = [None] * 4
    ev_sb = None
    e_exp_prev = None
    e_trp_prev = None
    e_rc_prev = None
    ev_ptq_last = None
    e_rbc = None
    ev_rsd_prev = None
    evm_hist = []
    ev_dO_prev = None
    for qb in range(NQB):
        q0 = OWN + qb * 512
        kchunks = q0 // 512 + 1
        klen = kchunks * 512
        ktiles = klen // 128
        for qt in range(4):
            qi = qb * 4 + qt
            qrow = OWN + qi * 128
            diag = qrow // 512
            for kc in range(kchunks):
                b.wait("tensor", ev_sb)
                em = b.op("tensor", lambda e, kc=kc, qrow=qrow: e.matmul(
                    ps_s[:, 0:512], q_fm[0:64, qrow:qrow + 128],
                    k_fm[0:64, kc * 512:(kc + 1) * 512],
                    start=True, stop=True), "eT")
                b.wait("vector", em, e_exp_prev)
                ev_sb = b.op("vector", lambda e, kc=kc: e.tensor_copy(
                    ar1[:, kc * 512:(kc + 1) * 512], ps_s[:, 0:512]), "eV")
            b.wait("gpsimd", ev_sb, e_pm, e_dm)
            e_msk = None
            for kc in range(min(8, kchunks)):
                e_msk = b.op("gpsimd", lambda e, kc=kc: e.tensor_tensor(
                    ar1[:, kc * 512:(kc + 1) * 512], ar1[:, kc * 512:(kc + 1) * 512],
                    prefm[:, kc * 512:(kc + 1) * 512], ALU.add), "eP")
            slot = qi % 4
            e_msk2 = b.op("gpsimd", lambda e, diag=diag, slot=slot: e.tensor_tensor(
                ar1[:, diag * 512:(diag + 1) * 512], ar1[:, diag * 512:(diag + 1) * 512],
                dmask[:, slot * 512:(slot + 1) * 512], ALU.add), "eP")
            b.wait("vector", e_msk, e_msk2)
            if dbg and qb == 0 and qt == 0:
                b.wait("gpsimd", e_msk, e_msk2, ev_sb)
                e_tap = b.dma("gpsimd", "dD", dbg_s, ar1[:, 0:4608])
                b.wait("vector", e_tap)
                b.wait("scalar", e_tap)
            e_mx = b.op("vector", lambda e, klen=klen: e.reduce_max(
                nmax[:, :], ar1[:, 0:klen], axis=AX.X, negate=True), "eV")
            b.wait("scalar", e_mx, e_trp_prev, e_rc_prev)
            e_exp = b.op("scalar", lambda e, klen=klen: e.activation(
                ar4[:, 0:klen], ar1[:, 0:klen], AF.Exp, bias=nmax[:, :],
                accum_out=sumr[:, :]), "eS")
            e_exp_prev = e_exp
            b.wait("vector", e_exp, ev_rsd_prev)
            e_rc_prev = b.op("vector", lambda e, qt=qt: e.reciprocal(
                rsum4[:, qt:qt + 1], sumr[:, :]), "eV")
            if dbg and qb == 0 and qt == 0:
                b.wait("vector", e_mx, e_rc_prev)
                ed1 = b.op("vector", lambda e: e.tensor_copy(dbg_stf[:, 0:1], nmax[:, :]), "eV")
                ed2 = b.op("vector", lambda e: e.tensor_copy(dbg_stf[:, 1:2], sumr[:, :]), "eV")
                ed3 = b.op("vector", lambda e: e.tensor_copy(dbg_stf[:, 2:3], rsum4[:, 0:1]), "eV")
                b.wait("gpsimd", ed1, ed2, ed3)
                b.dma("gpsimd", "dD", dbg_st, dbg_stf[:, :])
            for g0 in range(0, ktiles, 8):
                gn = min(8, ktiles - g0)
                b.wait("tensor", e_exp, ev_pt_used, ev_ptq_last)
                e_trp = None
                for g in range(gn):
                    e_trp = b.op("tensor", lambda e, g=g, g0=g0: e.transpose(
                        ps_tr[:, g * 128:(g + 1) * 128],
                        ar4[:, (g0 + g) * 128:(g0 + g + 1) * 128], identb[:, :]), "eT")
                e_trp_prev = e_trp
                b.wait("vector", e_trp)
                ev_ptq_last = b.op("vector", lambda e, g0=g0, gn=gn, qt=qt: e.tensor_copy(
                    pt_buf[:, :].rearrange("p (g x) -> p g x", x=512)[
                        :, g0:g0 + gn, qt * 128:(qt + 1) * 128],
                    ps_tr[:, 0:gn * 128].rearrange("p (g x) -> p g x", x=128)), "eV")
        if dbg and qb == 0:
            b.wait("vector", ev_ptq_last)
            edp = b.op("vector", lambda e: e.tensor_copy(dbg_ptf[:, :], pt_buf[:, 32 * 512:33 * 512]), "eV")
            b.wait("gpsimd", edp)
            b.dma("gpsimd", "dD", dbg_pt, dbg_ptf[:, :])
        b.wait("gpsimd", e_rc_prev)
        e_rsd = b.dma_excl("gpsimd", "dB", rs_dram, rsum4[:, :])
        ev_rsd_prev = e_rsd
        b.wait("sync", e_rsd, e_rbc)
        for cc in range(4):
            b.dma("sync", "dBc", rs_bc[:, cc * 128:(cc + 1) * 128],
                  rs_dram[:, cc:cc + 1].rearrange("p o -> o p").broadcast_to((128, 128)),
                  allow_slow_non_contiguous=True)
        e_rbc = b.total("dBc")
        for half in range(2):
            evm = [None] * 4
            for kt in range(ktiles):
                buf = kt % 2
                if len(evm_hist) >= 2:
                    b.wait("sync", evm_hist[-2])
                e_vl = b.dma("sync", "dV2%d" % buf, vstream[:, buf * 512:(buf + 1) * 512],
                             v_dram[kt, :, half * 512:(half + 1) * 512])
                b.wait("tensor", e_vl, e_rbc, ev_ptq_last)
                if kt == 0:
                    for dc in range(4):
                        b.wait("tensor", ev_o_ev[dc])
                for dc in range(4):
                    evm[dc] = b.op("tensor", lambda e, kt=kt, dc=dc, buf=buf, ktiles=ktiles: e.matmul(
                        ps_o[dc][:, 0:512],
                        vstream[:, buf * 512 + dc * 128: buf * 512 + (dc + 1) * 128],
                        pt_buf[:, kt * 512:(kt + 1) * 512],
                        start=(kt == 0), stop=(kt == ktiles - 1)), "eT")
                evm_hist.append(evm[3])
            ev_pt_used = evm[3]
            for dc in range(4):
                gdc = half * 4 + dc
                b.wait("vector", evm[dc], e_rbc, ev_dO_prev)
                e_oev = b.op("vector", lambda e, dc=dc, gdc=gdc: e.tensor_mul(
                    ar3[:, gdc * 512:(gdc + 1) * 512], ps_o[dc][:, 0:512], rs_bc[:, :]), "eV")
                ev_o_ev[dc] = e_oev
                b.wait("gpsimd", e_oev)
                ev_dO_prev = b.dma("gpsimd", "dO", s1_out[gdc, :, qb * 512:(qb + 1) * 512],
                                   ar3[:, gdc * 512:(gdc + 1) * 512])
        b.wait("sync", *ev_o_ev)

    b.wait("gpsimd", Ev("dO", b.cnt.get("dO", 0)), Ev("dD", b.cnt.get("dD", 0)))
    return b



def _identity_bf16():
    import ml_dtypes
    return np.eye(128, dtype=ml_dtypes.bfloat16)


def _l4_dmask():
    m = np.zeros((4, 128, 512), dtype=np.float32)
    r = np.arange(128)[:, None]
    k = np.arange(512)[None, :]
    for slot in range(4):
        m[slot][k > slot * 128 + r] = NEG
    return m


def _l4_prefmask(first_half):
    import ml_dtypes
    m = np.zeros((128, 4096), dtype=np.float32)
    if first_half:
        m[:, :] = NEG
    return m.astype(ml_dtypes.bfloat16)


def run_launch4(xL1_full, dbg=False):
    nc = None
    b = build_launch4(dbg)
    nc = b.finalize()
    in_maps = []
    for c in range(NCORES):
        bi, h = c // 2, c % 2
        if h == 1:
            x = xL1_full[bi]
        else:
            x = np.concatenate([np.zeros((OWN, D), np.float32), xL1_full[bi, :OWN]], 0)
        in_maps.append({
            "xL1": np.ascontiguousarray(x),
            "Wq": _G["L1_attn_Wq"], "Wk": _G["L1_attn_Wk"], "Wv": _G["L1_attn_Wv"],
            "identb_i": _identity_bf16(),
            "dmask_i": _l4_dmask(),
            "prefmask": _l4_prefmask(h == 0),
        })
    res = run_bass_kernel_spmd(nc, in_maps, core_ids=list(range(NCORES)))
    run_launch4.res = res
    out = np.zeros((B, S, D), np.float32)
    for c in range(NCORES):
        bi, h = c // 2, c % 2
        s1 = res.results[c]["s1"]
        rows = np.zeros((OWN, D), np.float32)
        for dc in range(8):
            rows[:, dc * 128:(dc + 1) * 128] = s1[dc].T
        out[bi, h * OWN:(h + 1) * OWN] = rows
    return out





def build_launch5(dbg=False):
    b = Builder()

    xL1 = b.dram("xL1", [OWN, D], F32, kind="ExternalInput").ap()
    xL1f = b.dram("xL1f", [8, 128, OWN], F32, kind="ExternalInput").ap()
    s1f = b.dram("s1f", [8, 128, OWN], F32, kind="ExternalInput").ap()
    Wg = b.dram("Wg", [D, 8], F32, kind="ExternalInput").ap()
    bg = b.dram("bg", [1, 8], F32, kind="ExternalInput").ap()
    b1 = b.dram("b1", [8, 4096], F32, kind="ExternalInput").ap()
    b2 = b.dram("b2", [8, D], F32, kind="ExternalInput").ap()
    w1b = b.dram("w1b", [8, 32, 8, 128, 128], BF16, kind="ExternalInput").ap()
    w2b = b.dram("w2b", [8, 8, 32, 128, 128], BF16, kind="ExternalInput").ap()
    identb_in = b.dram("identb_i", [128, 128], BF16, kind="ExternalInput").ap()

    out_fm = b.dram("outf", [8, 128, OWN], F32, kind="ExternalOutput").ap()
    if dbg:
        dbg_w = b.dram("dbg_w", [128, 8], F32, kind="ExternalOutput").ap()
        dbg_g = b.dram("dbg_g", [128, 1024], F32, kind="ExternalOutput").ap()
        dbg_acc = b.dram("dbg_acc", [128, 1024], F32, kind="ExternalOutput").ap()
        dbg_wb = b.dram("dbg_wb", [128, 1024], F32, kind="ExternalOutput").ap()
        dbg_xf = b.dram("dbg_xf", [128, 1024], F32, kind="ExternalOutput").ap()
        dbg_w1 = b.dram("dbg_w1", [128, 1024], F32, kind="ExternalOutput").ap()

        dbg_g0 = b.dram("dbg_g0", [128, 1024], F32, kind="ExternalOutput").ap()
        dbg_a0 = b.dram("dbg_a0", [128, 1024], F32, kind="ExternalOutput").ap()

    w_dram = b.dram("w_dram", [8, OWN], F32).ap()

    NCH = 4
    identb = b.sbuf("identb", [128, 128], BF16)
    x_t = [b.sbuf("x_t0", [128, D], F32), b.sbuf("x_t1", [128, D], F32)]
    scr = {"m": b.sbuf("scr_m", [128, 1], F32)[:, :],
           "ss": b.sbuf("scr_ss", [128, 1], F32)[:, :],
           "rstd": b.sbuf("scr_rstd", [128, 1], F32)[:, :],
           "sq": b.sbuf("scr_sq", [128, D], F32)[:, :]}
    xn_t16 = b.sbuf("xn_t16", [128, D], BF16)
    xn_fm = b.sbuf("xn_fm", [128, 8 * 1024], BF16)
    wg_f = b.sbuf("wg_f", [128, 8 * 8], F32)
    wg_s = b.sbuf("wg_s", [128, 8 * 8], BF16)
    bg_bc = b.sbuf("bg_bc", [128, 8], F32)
    b1_sb = b.sbuf("b1_sb", [128, 8 * 32], F32)
    b2_sb = b.sbuf("b2_sb", [128, 8 * 8], F32)
    gt = b.sbuf("gt", [128, 8], F32)
    m1 = b.sbuf("m1", [128, 1], F32)
    m2 = b.sbuf("m2", [128, 1], F32)
    eqm = b.sbuf("eqm", [128, 8], F32)
    g2s = b.sbuf("g2s", [128, 8], F32)
    msk = b.sbuf("msk", [128, 8], F32)
    gm = b.sbuf("gm", [128, 8], F32)
    den = b.sbuf("den", [128, 1], F32)
    wts = b.sbuf("wts", [128, 8], F32)
    w1_sb = [b.sbuf("w1_sb0", [128, 8 * 128], BF16), b.sbuf("w1_sb1", [128, 8 * 128], BF16)]
    w2_sb = [b.sbuf("w2_sb0", [128, 32 * 128], BF16), b.sbuf("w2_sb1", [128, 32 * 128], BF16)]
    g_fm = b.sbuf("g_fm", [128, 32 * 1024], BF16)
    acc = b.sbuf("acc", [128, 8 * 1024], F32)
    tmp = b.sbuf("tmp", [128, 8 * 1024], F32)
    wb = [b.sbuf("wb0", [128, 1024], F32), b.sbuf("wb1", [128, 1024], F32)]
    s1_st = b.sbuf("s1_st", [128, 1024], F32)
    xf_st = b.sbuf("xf_st", [128, 1024], F32)
    out_st = b.sbuf("out_st", [128, 1024], F32)
    scr["sq2"] = None
    ps_tr = b.psum("ps_tr", [128, 1024], BF16)
    ps_g = b.psum("ps_g", [128, 512], F32)
    ps_h = [b.psum("ps_h0", [128, 512], F32), b.psum("ps_h1", [128, 512], F32)]
    ps_e = [b.psum("ps_e0", [128, 512], F32), b.psum("ps_e1", [128, 512], F32)]

    scr["sq2"] = tmp[:, 0:D]
    e_id = b.dma("sync", "dW", identb[:, :], identb_in)
    e_wg = b.dma("sync", "dW", wg_f[:, :].rearrange("p (c n) -> p c n", n=8),
                 Wg.rearrange("(c p) n -> p c n", p=128))
    e_bg = b.dma("sync", "dW", bg_bc[:, :], bg.broadcast_to((128, 8)))
    e_b1 = b.dma("sync", "dW", b1_sb[:, :].rearrange("p (e h) -> p e h", h=32),
                 b1.rearrange("e (h p) -> p e h", p=128),
                 allow_slow_non_contiguous=True)
    e_b2 = b.dma("sync", "dW", b2_sb[:, :].rearrange("p (e c) -> p e c", c=8),
                 b2.rearrange("e (c p) -> p e c", p=128),
                 allow_slow_non_contiguous=True)
    e_id = e_wg = e_bg = e_b1 = e_b2 = b.total("dW")
    b.wait("vector", e_wg)
    e_cwg = b.op("vector", lambda e: e.tensor_copy(wg_s[:, :], wg_f[:, :]), "eV")

    ev_xread = [None, None]
    e_tr_prev = None
    ev_fm_used = None
    e_wdma_prev = None
    ev_g_ev = None
    e_sm_prev = None
    ev_w1l = [None, None]
    ev_w2l = [None, None]
    ev_wbl = [None, None]
    ev_gelu = {}
    ev_h_used = [None, None]
    ev_e_used = [None, None]
    ev_acc_rd = None
    ev_tmp_rd = None
    e_out_dma = None
    ev_gfm_rd = None

    for cc in range(NCH):
        for j in range(8):
            i = cc * 8 + j
            bi_ = i % 2
            b.wait("sync", ev_xread[bi_])
            e_x = b.dma("sync", "dA%d" % bi_, x_t[bi_], xL1[i * 128:(i + 1) * 128, :])
            ev_ln, ev16, ev_xread[bi_] = ln_tile(
                b, x_t[bi_], scr["sq"], scr, D, e_x,
                out16_ap=xn_t16, ev_out16_free=[e_tr_prev])
            b.wait("tensor", ev16, e_id)
            e_tr = None
            for c in range(8):
                e_tr = b.op("tensor", lambda e, c=c: e.transpose(
                    ps_tr[:, c * 128:(c + 1) * 128],
                    xn_t16[:, c * 128:(c + 1) * 128], identb[:, :]), "eT")
            e_tr_prev = e_tr
            b.wait("vector", e_tr, ev_fm_used)
            e_fm = b.op("vector", lambda e, j=j: e.tensor_copy(
                xn_fm[:, :].rearrange("p (c t) -> p c t", t=1024)[:, :, j * 128:(j + 1) * 128],
                ps_tr[:, 0:1024].rearrange("p (c t) -> p c t", t=128)), "eV")
            b.wait("tensor", e_fm, e_cwg, ev_g_ev)
            eg = None
            for c in range(8):
                eg = b.op("tensor", lambda e, c=c, j=j: e.matmul(
                    ps_g[:, 0:8],
                    xn_fm[:, c * 1024 + j * 128: c * 1024 + (j + 1) * 128],
                    wg_s[:, c * 8:(c + 1) * 8],
                    start=(c == 0), stop=(c == 7)), "eT")
            V = "vector"
            b.wait(V, eg, e_bg, e_sm_prev)
            e1 = b.op(V, lambda e: e.tensor_add(gt[:, :], ps_g[:, 0:8], bg_bc[:, :]), "eV")
            ev_g_ev = e1
            b.wait(V, e1)
            e2 = b.op(V, lambda e: e.reduce_max(m1[:, :], gt[:, :], axis=AX.X, negate=True), "eV")
            b.wait("scalar", e1, e2, e_sm_prev)
            e3 = b.op("scalar", lambda e: e.activation(gt[:, :], gt[:, :], AF.Exp,
                                                       bias=m1[:, :], accum_out=den[:, :]), "eS")
            b.wait(V, e3)
            e4 = b.op(V, lambda e: e.reciprocal(den[:, :], den[:, :]), "eV")
            b.wait(V, e4)
            e5 = b.op(V, lambda e: e.tensor_scalar_mul(gt[:, :], gt[:, :], den[:, :]), "eV")
            b.wait(V, e5)
            e6 = b.op(V, lambda e: e.reduce_max(m1[:, :], gt[:, :], axis=AX.X), "eV")
            b.wait(V, e6)
            e7 = b.op(V, lambda e: e.tensor_scalar(eqm[:, :], gt[:, :], m1[:, :], None, ALU.is_ge), "eV")
            b.wait(V, e7)
            e8 = b.op(V, lambda e: e.scalar_tensor_tensor(g2s[:, :], eqm[:, :], NEG, gt[:, :],
                                                          ALU.mult, ALU.add), "eV")
            b.wait(V, e8)
            e9 = b.op(V, lambda e: e.reduce_max(m2[:, :], g2s[:, :], axis=AX.X), "eV")
            b.wait(V, e9)
            e10 = b.op(V, lambda e: e.tensor_scalar(msk[:, :], gt[:, :], m2[:, :], None, ALU.is_ge), "eV")
            b.wait(V, e10)
            e11 = b.op(V, lambda e: e.tensor_mul(gm[:, :], gt[:, :], msk[:, :]), "eV")
            b.wait(V, e11)
            e12 = b.op(V, lambda e: e.reduce_sum(den[:, :], gm[:, :], axis=AX.X), "eV")
            b.wait(V, e12)
            e13 = b.op(V, lambda e: e.tensor_scalar_max(den[:, :], den[:, :], 1e-12), "eV")
            b.wait(V, e13)
            e14 = b.op(V, lambda e: e.reciprocal(den[:, :], den[:, :]), "eV")
            b.wait(V, e14)
            b.wait(V, e_wdma_prev)
            e_sm_prev = b.op(V, lambda e: e.tensor_scalar_mul(wts[:, :], gm[:, :], den[:, :]), "eV")
            if dbg and i == 0:
                b.wait("gpsimd", e_sm_prev)
                e_tap = b.dma("gpsimd", "dD", dbg_w, wts[:, :])
                b.wait("vector", e_tap)
            b.wait("gpsimd", e_sm_prev)
            e_wdma_prev = b.dma_excl("gpsimd", "dB", w_dram[:, i * 128:(i + 1) * 128].rearrange("e t -> t e"),
                                     wts[:, :], allow_slow_non_contiguous=True)
        for e_i in range(8):
            pb = e_i % 2
            b.wait("sync", e_wdma_prev, ev_wbl[pb])
            ev_wb = b.dma("sync", "dB2", wb[pb][:, :],
                          w_dram[e_i:e_i + 1, cc * 1024:(cc + 1) * 1024].broadcast_to((128, 1024)))
            if dbg and cc == 0 and e_i == 0:
                b.wait("gpsimd", ev_wb)
                e_tap = b.dma("gpsimd", "dD", dbg_wb, wb[pb][:, :])
                b.wait("sync", e_tap)
            for hc in range(32):
                wbuf = hc % 2
                b.wait("sync", ev_w1l[wbuf])
                e_w1 = b.dma("sync", "dWs%d" % wbuf, w1_sb[wbuf][:, :].rearrange("p (c m) -> p c m", m=128),
                             w1b[e_i, hc].rearrange("c p m -> p c m"))
                if dbg and cc == 0 and e_i == 0 and hc == 0:
                    b.wait("vector", e_w1)
                    et1 = b.op("vector", lambda e: e.tensor_copy(s1_st[:, :], xn_fm[:, 0:1024]), "eV")
                    b.wait("gpsimd", et1)
                    e_tap = b.dma("gpsimd", "dD", dbg_xf, s1_st[:, :])
                    b.wait("vector", e_tap)
                    et2 = b.op("vector", lambda e: e.tensor_copy(s1_st[:, :], w1_sb[0][:, :]), "eV")
                    b.wait("gpsimd", et2)
                    e_tap = b.dma("gpsimd", "dD", dbg_w1, s1_st[:, :])
                    b.wait("vector", e_tap)
                for st in range(2):
                    b.wait("tensor", e_w1, ev_h_used[st])
                    em = None
                    for c in range(8):
                        em = b.op("tensor", lambda e, c=c, st=st, wbuf=wbuf: e.matmul(
                            ps_h[st][:, 0:512],
                            w1_sb[wbuf][:, c * 128:(c + 1) * 128],
                            xn_fm[:, c * 1024 + st * 512: c * 1024 + st * 512 + 512],
                            start=(c == 0), stop=(c == 7)), "eT")
                    ev_w1l[wbuf] = em
                    b.wait("scalar", em, ev_gfm_rd)
                    ev_h_used[st] = b.op("scalar", lambda e, st=st, hc=hc, e_i=e_i: e.activation(
                        g_fm[:, hc * 1024 + st * 512: hc * 1024 + st * 512 + 512],
                        ps_h[st][:, 0:512], AF.Gelu,
                        bias=b1_sb[:, e_i * 32 + hc: e_i * 32 + hc + 1]), "eS")
            if dbg and cc == 0 and e_i == 0:
                b.wait("gpsimd", ev_h_used[0], ev_h_used[1])
                e_tap = b.dma("gpsimd", "dD", dbg_g0, g_fm[:, 0:1024])
                b.wait("scalar", e_tap)
            ev_gelu_all = ev_h_used[1]
            for dc in range(8):
                wbuf = dc % 2
                b.wait("sync", ev_w2l[wbuf])
                e_w2 = b.dma("sync", "dWs2%d" % wbuf, w2_sb[wbuf][:, :].rearrange("p (c m) -> p c m", m=128),
                             w2b[e_i, dc].rearrange("c p m -> p c m"))
                for st in range(2):
                    b.wait("tensor", e_w2, ev_h_used[0], ev_h_used[1], ev_e_used[st])
                    em = None
                    for hc in range(32):
                        em = b.op("tensor", lambda e, hc=hc, st=st, wbuf=wbuf: e.matmul(
                            ps_e[st][:, 0:512],
                            w2_sb[wbuf][:, hc * 128:(hc + 1) * 128],
                            g_fm[:, hc * 1024 + st * 512: hc * 1024 + st * 512 + 512],
                            start=(hc == 0), stop=(hc == 31)), "eT")
                    ev_w2l[wbuf] = em
                    ev_gfm_rd = em
                    dst = acc if e_i == 0 else tmp
                    b.wait("vector", em, ev_wb, ev_acc_rd if e_i == 0 else ev_tmp_rd)
                    e_ev = b.op("vector", lambda e, dc=dc, st=st, e_i=e_i, dst=dst, pb=pb: e.scalar_tensor_tensor(
                        dst[:, dc * 1024 + st * 512: dc * 1024 + st * 512 + 512],
                        ps_e[st][:, 0:512],
                        b2_sb[:, e_i * 8 + dc: e_i * 8 + dc + 1],
                        wb[pb][:, st * 512:(st + 1) * 512],
                        ALU.add, ALU.mult), "eV")
                    ev_e_used[st] = e_ev
                    if e_i > 0:
                        b.wait("gpsimd", e_ev)
                        ev_acc_rd = b.op("gpsimd", lambda e, dc=dc, st=st: e.tensor_tensor(
                            acc[:, dc * 1024 + st * 512: dc * 1024 + st * 512 + 512],
                            acc[:, dc * 1024 + st * 512: dc * 1024 + st * 512 + 512],
                            tmp[:, dc * 1024 + st * 512: dc * 1024 + st * 512 + 512],
                            ALU.add), "eP")
                        ev_tmp_rd = ev_acc_rd
            if dbg and cc == 0 and e_i == 0:
                b.wait("gpsimd", ev_e_used[0], ev_e_used[1])
                e_tap = b.dma("gpsimd", "dD", dbg_a0, acc[:, 0:1024])
                b.wait("vector", e_tap)
            ev_wbl[pb] = ev_e_used[1]
            ev_fm_used = ev_w1l[1] or ev_w1l[0]
        if dbg and cc == 0:
            b.wait("gpsimd", ev_acc_rd)
            e_tap = b.dma("gpsimd", "dD", dbg_g, g_fm[:, 0:1024])
            e_tap2 = b.dma("gpsimd", "dD", dbg_acc, acc[:, 0:1024])
            b.wait("scalar", e_tap2)
            b.wait("vector", e_tap2)
        for dc in range(8):
            b.wait("scalar", ev_acc_rd, ev_tmp_rd)
            e_si = b.op("scalar", lambda e, dc=dc: e.activation(
                tmp[:, dc * 1024:(dc + 1) * 1024], acc[:, dc * 1024:(dc + 1) * 1024],
                AF.Silu), "eS")
            e_s1 = b.dma("sync", "dA2", s1_st[:, :], s1f[dc, :, cc * 1024:(cc + 1) * 1024])
            e_xf = b.dma("sync", "dA3", xf_st[:, :], xL1f[dc, :, cc * 1024:(cc + 1) * 1024])
            b.wait("vector", e_si, e_s1, e_out_dma)
            e_m = b.op("vector", lambda e, dc=dc: e.tensor_mul(
                out_st[:, :], tmp[:, dc * 1024:(dc + 1) * 1024], s1_st[:, :]), "eV")
            b.wait("vector", e_m, e_xf)
            e_o = b.op("vector", lambda e: e.tensor_add(out_st[:, :], out_st[:, :], xf_st[:, :]), "eV")
            b.wait("gpsimd", e_o)
            e_out_dma = b.dma("gpsimd", "dO", out_fm[dc, :, cc * 1024:(cc + 1) * 1024], out_st[:, :])
            ev_tmp_rd = e_si if dc == 7 else ev_tmp_rd
            ev_acc_rd = e_si if dc == 7 else ev_acc_rd
            b.wait("sync", e_m)
            b.wait("sync", e_o)

    b.wait("gpsimd", Ev("dO", b.cnt.get("dO", 0)), Ev("dD", b.cnt.get("dD", 0)))
    return b


def run_launch5(xL1_own_list, s1f_list, w1b=None, w2b=None, dbg=False):
    import ml_dtypes
    b = build_launch5(dbg)
    nc = b.finalize()
    if w1b is None:
        W1 = _G["L1_ffn_W1"]; W2 = _G["L1_ffn_W2"]
        w1b = np.ascontiguousarray(
            W1.reshape(8, 8, 128, 32, 128).transpose(0, 3, 1, 2, 4)).astype(ml_dtypes.bfloat16)
        w2b = np.ascontiguousarray(
            W2.reshape(8, 32, 128, 8, 128).transpose(0, 3, 1, 2, 4)).astype(ml_dtypes.bfloat16)
    in_maps = []
    for c in range(NCORES):
        x = xL1_own_list[c]
        xf = np.ascontiguousarray(x.T.reshape(8, 128, OWN))
        in_maps.append({
            "xL1": x, "xL1f": xf, "s1f": s1f_list[c],
            "Wg": _G["L1_ffn_Wg"], "bg": _G["L1_ffn_bg"].reshape(1, 8),
            "b1": _G["L1_ffn_b1"], "b2": _G["L1_ffn_b2"],
            "w1b": w1b, "w2b": w2b,
            "identb_i": _identity_bf16(),
        })
    res = run_bass_kernel_spmd(nc, in_maps, core_ids=list(range(NCORES)))
    run_launch5.res = res
    outs = []
    for c in range(NCORES):
        of = res.results[c]["outf"]
        rows = np.zeros((OWN, D), np.float32)
        for dc in range(8):
            rows[:, dc * 128:(dc + 1) * 128] = of[dc].T
        outs.append(rows)
    return outs



def build_launch1(dbg=False):
    b = Builder()
    NT1 = 33
    KV = 4128

    x_in = b.dram("x", [NT1 * 128, D], F32, kind="ExternalInput").ap()
    P_in = b.dram("Pp", [32, D], F32, kind="ExternalInput").ap()
    mem_W = b.dram("mem_W", [D, D], F32, kind="ExternalInput").ap()
    Wi = b.dram("Wi", [D, D], F32, kind="ExternalInput").ap()
    Wl1 = b.dram("Wl1", [D, D], F32, kind="ExternalInput").ap()
    Wl2 = b.dram("Wl2", [D, D], F32, kind="ExternalInput").ap()
    bi_in = b.dram("bi", [1, D], F32, kind="ExternalInput").ap()
    bl1_in = b.dram("bl1", [1, D], F32, kind="ExternalInput").ap()
    bl2_in = b.dram("bl2", [1, D], F32, kind="ExternalInput").ap()
    memb_in = b.dram("memb", [1, D], F32, kind="ExternalInput").ap()
    g1_in = b.dram("g1", [1, D], F32, kind="ExternalInput").ap()
    be1_in = b.dram("be1", [1, D], F32, kind="ExternalInput").ap()
    wmask_in = b.dram("wmask_i", [2, 128, 160], F32, kind="ExternalInput").ap()
    identb_in = b.dram("identb_i", [128, 128], BF16, kind="ExternalInput").ap()

    x1_out = b.dram("x1", [OWN, D], F32, kind="ExternalOutput").ap()
    hdd2f_out = b.dram("hdd2f", [8, 128, KV], BF16, kind="ExternalOutput").ap()
    gradp_out = b.dram("gradp", [8, 128, D], F32, kind="ExternalOutput").ap()

    kn_dram = b.dram("kn_dram", [8, 128, KV], BF16).ap()
    if dbg:
        dbg_xn = b.dram("dbg_xn", [128, D], F32, kind="ExternalOutput").ap()
        dbg_sr = b.dram("dbg_sr", [128, 160], F32, kind="ExternalOutput").ap()
        dbg_at = b.dram("dbg_at", [128, D], F32, kind="ExternalOutput").ap()
        dbg_kn = b.dram("dbg_kn", [128, D], F32, kind="ExternalOutput").ap()
        dbg_fm = b.dram("dbg_fm", [128, 160], F32, kind="ExternalOutput").ap()
        dbg_knr = b.dram("dbg_knr", [128, 1024], F32, kind="ExternalOutput").ap()
        dbg_kp = b.dram("dbg_kp", [128, 1024], F32, kind="ExternalOutput").ap()
        dbg_h1 = b.dram("dbg_h1", [128, 512], F32, kind="ExternalOutput").ap()
        dbg_w1t = b.dram("dbg_w1t", [128, 1024], F32, kind="ExternalOutput").ap()

    identb = b.sbuf("identb", [128, 128], BF16)
    big = b.sbuf("big", [128, 8 * NT1 * 128], BF16)
    grad_acc = b.sbuf("grad_acc", [128, 8 * D], F32)
    kn_st = [b.sbuf("kn_st0", [128, 8 * 512], BF16), b.sbuf("kn_st1", [128, 8 * 512], BF16)]
    kp_f = b.sbuf("kp_f", [128, 8 * 512], F32)
    kp_16 = b.sbuf("kp_16", [128, 8 * 512], BF16)
    h1_16 = b.sbuf("h1_16", [128, 8 * 512], BF16)
    h2_16 = b.sbuf("h2_16", [128, 8 * 512], BF16)
    df_16 = b.sbuf("df_16", [128, 8 * 512], BF16)
    hT = b.sbuf("hT", [128, 4 * D], BF16)
    dT = b.sbuf("dT", [128, 4 * D], BF16)
    bias_c = b.sbuf("bias_c", [128, 4 * 8], F32)
    wmask = b.sbuf("wmask", [128, 2 * 160], F32)
    scr = {"m": b.sbuf("scr_m", [128, 1], F32)[:, :],
           "ss": b.sbuf("scr_ss", [128, 1], F32)[:, :],
           "rstd": b.sbuf("scr_rstd", [128, 1], F32)[:, :],
           "sq": b.sbuf("scr_sq", [128, D], F32)[:, :]}
    rk = b.sbuf("rk", [128, 1], F32)
    nmax = b.sbuf("nmax", [128, 1], F32)
    sumr = b.sbuf("sumr", [128, 1], F32)
    rsum = b.sbuf("rsum", [128, 1], F32)
    x_t = [grad_acc[:, 0:D], grad_acc[:, D:2 * D]]
    xn_roll = [grad_acc[:, 2 * D:3 * D], grad_acc[:, 3 * D:4 * D]]
    attn_f = grad_acc[:, 4 * D:5 * D]
    g1b = grad_acc[:, 5 * D:6 * D]
    be1b = grad_acc[:, 6 * D:7 * D]
    srow = grad_acc[:, 7 * D:7 * D + 160]
    xn_t16 = kn_st[0][:, 0:D]
    kn_t16 = kn_st[0][:, D:2 * D]
    s32a = kn_st[0][:, 2 * D:3 * D]
    s32b = kn_st[0][:, 3 * D:4 * D]
    kn_fmt = kn_st[1][:, 0:D]
    p16 = kn_st[1][:, D:D + 160]
    pt1 = kn_st[1][:, D + 160:D + 288]
    pt2 = kn_st[1][:, D + 288:D + 416]
    scr["sq2"] = kp_f[:, 0:D]

    ps_tr = b.psum("ps_tr", [128, 1024], BF16)
    ps_a = b.psum("ps_a", [128, 512], F32)
    ps_b = b.psum("ps_b", [128, 512], F32)
    ps_c = b.psum("ps_c", [128, 512], F32)
    ps_d = b.psum("ps_d", [128, 512], F32)

    def wT_ap(w, kc, mc, width):
        base = w * 8192 + kc * 1024 + mc * 128
        return big[:, base: base + width]

    def wT_wview(w, mc):
        return big[:, w * 8192:(w + 1) * 8192].rearrange(
            "p (k t) -> p k t", t=1024)[:, :, mc * 128:(mc + 1) * 128]

    b.dma("sync", "dW", identb[:, :], identb_in)
    b.dma("sync", "dW", g1b, g1_in.broadcast_to((128, D)))
    b.dma("sync", "dW", be1b, be1_in.broadcast_to((128, D)))
    b.dma("sync", "dW", wmask[:, :].rearrange("p (s n) -> p s n", n=160),
          wmask_in.rearrange("s p n -> p s n"))
    for w, src_ in enumerate([bi_in, bl1_in, bl2_in, memb_in]):
        b.dma("sync", "dW", bias_c[:, w * 8:(w + 1) * 8],
              src_.rearrange("o (c p) -> p (o c)", p=128), allow_slow_non_contiguous=True)
    ev_const = b.total("dW")

    ev_xread = [None, None]
    e_tr_prev = None
    ev_kn_dma = None
    e_fm_prev = None
    ev_sr = None
    e_exp_prev = None
    ev_ptrd = None
    ev_pv = [None, None]
    ev_x1dma = None
    e_attn_ev = None
    ev_kn16_rd = None
    ev_xnroll_rd = [None, None]
    e_rc = None
    ev_attnf_rd = None
    ev_kncopy = None
    fmv = big[:, :].rearrange("p (c t) -> p c t", t=NT1 * 128)
    for i in range(NT1):
        bi_ = i % 2
        b.wait("sync", ev_xread[bi_])
        e_x = b.dma("sync", "dA%d" % bi_, x_t[bi_], x_in[i * 128:(i + 1) * 128, :])
        ev_ln, ev16, ev_xread[bi_] = ln_tile(
            b, x_t[bi_], xn_roll[bi_], scr, D, e_x,
            out16_ap=xn_t16, ev_out16_free=[e_tr_prev],
            ev_out_free=[ev_xnroll_rd[bi_]])
        if dbg and i == 6:
            b.wait("gpsimd", ev_ln)
            e_tap = b.dma("gpsimd", "dD", dbg_xn, xn_roll[bi_])
            b.wait("vector", e_tap)
        if i >= 1:
            V = "vector"
            e1 = b.op(V, lambda e: e.tensor_scalar(rk[:, :], scr["ss"], 1e-5, float(D),
                                                   ALU.subtract, ALU.mult), "eV")
            b.wait("scalar", e1)
            e2 = b.op("scalar", lambda e: e.activation(rk[:, :], rk[:, :], AF.Sqrt), "eS")
            b.wait(V, e2)
            e3 = b.op(V, lambda e: e.tensor_scalar_max(rk[:, :], rk[:, :], 1e-12), "eV")
            b.wait(V, e3)
            e4 = b.op(V, lambda e: e.reciprocal(rk[:, :], rk[:, :]), "eV")
            b.wait(V, e4)
            e5 = b.op(V, lambda e: e.tensor_scalar_mul(rk[:, :], rk[:, :], scr["rstd"]), "eV")
            b.wait("scalar", e5, ev_kn16_rd)
            e_kn = b.op("scalar", lambda e: e.activation(kn_t16, scr["sq"], AF.Copy,
                                                         scale=rk[:, :]), "eS")
            ev_kncopy = e_kn
            if dbg and i == 6:
                b.wait("gpsimd", e_kn)
                e_tap = b.dma("gpsimd", "dD", dbg_kn, kn_t16)
                b.wait("tensor", e_tap)
                b.wait("scalar", e_tap)
            b.wait("tensor", e_kn, ev_const, e_fm_prev, ev_ptrd)
            e_trk = None
            for c in range(8):
                e_trk = b.op("tensor", lambda e, c=c: e.transpose(
                    ps_tr[:, c * 128:(c + 1) * 128], kn_t16[:, c * 128:(c + 1) * 128],
                    identb[:, :]), "eT")
            ev_kn16_rd = e_trk
            b.wait("vector", e_trk, ev_kn_dma)
            e_knc = b.op("vector", lambda e: e.tensor_copy(kn_fmt, ps_tr[:, 0:1024]), "eV")
            ev_ptrd = e_knc
            b.wait("gpsimd", e_knc)
            ev_kn_dma = b.dma("gpsimd", "dKn",
                              kn_dram[:, :, (i - 1) * 128: i * 128].rearrange("c p t -> p c t"),
                              kn_fmt.rearrange("p (c t) -> p c t", t=128))
        b.wait("tensor", ev16, ev_const, e_fm_prev, ev_ptrd)
        e_trx = None
        for c in range(8):
            e_trx = b.op("tensor", lambda e, c=c: e.transpose(
                ps_tr[:, c * 128:(c + 1) * 128], xn_t16[:, c * 128:(c + 1) * 128],
                identb[:, :]), "eT")
        e_tr_prev = e_trx
        b.wait("vector", e_trx)
        e_fm = b.op("vector", lambda e, i=i: e.tensor_copy(
            fmv[:, :, i * 128:(i + 1) * 128],
            ps_tr[:, 0:1024].rearrange("p (c t) -> p c t", t=128)), "eV")
        e_fm_prev = e_fm
        ev_ptrd = e_fm

        if i >= 1:
            j = i - 1
            qc0 = 128 + j * 128
            kc0 = j * 128 + 96
            b.wait("tensor", e_fm, e_attn_ev)
            em = None
            for c in range(8):
                em = b.op("tensor", lambda e, c=c, qc0=qc0, kc0=kc0: e.matmul(
                    ps_a[:, 0:160], fmv[:, c, qc0:qc0 + 128], fmv[:, c, kc0:kc0 + 160],
                    start=(c == 0), stop=(c == 7)), "eT")
            slot = 0 if j == 0 else 1
            b.wait("vector", em, ev_sr, e_exp_prev)
            e_sr = b.op("vector", lambda e, slot=slot: e.tensor_add(
                srow, ps_a[:, 0:160], wmask[:, slot * 160:(slot + 1) * 160]), "eV")
            if dbg and j == 5:
                b.wait("gpsimd", e_sr)
                e_tap = b.dma("gpsimd", "dD", dbg_sr, srow)
                b.wait("vector", e_tap)
                e_t2 = b.op("vector", lambda e, kc0=kc0: e.tensor_copy(
                    srow, fmv[0:128, 0, kc0:kc0 + 160]), "eV")
                b.wait("gpsimd", e_t2)
                e_tap = b.dma("gpsimd", "dD", dbg_fm, srow)
                b.wait("vector", e_tap)
                e_t3 = b.op("vector", lambda e: e.tensor_add(
                    srow, ps_a[:, 0:160], wmask[:, 160:320]), "eV")
                b.wait("vector", e_t3)
            ev_sr = e_sr
            e_attn_ev = e_sr
            b.wait("vector", e_sr)
            e_mx = b.op("vector", lambda e: e.reduce_max(nmax[:, :], srow,
                                                         axis=AX.X, negate=True), "eV")
            b.wait("vector", e_mx)
            e_mx2 = b.op("vector", lambda e: e.tensor_scalar_mul(nmax[:, :], nmax[:, :], 1.0 / 32.0), "eV")
            b.wait("scalar", e_mx2, e_rc)
            e_exp = b.op("scalar", lambda e: e.activation(
                p16, srow, AF.Exp, bias=nmax[:, :], scale=1.0 / 32.0,
                accum_out=sumr[:, :]), "eS")
            e_exp_prev = e_exp
            b.wait("vector", e_exp)
            e_rc = b.op("vector", lambda e: e.reciprocal(rsum[:, :], sumr[:, :]), "eV")
            b.wait("tensor", e_exp, ev_pv[0], ev_pv[1], ev_ptrd)
            et1 = b.op("tensor", lambda e: e.transpose(ps_tr[:, 0:128], p16[:, 0:128], identb[:, :]), "eT")
            et2 = b.op("tensor", lambda e: e.transpose(ps_tr[0:32, 128:256], p16[:, 128:160], identb[:, :]), "eT")
            eta = et2
            for c in range(6):
                eta = b.op("tensor", lambda e, c=c, kc0=kc0: e.transpose(
                    ps_tr[:, 256 + c * 128: 256 + (c + 1) * 128], fmv[:, c, kc0:kc0 + 128],
                    identb[:, :]), "eT")
            b.wait("vector", et1, eta)
            ec1 = b.op("vector", lambda e: e.tensor_copy(pt1, ps_tr[:, 0:128]), "eV")
            ec2 = b.op("vector", lambda e: e.tensor_copy(pt2[0:32, :], ps_tr[0:32, 128:256]), "eV")
            ec3 = b.op("vector", lambda e: e.tensor_copy(s32a[:, 0:768], ps_tr[:, 256:1024]), "eV")
            b.wait("tensor", ec1, ec2, ec3)
            etb = None
            for c in range(2):
                etb = b.op("tensor", lambda e, c=c, kc0=kc0: e.transpose(
                    ps_tr[:, c * 128:(c + 1) * 128], fmv[:, 6 + c, kc0:kc0 + 128],
                    identb[:, :]), "eT")
            for c in range(6):
                etb = b.op("tensor", lambda e, c=c, kc0=kc0: e.transpose(
                    ps_tr[0:32, 256 + c * 128: 256 + (c + 1) * 128],
                    fmv[:, c, kc0 + 128:kc0 + 160], identb[:, :]), "eT")
            b.wait("vector", etb)
            ec4 = b.op("vector", lambda e: e.tensor_copy(s32a[:, 768:1024], ps_tr[:, 0:256]), "eV")
            ec5 = b.op("vector", lambda e: e.tensor_copy(s32b[0:32, 0:768], ps_tr[0:32, 256:1024]), "eV")
            b.wait("tensor", ec4, ec5)
            etc = None
            for c in range(2):
                etc = b.op("tensor", lambda e, c=c, kc0=kc0: e.transpose(
                    ps_tr[0:32, c * 128:(c + 1) * 128],
                    fmv[:, 6 + c, kc0 + 128:kc0 + 160], identb[:, :]), "eT")
            b.wait("vector", etc)
            ec6 = b.op("vector", lambda e: e.tensor_copy(s32b[0:32, 768:1024], ps_tr[0:32, 0:256]), "eV")
            ev_ptrd = ec6
            b.wait("tensor", ec1, ec2, ec3, ec4, ec5, ec6)
            for h in range(2):
                ph = ps_b if h == 0 else ps_c
                b.op("tensor", lambda e, h=h, ph=ph: e.matmul(
                    ph[:, 0:512], pt1, s32a[:, h * 512:(h + 1) * 512],
                    start=True, stop=False), "eT")
                ev_pv[h] = b.op("tensor", lambda e, h=h, ph=ph: e.matmul(
                    ph[:, 0:512], pt2[0:32, :], s32b[0:32, h * 512:(h + 1) * 512],
                    start=False, stop=True), "eT")
            b.wait("scalar", ev_pv[0], ev_pv[1], e_rc, ev_attnf_rd, ev_x1dma)
            e_at = b.op("scalar", lambda e: e.activation(attn_f[:, 0:512], ps_b[:, 0:512],
                                                         AF.Copy, scale=rsum[:, :]), "eS")
            e_at2 = b.op("scalar", lambda e: e.activation(attn_f[:, 512:1024], ps_c[:, 0:512],
                                                          AF.Copy, scale=rsum[:, :]), "eS")
            if dbg and j == 5:
                b.wait("gpsimd", e_at, e_at2)
                e_tap = b.dma("gpsimd", "dD", dbg_at, attn_f)
                b.wait("vector", e_tap)
                b.wait("scalar", e_tap)
            b.wait("vector", e_at, e_at2)
            e_add = b.op("vector", lambda e, bj=(j + 1) % 2: e.tensor_add(
                attn_f, attn_f, xn_roll[bj]), "eV")
            ev_xnroll_rd[(j + 1) % 2] = e_add
            b.wait("vector", ev_kncopy)
            ev_x1, _, ev_attnf_rd = ln_tile(b, attn_f, attn_f, scr, D, e_add,
                                            g_bc=g1b, be_bc=be1b,
                                            ev_out_free=[ev_x1dma])
            b.wait("gpsimd", ev_x1)
            ev_x1dma = b.dma("gpsimd", "dX1", x1_out[j * 128:(j + 1) * 128, :], attn_f)

    b.barrier()
    e_p = b.dma("sync", "dA0", x_t[0][0:32, :], P_in)
    b.wait("scalar", e_p)
    e_sq = b.op("scalar", lambda e: e.activation(scr["sq2"][0:32, :], x_t[0][0:32, :],
                                                 AF.Square, accum_out=scr["ss"][0:32, :]), "eS")
    b.wait("scalar", e_sq)
    e_sqq = b.op("scalar", lambda e: e.activation(rk[0:32, :], scr["ss"][0:32, :], AF.Sqrt), "eS")
    b.wait("vector", e_sqq)
    e_m1 = b.op("vector", lambda e: e.tensor_scalar_max(rk[0:32, :], rk[0:32, :], 1e-12), "eV")
    b.wait("vector", e_m1)
    e_m2 = b.op("vector", lambda e: e.reciprocal(rk[0:32, :], rk[0:32, :]), "eV")
    b.wait("scalar", e_m2)
    e_knp = b.op("scalar", lambda e: e.activation(kn_t16[0:32, :], x_t[0][0:32, :], AF.Copy,
                                                  scale=rk[0:32, :]), "eS")
    b.wait("tensor", e_knp)
    e_trp = None
    for c in range(8):
        e_trp = b.op("tensor", lambda e, c=c: e.transpose(
            ps_tr[:, c * 32:(c + 1) * 32], kn_t16[0:32, c * 128:(c + 1) * 128],
            identb[0:32, 0:32]), "eT")
    b.wait("vector", e_trp)
    e_kc = b.op("vector", lambda e: e.tensor_copy(
        kn_fmt[:, 0:256].rearrange("p (c t) -> p c t", t=32),
        ps_tr[:, 0:256].rearrange("p (c t) -> p c t", t=32)), "eV")
    b.wait("gpsimd", e_kc)
    b.dma("gpsimd", "dKn", kn_dram[:, :, 4096:4128].rearrange("c p t -> p c t"),
          kn_fmt[:, 0:256].rearrange("p (c t) -> p c t", t=32))

    b.barrier()
    ev_wst = None
    ev_cst_prev = None
    for w, src_ in enumerate([Wi, Wl1, Wl2, mem_W]):
        b.wait("sync", ev_cst_prev)
        e_l = b.dma_excl("sync", "dWl", grad_acc[:, :].rearrange("p (c n) -> p c n", n=D),
                         src_.rearrange("(c p) n -> p c n", p=128))
        b.wait("vector", e_l, ev_wst)
        e_cst = b.op("vector", lambda e: e.tensor_copy(kn_st[0][:, 0:4096], grad_acc[:, 0:4096]), "eV")
        e_cst2 = b.op("vector", lambda e: e.tensor_copy(kn_st[1][:, 0:4096], grad_acc[:, 4096:8192]), "eV")
        ev_cst_prev = e_cst2
        evE = None
        for mc in range(8):
            buf = kn_st[mc // 4]
            off = (mc % 4) * D
            b.wait("tensor", e_cst if mc < 4 else e_cst2, evE)
            e_t = None
            for kc in range(8):
                e_t = b.op("tensor", lambda e, buf=buf, off=off, kc=kc: e.transpose(
                    ps_tr[:, kc * 128:(kc + 1) * 128],
                    buf[:, off + kc * 128: off + (kc + 1) * 128], identb[:, :]), "eT")
            b.wait("vector", e_t)
            evE = b.op("vector", lambda e, w=w, mc=mc: e.tensor_copy(
                wT_wview(w, mc), ps_tr[:, 0:1024].rearrange("p (k t) -> p k t", t=128)), "eV")
        ev_wst = evE

    b.barrier()
    ev_knrd = [None, None]
    e_kpe = None
    e_kp16 = None
    ev_h1 = None
    ev_h2 = None
    ev_hdma = None
    ev_df = None
    ev_grad_mm = None
    e_ga = None
    evcp_prev = None
    ev_h1m = ev_h2m = ev_pdm = None
    ev_htr = ev_dtr = None
    widths = [512] * 8 + [32]
    for tt in range(9):
        wd = widths[tt]
        t0 = tt * 512
        kb = tt % 2
        b.wait("sync", ev_knrd[kb])
        e_kn = b.dma("sync", "dKr%d" % kb,
                     kn_st[kb][:, 0:8 * wd].rearrange("p (c t) -> p c t", t=wd),
                     kn_dram[:, :, t0:t0 + wd].rearrange("c p t -> p c t"))
        if dbg and tt == 0:
            b.wait("vector", e_kn)
            e_t1 = b.op("vector", lambda e: e.tensor_copy(kp_f[:, 0:1024], kn_st[0][:, 0:1024]), "eV")
            b.wait("gpsimd", e_t1)
            e_tap = b.dma("gpsimd", "dD", dbg_knr, kp_f[:, 0:1024])
            b.wait("vector", e_tap)
            e_t2 = b.op("vector", lambda e: e.tensor_copy(kp_f[:, 0:1024], big[:, 8192:9216]), "eV")
            b.wait("gpsimd", e_t2)
            e_tap = b.dma("gpsimd", "dD", dbg_w1t, kp_f[:, 0:1024])
            b.wait("vector", e_tap)
        ev_kp_last = None
        for mc in range(8):
            b.wait("tensor", e_kn, e_kpe)
            em = None
            for kc in range(8):
                em = b.op("tensor", lambda e, mc=mc, kc=kc, kb=kb, wd=wd: e.matmul(
                    ps_a[:, 0:wd], wT_ap(0, kc, mc, 128), kn_st[kb][:, kc * wd:(kc + 1) * wd],
                    start=(kc == 0), stop=(kc == 7)), "eT")
            ev_knrd[kb] = em
            b.wait("vector", em, ev_df, e_kp16)
            e_kpe = b.op("vector", lambda e, mc=mc, wd=wd: e.tensor_scalar(
                kp_f[:, mc * wd:(mc + 1) * wd], ps_a[:, 0:wd],
                bias_c[:, 0 * 8 + mc: 0 * 8 + mc + 1], None, ALU.add), "eV")
            ev_kp_last = e_kpe
        if dbg and tt == 0:
            b.wait("gpsimd", ev_kp_last)
            e_tap = b.dma("gpsimd", "dD", dbg_kp, kp_f[:, 0:1024])
            b.wait("vector", e_tap)
            b.wait("scalar", e_tap)
        b.wait("vector", ev_kp_last, ev_h1m)
        e_kp16 = b.op("vector", lambda e, wd=wd: e.tensor_copy(kp_16[:, 0:8 * wd], kp_f[:, 0:8 * wd]), "eV")
        for mc in range(8):
            b.wait("tensor", e_kp16, ev_h1, ev_h2m)
            em = None
            for kc in range(8):
                em = b.op("tensor", lambda e, mc=mc, kc=kc, wd=wd: e.matmul(
                    ps_b[:, 0:wd], wT_ap(1, kc, mc, 128), kp_16[:, kc * wd:(kc + 1) * wd],
                    start=(kc == 0), stop=(kc == 7)), "eT")
            ev_h1m = em
            b.wait("scalar", em)
            ev_h1 = b.op("scalar", lambda e, mc=mc, wd=wd: e.activation(
                h1_16[:, mc * wd:(mc + 1) * wd], ps_b[:, 0:wd], AF.Silu,
                bias=bias_c[:, 1 * 8 + mc: 1 * 8 + mc + 1]), "eS")
        if dbg and tt == 0:
            b.wait("gpsimd", ev_h1)
            e_tap = b.dma("gpsimd", "dD", dbg_h1, h1_16[:, 0:512])
            b.wait("scalar", e_tap)
            b.wait("tensor", e_tap)
        for mc in range(8):
            b.wait("tensor", ev_h1, ev_h2, ev_pdm, ev_hdma, ev_htr)
            em = None
            for kc in range(8):
                em = b.op("tensor", lambda e, mc=mc, kc=kc, wd=wd: e.matmul(
                    ps_c[:, 0:wd], wT_ap(2, kc, mc, 128), h1_16[:, kc * wd:(kc + 1) * wd],
                    start=(kc == 0), stop=(kc == 7)), "eT")
            ev_h2m = em
            b.wait("scalar", em)
            ev_h2 = b.op("scalar", lambda e, mc=mc, wd=wd: e.activation(
                h2_16[:, mc * wd:(mc + 1) * wd], ps_c[:, 0:wd], AF.Silu,
                bias=bias_c[:, 2 * 8 + mc: 2 * 8 + mc + 1]), "eS")
        b.wait("gpsimd", ev_h2, ev_hdma)
        ev_hdma = b.dma("gpsimd", "dH", hdd2f_out[:, :, t0:t0 + wd].rearrange("c p t -> p c t"),
                        h2_16[:, 0:8 * wd].rearrange("p (c t) -> p c t", t=wd))
        for mc in range(8):
            b.wait("tensor", ev_h2, ev_df)
            em = None
            for kc in range(8):
                em = b.op("tensor", lambda e, mc=mc, kc=kc, wd=wd: e.matmul(
                    ps_d[:, 0:wd], wT_ap(3, kc, mc, 128), h2_16[:, kc * wd:(kc + 1) * wd],
                    start=(kc == 0), stop=(kc == 7)), "eT")
            ev_pdm = em
            b.wait("vector", em, e_kp16, ev_dtr)
            ev_df = b.op("vector", lambda e, mc=mc, wd=wd: e.scalar_tensor_tensor(
                df_16[:, mc * wd:(mc + 1) * wd], ps_d[:, 0:wd],
                bias_c[:, 3 * 8 + mc: 3 * 8 + mc + 1], kp_f[:, mc * wd:(mc + 1) * wd],
                ALU.add, ALU.subtract), "eV")
        if tt == 8:
            b.wait("vector", ev_df)
            ev_df = b.op("vector", lambda e, wd=wd: e.tensor_scalar_mul(
                df_16[:, 0:8 * wd], df_16[:, 0:8 * wd], 0.5), "eV")
        nts = max(1, wd // 128)
        prow = min(wd, 128)
        ev_hT = ev_dT = None
        for src16, dstT in ((h2_16, hT), (df_16, dT)):
            evcp = None
            for ts in range(nts):
                b.wait("tensor", ev_h2 if src16 is h2_16 else ev_df, ev_grad_mm, evcp_prev, ev_ptrd)
                ev_ptrd = None
                e_t = None
                for mc in range(8):
                    e_t = b.op("tensor", lambda e, mc=mc, ts=ts, src16=src16, wd=wd, prow=prow: e.transpose(
                        ps_tr[0:prow, mc * 128:(mc + 1) * 128],
                        src16[:, mc * wd + ts * 128: mc * wd + ts * 128 + prow],
                        identb[:, :]), "eT")
                if src16 is h2_16:
                    ev_htr = e_t
                else:
                    ev_dtr = e_t
                b.wait("vector", e_t)
                evcp = b.op("vector", lambda e, ts=ts, dstT=dstT, prow=prow: e.tensor_copy(
                    dstT[0:prow, ts * D:(ts + 1) * D], ps_tr[0:prow, 0:1024]), "eV")
                evcp_prev = evcp
            if src16 is h2_16:
                ev_hT = evcp
            else:
                ev_dT = evcp
        for hc in range(8):
            for ms in range(2):
                b.wait("tensor", ev_hT, ev_dT, e_ga)
                em = None
                for ts in range(nts):
                    em = b.op("tensor", lambda e, hc=hc, ms=ms, ts=ts, prow=prow: e.matmul(
                        ps_d[:, 0:512], hT[0:prow, ts * D + hc * 128: ts * D + (hc + 1) * 128],
                        dT[0:prow, ts * D + ms * 512: ts * D + (ms + 1) * 512],
                        start=(ts == 0), stop=(ts == nts - 1)), "eT")
                ev_grad_mm = em
                b.wait("vector", em)
                if tt == 0:
                    e_ga = b.op("vector", lambda e, hc=hc, ms=ms: e.tensor_copy(
                        grad_acc[:, hc * D + ms * 512: hc * D + (ms + 1) * 512], ps_d[:, 0:512]), "eV")
                else:
                    e_ga = b.op("vector", lambda e, hc=hc, ms=ms: e.tensor_add(
                        grad_acc[:, hc * D + ms * 512: hc * D + (ms + 1) * 512],
                        grad_acc[:, hc * D + ms * 512: hc * D + (ms + 1) * 512],
                        ps_d[:, 0:512]), "eV")
    b.wait("gpsimd", e_ga)
    b.dma("gpsimd", "dG", gradp_out.rearrange("c p t -> p c t"),
          grad_acc[:, :].rearrange("p (c t) -> p c t", t=D))
    b.wait("gpsimd", b.total("dG"), b.total("dX1"), b.total("dH"), b.total("dKn"),
           b.total("dD"))
    return b




def _l1_wmask(first_half):
    m = np.full((2, 128, 160), NEG, dtype=np.float32)
    r = np.arange(128)[:, None]
    k = np.arange(160)[None, :]
    band = (k >= r + 16) & (k <= r + 32)
    m[1][band] = 0.0
    if first_half:
        m[0][band & (k >= 32)] = 0.0
    else:
        m[0][band] = 0.0
    return m


def run_launch1(x_full, dbg=False):
    b = build_launch1(dbg)
    nc = b.finalize()
    Ppad = np.zeros((32, D), np.float32)
    Ppad[:10] = _G["L0_attn_P"]
    in_maps = []
    for c in range(NCORES):
        bi, h = c // 2, c % 2
        start = h * OWN
        if h == 0:
            xs = np.concatenate([np.zeros((128, D), np.float32), x_full[bi, :OWN]], 0)
        else:
            xs = x_full[bi, OWN - 128: 2 * OWN]
        in_maps.append({
            "x": np.ascontiguousarray(xs), "Pp": Ppad,
            "mem_W": _G["L0_attn_mem_W"], "Wi": _G["L0_attn_Wi"],
            "Wl1": _G["L0_attn_Wl1"], "Wl2": _G["L0_attn_Wl2"],
            "bi": _G["L0_attn_bi"].reshape(1, D), "bl1": _G["L0_attn_bl1"].reshape(1, D),
            "bl2": _G["L0_attn_bl2"].reshape(1, D), "memb": _G["L0_attn_mem_b"].reshape(1, D),
            "g1": _G["L0_attn_g1"].reshape(1, D), "be1": _G["L0_attn_be1"].reshape(1, D),
            "wmask_i": _l1_wmask(h == 0), "identb_i": _identity_bf16(),
        })
    res = run_bass_kernel_spmd(nc, in_maps, core_ids=list(range(NCORES)))
    run_launch1.res = res
    return res.results



def build_launch2(dbg=False):
    b = Builder()
    WT = 256
    NTI = OWN // WT
    KV = 4128

    gradp_all = b.dram("gradp_all", [8, 8, 128, D], F32, kind="ExternalInput").ap()
    fgate_in = b.dram("fgate", [1, 1], F32, kind="ExternalInput").ap()
    lrate_in = b.dram("lrate", [1, 1], F32, kind="ExternalInput").ap()
    mem_W = b.dram("mem_W", [D, D], F32, kind="ExternalInput").ap()
    memb_in = b.dram("memb", [1, D], F32, kind="ExternalInput").ap()
    hdd2f_in = b.dram("hdd2f", [8, 128, KV], BF16, kind="ExternalInput").ap()
    x1f_in = b.dram("x1f", [8, 128, OWN], F32, kind="ExternalInput").ap()
    Wo = b.dram("Wo", [D, D], F32, kind="ExternalInput").ap()
    bo_in = b.dram("bo", [1, D], F32, kind="ExternalInput").ap()
    Wg = b.dram("Wg", [D, 2 * D], F32, kind="ExternalInput").ap()
    bg_in = b.dram("bg", [1, D], F32, kind="ExternalInput").ap()
    W1 = b.dram("W1", [4 * D, D], F32, kind="ExternalInput").ap()
    b1_in = b.dram("b1", [1, 4 * D], F32, kind="ExternalInput").ap()
    W2 = b.dram("W2", [D, 4 * D], F32, kind="ExternalInput").ap()
    b2_in = b.dram("b2", [1, D], F32, kind="ExternalInput").ap()
    g2_in = b.dram("g2", [1, D], F32, kind="ExternalInput").ap()
    be2_in = b.dram("be2", [1, D], F32, kind="ExternalInput").ap()
    g3_in = b.dram("g3", [1, D], F32, kind="ExternalInput").ap()
    be3_in = b.dram("be3", [1, D], F32, kind="ExternalInput").ap()
    identb_in = b.dram("identb_i", [128, 128], BF16, kind="ExternalInput").ap()
    identf_in = b.dram("identf_i", [128, 128], F32, kind="ExternalInput").ap()

    s_out = b.dram("s", [8, 128, OWN], F32, kind="ExternalOutput").ap()

    sc_dram = b.dram("sc_dram", [1, 2], F32).ap()
    if dbg:
        dbg_wn = b.dram("dbg_wn", [128, 1024], F32, kind="ExternalOutput").ap()
        dbg_mo = b.dram("dbg_mo", [128, 4096], F32, kind="ExternalOutput").ap()
        dbg_gt = b.dram("dbg_gt", [128, 4096], F32, kind="ExternalOutput").ap()
        dbg_x2 = b.dram("dbg_x2", [128, 2048], F32, kind="ExternalOutput").ap()
        dbg_mu = b.dram("dbg_mu", [128, 2048], F32, kind="ExternalOutput").ap()
        dbg_wg = b.dram("dbg_wg", [128, 2048], F32, kind="ExternalOutput").ap()
    w1t_dram = b.dram("w1t_dram", [32, 8, 128, 128], BF16).ap()
    wgt_dram = b.dram("wgt_dram", [8, 16, 128, 128], BF16).ap()
    w2t_dram = b.dram("w2t_dram", [8, 32, 128, 128], BF16).ap()

    identb = b.sbuf("identb", [128, 128], BF16)
    identf = b.sbuf("identf", [128, 128], F32)
    wres = b.sbuf("wres", [128, (8 + 8) * D], BF16)

    def wnewT(kc, mc, width=128):
        return wres[:, kc * D + mc * 128: kc * D + mc * 128 + width]

    def woT(kc, mc, width=128):
        return wres[:, 8 * D + kc * D + mc * 128: 8 * D + kc * D + mc * 128 + width]



    ar = b.sbuf("ar", [128, 24576], F32)
    gsum = ar[:, 0:8192]
    stage = ar[:, 8192:16384]
    memWTf = ar[:, 16384:24576]
    x1f_t = ar[:, 0:2048]
    mo_f = ar[:, 2048:4096]
    gate_f = ar[:, 4096:6144]
    tmp_f = ar[:, 6144:8192]
    x2tm = ar[:, 8192:10240]
    x2fm_f = ar[:, 10240:12288]
    stm = ar[:, 12288:14336]
    sq_scr = ar[:, 14336:15360]
    spre_f = ar[:, 15360:17408]
    hdd_t = [b.sbuf("hdd_t0", [128, 8 * WT], BF16), b.sbuf("hdd_t1", [128, 8 * WT], BF16)]
    mout16 = b.sbuf("mout16", [128, 8 * WT], BF16)
    mo16 = b.sbuf("mo16", [128, 8 * WT], BF16)
    x1_16 = b.sbuf("x1_16", [128, 8 * WT], BF16)
    x2_16 = b.sbuf("x2_16", [128, 8 * WT], BF16)
    ff1_16 = b.sbuf("ff1_16", [128, 32 * WT], BF16)
    stage16 = ff1_16[:, 0:8192]
    x2tm16 = b.sbuf("x2tm16", [128, 1024], BF16)
    w1tb = [b.sbuf("w1tb0", [128, 8 * 128], BF16), b.sbuf("w1tb1", [128, 8 * 128], BF16)]
    w2tb = [b.sbuf("w2tb0", [128, 16 * 128], BF16), b.sbuf("w2tb1", [128, 16 * 128], BF16)]
    wgtb = [b.sbuf("wgtb0", [128, 16 * 128], BF16), b.sbuf("wgtb1", [128, 16 * 128], BF16)]
    biasc = b.sbuf("biasc", [128, 8 * 4 + 32], F32)
    gainc = b.sbuf("gainc", [128, 4 * 8], F32)
    s0b = b.sbuf("s0b", [128, 1], F32)
    s1b = b.sbuf("s1b", [128, 1], F32)
    sc01 = b.sbuf("sc01", [1, 2], F32)
    scr = {"m": b.sbuf("scr_m", [128, 1], F32)[:, :],
           "ss": b.sbuf("scr_ss", [128, 1], F32)[:, :],
           "rstd": b.sbuf("scr_rstd", [128, 1], F32)[:, :],
           "sq": sq_scr, "sq2": tmp_f[:, 0:1024]}

    ps_tr = b.psum("ps_tr", [128, 1024], BF16)
    ps_ftr = b.psum("ps_ftr", [128, 512], F32)
    ps_m = [b.psum(f"ps_m{i}", [128, 512], F32) for i in range(4)]

    b.dma("sync", "dW", identb[:, :], identb_in)
    b.dma("sync", "dW", identf[:, :], identf_in)
    for w, src_ in enumerate([g2_in, be2_in, g3_in, be3_in]):
        b.dma("sync", "dW", gainc[:, w * 8:(w + 1) * 8],
              src_.rearrange("o (c p) -> p (o c)", p=128), allow_slow_non_contiguous=True)
    for w, src_ in enumerate([memb_in, bo_in, bg_in, b2_in]):
        b.dma("sync", "dW", biasc[:, w * 8:(w + 1) * 8],
              src_.rearrange("o (c p) -> p (o c)", p=128), allow_slow_non_contiguous=True)
    b.dma("sync", "dW", biasc[:, 32:64],
          b1_in.rearrange("o (c p) -> p (o c)", p=128), allow_slow_non_contiguous=True)
    ev_const = b.total("dW")

    e_f = b.dma("sync", "dSc", sc01[:, 0:1], fgate_in)
    e_l = b.dma("sync", "dSc", sc01[:, 1:2], lrate_in)
    b.wait("vector", e_f, e_l)
    e_s0 = b.op("vector", lambda e: e.tensor_scalar(sc01[0:1, 0:1], sc01[0:1, 0:1], -1.0, 1.0,
                                                    ALU.mult, ALU.add), "eV")
    e_s1 = b.op("vector", lambda e: e.tensor_scalar_mul(sc01[0:1, 1:2], sc01[0:1, 1:2],
                                                        float(2.0 / (32808.0 * 1024.0))), "eV")
    b.wait("gpsimd", e_s0, e_s1)
    e_scd = b.dma("gpsimd", "dSc2", sc_dram, sc01[0:1, :])
    b.wait("sync", e_scd)
    e_s0b = b.dma("sync", "dSc3", s0b[:, :], sc_dram[0:1, 0:1].broadcast_to((128, 1)))
    e_s1b = b.dma("sync", "dSc3", s1b[:, :], sc_dram[0:1, 1:2].broadcast_to((128, 1)))
    ev_scb = b.total("dSc3")

    e_ga = None
    ev_stage_rd = [None]
    for c in range(8):
        b.wait("sync", ev_stage_rd[0])
        e_g = b.dma_excl("sync", "dGl", stage.rearrange("p (c n) -> p c n", n=D),
                         gradp_all[c].rearrange("c p n -> p c n"))
        b.wait("vector", e_g, e_ga)
        if c == 0:
            e_ga = b.op("vector", lambda e: e.tensor_copy(gsum, stage), "eV")
        else:
            e_ga = b.op("vector", lambda e: e.tensor_add(gsum, gsum, stage), "eV")
        ev_stage_rd[0] = e_ga

    b.wait("sync", ev_stage_rd[0])
    e_mw = b.dma_excl("sync", "dGl", stage.rearrange("p (c n) -> p c n", n=D),
                      mem_W.rearrange("(c p) n -> p c n", p=128))
    ev_ftr_rd = None
    for mc in range(8):
        e_t = None
        for kc2 in range(0, 8, 4):
            b.wait("tensor", e_mw, ev_const, ev_ftr_rd)
            for kk in range(4):
                kc = kc2 + kk
                e_t = b.op("tensor", lambda e, mc=mc, kc=kc, kk=kk: e.transpose(
                    ps_ftr[:, kk * 128:(kk + 1) * 128],
                    stage[:, mc * D + kc * 128: mc * D + (kc + 1) * 128], identf[:, :]), "eT")
            ev_stage_rd[0] = e_t
            b.wait("vector", e_t)
            ev_ftr_rd = b.op("vector", lambda e, mc=mc, kc2=kc2: e.tensor_copy(
                memWTf.rearrange("p (k t) -> p k t", t=1024)[:, kc2:kc2 + 4, mc * 128:(mc + 1) * 128],
                ps_ftr[:, 0:512].rearrange("p (k t) -> p k t", t=128)), "eV")
    b.wait("vector", ev_ftr_rd, e_ga, ev_scb)
    e_w1s = b.op("vector", lambda e: e.tensor_scalar_mul(gsum, gsum, s1b[:, :]), "eV")
    b.wait("vector", e_w1s)
    e_wn = b.op("vector", lambda e: e.scalar_tensor_tensor(
        wres[:, 0:8192], memWTf, s0b[:, :], gsum, ALU.mult, ALU.add), "eV")

    def prep_bf16(src_, n_in_chunks, n_out_chunks, dst_fn, dst_dram=None, dram_shape=None):
        evs = {"cast": None, "tr": None, "cp": None}
        w_in = n_in_chunks * 128
        mc_per_load = max(1, 8192 // w_in)
        n_row_loads = n_out_chunks // mc_per_load
        for rl in range(n_row_loads):
            rows = mc_per_load * 128
            b.wait("sync", ev_stage_rd[0])
            e_ld = b.dma_excl("sync", "dGl", stage[:, 0:mc_per_load * w_in].rearrange(
                "p (c n) -> p c n", n=w_in),
                src_[rl * rows:(rl + 1) * rows].rearrange("(c p) n -> p c n", p=128))
            b.wait("vector", e_ld, evs["tr"])
            e_c = b.op("vector", lambda e, w=w_in, m_=mc_per_load: e.tensor_copy(
                stage16[:, 0:m_ * w], stage[:, 0:m_ * w]), "eV")
            evs["cast"] = e_c
            ev_stage_rd[0] = e_c
            for mcl in range(mc_per_load):
                mc = rl * mc_per_load + mcl
                for kc2 in range(0, n_in_chunks, 8):
                    b.wait("tensor", e_c, evs["cp"])
                    e_t = None
                    for kk in range(min(8, n_in_chunks - kc2)):
                        kc = kc2 + kk
                        e_t = b.op("tensor", lambda e, mcl=mcl, kc=kc, kk=kk, w=w_in: e.transpose(
                            ps_tr[:, kk * 128:(kk + 1) * 128],
                            stage16[:, mcl * w + kc * 128: mcl * w + (kc + 1) * 128],
                            identb[:, :]), "eT")
                    evs["tr"] = e_t
                    b.wait("vector", e_t)
                    nk = min(8, n_in_chunks - kc2)
                    if dst_dram is None:
                        e_cp = b.op("vector", lambda e, mc=mc, kc2=kc2, nk=nk: e.tensor_copy(
                            dst_fn(kc2, mc, nk),
                            ps_tr[:, 0:nk * 128].rearrange("p (k t) -> p k t", t=128)), "eV")
                    else:
                        e_cp = b.op("vector", lambda e, mc=mc, kc2=kc2, nk=nk: e.tensor_copy(
                            stage16[:, 8000:8000] if False else
                            x2tm16[:, 0:nk * 128], ps_tr[:, 0:nk * 128]), "eV")
                        b.wait("gpsimd", e_cp)
                        e_dma = b.dma("gpsimd", "dWt", dst_dram(kc2, mc, nk).rearrange("k p t -> p k t"),
                                      x2tm16[:, 0:nk * 128].rearrange("p (k t) -> p k t", t=128))
                        b.wait("vector", e_dma)
                        e_cp = e_dma
                    evs["cp"] = e_cp
        return evs["cp"]

    def woT_dst(kc2, mc, nk):
        return wres[:, 8 * D:16 * D].rearrange("p (k t) -> p k t", t=1024)[
            :, kc2:kc2 + nk, mc * 128:(mc + 1) * 128]

    ev_wo = prep_bf16(Wo, 8, 8, woT_dst)
    ev_wg = prep_bf16(Wg, 16, 8, None,
                      dst_dram=lambda kc2, mc, nk: wgt_dram[mc, kc2:kc2 + nk],
                      dram_shape=None)
    ev_w1 = prep_bf16(W1, 8, 32, None,
                      dst_dram=lambda kc2, mc, nk: w1t_dram[mc, kc2:kc2 + nk],
                      dram_shape=None)
    ev_tr2 = None
    ev_cp2 = None
    for fl in range(4):
        b.wait("sync", ev_stage_rd[0])
        e_ld = b.dma_excl("sync", "dGl", stage.rearrange("p (c n) -> p c n", n=1024),
                          W2[:, fl * 1024:(fl + 1) * 1024].rearrange("(c p) n -> p c n", p=128))
        b.wait("vector", e_ld, ev_tr2)
        e_c = b.op("vector", lambda e: e.tensor_copy(stage16[:, :], stage), "eV")
        ev_stage_rd[0] = e_c
        for mc in range(8):
            b.wait("tensor", e_c, ev_cp2)
            e_t = None
            for kk in range(8):
                e_t = b.op("tensor", lambda e, mc=mc, kk=kk: e.transpose(
                    ps_tr[:, kk * 128:(kk + 1) * 128],
                    stage16[:, mc * 1024 + kk * 128: mc * 1024 + (kk + 1) * 128],
                    identb[:, :]), "eT")
            ev_tr2 = e_t
            b.wait("vector", e_t)
            e_cp = b.op("vector", lambda e: e.tensor_copy(x2tm16[:, 0:1024], ps_tr[:, 0:1024]), "eV")
            b.wait("gpsimd", e_cp)
            e_dma = b.dma("gpsimd", "dWt", w2t_dram[mc, fl * 8:(fl + 1) * 8].rearrange("k p t -> p k t"),
                          x2tm16[:, 0:1024].rearrange("p (k t) -> p k t", t=128))
            b.wait("vector", e_dma)
            ev_cp2 = e_dma

    if dbg:
        b.wait("vector", e_wn)
        e_tc = b.op("vector", lambda e: e.tensor_copy(x1f_t[:, 0:1024], wres[:, 0:1024]), "eV")
        b.wait("gpsimd", e_tc)
        e_tap = b.dma("gpsimd", "dD", dbg_wn, x1f_t[:, 0:1024])
        b.wait("vector", e_tap)
    b.barrier()

    ev_hrd = [None, None]
    e_mo_ev = None
    e_mout = None
    ev_mo16 = None
    e_gate = None
    ev_x1c = None
    e_x1l = None
    ev_w1l = [None, None]
    ev_w2l = [None, None]
    ev_ff1 = None
    ev_ff1_rd = None
    ev_sdma = None
    ev_d = None
    ev_cb = None
    ev_mof_rd = None
    ev_gate_rd = None
    ev_x1rd = None
    ev_x116_rd = None
    ev_spre_rd = None
    ev_sev = None
    ev_wgl = [None, None]
    ev_x2c = None
    ev_x2f = None
    e_stt = None
    ev_tr_busy = None
    e_x2c16 = None
    ev_mm_evs = [None] * 4
    for ti in range(NTI):
        t0 = ti * WT
        hb = ti % 2
        b.wait("sync", ev_hrd[hb])
        e_h = b.dma("sync", "dH%d" % hb, hdd_t[hb][:, :].rearrange("p (c t) -> p c t", t=WT),
                    hdd2f_in[:, :, t0:t0 + WT].rearrange("c p t -> p c t"))
        b.wait("sync", ev_x1rd, ev_x1c)
        e_x1l = b.dma("sync", "dX1l", x1f_t.rearrange("p (c t) -> p c t", t=WT),
                      x1f_in[:, :, t0:t0 + WT].rearrange("c p t -> p c t"))
        b.wait("vector", e_x1l, ev_x1c, ev_x116_rd)
        e_x1c = b.op("vector", lambda e: e.tensor_copy(x1_16[:, :], x1f_t), "eV")
        for mc in range(8):
            b.wait("tensor", e_h, e_wn, ev_mm_evs[0])
            em = None
            for kc in range(8):
                em = b.op("tensor", lambda e, mc=mc, kc=kc, hb=hb: e.matmul(
                    ps_m[0][:, 0:WT], wnewT(kc, mc), hdd_t[hb][:, kc * WT:(kc + 1) * WT],
                    start=(kc == 0), stop=(kc == 7)), "eT")
            ev_hrd[hb] = em
            b.wait("vector", em, e_mout if True else None, ev_mo16)
            e_mout = b.op("vector", lambda e, mc=mc: e.tensor_scalar(
                mout16[:, mc * WT:(mc + 1) * WT], ps_m[0][:, 0:WT],
                biasc[:, 0 * 8 + mc:0 * 8 + mc + 1], None, ALU.add), "eV")
            ev_mm_evs[0] = e_mout
        for mc in range(8):
            b.wait("tensor", e_mout, ev_wo, ev_mm_evs[1])
            em = None
            for kc in range(8):
                em = b.op("tensor", lambda e, mc=mc, kc=kc: e.matmul(
                    ps_m[1][:, 0:WT], woT(kc, mc), mout16[:, kc * WT:(kc + 1) * WT],
                    start=(kc == 0), stop=(kc == 7)), "eT")
            b.wait("vector", em, e_mo_ev, ev_mof_rd)
            e_mo_ev = b.op("vector", lambda e, mc=mc: e.tensor_scalar(
                mo_f[:, mc * WT:(mc + 1) * WT], ps_m[1][:, 0:WT],
                biasc[:, 1 * 8 + mc:1 * 8 + mc + 1], None, ALU.add), "eV")
            ev_mm_evs[1] = e_mo_ev
        if dbg and ti == 0:
            b.wait("gpsimd", e_mout, e_mo_ev)
            e_tap = b.dma("gpsimd", "dD", dbg_mu[:, 0:2048],
                          mout16[:, 0:2048], allow_slow_non_contiguous=True)
            e_tap = b.dma("gpsimd", "dD", dbg_mo[:, 0:2048], mo_f[:, 0:2048])
            b.wait("vector", e_tap)
        b.wait("vector", e_mo_ev, ev_mo16)
        ev_mo16 = b.op("vector", lambda e: e.tensor_copy(mo16[:, :], mo_f), "eV")
        for mc in range(8):
            gb_ = mc % 2
            b.wait("sync", ev_wgl[gb_])
            e_wgld = b.dma("sync", "dWg%d" % gb_, wgtb[gb_][:, :].rearrange("p (c m) -> p c m", m=128),
                           wgt_dram[mc].rearrange("c p m -> p c m"))
            if dbg and ti == 0 and mc == 0:
                b.wait("vector", e_wgld)
                e_tc = b.op("vector", lambda e: e.tensor_copy(x2tm[:, 0:2048], wgtb[0][:, :]), "eV")
                b.wait("gpsimd", e_tc)
                e_tap = b.dma("gpsimd", "dD", dbg_wg, x2tm[:, 0:2048])
                b.wait("vector", e_tap)
                b.wait("tensor", e_tap)
            b.wait("tensor", ev_mo16, e_x1c, e_wgld, ev_mm_evs[2])
            em = None
            for kc in range(16):
                rhs = x1_16 if kc < 8 else mo16
                kk = kc % 8
                em = b.op("tensor", lambda e, mc=mc, kc=kc, rhs=rhs, kk=kk, gb_=gb_: e.matmul(
                    ps_m[2][:, 0:WT], wgtb[gb_][:, kc * 128:(kc + 1) * 128],
                    rhs[:, kk * WT:(kk + 1) * WT],
                    start=(kc == 0), stop=(kc == 15)), "eT")
            ev_wgl[gb_] = em
            ev_x116_rd = em
            b.wait("scalar", em, e_gate, ev_gate_rd)
            e_gate = b.op("scalar", lambda e, mc=mc: e.activation(
                gate_f[:, mc * WT:(mc + 1) * WT], ps_m[2][:, 0:WT], AF.Sigmoid,
                bias=biasc[:, 2 * 8 + mc:2 * 8 + mc + 1]), "eS")
            ev_mm_evs[2] = e_gate
        if dbg and ti == 0:
            b.wait("gpsimd", e_gate)
            e_tap = b.dma("gpsimd", "dD", dbg_gt[:, 0:2048], gate_f[:, 0:2048])
            b.wait("vector", e_tap)
            b.wait("scalar", e_tap)
        b.wait("vector", e_gate, ev_d)
        ev_d = b.op("vector", lambda e: e.tensor_sub(tmp_f, x1f_t, mo_f), "eV")
        b.wait("vector", ev_d)
        e_cb = b.op("vector", lambda e: e.tensor_mul(gate_f, gate_f, tmp_f), "eV")
        b.wait("vector", e_cb)
        e_cb2 = b.op("vector", lambda e: e.tensor_add(gate_f, gate_f, mo_f), "eV")
        ev_mof_rd = e_cb2
        b.wait("vector", e_cb2)
        e_x2p = b.op("vector", lambda e: e.tensor_add(gate_f, gate_f, x1f_t), "eV")
        ev_cb = e_x2p
        ev_x1rd = e_x2p
        ev_x116_rd = None
        for st in range(2):
            e_t = None
            for g2_ in range(2):
                b.wait("tensor", e_x2p, ev_tr_busy)
                for kk in range(4):
                    c = g2_ * 4 + kk
                    e_t = b.op("tensor", lambda e, c=c, st=st, kk=kk: e.transpose(
                        ps_ftr[:, kk * 128:(kk + 1) * 128],
                        gate_f[:, c * WT + st * 128: c * WT + st * 128 + 128], identf[:, :]), "eT")
                ev_gate_rd = e_t
                b.wait("vector", e_t, ev_x2c)
                ev_x2c = b.op("vector", lambda e, g2_=g2_, st=st: e.tensor_copy(
                    x2tm[:, st * 1024 + g2_ * 512: st * 1024 + (g2_ + 1) * 512], ps_ftr[:, 0:512]), "eV")
                ev_tr_busy = ev_x2c
            ev_x2l, _, _ = ln_tile(b, x2tm[:, st * 1024:(st + 1) * 1024],
                                   x2tm[:, st * 1024:(st + 1) * 1024], scr, D, ev_x2c)
            for g2_ in range(2):
                b.wait("tensor", ev_x2l, ev_tr_busy)
                e_t = None
                for kk in range(4):
                    c = g2_ * 4 + kk
                    e_t = b.op("tensor", lambda e, c=c, st=st, kk=kk: e.transpose(
                        ps_ftr[:, kk * 128:(kk + 1) * 128],
                        x2tm[:, st * 1024 + c * 128: st * 1024 + (c + 1) * 128], identf[:, :]), "eT")
                b.wait("vector", e_t, ev_x2f, e_x2c16)
                for kk in range(4):
                    c = g2_ * 4 + kk
                    ev_x2f = b.op("vector", lambda e, c=c, st=st, kk=kk: e.tensor_scalar(
                        x2fm_f[:, c * WT + st * 128: c * WT + (st + 1) * 128 - (WT - 128) + (st * 0)] if False else
                        x2fm_f[:, c * WT + st * 128: c * WT + st * 128 + 128],
                        ps_ftr[:, kk * 128:(kk + 1) * 128],
                        gainc[:, 0 * 8 + c:0 * 8 + c + 1], gainc[:, 1 * 8 + c:1 * 8 + c + 1],
                        ALU.mult, ALU.add), "eV")
                ev_tr_busy = ev_x2f
        if dbg and ti == 0:
            b.wait("gpsimd", ev_x2f)
            e_tap = b.dma("gpsimd", "dD", dbg_x2, x2fm_f[:, 0:2048])
            b.wait("vector", e_tap)
        b.wait("vector", ev_x2f, e_x2c16)
        e_x2c16 = b.op("vector", lambda e: e.tensor_copy(x2_16[:, :], x2fm_f), "eV")
        ev_x2f = e_x2c16
        for fc in range(32):
            wb_ = fc % 2
            b.wait("sync", ev_w1l[wb_])
            e_wl = b.dma("sync", "dW1%d" % wb_, w1tb[wb_][:, :].rearrange("p (c m) -> p c m", m=128),
                         w1t_dram[fc].rearrange("c p m -> p c m"))
            b.wait("tensor", e_wl, ev_x2f, ev_ff1_rd, ev_mm_evs[3])
            em = None
            for kc in range(8):
                em = b.op("tensor", lambda e, fc=fc, kc=kc, wb_=wb_: e.matmul(
                    ps_m[3][:, 0:WT], w1tb[wb_][:, kc * 128:(kc + 1) * 128],
                    x2_16[:, kc * WT:(kc + 1) * WT],
                    start=(kc == 0), stop=(kc == 7)), "eT")
            ev_w1l[wb_] = em
            b.wait("scalar", em, ev_ff1)
            ev_ff1 = b.op("scalar", lambda e, fc=fc: e.activation(
                ff1_16[:, fc * WT:(fc + 1) * WT], ps_m[3][:, 0:WT], AF.Silu,
                bias=biasc[:, 32 + fc:32 + fc + 1]), "eS")
            ev_mm_evs[3] = ev_ff1
        for mc in range(8):
            em = None
            for fh in range(2):
                wb_ = (mc * 2 + fh) % 2
                b.wait("sync", ev_w2l[wb_])
                e_wl = b.dma("sync", "dW2%d" % wb_,
                             w2tb[wb_][:, :].rearrange("p (c m) -> p c m", m=128),
                             w2t_dram[mc, fh * 16:(fh + 1) * 16].rearrange("c p m -> p c m"))
                b.wait("tensor", e_wl, ev_ff1, e_stt)
                for fl in range(16):
                    fc = fh * 16 + fl
                    em = b.op("tensor", lambda e, fc=fc, fl=fl, wb_=wb_: e.matmul(
                        ps_m[0][:, 0:WT], w2tb[wb_][:, fl * 128:(fl + 1) * 128],
                        ff1_16[:, fc * WT:(fc + 1) * WT],
                        start=(fc == 0), stop=(fc == 31)), "eT")
                ev_w2l[wb_] = em
            ev_ff1_rd = em
            ev_mm_evs[0] = em
            b.wait("vector", em, e_stt, ev_spre_rd)
            e_stt = b.op("vector", lambda e, mc=mc: e.scalar_tensor_tensor(
                spre_f[:, mc * WT:(mc + 1) * WT], ps_m[0][:, 0:WT],
                biasc[:, 3 * 8 + mc:3 * 8 + mc + 1],
                x2fm_f[:, mc * WT:(mc + 1) * WT], ALU.add, ALU.add), "eV")
            ev_mm_evs[0] = e_stt
        for st in range(2):
            ev_stc = None
            for g2_ in range(2):
                b.wait("tensor", e_stt, ev_tr_busy)
                e_t = None
                for kk in range(4):
                    c = g2_ * 4 + kk
                    e_t = b.op("tensor", lambda e, c=c, st=st, kk=kk: e.transpose(
                        ps_ftr[:, kk * 128:(kk + 1) * 128],
                        spre_f[:, c * WT + st * 128: c * WT + st * 128 + 128], identf[:, :]), "eT")
                ev_spre_rd = e_t
                b.wait("vector", e_t, ev_stc)
                ev_stc = b.op("vector", lambda e, g2_=g2_, st=st: e.tensor_copy(
                    stm[:, st * 1024 + g2_ * 512: st * 1024 + (g2_ + 1) * 512], ps_ftr[:, 0:512]), "eV")
                ev_tr_busy = ev_stc
            scr_s = dict(scr)
            scr_s["sq2"] = x2tm[:, 0:1024]
            ev_sl, _, _ = ln_tile(b, stm[:, st * 1024:(st + 1) * 1024],
                                  stm[:, st * 1024:(st + 1) * 1024], scr_s, D, ev_stc)
            for g2_ in range(2):
                b.wait("tensor", ev_sl, ev_tr_busy)
                e_t = None
                for kk in range(4):
                    c = g2_ * 4 + kk
                    e_t = b.op("tensor", lambda e, c=c, st=st, kk=kk: e.transpose(
                        ps_ftr[:, kk * 128:(kk + 1) * 128],
                        stm[:, st * 1024 + c * 128: st * 1024 + (c + 1) * 128], identf[:, :]), "eT")
                b.wait("vector", e_t, ev_sev, ev_sdma)
                for kk in range(4):
                    c = g2_ * 4 + kk
                    ev_sev = b.op("vector", lambda e, c=c, st=st, kk=kk: e.tensor_scalar(
                        tmp_f[:, c * WT + st * 128: c * WT + st * 128 + 128],
                        ps_ftr[:, kk * 128:(kk + 1) * 128],
                        gainc[:, 2 * 8 + c:2 * 8 + c + 1], gainc[:, 3 * 8 + c:3 * 8 + c + 1],
                        ALU.mult, ALU.add), "eV")
                ev_tr_busy = ev_sev
        b.wait("gpsimd", ev_sev)
        ev_sdma = b.dma("gpsimd", "dS", s_out[:, :, t0:t0 + WT].rearrange("c p t -> p c t"),
                        tmp_f.rearrange("p (c t) -> p c t", t=WT))
        ev_d = ev_sev
    b.wait("gpsimd", b.total("dS"), b.total("dD"))
    return b


def run_launch2(x1_list, hdd2f_list, gradp_list, dbg=False):
    b = build_launch2(dbg)
    nc = b.finalize()
    gradp_all = np.stack(gradp_list)
    in_maps = []
    for c in range(NCORES):
        x1f = np.ascontiguousarray(x1_list[c].T.reshape(8, 128, OWN))
        in_maps.append({
            "gradp_all": gradp_all, "fgate": _G["L0_attn_fgate"].reshape(1, 1),
            "lrate": _G["L0_attn_lrate"].reshape(1, 1),
            "mem_W": _G["L0_attn_mem_W"], "memb": _G["L0_attn_mem_b"].reshape(1, D),
            "hdd2f": hdd2f_list[c], "x1f": x1f,
            "Wo": _G["L0_attn_Wo"], "bo": _G["L0_attn_bo"].reshape(1, D),
            "Wg": _G["L0_attn_Wg"], "bg": _G["L0_attn_bg"].reshape(1, D),
            "W1": _G["L0_attn_W1"], "b1": _G["L0_attn_b1"].reshape(1, 4 * D),
            "W2": _G["L0_attn_W2"], "b2": _G["L0_attn_b2"].reshape(1, D),
            "g2": _G["L0_attn_g2"].reshape(1, D), "be2": _G["L0_attn_be2"].reshape(1, D),
            "g3": _G["L0_attn_g3"].reshape(1, D), "be3": _G["L0_attn_be3"].reshape(1, D),
            "identb_i": _identity_bf16(), "identf_i": np.eye(128, dtype=np.float32),
        })
    res = run_bass_kernel_spmd(nc, in_maps, core_ids=list(range(NCORES)))
    run_launch2.res = res
    outs = []
    for c in range(NCORES):
        sf = res.results[c]["s"]
        rows = np.zeros((OWN, D), np.float32)
        for dc in range(8):
            rows[:, dc * 128:(dc + 1) * 128] = sf[dc].T
        outs.append(rows)
    return outs


NT3 = 45
QK0 = [4, 8]
QA0 = [8, 12]
FFG = [[(1024 + 512 * k, 512) for k in range(9)] + [(5632, 128)],
       [(1664 + 512 * k, 512) for k in range(8)]]
HID2 = 2730
FF2 = 5460
FFCH = [(c0, min(128, HID2 - c0)) for c0 in range(0, HID2, 128)]
SG = 512


R3 = 5760
NT3 = 45
QK0 = [4, 8]
QA0 = [8, 12]
FFG = [[(1024 + 512 * k, 512) for k in range(9)] + [(5632, 128)],
       [(1664 + 512 * k, 512) for k in range(8)]]
HID2 = 2730
FF2 = 5460
FFCH = [(c0, min(128, HID2 - c0)) for c0 in range(0, HID2, 128)]
SG = 512


def build_launch3(dbg=False):
    b = Builder()

    x_in = b.dram("x", [R3, D], F32, kind="ExternalInput").ap()
    xfm_in = b.dram("xfm", [8, 128, R3], F32, kind="ExternalInput").ap()
    s_in = b.dram("sin", [8, 128, OWN], F32, kind="ExternalInput").ap()
    wqkv_in = b.dram("wqkv", [2 * D, 1536], F32, kind="ExternalInput").ap()
    qs_in = b.dram("qs", [2, 64], F32, kind="ExternalInput").ap()
    ks_in = b.dram("ks", [2, 64], F32, kind="ExternalInput").ap()
    wo_in = b.dram("wo", [2 * 512, D], F32, kind="ExternalInput").ap()
    lng_in = b.dram("lng", [2, D], F32, kind="ExternalInput").ap()
    lnb_in = b.dram("lnb", [2, D], F32, kind="ExternalInput").ap()
    w1_in = b.dram("w1", [2 * D, FF2], F32, kind="ExternalInput").ap()
    b1p_in = b.dram("b1p", [2, 66, 128], F32, kind="ExternalInput").ap()
    w2_in = b.dram("w2", [2 * HID2, D], F32, kind="ExternalInput").ap()
    b2_in = b.dram("b2", [2, D], F32, kind="ExternalInput").ap()
    tabs_in = b.dram("tabs_i", [4, R3, 64], F32, kind="ExternalInput").ap()
    wm3_in = b.dram("wm3_i", [5, 128, 640], F32, kind="ExternalInput").ap()
    identb_in = b.dram("identb_i", [128, 128], BF16, kind="ExternalInput").ap()
    identf_in = b.dram("identf_i", [128, 128], F32, kind="ExternalInput").ap()
    colm_in = b.dram("colm_i", [128, 1], F32, kind="ExternalInput").ap()
    w1f_in = b.dram("w1f", [8 * D, 4096], F32, kind="ExternalInput").ap()
    w2f_in = b.dram("w2f", [8 * 4096, D], F32, kind="ExternalInput").ap()
    w1b_out = b.dram("w1b", [8, 32, 8, 128, 128], BF16, kind="ExternalOutput").ap()
    w2b_out = b.dram("w2b", [8, 8, 32, 128, 128], BF16, kind="ExternalOutput").ap()
    xL1_out = b.dram("xL1", [8, 128, OWN], F32, kind="ExternalOutput").ap()
    if dbg:
        dbg_xn = b.dram("dbg_xn", [8, 128, 128], F32, kind="ExternalOutput").ap()
        dbg_xa0 = b.dram("dbg_xa0", [8, 128, R3], F32, kind="ExternalOutput").ap()
        dbg_xl1 = b.dram("dbg_xl1", [8, 128, R3], F32, kind="ExternalOutput").ap()
        dbg_q0 = b.dram("dbg_q0", [8, 64, R3], BF16, kind="ExternalOutput").ap()
        dbg_k0 = b.dram("dbg_k0", [8, 64, R3], BF16, kind="ExternalOutput").ap()
        dbg_m0 = b.dram("dbg_m0", [2, R3], F32, kind="ExternalOutput").ap()
        dbg_st = b.dram("dbg_st", [4, R3], F32, kind="ExternalOutput").ap()
        dbg_hn = b.dram("dbg_hn", [10, 128, 4096], BF16, kind="ExternalOutput").ap()
        dbg_og = b.dram("dbg_og", [10, 128, 4096], F32, kind="ExternalOutput").ap()
        dbg_o16 = b.dram("dbg_o16", [10, 128, 4096], BF16, kind="ExternalOutput").ap()
        dbg_xa1 = b.dram("dbg_xa1", [8, 128, R3], F32, kind="ExternalOutput").ap()
        dbg_g = b.dram("dbg_g", [8, 128, R3], F32, kind="ExternalOutput").ap()
        dbg_vd = b.dram("dbg_vd", [NT3, 128, 512], BF16, kind="ExternalOutput").ap()
        dbg_od = b.dram("dbg_od", [NT3, 128, 512], F32, kind="ExternalOutput").ap()
        dbg_rs = b.dram("dbg_rs", [NT3, 128, 24], F32, kind="ExternalOutput").ap()
        dbg_sr = b.dram("dbg_sr", [128, 5120], F32, kind="ExternalOutput").ap()
        dbg_p16 = b.dram("dbg_p16", [128, 5120], BF16, kind="ExternalOutput").ap()
        dbg_ms = b.dram("dbg_ms", [128, 16], F32, kind="ExternalOutput").ap()

    xf32 = [b.dram("xf32a", [8, 128, R3], F32).ap(), b.dram("xf32b", [8, 128, R3], F32).ap()]
    x16d = [b.dram("x16a", [8, 128, R3], BF16).ap(), b.dram("x16b", [8, 128, R3], BF16).ap()]
    xaf_d = b.dram("xaf_d", [8, 128, R3], F32).ap()
    xa16_d = b.dram("xa16_d", [8, 128, R3], BF16).ap()
    qd = b.dram("qd", [8, 64, R3], BF16).ap()
    kd = b.dram("kd", [8, 64, R3], BF16).ap()
    vd = b.dram("vd", [NT3, 128, 512], BF16).ap()
    w1c_d = b.dram("w1c_d", [128, 8, FF2], BF16).ap()
    w2c_d = b.dram("w2c_d", [128, 22, D], BF16).ap()
    strow_d = b.dram("strow_d", [4, R3], F32).ap()
    mrow_d = b.dram("mrow_d", [2, R3], F32).ap()

    identb = b.sbuf("identb", [128, 128], BF16)
    identf = b.sbuf("identf", [128, 128], F32)
    x_t = [b.sbuf("x_t0", [128, D], F32), b.sbuf("x_t1", [128, D], F32)]
    scr = {"m": b.sbuf("scr_m", [128, 1], F32)[:, :],
           "ss": b.sbuf("scr_ss", [128, 1], F32)[:, :],
           "rstd": b.sbuf("scr_rstd", [128, 1], F32)[:, :],
           "sq": b.sbuf("scr_sq", [128, D], F32)[:, :],
           "sq2": b.sbuf("scr_sq2", [128, D], F32)[:, :]}
    tabs = b.sbuf("tabs", [128, 4 * NT3 * 64], BF16)
    tabs_f = b.sbuf("tabs_f", [128, 128], F32)
    wm3 = b.sbuf("wm3", [128, 5 * 640], F32)
    colm = b.sbuf("colm", [128, 1], F32)
    ARF = b.sbuf("ARF", [128, 8448], F32)
    ARB = b.sbuf("ARB", [128, 26624], BF16)
    qkv_f = b.sbuf("qkv_f", [128, 1024], F32)
    sqh = b.sbuf("sqh", [128, 512], F32)
    ssh = b.sbuf("ssh", [128, 16], F32)
    xa_f = b.sbuf("xa_f", [128, 1024], F32)
    xa_16 = b.sbuf("xa_16", [128, 1024], BF16)
    o_t = b.sbuf("o_t", [128, 512], F32)
    o16 = b.sbuf("o16", [128, 512], BF16)
    xt16 = b.sbuf("xt16", [128, 1024], BF16)
    v16 = b.sbuf("v16", [128, 512], BF16)
    qk16 = b.sbuf("qk16", [128, 512], BF16)
    fm_st = b.sbuf("fm_st", [128, 8 * 129], BF16)
    o_fm = b.sbuf("o_fm", [128, 1024], BF16)
    mx3 = b.sbuf("mx3", [128, 8], F32)
    sm3 = b.sbuf("sm3", [128, 8], F32)
    rs3 = b.sbuf("rs3", [128, 8], F32)
    stats = b.sbuf("stats", [128, NT3 * 4], F32)
    mbc = b.sbuf("mbc", [128, SG], F32)
    rbc = b.sbuf("rbc", [128, SG], F32)
    biasg = b.sbuf("biasg", [128, 66 + 8 + 8 + 8], F32)
    ac3 = b.sbuf("ac3", [128, SG], BF16)

    wqkv_f_v = ARF[:, 0:6144]
    wo_f_v = ARF[:, 0:4096]
    srow3 = ARF[:, 0:5120]
    xg_f = ARF[:, 0:8 * (SG + 1)]
    og_f = ARF[:, 4104:4104 + 2048] if SG == 256 else ARF[:, 4224:4224 + 4096]
    rowseg = [ARF[:, j * 1280:(j + 1) * 1280] for j in range(6)]
    wqkv_s = ARB[:, 0:12288]
    wo_s = ARB[:, 20480:24576]
    p163 = ARB[:, 0:5120]
    ptb = ARB[:, 5120:10240]
    kring = ARB[:, 10240:16384]
    vring = ARB[:, 16384:19456]
    qtile = ARB[:, 19456:20480]
    hn16 = ARB[:, 0:8 * SG]
    prod3 = ARB[:, 4096:4096 + 22 * SG]
    w1tb3 = [ARB[:, 15360:15360 + 2048], ARB[:, 17408:17408 + 2048]]
    w2tb3 = [ARB[:, 19456:19456 + 2816], ARB[:, 22272:22272 + 2816]]
    cb16 = ARB[:, 25088:25088 + 1024]

    ps_tr = b.psum("ps_tr", [128, 1024], BF16)
    ps_ftr = b.psum("ps_ftr", [128, 512], F32)
    ps_q = b.psum("ps_q", [128, 512], F32)
    ps_k = b.psum("ps_k", [128, 512], F32)
    ps_v = b.psum("ps_v", [128, 512], F32)
    ps_o = b.psum("ps_o", [128, 512], F32)

    b.dma("sync", "dW", identb[:, :], identb_in)
    b.dma("sync", "dW", identf[:, :], identf_in)
    b.dma("sync", "dW", wm3[:, :].rearrange("p (s n) -> p s n", n=640),
          wm3_in.rearrange("s p n -> p s n"))
    b.dma("sync", "dW", colm[:, :], colm_in)
    ev_const = b.total("dW")
    e_z = b.op("vector", lambda e: e.memset(xa_f[:, :], 0.0), "eV")
    e_z2 = b.op("vector", lambda e: e.memset(stats[:, :], 0.0), "eV")
    b.wait("gpsimd", e_z)
    b.dma("gpsimd", "dZi", xaf_d[:, :, 1023:1024].rearrange("c p t -> p c t"),
          xa_f[:, 0:8].rearrange("p (c t) -> p c t", t=1), allow_slow_non_contiguous=True)
    b.wait("vector", Ev("dZi", b.cnt["dZi"]))
    e_z3 = b.op("vector", lambda e: e.memset(xa_16[:, :], 0.0), "eV")
    b.wait("gpsimd", e_z3)
    b.dma("gpsimd", "dZi", x16d[1][:, :, 1023:1024].rearrange("c p t -> p c t"),
          xa_16[:, 0:8].rearrange("p (c t) -> p c t", t=1), allow_slow_non_contiguous=True)
    ev_zi = b.total("dZi")
    b.wait("sync", ev_zi)
    b.wait("tensor", Ev("eV", b.cnt["eV"]))

    for e_i in range(8):
        for hc2 in range(32):
            b.dma("gpsimd", "dMC", w1b_out[e_i, hc2].rearrange("c p m -> p c m"),
                  w1f_in.rearrange("(e c p) (h m) -> e h c p m", c=8, p=128, m=128)[
                      e_i, hc2].rearrange("c p m -> p c m"))
        for dc2 in range(8):
            b.dma("gpsimd", "dMC", w2b_out[e_i, dc2].rearrange("h p m -> p h m"),
                  w2f_in.rearrange("(e h p) (c m) -> e c h p m", h=32, p=128, m=128)[
                      e_i, dc2].rearrange("h p m -> p h m"))

    ev_xread = [None, None]
    ev_ftr_rd = None
    ev_fmw = None
    ev_f16 = None
    e_c16 = None
    for i in range(NT3):
        bi_ = i % 2
        b.wait("sync", ev_xread[bi_])
        e_x = b.dma("sync", "dA%d" % bi_, x_t[bi_][:, :], x_in[i * 128:(i + 1) * 128, :])
        ev_ln, _, ev_xread[bi_] = ln_tile(b, x_t[bi_][:, :], scr["sq2"], scr, D, e_x)
        for g_ in range(2):
            b.wait("tensor", ev_ln, ev_ftr_rd, ev_const)
            e_t = None
            for kk in range(4):
                c = g_ * 4 + kk
                e_t = b.op("tensor", lambda e, c=c, kk=kk: e.transpose(
                    ps_ftr[:, kk * 128:(kk + 1) * 128],
                    scr["sq2"][:, c * 128:(c + 1) * 128], identf[:, :]), "eT")
            b.wait("vector", e_t, ev_fmw, e_c16)
            ev_ftr_rd = b.op("vector", lambda e, g_=g_: e.tensor_copy(
                xa_f[:, g_ * 512:(g_ + 1) * 512], ps_ftr[:, 0:512]), "eV")
        b.wait("vector", ev_ftr_rd, ev_f16)
        e_c16 = b.op("vector", lambda e: e.tensor_copy(xa_16[:, :], xa_f[:, :]), "eV")
        b.wait("gpsimd", ev_ftr_rd, e_c16)
        ev_fmw = b.dma("gpsimd", "dXf", xf32[0][:, :, i * 128:(i + 1) * 128].rearrange("c p t -> p c t"),
                       xa_f[:, :].rearrange("p (c t) -> p c t", t=128))
        ev_f16 = b.dma("gpsimd", "dXs", x16d[0][:, :, i * 128:(i + 1) * 128].rearrange("c p t -> p c t"),
                       xa_16[:, :].rearrange("p (c t) -> p c t", t=128))
    b.barrier()
    if dbg:
        b.dma("gpsimd", "dDb", dbg_xn, xf32[0][:, :, 2560:2688])
        b.wait("gpsimd", b.total("dDb"))

    for l in range(2):
        cur16 = x16d[l % 2]
        curf = xf32[l % 2]
        nxt16 = x16d[(l + 1) % 2]
        nxtf = xf32[(l + 1) % 2]

        for hf in range(2):
            e_wq = b.dma_excl("sync", "dWl", ARF[:, 0:6144].rearrange("p (c n) -> p c n", n=1536),
                              wqkv_in[l * D:(l + 1) * D].rearrange("(c p) n -> p c n", p=128)[
                                  :, hf * 4:(hf + 1) * 4])
            b.wait("vector", e_wq)
            e_cw = b.op("vector", lambda e, hf=hf: e.tensor_copy(
                wqkv_s[:, hf * 6144:(hf + 1) * 6144], ARF[:, 0:6144]), "eV")
            b.wait("sync", e_cw)
        e_wo = b.dma_excl("sync", "dWl", wo_f_v.rearrange("p (c n) -> p c n", n=D),
                          wo_in[l * 512:(l + 1) * 512].rearrange("(c p) n -> p c n", p=128))
        b.wait("vector", e_wo)
        e_cwo = b.op("vector", lambda e: e.tensor_copy(wo_s[:, :], wo_f_v), "eV")
        b.wait("sync", e_cwo)
        b.dma("gpsimd", "dWc", w1c_d.rearrange("p c n -> p c n"),
              w1_in[l * D:(l + 1) * D].rearrange("(c p) n -> p c n", p=128))
        for k in range(22):
            rows = FFCH[k][1]
            b.dma("gpsimd", "dWc", w2c_d[0:rows, k],
                  w2_in[l * HID2 + k * 128: l * HID2 + k * 128 + rows])
        ev_wc = b.total("dWc")
        b.dma("sync", "dWb", biasg[:, 0:66],
              b1p_in[l].rearrange("c p -> p c"), allow_slow_non_contiguous=True)
        b.dma("sync", "dWb", biasg[:, 66:74],
              b2_in[l:l + 1].rearrange("o (c p) -> p (o c)", p=128), allow_slow_non_contiguous=True)
        b.dma("sync", "dWb", biasg[:, 74:82],
              lng_in[l:l + 1].rearrange("o (c p) -> p (o c)", p=128), allow_slow_non_contiguous=True)
        b.dma("sync", "dWb", biasg[:, 82:90],
              lnb_in[l:l + 1].rearrange("o (c p) -> p (o c)", p=128), allow_slow_non_contiguous=True)
        ev_bias = b.total("dWb")

        e_q1 = b.dma("sync", "dQs", tabs_f[:, 0:64], qs_in[l:l + 1, :].broadcast_to((128, 64)))
        e_q2 = b.dma("sync", "dQs", tabs_f[:, 64:96], qs_in[l:l + 1, 32:64].broadcast_to((128, 32)))
        e_q3 = b.dma("sync", "dQs", tabs_f[:, 96:128], qs_in[l:l + 1, 0:32].broadcast_to((128, 32)))
        ev_tt_prev = None
        for (w, qcol) in [(0, 0), (1, 64)]:
            for i in range(NT3):
                e_tl = b.dma_excl("sync", "dTl", sqh[:, 0:64], tabs_in[w, i * 128:(i + 1) * 128, :])
                b.wait("vector", e_tl, e_q1, e_q2, e_q3, ev_tt_prev)
                ev_tt_prev = b.op("vector", lambda e, w=w, i=i, qcol=qcol: e.tensor_mul(
                    tabs[:, (w * NT3 + i) * 64:(w * NT3 + i + 1) * 64],
                    sqh[:, 0:64], tabs_f[:, qcol:qcol + 64]), "eV")
        b.wait("sync", ev_tt_prev)
        e_q1 = b.dma("sync", "dQs", tabs_f[:, 0:64], ks_in[l:l + 1, :].broadcast_to((128, 64)))
        e_q2 = b.dma("sync", "dQs", tabs_f[:, 64:96], ks_in[l:l + 1, 32:64].broadcast_to((128, 32)))
        e_q3 = b.dma("sync", "dQs", tabs_f[:, 96:128], ks_in[l:l + 1, 0:32].broadcast_to((128, 32)))
        for (w, qcol) in [(2, 0), (3, 64)]:
            for i in range(NT3):
                e_tl = b.dma_excl("sync", "dTl", sqh[:, 0:64], tabs_in[w, i * 128:(i + 1) * 128, :])
                b.wait("vector", e_tl, e_q1, e_q2, e_q3, ev_tt_prev)
                ev_tt_prev = b.op("vector", lambda e, w=w, i=i, qcol=qcol: e.tensor_mul(
                    tabs[:, (w * NT3 + i) * 64:(w * NT3 + i + 1) * 64],
                    sqh[:, 0:64], tabs_f[:, qcol:qcol + 64]), "eV")
        b.barrier()

        ev_fm_rd = None
        e_qe = e_ke = e_ve = None
        e_qkd = e_kkd = e_vkd = None
        ev_rot = None
        ev_tr16_rd = None
        for i in range(QK0[l], NT3):
            b.wait("sync", ev_fm_rd)
            e_fm = b.dma("sync", "dFm", fm_st[:, :].rearrange("p (c t) -> p c t", t=129),
                         cur16[:, :, i * 128 - 1:(i + 1) * 128].rearrange("c p t -> p c t"))
            ems = {}
            for (ps, n0, nm) in ((ps_q, 0, "q"), (ps_k, 512, "k"), (ps_v, 1024, "v")):
                b.wait("tensor", e_fm,
                       e_qe if ps is ps_q else (e_ke if ps is ps_k else e_ve))
                em = None
                for c in range(8):
                    coff = 1 if c < 4 else 0
                    em = b.op("tensor", lambda e, c=c, coff=coff, n0=n0, ps=ps: e.matmul(
                        ps[:, 0:512],
                        fm_st[:, c * 129 + coff: c * 129 + coff + 128],
                        wqkv_s[:, c * 1536 + n0: c * 1536 + n0 + 512],
                        start=(c == 0), stop=(c == 7)), "eT")
                ems[nm] = em
            ev_fm_rd = ems["v"]
            b.wait("vector", ems["v"], e_vkd)
            e_ve = b.op("vector", lambda e: e.tensor_copy(v16[:, :], ps_v[:, 0:512]), "eV")
            b.wait("gpsimd", e_ve)
            e_vkd = b.dma("gpsimd", "dVd", vd[i], v16[:, :])
            for (ps, tw, in_q) in ((ps_q, 0, True), (ps_k, 2, False)):
                if in_q and i < QA0[l]:
                    e_qe = ems["q"]
                    continue
                b.wait("vector", ems["q" if in_q else "k"], ev_rot)
                e_qf = b.op("vector", lambda e, ps=ps: e.tensor_copy(qkv_f[:, 0:512], ps[:, 0:512]), "eV")
                if in_q:
                    e_qe = e_qf
                else:
                    e_ke = e_qf
                b.wait("scalar", e_qf)
                e_s3 = b.op("scalar", lambda e: e.activation(sqh[:, 0:512], qkv_f[:, 0:512], AF.Square), "eS")
                b.wait("vector", e_s3)
                e_rd = b.op("vector", lambda e: e.tensor_reduce(
                    ssh[:, 0:8].rearrange("p (h o) -> p h o", o=1),
                    sqh[:, 0:512].rearrange("p (h c) -> p h c", c=64), axis=AX.X, op=ALU.add), "eV")
                b.wait("scalar", e_rd)
                e_s4 = b.op("scalar", lambda e: e.activation(ssh[:, 8:16], ssh[:, 0:8], AF.Sqrt), "eS")
                b.wait("vector", e_s4)
                e_m4 = b.op("vector", lambda e: e.tensor_scalar_max(ssh[:, 8:16], ssh[:, 8:16], 1e-12), "eV")
                b.wait("vector", e_m4)
                e_r4 = b.op("vector", lambda e: e.reciprocal(ssh[:, 8:16], ssh[:, 8:16]), "eV")
                b.wait("vector", e_r4)
                e_qn = b.op("vector", lambda e: e.tensor_mul(
                    qkv_f[:, 0:512].rearrange("p (h c) -> p h c", c=64),
                    qkv_f[:, 0:512].rearrange("p (h c) -> p h c", c=64),
                    ssh[:, 8:16].rearrange("p (h o) -> p h o", o=1).broadcast_to((128, 8, 64))), "eV")
                tA = tabs[:, (tw * NT3 + i) * 64:(tw * NT3 + i + 1) * 64]
                tB = tabs[:, ((tw + 1) * NT3 + i) * 64:((tw + 1) * NT3 + i + 1) * 64]
                b.wait("vector", e_qn)
                e_r1 = b.op("vector", lambda e, tA=tA: e.tensor_mul(
                    sqh[:, 0:512].rearrange("p (h c) -> p h c", c=64),
                    qkv_f[:, 0:512].rearrange("p (h c) -> p h c", c=64),
                    tA.rearrange("p (o c) -> p o c", o=1).broadcast_to((128, 8, 64))), "eV")
                qv = qkv_f[:, 0:512].rearrange("p (h two c) -> p h two c", two=2, c=32)
                ov = o_t[:, 0:512].rearrange("p (h two c) -> p h two c", two=2, c=32)
                e_r2 = b.op("vector", lambda e, tB=tB, qv=qv, ov=ov: e.tensor_mul(
                    ov[:, :, 0, :], qv[:, :, 1, :],
                    tB.rearrange("p (o two c) -> p o two c", o=1, two=2)[:, :, 0, :]
                    .broadcast_to((128, 8, 32))), "eV")
                e_r3 = b.op("vector", lambda e, tB=tB, qv=qv, ov=ov: e.tensor_mul(
                    ov[:, :, 1, :], qv[:, :, 0, :],
                    tB.rearrange("p (o two c) -> p o two c", o=1, two=2)[:, :, 1, :]
                    .broadcast_to((128, 8, 32))), "eV")
                b.wait("vector", e_r1, e_r2, e_r3, ev_tr16_rd)
                ev_rot = b.op("vector", lambda e: e.tensor_add(qk16[:, :], sqh[:, 0:512], o_t[:, 0:512]), "eV")
                b.wait("tensor", ev_rot, e_qkd if in_q else e_kkd)
                e_t = None
                for h in range(8):
                    e_t = b.op("tensor", lambda e, h=h: e.transpose(
                        ps_tr[0:64, h * 128:(h + 1) * 128], qk16[:, h * 64:(h + 1) * 64],
                        identb[:, :]), "eT")
                ev_tr16_rd = e_t
                b.wait("vector", e_t)
                e_cpq = b.op("vector", lambda e: e.tensor_copy(o_fm[0:64, :], ps_tr[0:64, 0:1024]), "eV")
                b.wait("gpsimd", e_cpq)
                dst = qd if in_q else kd
                e_dd = b.dma("gpsimd", "dQd", dst[:, :, i * 128:(i + 1) * 128].rearrange("h d t -> d h t"),
                             o_fm[0:64, :].rearrange("d (h t) -> d h t", t=128))
                if in_q:
                    e_qkd = e_dd
                else:
                    e_kkd = e_dd
        b.barrier()
        _l3_attn(b, l, locals())
        b.barrier()
        _l3_stats_rows(b, l, locals())
        b.barrier()
        _l3_ffn(b, l, locals())
        b.barrier()
        if dbg and l == 0:
            b.dma("gpsimd", "dDb", dbg_xa0, xaf_d)
            b.dma("gpsimd", "dDb", dbg_xl1, nxtf)
            b.dma("gpsimd", "dDb", dbg_q0, qd)
            b.dma("gpsimd", "dDb", dbg_k0, kd)
            b.dma("gpsimd", "dDb", dbg_m0, mrow_d[:, :])
            b.dma("gpsimd", "dDb", dbg_st, strow_d)
            b.dma("gpsimd", "dDb", dbg_vd, vd)
            b.wait("gpsimd", b.total("dDb"))
        if dbg and l == 1:
            b.dma("gpsimd", "dDb", dbg_xa1, xaf_d)
            b.dma("gpsimd", "dDb", dbg_g, nxtf)
            b.wait("gpsimd", b.total("dDb"))
        if l == 1:
            _l3_combine(b, locals())

    b.wait("gpsimd", b.total("dXL"), b.total("dMC"))
    return b


def _l3_attn(b, l, g):
    V, SC, T, P = "vector", "scalar", "tensor", "gpsimd"
    identb = g["identb"]; identf = g["identf"]
    wm3 = g["wm3"]; kring = g["kring"]; vring = g["vring"]; qtile = g["qtile"]
    srow3 = g["srow3"]; p163 = g["p163"]; ptb = g["ptb"]
    mx3 = g["mx3"]; sm3 = g["sm3"]; rs3 = g["rs3"]
    o_t = g["o_t"]; o16 = g["o16"]; o_fm = g["o_fm"]
    xa_f = g["xa_f"]; xa_16 = g["xa_16"]; stats = g["stats"]
    wo_s = g["wo_s"]; sqh = g["sqh"]; qkv_f = g["qkv_f"]
    ps_tr = g["ps_tr"]; ps_ftr = g["ps_ftr"]
    ps_q = g["ps_q"]; ps_k = g["ps_k"]; ps_v = g["ps_v"]; ps_o = g["ps_o"]
    qd = g["qd"]; kd = g["kd"]; vd = g["vd"]
    curf = g["curf"]; xaf_d = g["xaf_d"]; xa16_d = g["xa16_d"]
    e_cwo = g["e_cwo"]
    A0 = QA0[l]

    ev_kr = {}
    ev_vr = {}
    ev_ring_rd = [None] * 6
    ev_sev = None
    e_exp3 = None
    ev_pt3 = None
    ev_pv3 = None
    e_oev = None
    ev_o16 = None
    ev_ofm = None
    e_xa = None
    ev_xa16 = None
    e_xad = e_xad2 = None
    ev_st_wr = None
    e_qload = None
    ev_qrd = None
    ev_strd = None
    for i in range(A0, NT3):
        need = list(range(i - 4, i + 1)) if i == A0 else [i]
        for kt in need:
            sl = kt % 6
            b.wait("sync", ev_ring_rd[sl])
            ev_kr[kt] = b.dma("sync", "dKr%d" % (sl % 2),
                              kring[0:64, sl * 1024:(sl + 1) * 1024].rearrange("d (h t) -> d h t", t=128),
                              kd[:, :, kt * 128:(kt + 1) * 128].rearrange("h d t -> d h t"))
            ev_vr[kt] = b.dma("sync", "dVr%d" % (sl % 2), vring[:, sl * 512:(sl + 1) * 512], vd[kt])
        b.wait("sync", ev_qrd)
        e_qload = b.dma("sync", "dQl", qtile[0:64, :].rearrange("d (h t) -> d h t", t=128),
                        qd[:, :, i * 128:(i + 1) * 128].rearrange("h d t -> d h t"))
        for h in range(8):
            b.wait(T, e_qload, ev_sev)
            em0 = em1 = None
            for j in range(4):
                kt = i - 4 + j
                sl = kt % 6
                b.wait(T, ev_kr[kt])
                em0 = b.op(T, lambda e, h=h, j=j, sl=sl: e.matmul(
                    ps_q[:, j * 128:(j + 1) * 128], qtile[0:64, h * 128:(h + 1) * 128],
                    kring[0:64, sl * 1024 + h * 128: sl * 1024 + (h + 1) * 128],
                    start=True, stop=True), "eT")
            sl = i % 6
            b.wait(T, ev_kr[i])
            em1 = b.op(T, lambda e, h=h, sl=sl: e.matmul(
                ps_k[:, 0:128], qtile[0:64, h * 128:(h + 1) * 128],
                kring[0:64, sl * 1024 + h * 128: sl * 1024 + (h + 1) * 128],
                start=True, stop=True), "eT")
            b.wait(V, em0, em1, e_exp3)
            ev_sev = b.op(V, lambda e, h=h: e.tensor_copy(
                srow3[:, h * 640:h * 640 + 512], ps_q[:, 0:512]), "eV")
            ev_sev = b.op(V, lambda e, h=h: e.tensor_copy(
                srow3[:, h * 640 + 512:h * 640 + 640], ps_k[:, 0:128]), "eV")
        slot = (i - 12) if 13 <= i <= 16 else 0
        b.wait(P, ev_sev)
        ev_msk3 = b.op(P, lambda e, slot=slot: e.tensor_add(
            srow3.rearrange("p (h n) -> p h n", n=640),
            srow3.rearrange("p (h n) -> p h n", n=640),
            wm3[:, slot * 640:(slot + 1) * 640].rearrange("p (o n) -> p o n", o=1)
            .broadcast_to((128, 8, 640))), "eP")
        if g.get("dbg") and l == 0 and i == 30:
            e_tap = b.dma(P, "dDs", g["dbg_sr"], srow3)
            b.wait(V, e_tap)
        b.wait(V, ev_msk3)
        e_mx3 = b.op(V, lambda e: e.tensor_reduce(
            mx3[:, :].rearrange("p (h o) -> p h o", o=1),
            srow3.rearrange("p (h n) -> p h n", n=640), axis=AX.X,
            op=ALU.max, negate=True), "eV")
        b.wait(SC, e_mx3, ev_pt3)
        for h in range(8):
            e_exp3 = b.op(SC, lambda e, h=h: e.activation(
                p163[:, h * 640:(h + 1) * 640], srow3[:, h * 640:(h + 1) * 640],
                AF.Exp, bias=mx3[:, h:h + 1], accum_out=sm3[:, h:h + 1]), "eS")
        if g.get("dbg") and l == 0 and i == 30:
            b.wait(P, e_exp3)
            e_tap2 = b.dma(P, "dDs", g["dbg_p16"], p163)
            b.dma(P, "dDs", g["dbg_ms"][:, 0:8], mx3[:, :])
            e_tap3 = b.dma(P, "dDs", g["dbg_ms"][:, 8:16], sm3[:, :])
            b.wait(SC, e_tap2, e_tap3)
        b.wait(V, e_exp3)
        e_rs3 = b.op(V, lambda e: e.reciprocal(rs3[:, :], sm3[:, :]), "eV")
        ev_ptrd3 = None
        for f in range(5):
            b.wait(T, e_exp3, ev_pv3 if f == 0 else None, ev_ptrd3)
            e_t = None
            for k8 in range(8):
                idx = f * 8 + k8
                hh, pp = idx // 5, idx % 5
                e_t = b.op(T, lambda e, hh=hh, pp=pp, k8=k8: e.transpose(
                    ps_tr[:, k8 * 128:(k8 + 1) * 128],
                    p163[:, hh * 640 + pp * 128: hh * 640 + (pp + 1) * 128], identb[:, :]), "eT")
            ev_pt3 = e_t
            b.wait(V, e_t)
            ev_ptrd3 = b.op(V, lambda e, f=f: e.tensor_copy(
                ptb[:, f * 1024:(f + 1) * 1024], ps_tr[:, 0:1024]), "eV")
        b.wait(T, ev_ptrd3, e_oev)
        for h in range(8):
            for j in range(5):
                kt = i - 4 + j
                sl = kt % 6
                idx = h * 5 + j
                b.wait(T, ev_vr[kt])
                ev_pv3 = b.op(T, lambda e, h=h, j=j, sl=sl, idx=idx: e.matmul(
                    ps_o[:, h * 64:(h + 1) * 64], ptb[:, idx * 128:(idx + 1) * 128],
                    vring[:, sl * 512 + h * 64: sl * 512 + (h + 1) * 64],
                    start=(j == 0), stop=(j == 4)), "eT")
        ev_ring_rd[(i - 4) % 6] = ev_pv3
        b.wait(V, ev_pv3, e_rs3, ev_o16,
               Ev("dDo", b.cnt.get("dDo", 0)) if g.get("dbg") and l == 0 else None)
        e_oev = b.op(V, lambda e: e.tensor_mul(
            o_t[:, :].rearrange("p (h c) -> p h c", c=64),
            ps_o[:, 0:512].rearrange("p (h c) -> p h c", c=64),
            rs3[:, :].rearrange("p (h o) -> p h o", o=1).broadcast_to((128, 8, 64))), "eV")
        if g.get("dbg") and l == 0:
            b.wait(P, e_oev)
            b.dma(P, "dDo", g["dbg_od"][i], o_t[:, :])
            b.dma(P, "dDo", g["dbg_rs"][i, :, 0:8], mx3[:, :])
            b.dma(P, "dDo", g["dbg_rs"][i, :, 8:16], sm3[:, :])
            b.dma(P, "dDo", g["dbg_rs"][i, :, 16:24], rs3[:, :])
        b.wait(V, e_oev, ev_ofm)
        ev_o16 = b.op(V, lambda e: e.tensor_copy(o16[:, :], o_t[:, :]), "eV")
        b.wait(T, ev_o16, ev_ptrd3)
        e_t = None
        for c in range(4):
            e_t = b.op(T, lambda e, c=c: e.transpose(
                ps_tr[:, c * 128:(c + 1) * 128], o16[:, c * 128:(c + 1) * 128], identb[:, :]), "eT")
        b.wait(V, e_t)
        ev_ofm = b.op(V, lambda e: e.tensor_copy(o_fm[:, 0:512], ps_tr[:, 0:512]), "eV")
        b.wait("sync", e_xa, ev_strd, e_xad)
        e_xcur = b.dma("sync", "dXc", xa_f[:, :].rearrange("p (c t) -> p c t", t=128),
                       curf[:, :, i * 128:(i + 1) * 128].rearrange("c p t -> p c t"))
        ev_wo_mm = None
        for mc in range(8):
            b.wait(T, ev_ofm, e_cwo, ev_wo_mm)
            em = None
            for kc in range(4):
                em = b.op(T, lambda e, mc=mc, kc=kc: e.matmul(
                    ps_v[:, 0:128], wo_s[:, kc * D + mc * 128: kc * D + (mc + 1) * 128],
                    o_fm[:, kc * 128:(kc + 1) * 128],
                    start=(kc == 0), stop=(kc == 3)), "eT")
            b.wait(V, em, e_xcur, ev_xa16, e_xa)
            e_xa = b.op(V, lambda e, mc=mc: e.tensor_add(
                xa_f[:, mc * 128:(mc + 1) * 128], xa_f[:, mc * 128:(mc + 1) * 128],
                ps_v[:, 0:128]), "eV")
            ev_wo_mm = e_xa
        if i == 12:
            b.wait(V, e_xa)
            for mc in range(8):
                e_xa = b.op(V, lambda e, mc=mc: e.tensor_mul(
                    xa_f[:, mc * 128 + 127: mc * 128 + 128],
                    xa_f[:, mc * 128 + 127: mc * 128 + 128], g["colm"][:, :]), "eV")
        b.wait(V, e_xa, ev_xa16)
        ev_xa16 = b.op(V, lambda e: e.tensor_copy(xa_16[:, :], xa_f[:, :]), "eV")
        b.wait(P, e_xa, ev_xa16, e_xad, e_xad2)
        e_xad = b.dma(P, "dXa", xaf_d[:, :, i * 128:(i + 1) * 128].rearrange("c p t -> p c t"),
                      xa_f[:, :].rearrange("p (c t) -> p c t", t=128))
        e_xad2 = b.dma(P, "dXa2", xa16_d[:, :, i * 128:(i + 1) * 128].rearrange("c p t -> p c t"),
                       xa_16[:, :].rearrange("p (c t) -> p c t", t=128))
        for g_ in range(2):
            b.wait(T, e_xa, ev_qrd)
            e_t = None
            for kk in range(4):
                c = g_ * 4 + kk
                e_t = b.op(T, lambda e, c=c, kk=kk: e.transpose(
                    ps_ftr[:, kk * 128:(kk + 1) * 128], xa_f[:, c * 128:(c + 1) * 128],
                    identf[:, :]), "eT")
            b.wait(V, e_t, ev_qrd, ev_st_wr)
            ev_qrd = b.op(V, lambda e, g_=g_: e.tensor_copy(
                qkv_f[:, g_ * 512:(g_ + 1) * 512], ps_ftr[:, 0:512]), "eV")
        b.wait(V, ev_qrd)
        e_a1 = b.op(V, lambda e, i=i: e.reduce_sum(stats[:, i * 4:i * 4 + 1], qkv_f[:, 0:512], axis=AX.X), "eV")
        e_b1 = b.op(V, lambda e, i=i: e.reduce_sum(stats[:, i * 4 + 1:i * 4 + 2], qkv_f[:, 512:1024], axis=AX.X), "eV")
        b.wait(SC, ev_qrd, e_a1, e_b1)
        e_q5 = b.op(SC, lambda e, i=i: e.activation(sqh[:, 0:512], qkv_f[:, 0:512], AF.Square,
                                                    accum_out=stats[:, i * 4 + 2:i * 4 + 3]), "eS")
        b.wait(SC, e_q5)
        e_q6 = b.op(SC, lambda e, i=i: e.activation(sqh[:, 0:512], qkv_f[:, 512:1024], AF.Square,
                                                    accum_out=stats[:, i * 4 + 3:i * 4 + 4]), "eS")
        ev_st_wr = e_q6
        ev_strd = e_q6
    g["_"] = None


def _l3_stats_rows(b, l, g):
    V, SC, P = "vector", "scalar", "gpsimd"
    stats = g["stats"]; strow_d = g["strow_d"]; mrow_d = g["mrow_d"]
    rowseg = g["rowseg"]
    r0 = FFG[l][0][0]
    for j in range(4):
        b.dma(P, "dSt", strow_d[j:j + 1, :].rearrange("o (t p) -> p t o", p=128),
              stats[:, :].rearrange("p (t j) -> p t j", j=4)[:, :, j:j + 1],
              allow_slow_non_contiguous=True)
    b.wait("sync", Ev("dSt", b.cnt["dSt"]))
    seglen = 1184
    e_md = None
    e6 = None
    for s0 in range(r0, R3, seglen):
        w = min(seglen, R3 - s0)
        b.wait("sync", e6)
        e_l0 = b.dma("sync", "dSr", rowseg[0][0:1, 0:w], strow_d[0:1, s0:s0 + w])
        e_l1 = b.dma("sync", "dSr2", rowseg[1][0:1, 0:w], strow_d[1:2, s0 - 1:s0 - 1 + w])
        e_l2 = b.dma("sync", "dSr3", rowseg[2][0:1, 0:w], strow_d[2:3, s0:s0 + w])
        e_l3 = b.dma("sync", "dSr4", rowseg[3][0:1, 0:w], strow_d[3:4, s0 - 1:s0 - 1 + w])
        b.wait(V, e_l0, e_l1, e_l2, e_l3, e_md)
        e1 = b.op(V, lambda e, w=w: e.tensor_add(rowseg[4][0:1, 0:w], rowseg[0][0:1, 0:w],
                                                 rowseg[1][0:1, 0:w]), "eV")
        b.wait(V, e1)
        e2 = b.op(V, lambda e, w=w: e.tensor_scalar_mul(rowseg[4][0:1, 0:w], rowseg[4][0:1, 0:w],
                                                        1.0 / D), "eV")
        e3 = b.op(V, lambda e, w=w: e.tensor_add(rowseg[5][0:1, 0:w], rowseg[2][0:1, 0:w],
                                                 rowseg[3][0:1, 0:w]), "eV")
        b.wait(V, e2, e3)
        e4 = b.op(V, lambda e, w=w: e.tensor_scalar_mul(rowseg[5][0:1, 0:w], rowseg[5][0:1, 0:w],
                                                        1.0 / D), "eV")
        e5 = b.op(V, lambda e, w=w: e.tensor_mul(rowseg[0][0:1, 0:w], rowseg[4][0:1, 0:w],
                                                 rowseg[4][0:1, 0:w]), "eV")
        b.wait(V, e4, e5)
        e6 = b.op(V, lambda e, w=w: e.tensor_sub(rowseg[5][0:1, 0:w], rowseg[5][0:1, 0:w],
                                                 rowseg[0][0:1, 0:w]), "eV")
        b.wait(V, e6)
        e7 = b.op(V, lambda e, w=w: e.tensor_scalar_add(rowseg[5][0:1, 0:w], rowseg[5][0:1, 0:w],
                                                        1e-5), "eV")
        b.wait(SC, e7)
        e8 = b.op(SC, lambda e, w=w: e.activation(rowseg[5][0:1, 0:w], rowseg[5][0:1, 0:w],
                                                  AF.Sqrt), "eS")
        b.wait(V, e8)
        e9 = b.op(V, lambda e, w=w: e.reciprocal(rowseg[5][0:1, 0:w], rowseg[5][0:1, 0:w]), "eV")
        b.wait(P, e2, e9)
        b.dma(P, "dMr", mrow_d[0:1, s0:s0 + w], rowseg[4][0:1, 0:w])
        e_md = b.dma(P, "dMr", mrow_d[1:2, s0:s0 + w], rowseg[5][0:1, 0:w])
    g["_"] = None


def _l3_ffn(b, l, g):
    V, SC, T, P = "vector", "scalar", "tensor", "gpsimd"
    xg_f = g["xg_f"]; og_f = g["og_f"]; hn16 = g["hn16"]; prod3 = g["prod3"]
    w1tb3 = g["w1tb3"]; w2tb3 = g["w2tb3"]; ac3 = g["ac3"]; cb16 = g["cb16"]
    sqh = g["sqh"]; mbc = g["mbc"]; rbc = g["rbc"]; biasg = g["biasg"]; colm = g["colm"]
    ps_q = g["ps_q"]; ps_k = g["ps_k"]; ps_v = g["ps_v"]
    xaf_d = g["xaf_d"]; xa16_d = g["xa16_d"]; mrow_d = g["mrow_d"]
    w1c_d = g["w1c_d"]; w2c_d = g["w2c_d"]
    nxtf = g["nxtf"]; nxt16 = g["nxt16"]
    ev_wc = g["ev_wc"]; ev_bias = g["ev_bias"]

    ev_w1l3 = [None, None]
    ev_w2l3 = [None, None]
    e_hn = None
    ev_hn_rd = None
    e_ac = None
    e_gg = None
    e_pr = None
    ev_pr_rd = None
    e_og = None
    ev_og16 = None
    e_ogd = e_ogd2 = None
    ev_xg_rd = None
    for gi, (t0, gw) in enumerate(FFG[l]):
        b.wait("sync", ev_xg_rd)
        e_xgl = b.dma("sync", "dXg", xg_f.rearrange("p (c t) -> p c t", t=SG + 1)[:, :, 0:gw + 1],
                      xaf_d[:, :, t0 - 1:t0 + gw].rearrange("c p t -> p c t"))
        e_mb = b.dma("sync", "dXg2", mbc[:, 0:gw], mrow_d[0:1, t0:t0 + gw].broadcast_to((128, gw)))
        e_rb = b.dma("sync", "dXg2", rbc[:, 0:gw], mrow_d[1:2, t0:t0 + gw].broadcast_to((128, gw)))
        for c in range(8):
            coff = 1 if c < 4 else 0
            b.wait(V, e_xgl, e_mb, e_rb,
                   ev_hn_rd if c == 0 else None, e_hn if c == 0 else None,
                   e_ogd if c == 0 else None)
            e1 = b.op(V, lambda e, c=c, coff=coff, gw=gw: e.tensor_sub(
                og_f[:, 0:gw], xg_f[:, c * (SG + 1) + coff: c * (SG + 1) + coff + gw], mbc[:, 0:gw]), "eV")
            b.wait(V, e1)
            e2 = b.op(V, lambda e, gw=gw: e.tensor_mul(og_f[:, 0:gw], og_f[:, 0:gw], rbc[:, 0:gw]), "eV")
            b.wait(V, e2)
            e_hn = b.op(V, lambda e, c=c, gw=gw: e.tensor_scalar(
                hn16[:, c * SG:c * SG + gw], og_f[:, 0:gw],
                biasg[:, 74 + c:74 + c + 1], biasg[:, 82 + c:82 + c + 1],
                ALU.mult, ALU.add), "eV")
        if g.get("dbg") and l == 0:
            b.wait(P, e_hn)
            e_tapH = b.dma(P, "dDg", g["dbg_hn"][gi], hn16[:, 0:4096])
            b.wait(V, e_tapH)
        for ci, (c0, w) in enumerate(FFCH):
            wb_ = ci % 2
            b.wait("sync", ev_w1l3[wb_])
            e_w1 = b.dma("sync", "dWf%d" % wb_,
                         w1tb3[wb_][:, 0:8 * w].rearrange("p (c m) -> p c m", m=w),
                         w1c_d[:, :, c0:c0 + w])
            e_w1g = b.dma("sync", "dWf%d" % wb_,
                          w1tb3[wb_][:, 1024:1024 + 8 * w].rearrange("p (c m) -> p c m", m=w),
                          w1c_d[:, :, HID2 + c0:HID2 + c0 + w])
            b.wait(T, e_w1, e_w1g, e_hn, ev_wc, e_ac, e_gg)
            ema = emg = None
            for kc in range(8):
                ema = b.op(T, lambda e, kc=kc, w=w, wb_=wb_, gw=gw: e.matmul(
                    ps_q[0:w, 0:gw], w1tb3[wb_][:, kc * w:(kc + 1) * w],
                    hn16[:, kc * SG:kc * SG + gw],
                    start=(kc == 0), stop=(kc == 7)), "eT")
            for kc in range(8):
                emg = b.op(T, lambda e, kc=kc, w=w, wb_=wb_, gw=gw: e.matmul(
                    ps_k[0:w, 0:gw], w1tb3[wb_][:, 1024 + kc * w: 1024 + (kc + 1) * w],
                    hn16[:, kc * SG:kc * SG + gw],
                    start=(kc == 0), stop=(kc == 7)), "eT")
            ev_w1l3[wb_] = emg
            b.wait(V, ema, e_pr, ev_bias)
            e_ac = b.op(V, lambda e, ci=ci, w=w, gw=gw: e.tensor_scalar(
                ac3[0:w, 0:gw], ps_q[0:w, 0:gw], biasg[0:w, ci:ci + 1], None, ALU.add), "eV")
            b.wait(SC, emg, e_pr)
            e_gg = b.op(SC, lambda e, ci=ci, w=w, gw=gw: e.activation(
                sqh[0:w, 0:gw], ps_k[0:w, 0:gw], AF.Gelu, bias=biasg[0:w, 44 + ci:44 + ci + 1]), "eS")
            b.wait(V, e_ac, e_gg, ev_pr_rd if ci == 0 else None)
            e_pr = b.op(V, lambda e, ci=ci, w=w, gw=gw: e.tensor_mul(
                prod3[0:w, ci * SG:ci * SG + gw], ac3[0:w, 0:gw], sqh[0:w, 0:gw]), "eV")
        for mc in range(8):
            wb_ = mc % 2
            b.wait("sync", ev_w2l3[wb_])
            e_w2 = b.dma("sync", "dWh%d" % wb_,
                         w2tb3[wb_][:, 0:2816].rearrange("p (c m) -> p c m", m=128),
                         w2c_d[:, :, mc * 128:(mc + 1) * 128])
            em = None
            b.wait(T, e_w2, e_pr, e_og)
            for ci, (c0, w) in enumerate(FFCH):
                em = b.op(T, lambda e, ci=ci, w=w, mc=mc, wb_=wb_, gw=gw: e.matmul(
                    ps_v[:, 0:gw], w2tb3[wb_][0:w, ci * 128:(ci + 1) * 128],
                    prod3[0:w, ci * SG:ci * SG + gw],
                    start=(ci == 0), stop=(ci == 21)), "eT")
            ev_w2l3[wb_] = em
            ev_pr_rd = em
            b.wait(V, em, e_og, ev_og16, e_ogd)
            e_og = b.op(V, lambda e, mc=mc, gw=gw: e.scalar_tensor_tensor(
                og_f[:, mc * SG:mc * SG + gw], ps_v[:, 0:gw],
                biasg[:, 66 + mc:66 + mc + 1],
                xg_f[:, mc * (SG + 1) + 1: mc * (SG + 1) + 1 + gw], ALU.add, ALU.add), "eV")
            ev_xg_rd = e_og
        if g.get("dbg") and l == 0:
            b.wait(P, e_og)
            e_tapG = b.dma(P, "dDg", g["dbg_og"][gi], og_f[:, 0:4096])
            b.wait(V, e_tapG)
        if l == 0 and t0 <= 1663 < t0 + gw:
            co = 1663 - t0
            b.wait(V, e_og)
            for mc in range(8):
                e_og = b.op(V, lambda e, mc=mc, co=co: e.tensor_mul(
                    og_f[:, mc * SG + co: mc * SG + co + 1],
                    og_f[:, mc * SG + co: mc * SG + co + 1], colm[:, :]), "eV")
        b.wait(V, e_og, ev_og16, Ev("dXo2", b.cnt.get("dXo2", 0)))
        ev_og16 = b.op(V, lambda e, gw=gw: e.tensor_copy(
            hn16[:, 0:8 * SG].rearrange("p (c t) -> p c t", t=SG)[:, :, 0:gw],
            og_f[:, 0:8 * SG].rearrange("p (c t) -> p c t", t=SG)[:, :, 0:gw]), "eV")
        b.wait(P, ev_og16, e_ogd2)
        if g.get("dbg") and l == 0:
            e_tapO = b.dma(P, "dDg", g["dbg_o16"][gi], hn16[:, 0:4096])
        e_ogd2 = b.dma(P, "dXo2", nxt16[:, :, t0:t0 + gw].rearrange("c p t -> p c t"),
                       hn16[:, 0:8 * SG].rearrange("p (c t) -> p c t", t=SG)[:, :, 0:gw])
        b.wait(P, e_og, e_ogd)
        e_ogd = b.dma(P, "dXo", nxtf[:, :, t0:t0 + gw].rearrange("c p t -> p c t"),
                      og_f.rearrange("p (c t) -> p c t", t=SG)[:, :, 0:gw])
        ev_hn_rd = e_ogd2
    g["_"] = None


def _l3_combine(b, g):
    V, SC, P = "vector", "scalar", "gpsimd"
    ARF = g["ARF"]
    nxtf = g["nxtf"]; s_in = g["s_in"]; xfm_in = g["xfm_in"]; xL1_out = g["xL1_out"]
    gbuf = ARF[:, 0:2048]
    sbuf_ = ARF[:, 2048:4096]
    xbuf = ARF[:, 4096:6144]
    SG2 = 256
    e_si = e_mu = e_ad = e_od = None
    for gi in range(16):
        t0 = 1664 + gi * SG2
        so = gi * SG2
        b.wait("sync", e_si, e_mu)
        e_gl = b.dma("sync", "dCg", gbuf.rearrange("p (c t) -> p c t", t=SG2),
                     nxtf[:, :, t0:t0 + SG2].rearrange("c p t -> p c t"))
        b.wait("sync", e_ad, e_od)
        e_sl = b.dma("sync", "dCs", sbuf_.rearrange("p (c t) -> p c t", t=SG2),
                     s_in[:, :, so:so + SG2].rearrange("c p t -> p c t"))
        b.wait("sync", e_ad, e_od)
        e_xl = b.dma("sync", "dCx", xbuf.rearrange("p (c t) -> p c t", t=SG2),
                     xfm_in[:, :, t0:t0 + SG2].rearrange("c p t -> p c t"))
        b.wait(SC, e_gl, e_mu)
        e_si = b.op(SC, lambda e: e.activation(gbuf, gbuf, AF.Silu), "eS")
        b.wait(V, e_si, e_sl)
        e_mu = b.op(V, lambda e: e.tensor_mul(sbuf_, sbuf_, gbuf), "eV")
        b.wait(V, e_mu, e_xl)
        e_ad = b.op(V, lambda e: e.tensor_add(sbuf_, sbuf_, xbuf), "eV")
        b.wait(P, e_ad)
        e_od = b.dma(P, "dXL", xL1_out[:, :, so:so + SG2].rearrange("c p t -> p c t"),
                     sbuf_.rearrange("p (c t) -> p c t", t=SG2))
    g["_"] = None



def _l3_tables(start_global):
    t = np.arange(start_global - 1664, start_global + OWN, dtype=np.float64)
    half_idx = np.arange(0, 64, 2, dtype=np.float64)
    inv_freq = 1.0 / 10000.0 ** (half_idx / 64.0)
    freqs = t[:, None] * inv_freq[None, :]
    freqs = np.concatenate([freqs, freqs], 1)
    base = (half_idx + 0.4 * 64.0) / (1.4 * 64.0)
    power = (t - 8192 // 2) / 256.0
    scale = base[None, :] ** power[:, None]
    scale = np.concatenate([scale, scale], 1)
    cos = np.cos(freqs)
    sin = np.sin(freqs)
    sgn = np.ones(64); sgn[:32] = -1.0
    cS8q = (cos * scale * 8.0).astype(np.float32)
    Bq8 = (np.concatenate([sin[:, :32] * scale[:, :32], sin[:, 32:] * scale[:, 32:]], 1)
           * sgn[None, :] * 8.0).astype(np.float32)
    cSk = (cos / scale).astype(np.float32)
    Bk = (np.concatenate([sin[:, :32] / scale[:, :32], sin[:, 32:] / scale[:, 32:]], 1)
          * sgn[None, :]).astype(np.float32)
    return np.stack([cS8q, Bq8, cSk, Bk]).astype(np.float32)


def _l3_masks(first_half):
    m = np.full((5, 128, 640), NEG, np.float32)
    r = np.arange(128)[:, None]
    k = np.arange(640)[None, :]
    band = (k >= r) & (k <= r + 512)
    m[0][band] = 0.0
    for s in range(1, 5):
        i = 12 + s
        if first_half:
            ok = band & (k >= 2176 - 128 * i)
        else:
            ok = band
        m[s][ok] = 0.0
    return m


def run_launch3(x_full, s_list, dbg=False):
    b = build_launch3(dbg)
    nc = b.finalize()
    G = _G
    wqkv = np.ascontiguousarray(G["L0_ffn_wqkv"].reshape(2 * D, 1536))
    wo = np.ascontiguousarray(G["L0_ffn_wo"].reshape(2 * 512, D))
    w1 = np.ascontiguousarray(G["L0_ffn_w1"].reshape(2 * D, FF2))
    w2 = np.ascontiguousarray(G["L0_ffn_w2"].reshape(2 * HID2, D))
    b1p = np.zeros((2, 66, 128), np.float32)
    for l in range(2):
        for ci, (c0, w) in enumerate(FFCH):
            b1p[l, ci, :w] = G["L0_ffn_b1"][l, c0:c0 + w]
            b1p[l, 44 + ci, :w] = G["L0_ffn_b1"][l, HID2 + c0:HID2 + c0 + w]
    in_maps = []
    for c in range(NCORES):
        bi, h = c // 2, c % 2
        start = h * OWN
        lo = start - 1664
        if lo < 0:
            xs = np.concatenate([np.zeros((-lo, D), np.float32), x_full[bi, :start + OWN]], 0)
        else:
            xs = x_full[bi, lo:start + OWN]
        xs = np.ascontiguousarray(xs)
        xfm = np.ascontiguousarray(xs.T.reshape(8, 128, R3))
        colm = np.full((128, 1), 0.0 if h == 0 else 1.0, np.float32)
        in_maps.append({
            "x": xs, "xfm": xfm, "sin": s_list[c],
            "wqkv": wqkv, "qs": G["L0_ffn_q_scale"], "ks": G["L0_ffn_k_scale"],
            "wo": wo, "lng": G["L0_ffn_ln_g"], "lnb": G["L0_ffn_ln_b"],
            "w1": w1, "b1p": b1p, "w2": w2, "b2": G["L0_ffn_b2"],
            "tabs_i": _l3_tables(start), "wm3_i": _l3_masks(h == 0),
            "identb_i": _identity_bf16(), "identf_i": np.eye(128, dtype=np.float32),
            "colm_i": colm,
            "w1f": np.ascontiguousarray(G["L1_ffn_W1"].reshape(8 * D, 4096)),
            "w2f": np.ascontiguousarray(G["L1_ffn_W2"].reshape(8 * 4096, D)),
        })
    res = run_bass_kernel_spmd(nc, in_maps, core_ids=list(range(NCORES)))
    run_launch3.res = res
    outs = []
    for c in range(NCORES):
        xf = res.results[c]["xL1"]
        rows = np.zeros((OWN, D), np.float32)
        for dc in range(8):
            rows[:, dc * 128:(dc + 1) * 128] = xf[dc].T
        outs.append(rows)
    return outs


_G = {}


def kernel(**inputs):
    _G.clear()
    for k, v in inputs.items():
        _G[k] = np.asarray(v)
    x_full = _G["L0_attn_x"]

    res1 = run_launch1(x_full)
    x1_list = [res1[c]["x1"] for c in range(NCORES)]
    hdd2f_list = [res1[c]["hdd2f"] for c in range(NCORES)]
    gradp_list = [res1[c]["gradp"] for c in range(NCORES)]
    s_list = run_launch2(x1_list, hdd2f_list, gradp_list)

    s_fm = [np.ascontiguousarray(s.T.reshape(8, 128, OWN)) for s in s_list]
    xL1_list = run_launch3(x_full, s_fm)
    r3 = run_launch3.res.results[0]
    w1b, w2b = r3["w1b"], r3["w2b"]

    xL1_full = np.zeros((B, S, D), np.float32)
    for c in range(NCORES):
        bi, h = c // 2, c % 2
        xL1_full[bi, h * OWN:(h + 1) * OWN] = xL1_list[c]
    run_launch4(xL1_full)
    s1f_list = [run_launch4.res.results[c]["s1"] for c in range(NCORES)]

    outs5 = run_launch5(xL1_list, s1f_list, w1b, w2b)
    out = np.zeros((B, S, D), np.float32)
    for c in range(NCORES):
        bi, h = c // 2, c % 2
        out[bi, h * OWN:(h + 1) * OWN] = outs5[c]
    return out


# version 06 (final)
# speedup vs baseline: 1.0359x; 1.0144x over previous
import numpy as np
from contextlib import ExitStack

import concourse.bass as bass
import concourse.mybir as mybir
from concourse.bass_utils import run_bass_kernel_spmd

F32 = mybir.dt.float32
BF16 = mybir.dt.bfloat16
AF = mybir.ActivationFunctionType
ALU = mybir.AluOpType
AX = mybir.AxisListType

NCORES = 8
D = 1024
B = 4
S = 8192
OWN = 4096
NEG = -1.0e30


class Ev:
    __slots__ = ("sem", "val")

    def __init__(self, sem, val):
        self.sem = sem
        self.val = val


class Builder:
    ENGINES = ("sync", "scalar", "vector", "gpsimd", "tensor")

    def __init__(self):
        self.nc = bass.Bass("TRN2", target_bir_lowering=False, debug=False)
        self.es = ExitStack()
        self.plan = []
        self.sems = {}
        self.cnt = {}
        self.hwm = {}

    def sem(self, name):
        if name not in self.sems:
            self.sems[name] = self.es.enter_context(self.nc.semaphore(name))
            self.cnt[name] = 0
        return self.sems[name]

    def sbuf(self, name, shape, dt):
        h = self.es.enter_context(self.nc.sbuf_tensor(name, shape, dt))
        return h[tuple(slice(None) for _ in shape)]

    def psum(self, name, shape, dt):
        return self.es.enter_context(self.nc.psum_tensor(name, shape, dt))

    def dram(self, name, shape, dt, kind="Internal"):
        return self.nc.dram_tensor(name, shape, dt, kind=kind)

    def op(self, engine, fn, sem=None, k=1):
        if sem is not None:
            h = self.sem(sem)
            self.cnt[sem] += k
            ev = Ev(sem, self.cnt[sem])

            def thunk(eng, fn=fn, h=h, k=k):
                fn(eng).then_inc(h, k)
            self.plan.append((engine, thunk))
            return ev
        else:
            def thunk(eng, fn=fn):
                fn(eng)
            self.plan.append((engine, thunk))
            return None

    def wait(self, engine, *evs):
        for ev in evs:
            if ev is None:
                continue
            key = (engine, ev.sem)
            if self.hwm.get(key, 0) >= ev.val:
                continue
            self.hwm[key] = ev.val
            h = self.sem(ev.sem)

            def thunk(eng, h=h, v=ev.val):
                eng.wait_ge(h, v)
            self.plan.append((engine, thunk))

    def dma(self, engine, sem, out, in_, **kw):
        return self.op(engine, lambda e: e.dma_start(out=out, in_=in_, **kw), sem, 16)

    def dma_excl(self, engine, sem, out, in_, **kw):
        if self.cnt.get(sem, 0) > 0:
            self.wait(engine, Ev(sem, self.cnt[sem]))
        return self.dma(engine, sem, out, in_, **kw)

    def total(self, sem):
        return Ev(sem, self.cnt.get(sem, 0))

    def barrier(self, engines=None):
        evs = [Ev(s, self.cnt[s]) for s in self.cnt if self.cnt[s] > 0]
        for eng in (engines or self.ENGINES):
            self.wait(eng, *evs)

    def finalize(self):
        with self.nc.Block() as block:
            for eng_name in self.ENGINES:
                thunks = [t for (e, t) in self.plan if e == eng_name]
                if not thunks:
                    continue
                dec = getattr(block, eng_name)

                def section(eng, thunks=thunks):
                    for t in thunks:
                        t(eng)
                dec(section)
        self.es.close()
        return self.nc


def ln_tile(b, x_ap, out_ap, scr, D_, ev_in, eps=1e-5, out16_ap=None,
            g_bc=None, be_bc=None, ev_out_free=(), ev_out16_free=()):
    V, SC = "vector", "scalar"
    b.wait(V, *([ev_in] if isinstance(ev_in, Ev) else list(ev_in or [])))
    e = b.op(V, lambda en: en.reduce_sum(scr["m"], x_ap, axis=AX.X), "eV")
    b.wait(V, e)
    e = b.op(V, lambda en: en.tensor_scalar_mul(scr["m"], scr["m"], 1.0 / D_), "eV")
    b.wait(V, e)
    e = b.op(V, lambda en: en.tensor_scalar(scr["sq"], x_ap, scr["m"], None, ALU.subtract), "eV")
    ev_xread = e
    b.wait(SC, e)
    e = b.op(SC, lambda en: en.activation(scr["sq2"], scr["sq"], AF.Square, accum_out=scr["ss"]), "eS")
    b.wait(V, e)
    e = b.op(V, lambda en: en.tensor_scalar(scr["ss"], scr["ss"], 1.0 / D_, eps, ALU.mult, ALU.add), "eV")
    b.wait(SC, e)
    e = b.op(SC, lambda en: en.activation(scr["rstd"], scr["ss"], AF.Sqrt), "eS")
    b.wait(V, e)
    e = b.op(V, lambda en: en.reciprocal(scr["rstd"], scr["rstd"]), "eV")
    b.wait(V, e)
    b.wait(V, *ev_out_free)
    if g_bc is None:
        ev_out = b.op(V, lambda en: en.tensor_scalar_mul(out_ap, scr["sq"], scr["rstd"]), "eV")
    else:
        e = b.op(V, lambda en: en.tensor_scalar_mul(scr["sq"], scr["sq"], scr["rstd"]), "eV")
        b.wait(V, e)
        e = b.op(V, lambda en: en.tensor_mul(scr["sq"], scr["sq"], g_bc), "eV")
        b.wait(V, e)
        ev_out = b.op(V, lambda en: en.tensor_add(out_ap, scr["sq"], be_bc), "eV")
    ev16 = None
    if out16_ap is not None:
        b.wait(V, ev_out)
        b.wait(V, *ev_out16_free)
        ev16 = b.op(V, lambda en: en.tensor_copy(out16_ap, out_ap), "eV")
    return ev_out, ev16, ev_xread



def build_launch4(dbg=False):
    b = Builder()

    xL1 = b.dram("xL1", [S, D], F32, kind="ExternalInput").ap()
    Wq = b.dram("Wq", [D, 64], F32, kind="ExternalInput").ap()
    Wk = b.dram("Wk", [D, 64], F32, kind="ExternalInput").ap()
    Wv = b.dram("Wv", [D, D], F32, kind="ExternalInput").ap()
    identb_in = b.dram("identb_i", [128, 128], BF16, kind="ExternalInput").ap()
    dmask_in = b.dram("dmask_i", [4, 128, 512], F32, kind="ExternalInput").ap()
    prefmask_in = b.dram("prefmask", [128, 4096], BF16, kind="ExternalInput").ap()

    s1_out = b.dram("s1", [8, 128, OWN], F32, kind="ExternalOutput").ap()
    if dbg:
        dbg_xn = b.dram("dbg_xn", [128, D], F32, kind="ExternalOutput").ap()
        dbg_qk = b.dram("dbg_qk", [64, 256], F32, kind="ExternalOutput").ap()
        dbg_v = b.dram("dbg_v", [128, D], F32, kind="ExternalOutput").ap()
        dbg_s = b.dram("dbg_s", [128, 4608], F32, kind="ExternalOutput").ap()
        dbg_st = b.dram("dbg_st", [128, 3], F32, kind="ExternalOutput").ap()
        dbg_pt = b.dram("dbg_pt", [128, 512], F32, kind="ExternalOutput").ap()
        dbg_qf = b.sbuf("dbg_qf", [64, 256], F32)
        dbg_ptf = b.sbuf("dbg_ptf", [128, 512], F32)
        dbg_stf = b.sbuf("dbg_stf", [128, 3], F32)

    v_dram = b.dram("v_dram", [S // 128, 128, D], BF16).ap()
    rs_dram = b.dram("rs_dram", [128, 4], F32).ap()

    NT = S // 128
    NQB = OWN // 512

    identb = b.sbuf("identb", [128, 128], BF16)
    x_t = [b.sbuf("x_t0", [128, D], F32), b.sbuf("x_t1", [128, D], F32)]
    scr = {"m": b.sbuf("scr_m", [128, 1], F32)[:, :],
           "ss": b.sbuf("scr_ss", [128, 1], F32)[:, :],
           "rstd": b.sbuf("scr_rstd", [128, 1], F32)[:, :],
           "sq": b.sbuf("scr_sq", [128, D], F32)[:, :]}
    xn_t16 = b.sbuf("xn_t16", [128, D], BF16)
    xn_fm = b.sbuf("xn_fm", [128, 8 * 128], BF16)
    wq_f = b.sbuf("wq_f", [128, 8 * 64], F32)
    wk_f = b.sbuf("wk_f", [128, 8 * 64], F32)
    wq_s = b.sbuf("wq_s", [128, 8 * 64], BF16)
    wk_s = b.sbuf("wk_s", [128, 8 * 64], BF16)
    ar1 = b.sbuf("ar1", [128, 8192], F32)
    ar4 = b.sbuf("ar4", [128, 8192], BF16)
    q_fm = b.sbuf("q_fm", [64, S], BF16)
    k_fm = b.sbuf("k_fm", [64, S], BF16)
    v_t16 = b.sbuf("v_t16", [128, D], BF16)
    pt_buf = b.sbuf("pt_buf", [128, NT * 512], BF16)
    ar3 = b.sbuf("ar3", [128, 8 * 512], F32)
    prefm = b.sbuf("prefm", [128, 4096], BF16)
    dmask = b.sbuf("dmask", [128, 4 * 512], F32)
    vstream = b.sbuf("vstream", [128, 2 * 512], BF16)
    nmax = b.sbuf("nmax", [128, 1], F32)
    sumr = b.sbuf("sumr", [128, 1], F32)
    rsum4 = b.sbuf("rsum4", [128, 4], F32)
    rs_bc = b.sbuf("rs_bc", [128, 512], F32)

    scr["sq2"] = ar3[:, 0:D]
    ps_s = b.psum("ps_s", [128, 512], F32)
    ps_tr = b.psum("ps_tr", [128, 1024], BF16)
    ps_o = [b.psum(f"ps_{n}", [128, 512], F32) for n in "abcd"]

    e_id = b.dma("sync", "dW", identb[:, :], identb_in)
    e_wq = b.dma("sync", "dW", wq_f[:, :].rearrange("p (c n) -> p c n", n=64),
                 Wq.rearrange("(c p) n -> p c n", p=128))
    e_wk = b.dma("sync", "dW", wk_f[:, :].rearrange("p (c n) -> p c n", n=64),
                 Wk.rearrange("(c p) n -> p c n", p=128))
    e_wv = b.dma("sync", "dW", ar1[:, :].rearrange("p (c n) -> p c n", n=D),
                 Wv.rearrange("(c p) n -> p c n", p=128))
    e_dm = b.dma("sync", "dW", dmask[:, :].rearrange("p (c n) -> p c n", n=512),
                 dmask_in.rearrange("c p n -> p c n"))
    e_pm = b.dma("sync", "dW", prefm[:, :], prefmask_in)
    e_id = e_wq = e_wk = e_wv = e_dm = e_pm = b.total("dW")
    b.wait("vector", e_wq, e_wk, e_wv)
    e_cwq = b.op("vector", lambda e: e.tensor_copy(wq_s[:, :], wq_f[:, :]), "eV")
    e_cwk = b.op("vector", lambda e: e.tensor_copy(wk_s[:, :], wk_f[:, :]), "eV")
    e_cwv = b.op("vector", lambda e: e.tensor_copy(ar4[:, :], ar1[:, :]), "eV")

    ev_q_ev = ev_k_ev = None
    ev_vdma = None
    ev_fm_used = None
    ev_fm_prev = None
    ev_xread = [None, None]
    e_tr_prev = None
    for i in range(NT):
        bi_ = i % 2
        b.wait("sync", ev_xread[bi_])
        e_x = b.dma("sync", "dA%d" % bi_, x_t[bi_][:, :], xL1[i * 128:(i + 1) * 128, :])
        ev_ln, ev16, ev_xread[bi_] = ln_tile(b, x_t[bi_][:, :], scr["sq"], scr, D, e_x,
                                             out16_ap=xn_t16[:, :],
                                             ev_out16_free=[e_tr_prev])
        if dbg and i == 32:
            b.wait("gpsimd", ev_ln)
            e_tap = b.dma("gpsimd", "dD", dbg_xn, scr["sq"])
            b.wait("vector", e_tap)
        b.wait("tensor", ev16, e_id, ev_fm_prev)
        e_tr = None
        for c in range(8):
            e_tr = b.op("tensor", lambda e, c=c: e.transpose(
                ps_tr[:, c * 128:(c + 1) * 128],
                xn_t16[:, c * 128:(c + 1) * 128], identb[:, :]), "eT")
        e_tr_prev = e_tr
        b.wait("vector", e_tr, ev_fm_used)
        e_fm = b.op("vector", lambda e: e.tensor_copy(xn_fm[:, :], ps_tr[:, 0:1024]), "eV")
        ev_fm_prev = e_fm
        b.wait("tensor", e_fm, e_cwq, e_cwk, e_cwv, ev_q_ev, ev_k_ev)
        eq = ek = None
        for c in range(8):
            eq = b.op("tensor", lambda e, c=c: e.matmul(
                ps_o[0][0:64, 0:128], wq_s[:, c * 64:(c + 1) * 64],
                xn_fm[:, c * 128:(c + 1) * 128],
                start=(c == 0), stop=(c == 7)), "eT")
        for c in range(8):
            ek = b.op("tensor", lambda e, c=c: e.matmul(
                ps_o[1][0:64, 0:128], wk_s[:, c * 64:(c + 1) * 64],
                xn_fm[:, c * 128:(c + 1) * 128],
                start=(c == 0), stop=(c == 7)), "eT")
        if i > 0:
            b.wait("tensor", ev_vdma)
        ev_half = []
        for h in range(2):
            ph = ps_o[2 + h]
            evv = None
            for c in range(8):
                evv = b.op("tensor", lambda e, c=c, h=h, ph=ph: e.matmul(
                    ph[:, 0:512], xn_fm[:, c * 128:(c + 1) * 128],
                    ar4[:, c * 1024 + h * 512: c * 1024 + (h + 1) * 512],
                    start=(c == 0), stop=(c == 7)), "eT")
            ev_half.append(evv)
        ev_fm_used = ev_half[-1]
        b.wait("scalar", eq)
        ev_q_ev = b.op("scalar", lambda e, i=i: e.activation(
            q_fm[0:64, i * 128:(i + 1) * 128], ps_o[0][0:64, 0:128], AF.Copy,
            scale=0.125), "eS")
        b.wait("scalar", ek)
        ev_k_ev = b.op("scalar", lambda e, i=i: e.copy(
            k_fm[0:64, i * 128:(i + 1) * 128], ps_o[1][0:64, 0:128]), "eS")
        b.wait("vector", ev_half[0], ev_half[1])
        if i > 0:
            b.wait("vector", ev_vdma)
        e_v0 = b.op("vector", lambda e: e.tensor_copy(v_t16[:, 0:512], ps_o[2][:, 0:512]), "eV")
        e_v1 = b.op("vector", lambda e: e.tensor_copy(v_t16[:, 512:1024], ps_o[3][:, 0:512]), "eV")
        b.wait("gpsimd", e_v0, e_v1)
        ev_vdma = b.dma("gpsimd", "dV", v_dram[i], v_t16[:, :])

    if dbg:
        b.wait("vector", ev_q_ev, ev_k_ev)
        e_d1 = b.op("vector", lambda e: e.tensor_copy(dbg_qf[0:64, 0:128], q_fm[0:64, 4096:4224]), "eV")
        e_d2 = b.op("vector", lambda e: e.tensor_copy(dbg_qf[0:64, 128:256], k_fm[0:64, 4096:4224]), "eV")
        b.wait("gpsimd", e_d1, e_d2)
        b.dma("gpsimd", "dD", dbg_qk, dbg_qf[0:64, :])
        b.dma("gpsimd", "dD", dbg_v, v_dram[32])

    b.barrier()

    ev_pt_used = None
    ev_o_ev = [None] * 4
    ev_sb = None
    e_exp_prev = None
    e_trp_prev = None
    e_rc_prev = None
    ev_ptq_last = None
    e_rbc = None
    ev_rsd_prev = None
    evm_hist = []
    ev_dO_prev = None
    for qb in range(NQB):
        q0 = OWN + qb * 512
        kchunks = q0 // 512 + 1
        klen = kchunks * 512
        ktiles = klen // 128
        for qt in range(4):
            qi = qb * 4 + qt
            qrow = OWN + qi * 128
            diag = qrow // 512
            for kc in range(kchunks):
                b.wait("tensor", ev_sb)
                em = b.op("tensor", lambda e, kc=kc, qrow=qrow: e.matmul(
                    ps_s[:, 0:512], q_fm[0:64, qrow:qrow + 128],
                    k_fm[0:64, kc * 512:(kc + 1) * 512],
                    start=True, stop=True), "eT")
                b.wait("vector", em, e_exp_prev)
                ev_sb = b.op("vector", lambda e, kc=kc: e.tensor_copy(
                    ar1[:, kc * 512:(kc + 1) * 512], ps_s[:, 0:512]), "eV")
            b.wait("gpsimd", ev_sb, e_pm, e_dm)
            e_msk = None
            for kc in range(min(8, kchunks)):
                e_msk = b.op("gpsimd", lambda e, kc=kc: e.tensor_tensor(
                    ar1[:, kc * 512:(kc + 1) * 512], ar1[:, kc * 512:(kc + 1) * 512],
                    prefm[:, kc * 512:(kc + 1) * 512], ALU.add), "eP")
            slot = qi % 4
            e_msk2 = b.op("gpsimd", lambda e, diag=diag, slot=slot: e.tensor_tensor(
                ar1[:, diag * 512:(diag + 1) * 512], ar1[:, diag * 512:(diag + 1) * 512],
                dmask[:, slot * 512:(slot + 1) * 512], ALU.add), "eP")
            b.wait("vector", e_msk, e_msk2)
            if dbg and qb == 0 and qt == 0:
                b.wait("gpsimd", e_msk, e_msk2, ev_sb)
                e_tap = b.dma("gpsimd", "dD", dbg_s, ar1[:, 0:4608])
                b.wait("vector", e_tap)
                b.wait("scalar", e_tap)
            e_mx = b.op("vector", lambda e, klen=klen: e.reduce_max(
                nmax[:, :], ar1[:, 0:klen], axis=AX.X, negate=True), "eV")
            b.wait("scalar", e_mx, e_trp_prev, e_rc_prev)
            e_exp = b.op("scalar", lambda e, klen=klen: e.activation(
                ar4[:, 0:klen], ar1[:, 0:klen], AF.Exp, bias=nmax[:, :],
                accum_out=sumr[:, :]), "eS")
            e_exp_prev = e_exp
            b.wait("vector", e_exp, ev_rsd_prev)
            e_rc_prev = b.op("vector", lambda e, qt=qt: e.reciprocal(
                rsum4[:, qt:qt + 1], sumr[:, :]), "eV")
            if dbg and qb == 0 and qt == 0:
                b.wait("vector", e_mx, e_rc_prev)
                ed1 = b.op("vector", lambda e: e.tensor_copy(dbg_stf[:, 0:1], nmax[:, :]), "eV")
                ed2 = b.op("vector", lambda e: e.tensor_copy(dbg_stf[:, 1:2], sumr[:, :]), "eV")
                ed3 = b.op("vector", lambda e: e.tensor_copy(dbg_stf[:, 2:3], rsum4[:, 0:1]), "eV")
                b.wait("gpsimd", ed1, ed2, ed3)
                b.dma("gpsimd", "dD", dbg_st, dbg_stf[:, :])
            for g0 in range(0, ktiles, 8):
                gn = min(8, ktiles - g0)
                b.wait("tensor", e_exp, ev_pt_used, ev_ptq_last)
                e_trp = None
                for g in range(gn):
                    e_trp = b.op("tensor", lambda e, g=g, g0=g0: e.transpose(
                        ps_tr[:, g * 128:(g + 1) * 128],
                        ar4[:, (g0 + g) * 128:(g0 + g + 1) * 128], identb[:, :]), "eT")
                e_trp_prev = e_trp
                b.wait("vector", e_trp)
                ev_ptq_last = b.op("vector", lambda e, g0=g0, gn=gn, qt=qt: e.tensor_copy(
                    pt_buf[:, :].rearrange("p (g x) -> p g x", x=512)[
                        :, g0:g0 + gn, qt * 128:(qt + 1) * 128],
                    ps_tr[:, 0:gn * 128].rearrange("p (g x) -> p g x", x=128)), "eV")
        if dbg and qb == 0:
            b.wait("vector", ev_ptq_last)
            edp = b.op("vector", lambda e: e.tensor_copy(dbg_ptf[:, :], pt_buf[:, 32 * 512:33 * 512]), "eV")
            b.wait("gpsimd", edp)
            b.dma("gpsimd", "dD", dbg_pt, dbg_ptf[:, :])
        b.wait("gpsimd", e_rc_prev)
        e_rsd = b.dma_excl("gpsimd", "dB", rs_dram, rsum4[:, :])
        ev_rsd_prev = e_rsd
        b.wait("sync", e_rsd, e_rbc)
        for cc in range(4):
            b.dma("sync", "dBc", rs_bc[:, cc * 128:(cc + 1) * 128],
                  rs_dram[:, cc:cc + 1].rearrange("p o -> o p").broadcast_to((128, 128)),
                  allow_slow_non_contiguous=True)
        e_rbc = b.total("dBc")
        for half in range(2):
            evm = [None] * 4
            for kt in range(ktiles):
                buf = kt % 2
                if len(evm_hist) >= 2:
                    b.wait("sync", evm_hist[-2])
                e_vl = b.dma("sync", "dV2%d" % buf, vstream[:, buf * 512:(buf + 1) * 512],
                             v_dram[kt, :, half * 512:(half + 1) * 512])
                b.wait("tensor", e_vl, e_rbc, ev_ptq_last)
                if kt == 0:
                    for dc in range(4):
                        b.wait("tensor", ev_o_ev[dc])
                for dc in range(4):
                    evm[dc] = b.op("tensor", lambda e, kt=kt, dc=dc, buf=buf, ktiles=ktiles: e.matmul(
                        ps_o[dc][:, 0:512],
                        vstream[:, buf * 512 + dc * 128: buf * 512 + (dc + 1) * 128],
                        pt_buf[:, kt * 512:(kt + 1) * 512],
                        start=(kt == 0), stop=(kt == ktiles - 1)), "eT")
                evm_hist.append(evm[3])
            ev_pt_used = evm[3]
            for dc in range(4):
                gdc = half * 4 + dc
                b.wait("vector", evm[dc], e_rbc, ev_dO_prev)
                e_oev = b.op("vector", lambda e, dc=dc, gdc=gdc: e.tensor_mul(
                    ar3[:, gdc * 512:(gdc + 1) * 512], ps_o[dc][:, 0:512], rs_bc[:, :]), "eV")
                ev_o_ev[dc] = e_oev
                b.wait("gpsimd", e_oev)
                ev_dO_prev = b.dma("gpsimd", "dO", s1_out[gdc, :, qb * 512:(qb + 1) * 512],
                                   ar3[:, gdc * 512:(gdc + 1) * 512])
        b.wait("sync", *ev_o_ev)

    b.wait("gpsimd", Ev("dO", b.cnt.get("dO", 0)), Ev("dD", b.cnt.get("dD", 0)))
    return b



def _identity_bf16():
    import ml_dtypes
    return np.eye(128, dtype=ml_dtypes.bfloat16)


def _l4_dmask():
    m = np.zeros((4, 128, 512), dtype=np.float32)
    r = np.arange(128)[:, None]
    k = np.arange(512)[None, :]
    for slot in range(4):
        m[slot][k > slot * 128 + r] = NEG
    return m


def _l4_prefmask(first_half):
    import ml_dtypes
    m = np.zeros((128, 4096), dtype=np.float32)
    if first_half:
        m[:, :] = NEG
    return m.astype(ml_dtypes.bfloat16)


def run_launch4(xL1_full, dbg=False):
    nc = None
    b = build_launch4(dbg)
    nc = b.finalize()
    in_maps = []
    for c in range(NCORES):
        bi, h = c // 2, c % 2
        if h == 1:
            x = xL1_full[bi]
        else:
            x = np.concatenate([np.zeros((OWN, D), np.float32), xL1_full[bi, :OWN]], 0)
        in_maps.append({
            "xL1": np.ascontiguousarray(x),
            "Wq": _G["L1_attn_Wq"], "Wk": _G["L1_attn_Wk"], "Wv": _G["L1_attn_Wv"],
            "identb_i": _identity_bf16(),
            "dmask_i": _l4_dmask(),
            "prefmask": _l4_prefmask(h == 0),
        })
    res = run_bass_kernel_spmd(nc, in_maps, core_ids=list(range(NCORES)))
    run_launch4.res = res
    out = np.zeros((B, S, D), np.float32)
    for c in range(NCORES):
        bi, h = c // 2, c % 2
        s1 = res.results[c]["s1"]
        rows = np.zeros((OWN, D), np.float32)
        for dc in range(8):
            rows[:, dc * 128:(dc + 1) * 128] = s1[dc].T
        out[bi, h * OWN:(h + 1) * OWN] = rows
    return out





def build_launch5(dbg=False):
    b = Builder()

    xL1 = b.dram("xL1", [OWN, D], F32, kind="ExternalInput").ap()
    xL1f = b.dram("xL1f", [8, 128, OWN], F32, kind="ExternalInput").ap()
    s1f = b.dram("s1f", [8, 128, OWN], F32, kind="ExternalInput").ap()
    Wg = b.dram("Wg", [D, 8], F32, kind="ExternalInput").ap()
    bg = b.dram("bg", [1, 8], F32, kind="ExternalInput").ap()
    b1 = b.dram("b1", [8, 4096], F32, kind="ExternalInput").ap()
    b2 = b.dram("b2", [8, D], F32, kind="ExternalInput").ap()
    w1b = b.dram("w1b", [8, 32, 8, 128, 128], BF16, kind="ExternalInput").ap()
    w2b = b.dram("w2b", [8, 8, 32, 128, 128], BF16, kind="ExternalInput").ap()
    identb_in = b.dram("identb_i", [128, 128], BF16, kind="ExternalInput").ap()

    out_fm = b.dram("outf", [8, 128, OWN], F32, kind="ExternalOutput").ap()
    if dbg:
        dbg_w = b.dram("dbg_w", [128, 8], F32, kind="ExternalOutput").ap()
        dbg_g = b.dram("dbg_g", [128, 1024], F32, kind="ExternalOutput").ap()
        dbg_acc = b.dram("dbg_acc", [128, 1024], F32, kind="ExternalOutput").ap()
        dbg_wb = b.dram("dbg_wb", [128, 1024], F32, kind="ExternalOutput").ap()
        dbg_xf = b.dram("dbg_xf", [128, 1024], F32, kind="ExternalOutput").ap()
        dbg_w1 = b.dram("dbg_w1", [128, 1024], F32, kind="ExternalOutput").ap()

        dbg_g0 = b.dram("dbg_g0", [128, 1024], F32, kind="ExternalOutput").ap()
        dbg_a0 = b.dram("dbg_a0", [128, 1024], F32, kind="ExternalOutput").ap()

    w_dram = b.dram("w_dram", [8, OWN], F32).ap()

    NCH = 4
    identb = b.sbuf("identb", [128, 128], BF16)
    x_t = [b.sbuf("x_t0", [128, D], F32), b.sbuf("x_t1", [128, D], F32)]
    scr = {"m": b.sbuf("scr_m", [128, 1], F32)[:, :],
           "ss": b.sbuf("scr_ss", [128, 1], F32)[:, :],
           "rstd": b.sbuf("scr_rstd", [128, 1], F32)[:, :],
           "sq": b.sbuf("scr_sq", [128, D], F32)[:, :]}
    xn_t16 = b.sbuf("xn_t16", [128, D], BF16)
    xn_fm = b.sbuf("xn_fm", [128, 8 * 1024], BF16)
    wg_f = b.sbuf("wg_f", [128, 8 * 8], F32)
    wg_s = b.sbuf("wg_s", [128, 8 * 8], BF16)
    bg_bc = b.sbuf("bg_bc", [128, 8], F32)
    b1_sb = b.sbuf("b1_sb", [128, 8 * 32], F32)
    b2_sb = b.sbuf("b2_sb", [128, 8 * 8], F32)
    gt = b.sbuf("gt", [128, 8], F32)
    m1 = b.sbuf("m1", [128, 1], F32)
    m2 = b.sbuf("m2", [128, 1], F32)
    eqm = b.sbuf("eqm", [128, 8], F32)
    g2s = b.sbuf("g2s", [128, 8], F32)
    msk = b.sbuf("msk", [128, 8], F32)
    gm = b.sbuf("gm", [128, 8], F32)
    den = b.sbuf("den", [128, 1], F32)
    wts = b.sbuf("wts", [128, 8], F32)
    w1_sb = [b.sbuf("w1_sb0", [128, 8 * 128], BF16), b.sbuf("w1_sb1", [128, 8 * 128], BF16)]
    w2_sb = [b.sbuf("w2_sb0", [128, 32 * 128], BF16), b.sbuf("w2_sb1", [128, 32 * 128], BF16)]
    g_fm = b.sbuf("g_fm", [128, 32 * 1024], BF16)
    acc = b.sbuf("acc", [128, 8 * 1024], F32)
    tmp = b.sbuf("tmp", [128, 8 * 1024], F32)
    wb = [b.sbuf("wb0", [128, 1024], F32), b.sbuf("wb1", [128, 1024], F32)]
    s1_st = b.sbuf("s1_st", [128, 1024], F32)
    xf_st = b.sbuf("xf_st", [128, 1024], F32)
    out_st = b.sbuf("out_st", [128, 1024], F32)
    scr["sq2"] = None
    ps_tr = b.psum("ps_tr", [128, 1024], BF16)
    ps_g = b.psum("ps_g", [128, 512], F32)
    ps_h = [b.psum("ps_h0", [128, 512], F32), b.psum("ps_h1", [128, 512], F32)]
    ps_e = [b.psum("ps_e0", [128, 512], F32), b.psum("ps_e1", [128, 512], F32)]

    scr["sq2"] = tmp[:, 0:D]
    e_id = b.dma("sync", "dW", identb[:, :], identb_in)
    e_wg = b.dma("sync", "dW", wg_f[:, :].rearrange("p (c n) -> p c n", n=8),
                 Wg.rearrange("(c p) n -> p c n", p=128))
    e_bg = b.dma("sync", "dW", bg_bc[:, :], bg.broadcast_to((128, 8)))
    e_b1 = b.dma("sync", "dW", b1_sb[:, :].rearrange("p (e h) -> p e h", h=32),
                 b1.rearrange("e (h p) -> p e h", p=128),
                 allow_slow_non_contiguous=True)
    e_b2 = b.dma("sync", "dW", b2_sb[:, :].rearrange("p (e c) -> p e c", c=8),
                 b2.rearrange("e (c p) -> p e c", p=128),
                 allow_slow_non_contiguous=True)
    e_id = e_wg = e_bg = e_b1 = e_b2 = b.total("dW")
    b.wait("vector", e_wg)
    e_cwg = b.op("vector", lambda e: e.tensor_copy(wg_s[:, :], wg_f[:, :]), "eV")

    ev_xread = [None, None]
    e_tr_prev = None
    ev_fm_used = None
    e_wdma_prev = None
    ev_g_ev = None
    e_sm_prev = None
    ev_w1l = [None, None]
    ev_w2l = [None, None]
    ev_wbl = [None, None]
    ev_gelu = {}
    ev_h_used = [None, None]
    ev_e_used = [None, None]
    ev_acc_rd = None
    ev_tmp_rd = None
    e_out_dma = None
    ev_gfm_rd = None

    for cc in range(NCH):
        for j in range(8):
            i = cc * 8 + j
            bi_ = i % 2
            b.wait("sync", ev_xread[bi_])
            e_x = b.dma("sync", "dA%d" % bi_, x_t[bi_], xL1[i * 128:(i + 1) * 128, :])
            ev_ln, ev16, ev_xread[bi_] = ln_tile(
                b, x_t[bi_], scr["sq"], scr, D, e_x,
                out16_ap=xn_t16, ev_out16_free=[e_tr_prev])
            b.wait("tensor", ev16, e_id)
            e_tr = None
            for c in range(8):
                e_tr = b.op("tensor", lambda e, c=c: e.transpose(
                    ps_tr[:, c * 128:(c + 1) * 128],
                    xn_t16[:, c * 128:(c + 1) * 128], identb[:, :]), "eT")
            e_tr_prev = e_tr
            b.wait("vector", e_tr, ev_fm_used)
            e_fm = b.op("vector", lambda e, j=j: e.tensor_copy(
                xn_fm[:, :].rearrange("p (c t) -> p c t", t=1024)[:, :, j * 128:(j + 1) * 128],
                ps_tr[:, 0:1024].rearrange("p (c t) -> p c t", t=128)), "eV")
            b.wait("tensor", e_fm, e_cwg, ev_g_ev)
            eg = None
            for c in range(8):
                eg = b.op("tensor", lambda e, c=c, j=j: e.matmul(
                    ps_g[:, 0:8],
                    xn_fm[:, c * 1024 + j * 128: c * 1024 + (j + 1) * 128],
                    wg_s[:, c * 8:(c + 1) * 8],
                    start=(c == 0), stop=(c == 7)), "eT")
            V = "vector"
            b.wait(V, eg, e_bg, e_sm_prev)
            e1 = b.op(V, lambda e: e.tensor_add(gt[:, :], ps_g[:, 0:8], bg_bc[:, :]), "eV")
            ev_g_ev = e1
            b.wait(V, e1)
            e2 = b.op(V, lambda e: e.reduce_max(m1[:, :], gt[:, :], axis=AX.X, negate=True), "eV")
            b.wait("scalar", e1, e2, e_sm_prev)
            e3 = b.op("scalar", lambda e: e.activation(gt[:, :], gt[:, :], AF.Exp,
                                                       bias=m1[:, :], accum_out=den[:, :]), "eS")
            b.wait(V, e3)
            e4 = b.op(V, lambda e: e.reciprocal(den[:, :], den[:, :]), "eV")
            b.wait(V, e4)
            e5 = b.op(V, lambda e: e.tensor_scalar_mul(gt[:, :], gt[:, :], den[:, :]), "eV")
            b.wait(V, e5)
            e6 = b.op(V, lambda e: e.reduce_max(m1[:, :], gt[:, :], axis=AX.X), "eV")
            b.wait(V, e6)
            e7 = b.op(V, lambda e: e.tensor_scalar(eqm[:, :], gt[:, :], m1[:, :], None, ALU.is_ge), "eV")
            b.wait(V, e7)
            e8 = b.op(V, lambda e: e.scalar_tensor_tensor(g2s[:, :], eqm[:, :], NEG, gt[:, :],
                                                          ALU.mult, ALU.add), "eV")
            b.wait(V, e8)
            e9 = b.op(V, lambda e: e.reduce_max(m2[:, :], g2s[:, :], axis=AX.X), "eV")
            b.wait(V, e9)
            e10 = b.op(V, lambda e: e.tensor_scalar(msk[:, :], gt[:, :], m2[:, :], None, ALU.is_ge), "eV")
            b.wait(V, e10)
            e11 = b.op(V, lambda e: e.tensor_mul(gm[:, :], gt[:, :], msk[:, :]), "eV")
            b.wait(V, e11)
            e12 = b.op(V, lambda e: e.reduce_sum(den[:, :], gm[:, :], axis=AX.X), "eV")
            b.wait(V, e12)
            e13 = b.op(V, lambda e: e.tensor_scalar_max(den[:, :], den[:, :], 1e-12), "eV")
            b.wait(V, e13)
            e14 = b.op(V, lambda e: e.reciprocal(den[:, :], den[:, :]), "eV")
            b.wait(V, e14)
            b.wait(V, e_wdma_prev)
            e_sm_prev = b.op(V, lambda e: e.tensor_scalar_mul(wts[:, :], gm[:, :], den[:, :]), "eV")
            if dbg and i == 0:
                b.wait("gpsimd", e_sm_prev)
                e_tap = b.dma("gpsimd", "dD", dbg_w, wts[:, :])
                b.wait("vector", e_tap)
            b.wait("gpsimd", e_sm_prev)
            e_wdma_prev = b.dma_excl("gpsimd", "dB", w_dram[:, i * 128:(i + 1) * 128].rearrange("e t -> t e"),
                                     wts[:, :], allow_slow_non_contiguous=True)
        for e_i in range(8):
            pb = e_i % 2
            b.wait("sync", e_wdma_prev, ev_wbl[pb])
            ev_wb = b.dma("sync", "dB2", wb[pb][:, :],
                          w_dram[e_i:e_i + 1, cc * 1024:(cc + 1) * 1024].broadcast_to((128, 1024)))
            if dbg and cc == 0 and e_i == 0:
                b.wait("gpsimd", ev_wb)
                e_tap = b.dma("gpsimd", "dD", dbg_wb, wb[pb][:, :])
                b.wait("sync", e_tap)
            for hc in range(32):
                wbuf = hc % 2
                b.wait("sync", ev_w1l[wbuf])
                e_w1 = b.dma("sync", "dWs%d" % wbuf, w1_sb[wbuf][:, :].rearrange("p (c m) -> p c m", m=128),
                             w1b[e_i, hc].rearrange("c p m -> p c m"))
                if dbg and cc == 0 and e_i == 0 and hc == 0:
                    b.wait("vector", e_w1)
                    et1 = b.op("vector", lambda e: e.tensor_copy(s1_st[:, :], xn_fm[:, 0:1024]), "eV")
                    b.wait("gpsimd", et1)
                    e_tap = b.dma("gpsimd", "dD", dbg_xf, s1_st[:, :])
                    b.wait("vector", e_tap)
                    et2 = b.op("vector", lambda e: e.tensor_copy(s1_st[:, :], w1_sb[0][:, :]), "eV")
                    b.wait("gpsimd", et2)
                    e_tap = b.dma("gpsimd", "dD", dbg_w1, s1_st[:, :])
                    b.wait("vector", e_tap)
                for st in range(2):
                    b.wait("tensor", e_w1, ev_h_used[st])
                    em = None
                    for c in range(8):
                        em = b.op("tensor", lambda e, c=c, st=st, wbuf=wbuf: e.matmul(
                            ps_h[st][:, 0:512],
                            w1_sb[wbuf][:, c * 128:(c + 1) * 128],
                            xn_fm[:, c * 1024 + st * 512: c * 1024 + st * 512 + 512],
                            start=(c == 0), stop=(c == 7)), "eT")
                    ev_w1l[wbuf] = em
                    b.wait("scalar", em, ev_gfm_rd)
                    ev_h_used[st] = b.op("scalar", lambda e, st=st, hc=hc, e_i=e_i: e.activation(
                        g_fm[:, hc * 1024 + st * 512: hc * 1024 + st * 512 + 512],
                        ps_h[st][:, 0:512], AF.Gelu,
                        bias=b1_sb[:, e_i * 32 + hc: e_i * 32 + hc + 1]), "eS")
            if dbg and cc == 0 and e_i == 0:
                b.wait("gpsimd", ev_h_used[0], ev_h_used[1])
                e_tap = b.dma("gpsimd", "dD", dbg_g0, g_fm[:, 0:1024])
                b.wait("scalar", e_tap)
            ev_gelu_all = ev_h_used[1]
            for dc in range(8):
                wbuf = dc % 2
                b.wait("sync", ev_w2l[wbuf])
                e_w2 = b.dma("sync", "dWs2%d" % wbuf, w2_sb[wbuf][:, :].rearrange("p (c m) -> p c m", m=128),
                             w2b[e_i, dc].rearrange("c p m -> p c m"))
                for st in range(2):
                    b.wait("tensor", e_w2, ev_h_used[0], ev_h_used[1], ev_e_used[st])
                    em = None
                    for hc in range(32):
                        em = b.op("tensor", lambda e, hc=hc, st=st, wbuf=wbuf: e.matmul(
                            ps_e[st][:, 0:512],
                            w2_sb[wbuf][:, hc * 128:(hc + 1) * 128],
                            g_fm[:, hc * 1024 + st * 512: hc * 1024 + st * 512 + 512],
                            start=(hc == 0), stop=(hc == 31)), "eT")
                    ev_w2l[wbuf] = em
                    ev_gfm_rd = em
                    dst = acc if e_i == 0 else tmp
                    b.wait("vector", em, ev_wb, ev_acc_rd if e_i == 0 else ev_tmp_rd)
                    e_ev = b.op("vector", lambda e, dc=dc, st=st, e_i=e_i, dst=dst, pb=pb: e.scalar_tensor_tensor(
                        dst[:, dc * 1024 + st * 512: dc * 1024 + st * 512 + 512],
                        ps_e[st][:, 0:512],
                        b2_sb[:, e_i * 8 + dc: e_i * 8 + dc + 1],
                        wb[pb][:, st * 512:(st + 1) * 512],
                        ALU.add, ALU.mult), "eV")
                    ev_e_used[st] = e_ev
                    if e_i > 0:
                        b.wait("gpsimd", e_ev)
                        ev_acc_rd = b.op("gpsimd", lambda e, dc=dc, st=st: e.tensor_tensor(
                            acc[:, dc * 1024 + st * 512: dc * 1024 + st * 512 + 512],
                            acc[:, dc * 1024 + st * 512: dc * 1024 + st * 512 + 512],
                            tmp[:, dc * 1024 + st * 512: dc * 1024 + st * 512 + 512],
                            ALU.add), "eP")
                        ev_tmp_rd = ev_acc_rd
            if dbg and cc == 0 and e_i == 0:
                b.wait("gpsimd", ev_e_used[0], ev_e_used[1])
                e_tap = b.dma("gpsimd", "dD", dbg_a0, acc[:, 0:1024])
                b.wait("vector", e_tap)
            ev_wbl[pb] = ev_e_used[1]
            ev_fm_used = ev_w1l[1] or ev_w1l[0]
        if dbg and cc == 0:
            b.wait("gpsimd", ev_acc_rd)
            e_tap = b.dma("gpsimd", "dD", dbg_g, g_fm[:, 0:1024])
            e_tap2 = b.dma("gpsimd", "dD", dbg_acc, acc[:, 0:1024])
            b.wait("scalar", e_tap2)
            b.wait("vector", e_tap2)
        for dc in range(8):
            b.wait("scalar", ev_acc_rd, ev_tmp_rd)
            e_si = b.op("scalar", lambda e, dc=dc: e.activation(
                tmp[:, dc * 1024:(dc + 1) * 1024], acc[:, dc * 1024:(dc + 1) * 1024],
                AF.Silu), "eS")
            e_s1 = b.dma("sync", "dA2", s1_st[:, :], s1f[dc, :, cc * 1024:(cc + 1) * 1024])
            e_xf = b.dma("sync", "dA3", xf_st[:, :], xL1f[dc, :, cc * 1024:(cc + 1) * 1024])
            b.wait("vector", e_si, e_s1, e_out_dma)
            e_m = b.op("vector", lambda e, dc=dc: e.tensor_mul(
                out_st[:, :], tmp[:, dc * 1024:(dc + 1) * 1024], s1_st[:, :]), "eV")
            b.wait("vector", e_m, e_xf)
            e_o = b.op("vector", lambda e: e.tensor_add(out_st[:, :], out_st[:, :], xf_st[:, :]), "eV")
            b.wait("gpsimd", e_o)
            e_out_dma = b.dma("gpsimd", "dO", out_fm[dc, :, cc * 1024:(cc + 1) * 1024], out_st[:, :])
            ev_tmp_rd = e_si if dc == 7 else ev_tmp_rd
            ev_acc_rd = e_si if dc == 7 else ev_acc_rd
            b.wait("sync", e_m)
            b.wait("sync", e_o)

    b.wait("gpsimd", Ev("dO", b.cnt.get("dO", 0)), Ev("dD", b.cnt.get("dD", 0)))
    return b


def run_launch5(xL1_own_list, s1f_list, w1b=None, w2b=None, dbg=False):
    import ml_dtypes
    b = build_launch5(dbg)
    nc = b.finalize()
    if w1b is None:
        W1 = _G["L1_ffn_W1"]; W2 = _G["L1_ffn_W2"]
        w1b = np.ascontiguousarray(
            W1.reshape(8, 8, 128, 32, 128).transpose(0, 3, 1, 2, 4)).astype(ml_dtypes.bfloat16)
        w2b = np.ascontiguousarray(
            W2.reshape(8, 32, 128, 8, 128).transpose(0, 3, 1, 2, 4)).astype(ml_dtypes.bfloat16)
    in_maps = []
    for c in range(NCORES):
        x = xL1_own_list[c]
        xf = np.ascontiguousarray(x.T.reshape(8, 128, OWN))
        in_maps.append({
            "xL1": x, "xL1f": xf, "s1f": s1f_list[c],
            "Wg": _G["L1_ffn_Wg"], "bg": _G["L1_ffn_bg"].reshape(1, 8),
            "b1": _G["L1_ffn_b1"], "b2": _G["L1_ffn_b2"],
            "w1b": w1b, "w2b": w2b,
            "identb_i": _identity_bf16(),
        })
    res = run_bass_kernel_spmd(nc, in_maps, core_ids=list(range(NCORES)))
    run_launch5.res = res
    outs = []
    for c in range(NCORES):
        of = res.results[c]["outf"]
        rows = np.zeros((OWN, D), np.float32)
        for dc in range(8):
            rows[:, dc * 128:(dc + 1) * 128] = of[dc].T
        outs.append(rows)
    return outs



def build_launch1(dbg=False):
    b = Builder()
    NT1 = 33
    KV = 4128

    x_in = b.dram("x", [NT1 * 128, D], F32, kind="ExternalInput").ap()
    P_in = b.dram("Pp", [32, D], F32, kind="ExternalInput").ap()
    mem_W = b.dram("mem_W", [D, D], F32, kind="ExternalInput").ap()
    Wi = b.dram("Wi", [D, D], F32, kind="ExternalInput").ap()
    Wl1 = b.dram("Wl1", [D, D], F32, kind="ExternalInput").ap()
    Wl2 = b.dram("Wl2", [D, D], F32, kind="ExternalInput").ap()
    bi_in = b.dram("bi", [1, D], F32, kind="ExternalInput").ap()
    bl1_in = b.dram("bl1", [1, D], F32, kind="ExternalInput").ap()
    bl2_in = b.dram("bl2", [1, D], F32, kind="ExternalInput").ap()
    memb_in = b.dram("memb", [1, D], F32, kind="ExternalInput").ap()
    g1_in = b.dram("g1", [1, D], F32, kind="ExternalInput").ap()
    be1_in = b.dram("be1", [1, D], F32, kind="ExternalInput").ap()
    wmask_in = b.dram("wmask_i", [2, 128, 160], F32, kind="ExternalInput").ap()
    identb_in = b.dram("identb_i", [128, 128], BF16, kind="ExternalInput").ap()

    x1_out = b.dram("x1", [OWN, D], F32, kind="ExternalOutput").ap()
    hdd2f_out = b.dram("hdd2f", [8, 128, KV], BF16, kind="ExternalOutput").ap()
    gradp_out = b.dram("gradp", [8, 128, D], F32, kind="ExternalOutput").ap()

    kn_dram = b.dram("kn_dram", [8, 128, KV], BF16).ap()
    if dbg:
        dbg_xn = b.dram("dbg_xn", [128, D], F32, kind="ExternalOutput").ap()
        dbg_sr = b.dram("dbg_sr", [128, 160], F32, kind="ExternalOutput").ap()
        dbg_at = b.dram("dbg_at", [128, D], F32, kind="ExternalOutput").ap()
        dbg_kn = b.dram("dbg_kn", [128, D], F32, kind="ExternalOutput").ap()
        dbg_fm = b.dram("dbg_fm", [128, 160], F32, kind="ExternalOutput").ap()
        dbg_knr = b.dram("dbg_knr", [128, 1024], F32, kind="ExternalOutput").ap()
        dbg_kp = b.dram("dbg_kp", [128, 1024], F32, kind="ExternalOutput").ap()
        dbg_h1 = b.dram("dbg_h1", [128, 512], F32, kind="ExternalOutput").ap()
        dbg_w1t = b.dram("dbg_w1t", [128, 1024], F32, kind="ExternalOutput").ap()

    identb = b.sbuf("identb", [128, 128], BF16)
    big = b.sbuf("big", [128, 8 * NT1 * 128], BF16)
    grad_acc = b.sbuf("grad_acc", [128, 8 * D], F32)
    kn_st = [b.sbuf("kn_st0", [128, 8 * 512], BF16), b.sbuf("kn_st1", [128, 8 * 512], BF16)]
    kp_f = b.sbuf("kp_f", [128, 8 * 512], F32)
    kp_16 = b.sbuf("kp_16", [128, 8 * 512], BF16)
    h1_16 = b.sbuf("h1_16", [128, 8 * 512], BF16)
    h2_16 = b.sbuf("h2_16", [128, 8 * 512], BF16)
    df_16 = b.sbuf("df_16", [128, 8 * 512], BF16)
    hT = b.sbuf("hT", [128, 4 * D], BF16)
    dT = b.sbuf("dT", [128, 4 * D], BF16)
    bias_c = b.sbuf("bias_c", [128, 4 * 8], F32)
    wmask = b.sbuf("wmask", [128, 2 * 160], F32)
    scr = {"m": b.sbuf("scr_m", [128, 1], F32)[:, :],
           "ss": b.sbuf("scr_ss", [128, 1], F32)[:, :],
           "rstd": b.sbuf("scr_rstd", [128, 1], F32)[:, :],
           "sq": b.sbuf("scr_sq", [128, D], F32)[:, :]}
    rk = b.sbuf("rk", [128, 1], F32)
    nmax = b.sbuf("nmax", [128, 1], F32)
    sumr = b.sbuf("sumr", [128, 1], F32)
    rsum = b.sbuf("rsum", [128, 1], F32)
    x_t = [grad_acc[:, 0:D], grad_acc[:, D:2 * D]]
    xn_roll = [grad_acc[:, 2 * D:3 * D], grad_acc[:, 3 * D:4 * D]]
    attn_f = grad_acc[:, 4 * D:5 * D]
    g1b = grad_acc[:, 5 * D:6 * D]
    be1b = grad_acc[:, 6 * D:7 * D]
    srow = grad_acc[:, 7 * D:7 * D + 160]
    xn_t16 = kn_st[0][:, 0:D]
    kn_t16 = kn_st[0][:, D:2 * D]
    s32a = kn_st[0][:, 2 * D:3 * D]
    s32b = kn_st[0][:, 3 * D:4 * D]
    kn_fmt = kn_st[1][:, 0:D]
    p16 = kn_st[1][:, D:D + 160]
    pt1 = kn_st[1][:, D + 160:D + 288]
    pt2 = kn_st[1][:, D + 288:D + 416]
    scr["sq2"] = kp_f[:, 0:D]

    ps_tr = b.psum("ps_tr", [128, 1024], BF16)
    ps_a = b.psum("ps_a", [128, 512], F32)
    ps_b = b.psum("ps_b", [128, 512], F32)
    ps_c = b.psum("ps_c", [128, 512], F32)
    ps_d = b.psum("ps_d", [128, 512], F32)
    ps_e = b.psum("ps_e", [128, 512], F32)

    def wT_ap(w, kc, mc, width):
        base = w * 8192 + kc * 1024 + mc * 128
        return big[:, base: base + width]

    def wT_wview(w, mc):
        return big[:, w * 8192:(w + 1) * 8192].rearrange(
            "p (k t) -> p k t", t=1024)[:, :, mc * 128:(mc + 1) * 128]

    b.dma("sync", "dW", identb[:, :], identb_in)
    b.dma("sync", "dW", g1b, g1_in.broadcast_to((128, D)))
    b.dma("sync", "dW", be1b, be1_in.broadcast_to((128, D)))
    b.dma("sync", "dW", wmask[:, :].rearrange("p (s n) -> p s n", n=160),
          wmask_in.rearrange("s p n -> p s n"))
    for w, src_ in enumerate([bi_in, bl1_in, bl2_in, memb_in]):
        b.dma("sync", "dW", bias_c[:, w * 8:(w + 1) * 8],
              src_.rearrange("o (c p) -> p (o c)", p=128), allow_slow_non_contiguous=True)
    ev_const = b.total("dW")

    ev_xread = [None, None]
    e_tr_prev = None
    ev_kn_dma = None
    e_fm_prev = None
    ev_sr = None
    e_exp_prev = None
    ev_ptrd = None
    ev_pv = [None, None]
    ev_x1dma = None
    e_attn_ev = None
    ev_kn16_rd = None
    ev_xnroll_rd = [None, None]
    e_rc = None
    ev_attnf_rd = None
    ev_kncopy = None
    fmv = big[:, :].rearrange("p (c t) -> p c t", t=NT1 * 128)
    for i in range(NT1):
        bi_ = i % 2
        b.wait("sync", ev_xread[bi_])
        e_x = b.dma("sync", "dA%d" % bi_, x_t[bi_], x_in[i * 128:(i + 1) * 128, :])
        ev_ln, ev16, ev_xread[bi_] = ln_tile(
            b, x_t[bi_], xn_roll[bi_], scr, D, e_x,
            out16_ap=xn_t16, ev_out16_free=[e_tr_prev],
            ev_out_free=[ev_xnroll_rd[bi_]])
        if dbg and i == 6:
            b.wait("gpsimd", ev_ln)
            e_tap = b.dma("gpsimd", "dD", dbg_xn, xn_roll[bi_])
            b.wait("vector", e_tap)
        if i >= 1:
            V = "vector"
            e1 = b.op(V, lambda e: e.tensor_scalar(rk[:, :], scr["ss"], 1e-5, float(D),
                                                   ALU.subtract, ALU.mult), "eV")
            b.wait("scalar", e1)
            e2 = b.op("scalar", lambda e: e.activation(rk[:, :], rk[:, :], AF.Sqrt), "eS")
            b.wait(V, e2)
            e3 = b.op(V, lambda e: e.tensor_scalar_max(rk[:, :], rk[:, :], 1e-12), "eV")
            b.wait(V, e3)
            e4 = b.op(V, lambda e: e.reciprocal(rk[:, :], rk[:, :]), "eV")
            b.wait(V, e4)
            e5 = b.op(V, lambda e: e.tensor_scalar_mul(rk[:, :], rk[:, :], scr["rstd"]), "eV")
            b.wait("scalar", e5, ev_kn16_rd)
            e_kn = b.op("scalar", lambda e: e.activation(kn_t16, scr["sq"], AF.Copy,
                                                         scale=rk[:, :]), "eS")
            ev_kncopy = e_kn
            if dbg and i == 6:
                b.wait("gpsimd", e_kn)
                e_tap = b.dma("gpsimd", "dD", dbg_kn, kn_t16)
                b.wait("tensor", e_tap)
                b.wait("scalar", e_tap)
            b.wait("tensor", e_kn, ev_const, e_fm_prev, ev_ptrd)
            e_trk = None
            for c in range(8):
                e_trk = b.op("tensor", lambda e, c=c: e.transpose(
                    ps_tr[:, c * 128:(c + 1) * 128], kn_t16[:, c * 128:(c + 1) * 128],
                    identb[:, :]), "eT")
            ev_kn16_rd = e_trk
            b.wait("vector", e_trk, ev_kn_dma)
            e_knc = b.op("vector", lambda e: e.tensor_copy(kn_fmt, ps_tr[:, 0:1024]), "eV")
            ev_ptrd = e_knc
            b.wait("gpsimd", e_knc)
            ev_kn_dma = b.dma("gpsimd", "dKn",
                              kn_dram[:, :, (i - 1) * 128: i * 128].rearrange("c p t -> p c t"),
                              kn_fmt.rearrange("p (c t) -> p c t", t=128))
        b.wait("tensor", ev16, ev_const, e_fm_prev, ev_ptrd)
        e_trx = None
        for c in range(8):
            e_trx = b.op("tensor", lambda e, c=c: e.transpose(
                ps_tr[:, c * 128:(c + 1) * 128], xn_t16[:, c * 128:(c + 1) * 128],
                identb[:, :]), "eT")
        e_tr_prev = e_trx
        b.wait("vector", e_trx)
        e_fm = b.op("vector", lambda e, i=i: e.tensor_copy(
            fmv[:, :, i * 128:(i + 1) * 128],
            ps_tr[:, 0:1024].rearrange("p (c t) -> p c t", t=128)), "eV")
        e_fm_prev = e_fm
        ev_ptrd = e_fm

        if i >= 1:
            j = i - 1
            qc0 = 128 + j * 128
            kc0 = j * 128 + 96
            b.wait("tensor", e_fm, e_attn_ev)
            em = None
            for c in range(8):
                em = b.op("tensor", lambda e, c=c, qc0=qc0, kc0=kc0: e.matmul(
                    ps_a[:, 0:160], fmv[:, c, qc0:qc0 + 128], fmv[:, c, kc0:kc0 + 160],
                    start=(c == 0), stop=(c == 7)), "eT")
            slot = 0 if j == 0 else 1
            b.wait("vector", em, ev_sr, e_exp_prev)
            e_sr = b.op("vector", lambda e, slot=slot: e.tensor_add(
                srow, ps_a[:, 0:160], wmask[:, slot * 160:(slot + 1) * 160]), "eV")
            if dbg and j == 5:
                b.wait("gpsimd", e_sr)
                e_tap = b.dma("gpsimd", "dD", dbg_sr, srow)
                b.wait("vector", e_tap)
                e_t2 = b.op("vector", lambda e, kc0=kc0: e.tensor_copy(
                    srow, fmv[0:128, 0, kc0:kc0 + 160]), "eV")
                b.wait("gpsimd", e_t2)
                e_tap = b.dma("gpsimd", "dD", dbg_fm, srow)
                b.wait("vector", e_tap)
                e_t3 = b.op("vector", lambda e: e.tensor_add(
                    srow, ps_a[:, 0:160], wmask[:, 160:320]), "eV")
                b.wait("vector", e_t3)
            ev_sr = e_sr
            e_attn_ev = e_sr
            b.wait("vector", e_sr)
            e_mx = b.op("vector", lambda e: e.reduce_max(nmax[:, :], srow,
                                                         axis=AX.X, negate=True), "eV")
            b.wait("vector", e_mx)
            e_mx2 = b.op("vector", lambda e: e.tensor_scalar_mul(nmax[:, :], nmax[:, :], 1.0 / 32.0), "eV")
            b.wait("scalar", e_mx2, e_rc)
            e_exp = b.op("scalar", lambda e: e.activation(
                p16, srow, AF.Exp, bias=nmax[:, :], scale=1.0 / 32.0,
                accum_out=sumr[:, :]), "eS")
            e_exp_prev = e_exp
            b.wait("vector", e_exp)
            e_rc = b.op("vector", lambda e: e.reciprocal(rsum[:, :], sumr[:, :]), "eV")
            b.wait("tensor", e_exp, ev_pv[0], ev_pv[1], ev_ptrd)
            et1 = b.op("tensor", lambda e: e.transpose(ps_tr[:, 0:128], p16[:, 0:128], identb[:, :]), "eT")
            et2 = b.op("tensor", lambda e: e.transpose(ps_tr[0:32, 128:256], p16[:, 128:160], identb[:, :]), "eT")
            eta = et2
            for c in range(6):
                eta = b.op("tensor", lambda e, c=c, kc0=kc0: e.transpose(
                    ps_tr[:, 256 + c * 128: 256 + (c + 1) * 128], fmv[:, c, kc0:kc0 + 128],
                    identb[:, :]), "eT")
            b.wait("vector", et1, eta)
            ec1 = b.op("vector", lambda e: e.tensor_copy(pt1, ps_tr[:, 0:128]), "eV")
            ec2 = b.op("vector", lambda e: e.tensor_copy(pt2[0:32, :], ps_tr[0:32, 128:256]), "eV")
            ec3 = b.op("vector", lambda e: e.tensor_copy(s32a[:, 0:768], ps_tr[:, 256:1024]), "eV")
            b.wait("tensor", ec1, ec2, ec3)
            etb = None
            for c in range(2):
                etb = b.op("tensor", lambda e, c=c, kc0=kc0: e.transpose(
                    ps_tr[:, c * 128:(c + 1) * 128], fmv[:, 6 + c, kc0:kc0 + 128],
                    identb[:, :]), "eT")
            for c in range(6):
                etb = b.op("tensor", lambda e, c=c, kc0=kc0: e.transpose(
                    ps_tr[0:32, 256 + c * 128: 256 + (c + 1) * 128],
                    fmv[:, c, kc0 + 128:kc0 + 160], identb[:, :]), "eT")
            b.wait("vector", etb)
            ec4 = b.op("vector", lambda e: e.tensor_copy(s32a[:, 768:1024], ps_tr[:, 0:256]), "eV")
            ec5 = b.op("vector", lambda e: e.tensor_copy(s32b[0:32, 0:768], ps_tr[0:32, 256:1024]), "eV")
            b.wait("tensor", ec4, ec5)
            etc = None
            for c in range(2):
                etc = b.op("tensor", lambda e, c=c, kc0=kc0: e.transpose(
                    ps_tr[0:32, c * 128:(c + 1) * 128],
                    fmv[:, 6 + c, kc0 + 128:kc0 + 160], identb[:, :]), "eT")
            b.wait("vector", etc)
            ec6 = b.op("vector", lambda e: e.tensor_copy(s32b[0:32, 768:1024], ps_tr[0:32, 0:256]), "eV")
            ev_ptrd = ec6
            b.wait("tensor", ec1, ec2, ec3, ec4, ec5, ec6)
            for h in range(2):
                ph = ps_b if h == 0 else ps_c
                b.op("tensor", lambda e, h=h, ph=ph: e.matmul(
                    ph[:, 0:512], pt1, s32a[:, h * 512:(h + 1) * 512],
                    start=True, stop=False), "eT")
                ev_pv[h] = b.op("tensor", lambda e, h=h, ph=ph: e.matmul(
                    ph[:, 0:512], pt2[0:32, :], s32b[0:32, h * 512:(h + 1) * 512],
                    start=False, stop=True), "eT")
            b.wait("scalar", ev_pv[0], ev_pv[1], e_rc, ev_attnf_rd, ev_x1dma)
            e_at = b.op("scalar", lambda e: e.activation(attn_f[:, 0:512], ps_b[:, 0:512],
                                                         AF.Copy, scale=rsum[:, :]), "eS")
            e_at2 = b.op("scalar", lambda e: e.activation(attn_f[:, 512:1024], ps_c[:, 0:512],
                                                          AF.Copy, scale=rsum[:, :]), "eS")
            if dbg and j == 5:
                b.wait("gpsimd", e_at, e_at2)
                e_tap = b.dma("gpsimd", "dD", dbg_at, attn_f)
                b.wait("vector", e_tap)
                b.wait("scalar", e_tap)
            b.wait("vector", e_at, e_at2)
            e_add = b.op("vector", lambda e, bj=(j + 1) % 2: e.tensor_add(
                attn_f, attn_f, xn_roll[bj]), "eV")
            ev_xnroll_rd[(j + 1) % 2] = e_add
            b.wait("vector", ev_kncopy)
            ev_x1, _, ev_attnf_rd = ln_tile(b, attn_f, attn_f, scr, D, e_add,
                                            g_bc=g1b, be_bc=be1b,
                                            ev_out_free=[ev_x1dma])
            b.wait("gpsimd", ev_x1)
            ev_x1dma = b.dma("gpsimd", "dX1", x1_out[j * 128:(j + 1) * 128, :], attn_f)

    b.barrier()
    e_p = b.dma("sync", "dA0", x_t[0][0:32, :], P_in)
    b.wait("scalar", e_p)
    e_sq = b.op("scalar", lambda e: e.activation(scr["sq2"][0:32, :], x_t[0][0:32, :],
                                                 AF.Square, accum_out=scr["ss"][0:32, :]), "eS")
    b.wait("scalar", e_sq)
    e_sqq = b.op("scalar", lambda e: e.activation(rk[0:32, :], scr["ss"][0:32, :], AF.Sqrt), "eS")
    b.wait("vector", e_sqq)
    e_m1 = b.op("vector", lambda e: e.tensor_scalar_max(rk[0:32, :], rk[0:32, :], 1e-12), "eV")
    b.wait("vector", e_m1)
    e_m2 = b.op("vector", lambda e: e.reciprocal(rk[0:32, :], rk[0:32, :]), "eV")
    b.wait("scalar", e_m2)
    e_knp = b.op("scalar", lambda e: e.activation(kn_t16[0:32, :], x_t[0][0:32, :], AF.Copy,
                                                  scale=rk[0:32, :]), "eS")
    b.wait("tensor", e_knp)
    e_trp = None
    for c in range(8):
        e_trp = b.op("tensor", lambda e, c=c: e.transpose(
            ps_tr[:, c * 32:(c + 1) * 32], kn_t16[0:32, c * 128:(c + 1) * 128],
            identb[0:32, 0:32]), "eT")
    b.wait("vector", e_trp)
    e_kc = b.op("vector", lambda e: e.tensor_copy(
        kn_fmt[:, 0:256].rearrange("p (c t) -> p c t", t=32),
        ps_tr[:, 0:256].rearrange("p (c t) -> p c t", t=32)), "eV")
    b.wait("gpsimd", e_kc)
    b.dma("gpsimd", "dKn", kn_dram[:, :, 4096:4128].rearrange("c p t -> p c t"),
          kn_fmt[:, 0:256].rearrange("p (c t) -> p c t", t=32))

    b.barrier()
    ev_wst = None
    ev_cst_prev = None
    for w, src_ in enumerate([Wi, Wl1, Wl2, mem_W]):
        b.wait("sync", ev_cst_prev)
        e_l = b.dma_excl("sync", "dWl", grad_acc[:, :].rearrange("p (c n) -> p c n", n=D),
                         src_.rearrange("(c p) n -> p c n", p=128))
        b.wait("vector", e_l, ev_wst)
        e_cst = b.op("vector", lambda e: e.tensor_copy(kn_st[0][:, 0:4096], grad_acc[:, 0:4096]), "eV")
        e_cst2 = b.op("vector", lambda e: e.tensor_copy(kn_st[1][:, 0:4096], grad_acc[:, 4096:8192]), "eV")
        ev_cst_prev = e_cst2
        evE = None
        for mc in range(8):
            buf = kn_st[mc // 4]
            off = (mc % 4) * D
            b.wait("tensor", e_cst if mc < 4 else e_cst2, evE)
            e_t = None
            for kc in range(8):
                e_t = b.op("tensor", lambda e, buf=buf, off=off, kc=kc: e.transpose(
                    ps_tr[:, kc * 128:(kc + 1) * 128],
                    buf[:, off + kc * 128: off + (kc + 1) * 128], identb[:, :]), "eT")
            b.wait("vector", e_t)
            evE = b.op("vector", lambda e, w=w, mc=mc: e.tensor_copy(
                wT_wview(w, mc), ps_tr[:, 0:1024].rearrange("p (k t) -> p k t", t=128)), "eV")
        ev_wst = evE

    b.barrier()
    ev_knrd = [None, None]
    e_kpe = None
    e_kp16 = None
    ev_h1 = None
    ev_h2 = None
    ev_hdma = None
    ev_df = None
    ev_grad_mm = None
    e_ga = None
    evcp_prev = None
    ev_h1m = ev_h2m = ev_pdm = None
    ev_htr = ev_dtr = None
    widths = [512] * 8 + [32]
    for tt in range(9):
        wd = widths[tt]
        t0 = tt * 512
        kb = tt % 2
        b.wait("sync", ev_knrd[kb])
        e_kn = b.dma("sync", "dKr%d" % kb,
                     kn_st[kb][:, 0:8 * wd].rearrange("p (c t) -> p c t", t=wd),
                     kn_dram[:, :, t0:t0 + wd].rearrange("c p t -> p c t"))
        if dbg and tt == 0:
            b.wait("vector", e_kn)
            e_t1 = b.op("vector", lambda e: e.tensor_copy(kp_f[:, 0:1024], kn_st[0][:, 0:1024]), "eV")
            b.wait("gpsimd", e_t1)
            e_tap = b.dma("gpsimd", "dD", dbg_knr, kp_f[:, 0:1024])
            b.wait("vector", e_tap)
            e_t2 = b.op("vector", lambda e: e.tensor_copy(kp_f[:, 0:1024], big[:, 8192:9216]), "eV")
            b.wait("gpsimd", e_t2)
            e_tap = b.dma("gpsimd", "dD", dbg_w1t, kp_f[:, 0:1024])
            b.wait("vector", e_tap)
        ev_kp_last = None
        for mc in range(8):
            b.wait("tensor", e_kn, e_kpe)
            em = None
            for kc in range(8):
                em = b.op("tensor", lambda e, mc=mc, kc=kc, kb=kb, wd=wd: e.matmul(
                    ps_a[:, 0:wd], wT_ap(0, kc, mc, 128), kn_st[kb][:, kc * wd:(kc + 1) * wd],
                    start=(kc == 0), stop=(kc == 7)), "eT")
            ev_knrd[kb] = em
            b.wait("vector", em, ev_df, e_kp16)
            e_kpe = b.op("vector", lambda e, mc=mc, wd=wd: e.tensor_scalar(
                kp_f[:, mc * wd:(mc + 1) * wd], ps_a[:, 0:wd],
                bias_c[:, 0 * 8 + mc: 0 * 8 + mc + 1], None, ALU.add), "eV")
            ev_kp_last = e_kpe
        if dbg and tt == 0:
            b.wait("gpsimd", ev_kp_last)
            e_tap = b.dma("gpsimd", "dD", dbg_kp, kp_f[:, 0:1024])
            b.wait("vector", e_tap)
            b.wait("scalar", e_tap)
        b.wait("vector", ev_kp_last, ev_h1m)
        e_kp16 = b.op("vector", lambda e, wd=wd: e.tensor_copy(kp_16[:, 0:8 * wd], kp_f[:, 0:8 * wd]), "eV")
        for mc in range(8):
            b.wait("tensor", e_kp16, ev_h1, ev_h2m)
            em = None
            for kc in range(8):
                em = b.op("tensor", lambda e, mc=mc, kc=kc, wd=wd: e.matmul(
                    ps_b[:, 0:wd], wT_ap(1, kc, mc, 128), kp_16[:, kc * wd:(kc + 1) * wd],
                    start=(kc == 0), stop=(kc == 7)), "eT")
            ev_h1m = em
            b.wait("scalar", em)
            ev_h1 = b.op("scalar", lambda e, mc=mc, wd=wd: e.activation(
                h1_16[:, mc * wd:(mc + 1) * wd], ps_b[:, 0:wd], AF.Silu,
                bias=bias_c[:, 1 * 8 + mc: 1 * 8 + mc + 1]), "eS")
        if dbg and tt == 0:
            b.wait("gpsimd", ev_h1)
            e_tap = b.dma("gpsimd", "dD", dbg_h1, h1_16[:, 0:512])
            b.wait("scalar", e_tap)
            b.wait("tensor", e_tap)
        for mc in range(8):
            b.wait("tensor", ev_h1, ev_h2, ev_pdm, ev_hdma, ev_htr)
            em = None
            for kc in range(8):
                em = b.op("tensor", lambda e, mc=mc, kc=kc, wd=wd: e.matmul(
                    ps_c[:, 0:wd], wT_ap(2, kc, mc, 128), h1_16[:, kc * wd:(kc + 1) * wd],
                    start=(kc == 0), stop=(kc == 7)), "eT")
            ev_h2m = em
            b.wait("scalar", em)
            ev_h2 = b.op("scalar", lambda e, mc=mc, wd=wd: e.activation(
                h2_16[:, mc * wd:(mc + 1) * wd], ps_c[:, 0:wd], AF.Silu,
                bias=bias_c[:, 2 * 8 + mc: 2 * 8 + mc + 1]), "eS")
        b.wait("gpsimd", ev_h2, ev_hdma)
        ev_hdma = b.dma("gpsimd", "dH", hdd2f_out[:, :, t0:t0 + wd].rearrange("c p t -> p c t"),
                        h2_16[:, 0:8 * wd].rearrange("p (c t) -> p c t", t=wd))
        for mc in range(8):
            b.wait("tensor", ev_h2, ev_df)
            em = None
            for kc in range(8):
                em = b.op("tensor", lambda e, mc=mc, kc=kc, wd=wd: e.matmul(
                    ps_d[:, 0:wd], wT_ap(3, kc, mc, 128), h2_16[:, kc * wd:(kc + 1) * wd],
                    start=(kc == 0), stop=(kc == 7)), "eT")
            ev_pdm = em
            b.wait("vector", em, e_kp16, ev_dtr)
            ev_df = b.op("vector", lambda e, mc=mc, wd=wd: e.scalar_tensor_tensor(
                df_16[:, mc * wd:(mc + 1) * wd], ps_d[:, 0:wd],
                bias_c[:, 3 * 8 + mc: 3 * 8 + mc + 1], kp_f[:, mc * wd:(mc + 1) * wd],
                ALU.add, ALU.subtract), "eV")
        if tt == 8:
            b.wait("vector", ev_df)
            ev_df = b.op("vector", lambda e, wd=wd: e.tensor_scalar_mul(
                df_16[:, 0:8 * wd], df_16[:, 0:8 * wd], 0.5), "eV")
        nts = max(1, wd // 128)
        prow = min(wd, 128)
        ev_hT = ev_dT = None
        for src16, dstT in ((h2_16, hT), (df_16, dT)):
            evcp = None
            for ts in range(nts):
                b.wait("tensor", ev_h2 if src16 is h2_16 else ev_df, ev_grad_mm, evcp_prev, ev_ptrd)
                ev_ptrd = None
                e_t = None
                for mc in range(8):
                    e_t = b.op("tensor", lambda e, mc=mc, ts=ts, src16=src16, wd=wd, prow=prow: e.transpose(
                        ps_tr[0:prow, mc * 128:(mc + 1) * 128],
                        src16[:, mc * wd + ts * 128: mc * wd + ts * 128 + prow],
                        identb[:, :]), "eT")
                if src16 is h2_16:
                    ev_htr = e_t
                else:
                    ev_dtr = e_t
                b.wait("vector", e_t)
                evcp = b.op("vector", lambda e, ts=ts, dstT=dstT, prow=prow: e.tensor_copy(
                    dstT[0:prow, ts * D:(ts + 1) * D], ps_tr[0:prow, 0:1024]), "eV")
                evcp_prev = evcp
            if src16 is h2_16:
                ev_hT = evcp
            else:
                ev_dT = evcp
        for hc in range(8):
            for ms in range(2):
                b.wait("tensor", ev_hT, ev_dT, e_ga)
                em = None
                for ts in range(nts):
                    em = b.op("tensor", lambda e, hc=hc, ms=ms, ts=ts, prow=prow, nts=nts: e.matmul(
                        ps_e[:, 0:512], hT[0:prow, ts * D + hc * 128: ts * D + (hc + 1) * 128],
                        dT[0:prow, ts * D + ms * 512: ts * D + (ms + 1) * 512],
                        start=(ts == 0), stop=(ts == nts - 1)), "eT")
                ev_grad_mm = em
                b.wait("vector", em)
                if tt == 0:
                    e_ga = b.op("vector", lambda e, hc=hc, ms=ms: e.tensor_copy(
                        grad_acc[:, hc * D + ms * 512: hc * D + (ms + 1) * 512], ps_e[:, 0:512]), "eV")
                else:
                    e_ga = b.op("vector", lambda e, hc=hc, ms=ms: e.tensor_add(
                        grad_acc[:, hc * D + ms * 512: hc * D + (ms + 1) * 512],
                        grad_acc[:, hc * D + ms * 512: hc * D + (ms + 1) * 512],
                        ps_e[:, 0:512]), "eV")
    b.wait("gpsimd", e_ga)
    b.dma("gpsimd", "dG", gradp_out.rearrange("c p t -> p c t"),
          grad_acc[:, :].rearrange("p (c t) -> p c t", t=D))
    b.wait("gpsimd", b.total("dG"), b.total("dX1"), b.total("dH"), b.total("dKn"),
           b.total("dD"))
    return b




def _l1_wmask(first_half):
    m = np.full((2, 128, 160), NEG, dtype=np.float32)
    r = np.arange(128)[:, None]
    k = np.arange(160)[None, :]
    band = (k >= r + 16) & (k <= r + 32)
    m[1][band] = 0.0
    if first_half:
        m[0][band & (k >= 32)] = 0.0
    else:
        m[0][band] = 0.0
    return m


def run_launch1(x_full, dbg=False):
    b = build_launch1(dbg)
    nc = b.finalize()
    Ppad = np.zeros((32, D), np.float32)
    Ppad[:10] = _G["L0_attn_P"]
    in_maps = []
    for c in range(NCORES):
        bi, h = c // 2, c % 2
        start = h * OWN
        if h == 0:
            xs = np.concatenate([np.zeros((128, D), np.float32), x_full[bi, :OWN]], 0)
        else:
            xs = x_full[bi, OWN - 128: 2 * OWN]
        in_maps.append({
            "x": np.ascontiguousarray(xs), "Pp": Ppad,
            "mem_W": _G["L0_attn_mem_W"], "Wi": _G["L0_attn_Wi"],
            "Wl1": _G["L0_attn_Wl1"], "Wl2": _G["L0_attn_Wl2"],
            "bi": _G["L0_attn_bi"].reshape(1, D), "bl1": _G["L0_attn_bl1"].reshape(1, D),
            "bl2": _G["L0_attn_bl2"].reshape(1, D), "memb": _G["L0_attn_mem_b"].reshape(1, D),
            "g1": _G["L0_attn_g1"].reshape(1, D), "be1": _G["L0_attn_be1"].reshape(1, D),
            "wmask_i": _l1_wmask(h == 0), "identb_i": _identity_bf16(),
        })
    res = run_bass_kernel_spmd(nc, in_maps, core_ids=list(range(NCORES)))
    run_launch1.res = res
    return res.results



def build_launch2(dbg=False):
    b = Builder()
    WT = 256
    NTI = OWN // WT
    KV = 4128

    gradp_all = b.dram("gradp_all", [8, 8, 128, D], F32, kind="ExternalInput").ap()
    fgate_in = b.dram("fgate", [1, 1], F32, kind="ExternalInput").ap()
    lrate_in = b.dram("lrate", [1, 1], F32, kind="ExternalInput").ap()
    mem_W = b.dram("mem_W", [D, D], F32, kind="ExternalInput").ap()
    memb_in = b.dram("memb", [1, D], F32, kind="ExternalInput").ap()
    hdd2f_in = b.dram("hdd2f", [8, 128, KV], BF16, kind="ExternalInput").ap()
    x1f_in = b.dram("x1f", [8, 128, OWN], F32, kind="ExternalInput").ap()
    Wo = b.dram("Wo", [D, D], F32, kind="ExternalInput").ap()
    bo_in = b.dram("bo", [1, D], F32, kind="ExternalInput").ap()
    Wg = b.dram("Wg", [D, 2 * D], F32, kind="ExternalInput").ap()
    bg_in = b.dram("bg", [1, D], F32, kind="ExternalInput").ap()
    W1 = b.dram("W1", [4 * D, D], F32, kind="ExternalInput").ap()
    b1_in = b.dram("b1", [1, 4 * D], F32, kind="ExternalInput").ap()
    W2 = b.dram("W2", [D, 4 * D], F32, kind="ExternalInput").ap()
    b2_in = b.dram("b2", [1, D], F32, kind="ExternalInput").ap()
    g2_in = b.dram("g2", [1, D], F32, kind="ExternalInput").ap()
    be2_in = b.dram("be2", [1, D], F32, kind="ExternalInput").ap()
    g3_in = b.dram("g3", [1, D], F32, kind="ExternalInput").ap()
    be3_in = b.dram("be3", [1, D], F32, kind="ExternalInput").ap()
    identb_in = b.dram("identb_i", [128, 128], BF16, kind="ExternalInput").ap()
    identf_in = b.dram("identf_i", [128, 128], F32, kind="ExternalInput").ap()

    s_out = b.dram("s", [8, 128, OWN], F32, kind="ExternalOutput").ap()

    sc_dram = b.dram("sc_dram", [1, 2], F32).ap()
    if dbg:
        dbg_wn = b.dram("dbg_wn", [128, 1024], F32, kind="ExternalOutput").ap()
        dbg_mo = b.dram("dbg_mo", [128, 4096], F32, kind="ExternalOutput").ap()
        dbg_gt = b.dram("dbg_gt", [128, 4096], F32, kind="ExternalOutput").ap()
        dbg_x2 = b.dram("dbg_x2", [128, 2048], F32, kind="ExternalOutput").ap()
        dbg_mu = b.dram("dbg_mu", [128, 2048], F32, kind="ExternalOutput").ap()
        dbg_wg = b.dram("dbg_wg", [128, 2048], F32, kind="ExternalOutput").ap()
    w1t_dram = b.dram("w1t_dram", [32, 8, 128, 128], BF16).ap()
    wgt_dram = b.dram("wgt_dram", [8, 16, 128, 128], BF16).ap()
    w2t_dram = b.dram("w2t_dram", [8, 32, 128, 128], BF16).ap()

    identb = b.sbuf("identb", [128, 128], BF16)
    identf = b.sbuf("identf", [128, 128], F32)
    wres = b.sbuf("wres", [128, (8 + 8) * D], BF16)

    def wnewT(kc, mc, width=128):
        return wres[:, kc * D + mc * 128: kc * D + mc * 128 + width]

    def woT(kc, mc, width=128):
        return wres[:, 8 * D + kc * D + mc * 128: 8 * D + kc * D + mc * 128 + width]



    ar = b.sbuf("ar", [128, 24576], F32)
    gsum = ar[:, 0:8192]
    stage = ar[:, 8192:16384]
    memWTf = ar[:, 16384:24576]
    x1f_t = ar[:, 0:2048]
    mo_f = ar[:, 2048:4096]
    gate_f = ar[:, 4096:6144]
    tmp_f = ar[:, 6144:8192]
    x2tm = ar[:, 8192:10240]
    x2fm_f = ar[:, 10240:12288]
    stm = ar[:, 12288:14336]
    sq_scr = ar[:, 14336:15360]
    spre_f = ar[:, 15360:17408]
    hdd_t = [b.sbuf("hdd_t0", [128, 8 * WT], BF16), b.sbuf("hdd_t1", [128, 8 * WT], BF16)]
    mout16 = b.sbuf("mout16", [128, 8 * WT], BF16)
    mo16 = b.sbuf("mo16", [128, 8 * WT], BF16)
    x1_16 = b.sbuf("x1_16", [128, 8 * WT], BF16)
    x2_16 = b.sbuf("x2_16", [128, 8 * WT], BF16)
    ff1_16 = b.sbuf("ff1_16", [128, 32 * WT], BF16)
    stage16 = ff1_16[:, 0:8192]
    x2tm16 = b.sbuf("x2tm16", [128, 1024], BF16)
    w1tb = [b.sbuf("w1tb0", [128, 8 * 128], BF16), b.sbuf("w1tb1", [128, 8 * 128], BF16)]
    w2tb = [b.sbuf("w2tb0", [128, 16 * 128], BF16), b.sbuf("w2tb1", [128, 16 * 128], BF16)]
    wgtb = [b.sbuf("wgtb0", [128, 16 * 128], BF16), b.sbuf("wgtb1", [128, 16 * 128], BF16)]
    biasc = b.sbuf("biasc", [128, 8 * 4 + 32], F32)
    gainc = b.sbuf("gainc", [128, 4 * 8], F32)
    s0b = b.sbuf("s0b", [128, 1], F32)
    s1b = b.sbuf("s1b", [128, 1], F32)
    sc01 = b.sbuf("sc01", [1, 2], F32)
    scr = {"m": b.sbuf("scr_m", [128, 1], F32)[:, :],
           "ss": b.sbuf("scr_ss", [128, 1], F32)[:, :],
           "rstd": b.sbuf("scr_rstd", [128, 1], F32)[:, :],
           "sq": sq_scr, "sq2": tmp_f[:, 0:1024]}

    ps_tr = b.psum("ps_tr", [128, 1024], BF16)
    ps_ftr = b.psum("ps_ftr", [128, 512], F32)
    ps_m = [b.psum(f"ps_m{i}", [128, 512], F32) for i in range(4)]

    b.dma("sync", "dW", identb[:, :], identb_in)
    b.dma("sync", "dW", identf[:, :], identf_in)
    for w, src_ in enumerate([g2_in, be2_in, g3_in, be3_in]):
        b.dma("sync", "dW", gainc[:, w * 8:(w + 1) * 8],
              src_.rearrange("o (c p) -> p (o c)", p=128), allow_slow_non_contiguous=True)
    for w, src_ in enumerate([memb_in, bo_in, bg_in, b2_in]):
        b.dma("sync", "dW", biasc[:, w * 8:(w + 1) * 8],
              src_.rearrange("o (c p) -> p (o c)", p=128), allow_slow_non_contiguous=True)
    b.dma("sync", "dW", biasc[:, 32:64],
          b1_in.rearrange("o (c p) -> p (o c)", p=128), allow_slow_non_contiguous=True)
    ev_const = b.total("dW")

    e_f = b.dma("sync", "dSc", sc01[:, 0:1], fgate_in)
    e_l = b.dma("sync", "dSc", sc01[:, 1:2], lrate_in)
    b.wait("vector", e_f, e_l)
    e_s0 = b.op("vector", lambda e: e.tensor_scalar(sc01[0:1, 0:1], sc01[0:1, 0:1], -1.0, 1.0,
                                                    ALU.mult, ALU.add), "eV")
    e_s1 = b.op("vector", lambda e: e.tensor_scalar_mul(sc01[0:1, 1:2], sc01[0:1, 1:2],
                                                        float(2.0 / (32808.0 * 1024.0))), "eV")
    b.wait("gpsimd", e_s0, e_s1)
    e_scd = b.dma("gpsimd", "dSc2", sc_dram, sc01[0:1, :])
    b.wait("sync", e_scd)
    e_s0b = b.dma("sync", "dSc3", s0b[:, :], sc_dram[0:1, 0:1].broadcast_to((128, 1)))
    e_s1b = b.dma("sync", "dSc3", s1b[:, :], sc_dram[0:1, 1:2].broadcast_to((128, 1)))
    ev_scb = b.total("dSc3")

    e_ga = None
    ev_stage_rd = [None]
    for c in range(8):
        b.wait("sync", ev_stage_rd[0])
        e_g = b.dma_excl("sync", "dGl", stage.rearrange("p (c n) -> p c n", n=D),
                         gradp_all[c].rearrange("c p n -> p c n"))
        b.wait("vector", e_g, e_ga)
        if c == 0:
            e_ga = b.op("vector", lambda e: e.tensor_copy(gsum, stage), "eV")
        else:
            e_ga = b.op("vector", lambda e: e.tensor_add(gsum, gsum, stage), "eV")
        ev_stage_rd[0] = e_ga

    b.wait("sync", ev_stage_rd[0])
    e_mw = b.dma_excl("sync", "dGl", stage.rearrange("p (c n) -> p c n", n=D),
                      mem_W.rearrange("(c p) n -> p c n", p=128))
    ev_ftr_rd = None
    for mc in range(8):
        e_t = None
        for kc2 in range(0, 8, 4):
            b.wait("tensor", e_mw, ev_const, ev_ftr_rd)
            for kk in range(4):
                kc = kc2 + kk
                e_t = b.op("tensor", lambda e, mc=mc, kc=kc, kk=kk: e.transpose(
                    ps_ftr[:, kk * 128:(kk + 1) * 128],
                    stage[:, mc * D + kc * 128: mc * D + (kc + 1) * 128], identf[:, :]), "eT")
            ev_stage_rd[0] = e_t
            b.wait("vector", e_t)
            ev_ftr_rd = b.op("vector", lambda e, mc=mc, kc2=kc2: e.tensor_copy(
                memWTf.rearrange("p (k t) -> p k t", t=1024)[:, kc2:kc2 + 4, mc * 128:(mc + 1) * 128],
                ps_ftr[:, 0:512].rearrange("p (k t) -> p k t", t=128)), "eV")
    b.wait("vector", ev_ftr_rd, e_ga, ev_scb)
    e_w1s = b.op("vector", lambda e: e.tensor_scalar_mul(gsum, gsum, s1b[:, :]), "eV")
    b.wait("vector", e_w1s)
    e_wn = b.op("vector", lambda e: e.scalar_tensor_tensor(
        wres[:, 0:8192], memWTf, s0b[:, :], gsum, ALU.mult, ALU.add), "eV")

    def prep_bf16(src_, n_in_chunks, n_out_chunks, dst_fn, dst_dram=None, dram_shape=None):
        evs = {"cast": None, "tr": None, "cp": None}
        w_in = n_in_chunks * 128
        mc_per_load = max(1, 8192 // w_in)
        n_row_loads = n_out_chunks // mc_per_load
        for rl in range(n_row_loads):
            rows = mc_per_load * 128
            b.wait("sync", ev_stage_rd[0])
            e_ld = b.dma_excl("sync", "dGl", stage[:, 0:mc_per_load * w_in].rearrange(
                "p (c n) -> p c n", n=w_in),
                src_[rl * rows:(rl + 1) * rows].rearrange("(c p) n -> p c n", p=128))
            b.wait("vector", e_ld, evs["tr"])
            e_c = b.op("vector", lambda e, w=w_in, m_=mc_per_load: e.tensor_copy(
                stage16[:, 0:m_ * w], stage[:, 0:m_ * w]), "eV")
            evs["cast"] = e_c
            ev_stage_rd[0] = e_c
            for mcl in range(mc_per_load):
                mc = rl * mc_per_load + mcl
                for kc2 in range(0, n_in_chunks, 8):
                    b.wait("tensor", e_c, evs["cp"])
                    e_t = None
                    for kk in range(min(8, n_in_chunks - kc2)):
                        kc = kc2 + kk
                        e_t = b.op("tensor", lambda e, mcl=mcl, kc=kc, kk=kk, w=w_in: e.transpose(
                            ps_tr[:, kk * 128:(kk + 1) * 128],
                            stage16[:, mcl * w + kc * 128: mcl * w + (kc + 1) * 128],
                            identb[:, :]), "eT")
                    evs["tr"] = e_t
                    b.wait("vector", e_t)
                    nk = min(8, n_in_chunks - kc2)
                    if dst_dram is None:
                        e_cp = b.op("vector", lambda e, mc=mc, kc2=kc2, nk=nk: e.tensor_copy(
                            dst_fn(kc2, mc, nk),
                            ps_tr[:, 0:nk * 128].rearrange("p (k t) -> p k t", t=128)), "eV")
                    else:
                        e_cp = b.op("vector", lambda e, mc=mc, kc2=kc2, nk=nk: e.tensor_copy(
                            stage16[:, 8000:8000] if False else
                            x2tm16[:, 0:nk * 128], ps_tr[:, 0:nk * 128]), "eV")
                        b.wait("gpsimd", e_cp)
                        e_dma = b.dma("gpsimd", "dWt", dst_dram(kc2, mc, nk).rearrange("k p t -> p k t"),
                                      x2tm16[:, 0:nk * 128].rearrange("p (k t) -> p k t", t=128))
                        b.wait("vector", e_dma)
                        e_cp = e_dma
                    evs["cp"] = e_cp
        return evs["cp"]

    def woT_dst(kc2, mc, nk):
        return wres[:, 8 * D:16 * D].rearrange("p (k t) -> p k t", t=1024)[
            :, kc2:kc2 + nk, mc * 128:(mc + 1) * 128]

    ev_wo = prep_bf16(Wo, 8, 8, woT_dst)
    ev_wg = prep_bf16(Wg, 16, 8, None,
                      dst_dram=lambda kc2, mc, nk: wgt_dram[mc, kc2:kc2 + nk],
                      dram_shape=None)
    ev_w1 = prep_bf16(W1, 8, 32, None,
                      dst_dram=lambda kc2, mc, nk: w1t_dram[mc, kc2:kc2 + nk],
                      dram_shape=None)
    ev_tr2 = None
    ev_cp2 = None
    for fl in range(4):
        b.wait("sync", ev_stage_rd[0])
        e_ld = b.dma_excl("sync", "dGl", stage.rearrange("p (c n) -> p c n", n=1024),
                          W2[:, fl * 1024:(fl + 1) * 1024].rearrange("(c p) n -> p c n", p=128))
        b.wait("vector", e_ld, ev_tr2)
        e_c = b.op("vector", lambda e: e.tensor_copy(stage16[:, :], stage), "eV")
        ev_stage_rd[0] = e_c
        for mc in range(8):
            b.wait("tensor", e_c, ev_cp2)
            e_t = None
            for kk in range(8):
                e_t = b.op("tensor", lambda e, mc=mc, kk=kk: e.transpose(
                    ps_tr[:, kk * 128:(kk + 1) * 128],
                    stage16[:, mc * 1024 + kk * 128: mc * 1024 + (kk + 1) * 128],
                    identb[:, :]), "eT")
            ev_tr2 = e_t
            b.wait("vector", e_t)
            e_cp = b.op("vector", lambda e: e.tensor_copy(x2tm16[:, 0:1024], ps_tr[:, 0:1024]), "eV")
            b.wait("gpsimd", e_cp)
            e_dma = b.dma("gpsimd", "dWt", w2t_dram[mc, fl * 8:(fl + 1) * 8].rearrange("k p t -> p k t"),
                          x2tm16[:, 0:1024].rearrange("p (k t) -> p k t", t=128))
            b.wait("vector", e_dma)
            ev_cp2 = e_dma

    if dbg:
        b.wait("vector", e_wn)
        e_tc = b.op("vector", lambda e: e.tensor_copy(x1f_t[:, 0:1024], wres[:, 0:1024]), "eV")
        b.wait("gpsimd", e_tc)
        e_tap = b.dma("gpsimd", "dD", dbg_wn, x1f_t[:, 0:1024])
        b.wait("vector", e_tap)
    b.barrier()

    ev_hrd = [None, None]
    e_mo_ev = None
    e_mout = None
    ev_mo16 = None
    e_gate = None
    ev_x1c = None
    e_x1l = None
    ev_w1l = [None, None]
    ev_w2l = [None, None]
    ev_ff1 = None
    ev_ff1_rd = None
    ev_sdma = None
    ev_d = None
    ev_cb = None
    ev_mof_rd = None
    ev_gate_rd = None
    ev_x1rd = None
    ev_x116_rd = None
    ev_spre_rd = None
    ev_sev = None
    ev_wgl = [None, None]
    ev_x2c = None
    ev_x2f = None
    e_stt = None
    ev_tr_busy = None
    e_x2c16 = None
    ev_mm_evs = [None] * 4
    for ti in range(NTI):
        t0 = ti * WT
        hb = ti % 2
        b.wait("sync", ev_hrd[hb])
        e_h = b.dma("sync", "dH%d" % hb, hdd_t[hb][:, :].rearrange("p (c t) -> p c t", t=WT),
                    hdd2f_in[:, :, t0:t0 + WT].rearrange("c p t -> p c t"))
        b.wait("sync", ev_x1rd, ev_x1c)
        e_x1l = b.dma("sync", "dX1l", x1f_t.rearrange("p (c t) -> p c t", t=WT),
                      x1f_in[:, :, t0:t0 + WT].rearrange("c p t -> p c t"))
        b.wait("vector", e_x1l, ev_x1c, ev_x116_rd)
        e_x1c = b.op("vector", lambda e: e.tensor_copy(x1_16[:, :], x1f_t), "eV")
        for mc in range(8):
            b.wait("tensor", e_h, e_wn, ev_mm_evs[0])
            em = None
            for kc in range(8):
                em = b.op("tensor", lambda e, mc=mc, kc=kc, hb=hb: e.matmul(
                    ps_m[0][:, 0:WT], wnewT(kc, mc), hdd_t[hb][:, kc * WT:(kc + 1) * WT],
                    start=(kc == 0), stop=(kc == 7)), "eT")
            ev_hrd[hb] = em
            b.wait("vector", em, e_mout if True else None, ev_mo16)
            e_mout = b.op("vector", lambda e, mc=mc: e.tensor_scalar(
                mout16[:, mc * WT:(mc + 1) * WT], ps_m[0][:, 0:WT],
                biasc[:, 0 * 8 + mc:0 * 8 + mc + 1], None, ALU.add), "eV")
            ev_mm_evs[0] = e_mout
        for mc in range(8):
            b.wait("tensor", e_mout, ev_wo, ev_mm_evs[1])
            em = None
            for kc in range(8):
                em = b.op("tensor", lambda e, mc=mc, kc=kc: e.matmul(
                    ps_m[1][:, 0:WT], woT(kc, mc), mout16[:, kc * WT:(kc + 1) * WT],
                    start=(kc == 0), stop=(kc == 7)), "eT")
            b.wait("vector", em, e_mo_ev, ev_mof_rd)
            e_mo_ev = b.op("vector", lambda e, mc=mc: e.tensor_scalar(
                mo_f[:, mc * WT:(mc + 1) * WT], ps_m[1][:, 0:WT],
                biasc[:, 1 * 8 + mc:1 * 8 + mc + 1], None, ALU.add), "eV")
            ev_mm_evs[1] = e_mo_ev
        if dbg and ti == 0:
            b.wait("gpsimd", e_mout, e_mo_ev)
            e_tap = b.dma("gpsimd", "dD", dbg_mu[:, 0:2048],
                          mout16[:, 0:2048], allow_slow_non_contiguous=True)
            e_tap = b.dma("gpsimd", "dD", dbg_mo[:, 0:2048], mo_f[:, 0:2048])
            b.wait("vector", e_tap)
        b.wait("vector", e_mo_ev, ev_mo16)
        ev_mo16 = b.op("vector", lambda e: e.tensor_copy(mo16[:, :], mo_f), "eV")
        for mc in range(8):
            gb_ = mc % 2
            b.wait("sync", ev_wgl[gb_])
            e_wgld = b.dma("sync", "dWg%d" % gb_, wgtb[gb_][:, :].rearrange("p (c m) -> p c m", m=128),
                           wgt_dram[mc].rearrange("c p m -> p c m"))
            if dbg and ti == 0 and mc == 0:
                b.wait("vector", e_wgld)
                e_tc = b.op("vector", lambda e: e.tensor_copy(x2tm[:, 0:2048], wgtb[0][:, :]), "eV")
                b.wait("gpsimd", e_tc)
                e_tap = b.dma("gpsimd", "dD", dbg_wg, x2tm[:, 0:2048])
                b.wait("vector", e_tap)
                b.wait("tensor", e_tap)
            b.wait("tensor", ev_mo16, e_x1c, e_wgld, ev_mm_evs[2])
            em = None
            for kc in range(16):
                rhs = x1_16 if kc < 8 else mo16
                kk = kc % 8
                em = b.op("tensor", lambda e, mc=mc, kc=kc, rhs=rhs, kk=kk, gb_=gb_: e.matmul(
                    ps_m[2][:, 0:WT], wgtb[gb_][:, kc * 128:(kc + 1) * 128],
                    rhs[:, kk * WT:(kk + 1) * WT],
                    start=(kc == 0), stop=(kc == 15)), "eT")
            ev_wgl[gb_] = em
            ev_x116_rd = em
            b.wait("scalar", em, e_gate, ev_gate_rd)
            e_gate = b.op("scalar", lambda e, mc=mc: e.activation(
                gate_f[:, mc * WT:(mc + 1) * WT], ps_m[2][:, 0:WT], AF.Sigmoid,
                bias=biasc[:, 2 * 8 + mc:2 * 8 + mc + 1]), "eS")
            ev_mm_evs[2] = e_gate
        if dbg and ti == 0:
            b.wait("gpsimd", e_gate)
            e_tap = b.dma("gpsimd", "dD", dbg_gt[:, 0:2048], gate_f[:, 0:2048])
            b.wait("vector", e_tap)
            b.wait("scalar", e_tap)
        b.wait("vector", e_gate, ev_d)
        ev_d = b.op("vector", lambda e: e.tensor_sub(tmp_f, x1f_t, mo_f), "eV")
        b.wait("vector", ev_d)
        e_cb = b.op("vector", lambda e: e.tensor_mul(gate_f, gate_f, tmp_f), "eV")
        b.wait("vector", e_cb)
        e_cb2 = b.op("vector", lambda e: e.tensor_add(gate_f, gate_f, mo_f), "eV")
        ev_mof_rd = e_cb2
        b.wait("vector", e_cb2)
        e_x2p = b.op("vector", lambda e: e.tensor_add(gate_f, gate_f, x1f_t), "eV")
        ev_cb = e_x2p
        ev_x1rd = e_x2p
        ev_x116_rd = None
        for st in range(2):
            e_t = None
            for g2_ in range(2):
                b.wait("tensor", e_x2p, ev_tr_busy)
                for kk in range(4):
                    c = g2_ * 4 + kk
                    e_t = b.op("tensor", lambda e, c=c, st=st, kk=kk: e.transpose(
                        ps_ftr[:, kk * 128:(kk + 1) * 128],
                        gate_f[:, c * WT + st * 128: c * WT + st * 128 + 128], identf[:, :]), "eT")
                ev_gate_rd = e_t
                b.wait("vector", e_t, ev_x2c)
                ev_x2c = b.op("vector", lambda e, g2_=g2_, st=st: e.tensor_copy(
                    x2tm[:, st * 1024 + g2_ * 512: st * 1024 + (g2_ + 1) * 512], ps_ftr[:, 0:512]), "eV")
                ev_tr_busy = ev_x2c
            ev_x2l, _, _ = ln_tile(b, x2tm[:, st * 1024:(st + 1) * 1024],
                                   x2tm[:, st * 1024:(st + 1) * 1024], scr, D, ev_x2c)
            for g2_ in range(2):
                b.wait("tensor", ev_x2l, ev_tr_busy)
                e_t = None
                for kk in range(4):
                    c = g2_ * 4 + kk
                    e_t = b.op("tensor", lambda e, c=c, st=st, kk=kk: e.transpose(
                        ps_ftr[:, kk * 128:(kk + 1) * 128],
                        x2tm[:, st * 1024 + c * 128: st * 1024 + (c + 1) * 128], identf[:, :]), "eT")
                b.wait("vector", e_t, ev_x2f, e_x2c16)
                for kk in range(4):
                    c = g2_ * 4 + kk
                    ev_x2f = b.op("vector", lambda e, c=c, st=st, kk=kk: e.tensor_scalar(
                        x2fm_f[:, c * WT + st * 128: c * WT + (st + 1) * 128 - (WT - 128) + (st * 0)] if False else
                        x2fm_f[:, c * WT + st * 128: c * WT + st * 128 + 128],
                        ps_ftr[:, kk * 128:(kk + 1) * 128],
                        gainc[:, 0 * 8 + c:0 * 8 + c + 1], gainc[:, 1 * 8 + c:1 * 8 + c + 1],
                        ALU.mult, ALU.add), "eV")
                ev_tr_busy = ev_x2f
        if dbg and ti == 0:
            b.wait("gpsimd", ev_x2f)
            e_tap = b.dma("gpsimd", "dD", dbg_x2, x2fm_f[:, 0:2048])
            b.wait("vector", e_tap)
        b.wait("vector", ev_x2f, e_x2c16)
        e_x2c16 = b.op("vector", lambda e: e.tensor_copy(x2_16[:, :], x2fm_f), "eV")
        ev_x2f = e_x2c16
        for fc in range(32):
            wb_ = fc % 2
            b.wait("sync", ev_w1l[wb_])
            e_wl = b.dma("sync", "dW1%d" % wb_, w1tb[wb_][:, :].rearrange("p (c m) -> p c m", m=128),
                         w1t_dram[fc].rearrange("c p m -> p c m"))
            b.wait("tensor", e_wl, ev_x2f, ev_ff1_rd, ev_mm_evs[3])
            em = None
            for kc in range(8):
                em = b.op("tensor", lambda e, fc=fc, kc=kc, wb_=wb_: e.matmul(
                    ps_m[3][:, 0:WT], w1tb[wb_][:, kc * 128:(kc + 1) * 128],
                    x2_16[:, kc * WT:(kc + 1) * WT],
                    start=(kc == 0), stop=(kc == 7)), "eT")
            ev_w1l[wb_] = em
            b.wait("scalar", em, ev_ff1)
            ev_ff1 = b.op("scalar", lambda e, fc=fc: e.activation(
                ff1_16[:, fc * WT:(fc + 1) * WT], ps_m[3][:, 0:WT], AF.Silu,
                bias=biasc[:, 32 + fc:32 + fc + 1]), "eS")
            ev_mm_evs[3] = ev_ff1
        for mc in range(8):
            em = None
            for fh in range(2):
                wb_ = (mc * 2 + fh) % 2
                b.wait("sync", ev_w2l[wb_])
                e_wl = b.dma("sync", "dW2%d" % wb_,
                             w2tb[wb_][:, :].rearrange("p (c m) -> p c m", m=128),
                             w2t_dram[mc, fh * 16:(fh + 1) * 16].rearrange("c p m -> p c m"))
                b.wait("tensor", e_wl, ev_ff1, e_stt)
                for fl in range(16):
                    fc = fh * 16 + fl
                    em = b.op("tensor", lambda e, fc=fc, fl=fl, wb_=wb_: e.matmul(
                        ps_m[0][:, 0:WT], w2tb[wb_][:, fl * 128:(fl + 1) * 128],
                        ff1_16[:, fc * WT:(fc + 1) * WT],
                        start=(fc == 0), stop=(fc == 31)), "eT")
                ev_w2l[wb_] = em
            ev_ff1_rd = em
            ev_mm_evs[0] = em
            b.wait("vector", em, e_stt, ev_spre_rd)
            e_stt = b.op("vector", lambda e, mc=mc: e.scalar_tensor_tensor(
                spre_f[:, mc * WT:(mc + 1) * WT], ps_m[0][:, 0:WT],
                biasc[:, 3 * 8 + mc:3 * 8 + mc + 1],
                x2fm_f[:, mc * WT:(mc + 1) * WT], ALU.add, ALU.add), "eV")
            ev_mm_evs[0] = e_stt
        for st in range(2):
            ev_stc = None
            for g2_ in range(2):
                b.wait("tensor", e_stt, ev_tr_busy)
                e_t = None
                for kk in range(4):
                    c = g2_ * 4 + kk
                    e_t = b.op("tensor", lambda e, c=c, st=st, kk=kk: e.transpose(
                        ps_ftr[:, kk * 128:(kk + 1) * 128],
                        spre_f[:, c * WT + st * 128: c * WT + st * 128 + 128], identf[:, :]), "eT")
                ev_spre_rd = e_t
                b.wait("vector", e_t, ev_stc)
                ev_stc = b.op("vector", lambda e, g2_=g2_, st=st: e.tensor_copy(
                    stm[:, st * 1024 + g2_ * 512: st * 1024 + (g2_ + 1) * 512], ps_ftr[:, 0:512]), "eV")
                ev_tr_busy = ev_stc
            scr_s = dict(scr)
            scr_s["sq2"] = x2tm[:, 0:1024]
            ev_sl, _, _ = ln_tile(b, stm[:, st * 1024:(st + 1) * 1024],
                                  stm[:, st * 1024:(st + 1) * 1024], scr_s, D, ev_stc)
            for g2_ in range(2):
                b.wait("tensor", ev_sl, ev_tr_busy)
                e_t = None
                for kk in range(4):
                    c = g2_ * 4 + kk
                    e_t = b.op("tensor", lambda e, c=c, st=st, kk=kk: e.transpose(
                        ps_ftr[:, kk * 128:(kk + 1) * 128],
                        stm[:, st * 1024 + c * 128: st * 1024 + (c + 1) * 128], identf[:, :]), "eT")
                b.wait("vector", e_t, ev_sev, ev_sdma)
                for kk in range(4):
                    c = g2_ * 4 + kk
                    ev_sev = b.op("vector", lambda e, c=c, st=st, kk=kk: e.tensor_scalar(
                        tmp_f[:, c * WT + st * 128: c * WT + st * 128 + 128],
                        ps_ftr[:, kk * 128:(kk + 1) * 128],
                        gainc[:, 2 * 8 + c:2 * 8 + c + 1], gainc[:, 3 * 8 + c:3 * 8 + c + 1],
                        ALU.mult, ALU.add), "eV")
                ev_tr_busy = ev_sev
        b.wait("gpsimd", ev_sev)
        ev_sdma = b.dma("gpsimd", "dS", s_out[:, :, t0:t0 + WT].rearrange("c p t -> p c t"),
                        tmp_f.rearrange("p (c t) -> p c t", t=WT))
        ev_d = ev_sev
    b.wait("gpsimd", b.total("dS"), b.total("dD"))
    return b


def run_launch2(x1_list, hdd2f_list, gradp_list, dbg=False):
    b = build_launch2(dbg)
    nc = b.finalize()
    gradp_all = np.stack(gradp_list)
    in_maps = []
    for c in range(NCORES):
        x1f = np.ascontiguousarray(x1_list[c].T.reshape(8, 128, OWN))
        in_maps.append({
            "gradp_all": gradp_all, "fgate": _G["L0_attn_fgate"].reshape(1, 1),
            "lrate": _G["L0_attn_lrate"].reshape(1, 1),
            "mem_W": _G["L0_attn_mem_W"], "memb": _G["L0_attn_mem_b"].reshape(1, D),
            "hdd2f": hdd2f_list[c], "x1f": x1f,
            "Wo": _G["L0_attn_Wo"], "bo": _G["L0_attn_bo"].reshape(1, D),
            "Wg": _G["L0_attn_Wg"], "bg": _G["L0_attn_bg"].reshape(1, D),
            "W1": _G["L0_attn_W1"], "b1": _G["L0_attn_b1"].reshape(1, 4 * D),
            "W2": _G["L0_attn_W2"], "b2": _G["L0_attn_b2"].reshape(1, D),
            "g2": _G["L0_attn_g2"].reshape(1, D), "be2": _G["L0_attn_be2"].reshape(1, D),
            "g3": _G["L0_attn_g3"].reshape(1, D), "be3": _G["L0_attn_be3"].reshape(1, D),
            "identb_i": _identity_bf16(), "identf_i": np.eye(128, dtype=np.float32),
        })
    res = run_bass_kernel_spmd(nc, in_maps, core_ids=list(range(NCORES)))
    run_launch2.res = res
    outs = []
    for c in range(NCORES):
        sf = res.results[c]["s"]
        rows = np.zeros((OWN, D), np.float32)
        for dc in range(8):
            rows[:, dc * 128:(dc + 1) * 128] = sf[dc].T
        outs.append(rows)
    return outs


NT3 = 45
QK0 = [4, 8]
QA0 = [8, 12]
FFG = [[(1024 + 512 * k, 512) for k in range(9)] + [(5632, 128)],
       [(1664 + 512 * k, 512) for k in range(8)]]
HID2 = 2730
FF2 = 5460
FFCH = [(c0, min(128, HID2 - c0)) for c0 in range(0, HID2, 128)]
SG = 512


R3 = 5760
NT3 = 45
QK0 = [4, 8]
QA0 = [8, 12]
FFG = [[(1024 + 512 * k, 512) for k in range(9)] + [(5632, 128)],
       [(1664 + 512 * k, 512) for k in range(8)]]
HID2 = 2730
FF2 = 5460
FFCH = [(c0, min(128, HID2 - c0)) for c0 in range(0, HID2, 128)]
SG = 512


def build_launch3(dbg=False):
    b = Builder()

    x_in = b.dram("x", [R3, D], F32, kind="ExternalInput").ap()
    xfm_in = b.dram("xfm", [8, 128, R3], F32, kind="ExternalInput").ap()
    s_in = b.dram("sin", [8, 128, OWN], F32, kind="ExternalInput").ap()
    wqkv_in = b.dram("wqkv", [2 * D, 1536], F32, kind="ExternalInput").ap()
    qs_in = b.dram("qs", [2, 64], F32, kind="ExternalInput").ap()
    ks_in = b.dram("ks", [2, 64], F32, kind="ExternalInput").ap()
    wo_in = b.dram("wo", [2 * 512, D], F32, kind="ExternalInput").ap()
    lng_in = b.dram("lng", [2, D], F32, kind="ExternalInput").ap()
    lnb_in = b.dram("lnb", [2, D], F32, kind="ExternalInput").ap()
    w1_in = b.dram("w1", [2 * D, FF2], F32, kind="ExternalInput").ap()
    b1p_in = b.dram("b1p", [2, 66, 128], F32, kind="ExternalInput").ap()
    w2_in = b.dram("w2", [2 * HID2, D], F32, kind="ExternalInput").ap()
    b2_in = b.dram("b2", [2, D], F32, kind="ExternalInput").ap()
    tabs_in = b.dram("tabs_i", [4, R3, 64], F32, kind="ExternalInput").ap()
    wm3_in = b.dram("wm3_i", [5, 128, 640], F32, kind="ExternalInput").ap()
    identb_in = b.dram("identb_i", [128, 128], BF16, kind="ExternalInput").ap()
    identf_in = b.dram("identf_i", [128, 128], F32, kind="ExternalInput").ap()
    colm_in = b.dram("colm_i", [128, 1], F32, kind="ExternalInput").ap()
    w1f_in = b.dram("w1f", [8 * D, 4096], F32, kind="ExternalInput").ap()
    w2f_in = b.dram("w2f", [8 * 4096, D], F32, kind="ExternalInput").ap()
    w1b_out = b.dram("w1b", [8, 32, 8, 128, 128], BF16, kind="ExternalOutput").ap()
    w2b_out = b.dram("w2b", [8, 8, 32, 128, 128], BF16, kind="ExternalOutput").ap()
    xL1_out = b.dram("xL1", [8, 128, OWN], F32, kind="ExternalOutput").ap()
    if dbg:
        dbg_xn = b.dram("dbg_xn", [8, 128, 128], F32, kind="ExternalOutput").ap()
        dbg_xa0 = b.dram("dbg_xa0", [8, 128, R3], F32, kind="ExternalOutput").ap()
        dbg_xl1 = b.dram("dbg_xl1", [8, 128, R3], F32, kind="ExternalOutput").ap()
        dbg_q0 = b.dram("dbg_q0", [8, 64, R3], BF16, kind="ExternalOutput").ap()
        dbg_k0 = b.dram("dbg_k0", [8, 64, R3], BF16, kind="ExternalOutput").ap()
        dbg_m0 = b.dram("dbg_m0", [2, R3], F32, kind="ExternalOutput").ap()
        dbg_st = b.dram("dbg_st", [4, R3], F32, kind="ExternalOutput").ap()
        dbg_hn = b.dram("dbg_hn", [10, 128, 4096], BF16, kind="ExternalOutput").ap()
        dbg_og = b.dram("dbg_og", [10, 128, 4096], F32, kind="ExternalOutput").ap()
        dbg_o16 = b.dram("dbg_o16", [10, 128, 4096], BF16, kind="ExternalOutput").ap()
        dbg_xa1 = b.dram("dbg_xa1", [8, 128, R3], F32, kind="ExternalOutput").ap()
        dbg_g = b.dram("dbg_g", [8, 128, R3], F32, kind="ExternalOutput").ap()
        dbg_vd = b.dram("dbg_vd", [NT3, 128, 512], BF16, kind="ExternalOutput").ap()
        dbg_od = b.dram("dbg_od", [NT3, 128, 512], F32, kind="ExternalOutput").ap()
        dbg_rs = b.dram("dbg_rs", [NT3, 128, 24], F32, kind="ExternalOutput").ap()
        dbg_sr = b.dram("dbg_sr", [128, 5120], F32, kind="ExternalOutput").ap()
        dbg_p16 = b.dram("dbg_p16", [128, 5120], BF16, kind="ExternalOutput").ap()
        dbg_ms = b.dram("dbg_ms", [128, 16], F32, kind="ExternalOutput").ap()

    xf32 = [b.dram("xf32a", [8, 128, R3], F32).ap(), b.dram("xf32b", [8, 128, R3], F32).ap()]
    x16d = [b.dram("x16a", [8, 128, R3], BF16).ap(), b.dram("x16b", [8, 128, R3], BF16).ap()]
    xaf_d = b.dram("xaf_d", [8, 128, R3], F32).ap()
    xa16_d = b.dram("xa16_d", [8, 128, R3], BF16).ap()
    qd = b.dram("qd", [8, 64, R3], BF16).ap()
    kd = b.dram("kd", [8, 64, R3], BF16).ap()
    vd = b.dram("vd", [NT3, 128, 512], BF16).ap()
    w1c_d = b.dram("w1c_d", [128, 8, FF2], BF16).ap()
    w2c_d = b.dram("w2c_d", [128, 22, D], BF16).ap()
    strow_d = b.dram("strow_d", [4, R3], F32).ap()
    mrow_d = b.dram("mrow_d", [2, R3], F32).ap()

    identb = b.sbuf("identb", [128, 128], BF16)
    identf = b.sbuf("identf", [128, 128], F32)
    x_t = [b.sbuf("x_t0", [128, D], F32), b.sbuf("x_t1", [128, D], F32)]
    scr = {"m": b.sbuf("scr_m", [128, 1], F32)[:, :],
           "ss": b.sbuf("scr_ss", [128, 1], F32)[:, :],
           "rstd": b.sbuf("scr_rstd", [128, 1], F32)[:, :],
           "sq": b.sbuf("scr_sq", [128, D], F32)[:, :],
           "sq2": b.sbuf("scr_sq2", [128, D], F32)[:, :]}
    tabs = b.sbuf("tabs", [128, 4 * NT3 * 64], BF16)
    tabs_f = b.sbuf("tabs_f", [128, 128], F32)
    wm3 = b.sbuf("wm3", [128, 5 * 640], F32)
    colm = b.sbuf("colm", [128, 1], F32)
    ARF = b.sbuf("ARF", [128, 8448], F32)
    ARB = b.sbuf("ARB", [128, 26624], BF16)
    qkv_f = b.sbuf("qkv_f", [128, 1024], F32)
    sqh = b.sbuf("sqh", [128, 512], F32)
    ssh = b.sbuf("ssh", [128, 16], F32)
    xa_f = b.sbuf("xa_f", [128, 1024], F32)
    xa_16 = b.sbuf("xa_16", [128, 1024], BF16)
    o_t = b.sbuf("o_t", [128, 512], F32)
    o16 = b.sbuf("o16", [128, 512], BF16)
    xt16 = b.sbuf("xt16", [128, 1024], BF16)
    v16 = b.sbuf("v16", [128, 512], BF16)
    qk16 = b.sbuf("qk16", [128, 512], BF16)
    fm_st = b.sbuf("fm_st", [128, 8 * 129], BF16)
    o_fm = b.sbuf("o_fm", [128, 1024], BF16)
    mx3 = b.sbuf("mx3", [128, 8], F32)
    sm3 = b.sbuf("sm3", [128, 8], F32)
    rs3 = b.sbuf("rs3", [128, 8], F32)
    stats = b.sbuf("stats", [128, NT3 * 4], F32)
    mbc = b.sbuf("mbc", [128, SG], F32)
    rbc = b.sbuf("rbc", [128, SG], F32)
    biasg = b.sbuf("biasg", [128, 66 + 8 + 8 + 8], F32)
    ac3 = b.sbuf("ac3", [128, SG], BF16)

    wqkv_f_v = ARF[:, 0:6144]
    wo_f_v = ARF[:, 0:4096]
    srow3 = ARF[:, 0:5120]
    xg_f = ARF[:, 0:8 * (SG + 1)]
    og_f = ARF[:, 4104:4104 + 2048] if SG == 256 else ARF[:, 4224:4224 + 4096]
    rowseg = [ARF[:, j * 1280:(j + 1) * 1280] for j in range(6)]
    wqkv_s = ARB[:, 0:12288]
    wo_s = ARB[:, 20480:24576]
    p163 = ARB[:, 0:5120]
    ptb = ARB[:, 5120:10240]
    kring = ARB[:, 10240:16384]
    vring = ARB[:, 16384:19456]
    qtile = ARB[:, 19456:20480]
    hn16 = ARB[:, 0:8 * SG]
    prod3 = ARB[:, 4096:4096 + 22 * SG]
    w1tb3 = [ARB[:, 15360:15360 + 2048], ARB[:, 17408:17408 + 2048]]
    w2tb3 = [ARB[:, 19456:19456 + 2816], ARB[:, 22272:22272 + 2816]]
    cb16 = ARB[:, 25088:25088 + 1024]

    ps_tr = b.psum("ps_tr", [128, 1024], BF16)
    ps_ftr = b.psum("ps_ftr", [128, 512], F32)
    ps_q = b.psum("ps_q", [128, 512], F32)
    ps_k = b.psum("ps_k", [128, 512], F32)
    ps_v = b.psum("ps_v", [128, 512], F32)
    ps_o = b.psum("ps_o", [128, 512], F32)

    b.dma("sync", "dW", identb[:, :], identb_in)
    b.dma("sync", "dW", identf[:, :], identf_in)
    b.dma("sync", "dW", wm3[:, :].rearrange("p (s n) -> p s n", n=640),
          wm3_in.rearrange("s p n -> p s n"))
    b.dma("sync", "dW", colm[:, :], colm_in)
    ev_const = b.total("dW")
    e_z = b.op("vector", lambda e: e.memset(xa_f[:, :], 0.0), "eV")
    e_z2 = b.op("vector", lambda e: e.memset(stats[:, :], 0.0), "eV")
    b.wait("gpsimd", e_z)
    b.dma("gpsimd", "dZi", xaf_d[:, :, 1023:1024].rearrange("c p t -> p c t"),
          xa_f[:, 0:8].rearrange("p (c t) -> p c t", t=1), allow_slow_non_contiguous=True)
    b.wait("vector", Ev("dZi", b.cnt["dZi"]))
    e_z3 = b.op("vector", lambda e: e.memset(xa_16[:, :], 0.0), "eV")
    b.wait("gpsimd", e_z3)
    b.dma("gpsimd", "dZi", x16d[1][:, :, 1023:1024].rearrange("c p t -> p c t"),
          xa_16[:, 0:8].rearrange("p (c t) -> p c t", t=1), allow_slow_non_contiguous=True)
    ev_zi = b.total("dZi")
    b.wait("sync", ev_zi)
    b.wait("tensor", Ev("eV", b.cnt["eV"]))

    for e_i in range(8):
        for hc2 in range(32):
            b.dma("gpsimd", "dMC", w1b_out[e_i, hc2].rearrange("c p m -> p c m"),
                  w1f_in.rearrange("(e c p) (h m) -> e h c p m", c=8, p=128, m=128)[
                      e_i, hc2].rearrange("c p m -> p c m"))
        for dc2 in range(8):
            b.dma("gpsimd", "dMC", w2b_out[e_i, dc2].rearrange("h p m -> p h m"),
                  w2f_in.rearrange("(e h p) (c m) -> e c h p m", h=32, p=128, m=128)[
                      e_i, dc2].rearrange("h p m -> p h m"))

    ev_xread = [None, None]
    ev_ftr_rd = None
    ev_fmw = None
    ev_f16 = None
    e_c16 = None
    for i in range(NT3):
        bi_ = i % 2
        b.wait("sync", ev_xread[bi_])
        e_x = b.dma("sync", "dA%d" % bi_, x_t[bi_][:, :], x_in[i * 128:(i + 1) * 128, :])
        ev_ln, _, ev_xread[bi_] = ln_tile(b, x_t[bi_][:, :], scr["sq2"], scr, D, e_x)
        for g_ in range(2):
            b.wait("tensor", ev_ln, ev_ftr_rd, ev_const)
            e_t = None
            for kk in range(4):
                c = g_ * 4 + kk
                e_t = b.op("tensor", lambda e, c=c, kk=kk: e.transpose(
                    ps_ftr[:, kk * 128:(kk + 1) * 128],
                    scr["sq2"][:, c * 128:(c + 1) * 128], identf[:, :]), "eT")
            b.wait("vector", e_t, ev_fmw, e_c16)
            ev_ftr_rd = b.op("vector", lambda e, g_=g_: e.tensor_copy(
                xa_f[:, g_ * 512:(g_ + 1) * 512], ps_ftr[:, 0:512]), "eV")
        b.wait("vector", ev_ftr_rd, ev_f16)
        e_c16 = b.op("vector", lambda e: e.tensor_copy(xa_16[:, :], xa_f[:, :]), "eV")
        b.wait("gpsimd", ev_ftr_rd, e_c16)
        ev_fmw = b.dma("gpsimd", "dXf", xf32[0][:, :, i * 128:(i + 1) * 128].rearrange("c p t -> p c t"),
                       xa_f[:, :].rearrange("p (c t) -> p c t", t=128))
        ev_f16 = b.dma("gpsimd", "dXs", x16d[0][:, :, i * 128:(i + 1) * 128].rearrange("c p t -> p c t"),
                       xa_16[:, :].rearrange("p (c t) -> p c t", t=128))
    b.barrier()
    if dbg:
        b.dma("gpsimd", "dDb", dbg_xn, xf32[0][:, :, 2560:2688])
        b.wait("gpsimd", b.total("dDb"))

    for l in range(2):
        cur16 = x16d[l % 2]
        curf = xf32[l % 2]
        nxt16 = x16d[(l + 1) % 2]
        nxtf = xf32[(l + 1) % 2]

        for hf in range(2):
            e_wq = b.dma_excl("sync", "dWl", ARF[:, 0:6144].rearrange("p (c n) -> p c n", n=1536),
                              wqkv_in[l * D:(l + 1) * D].rearrange("(c p) n -> p c n", p=128)[
                                  :, hf * 4:(hf + 1) * 4])
            b.wait("vector", e_wq)
            e_cw = b.op("vector", lambda e, hf=hf: e.tensor_copy(
                wqkv_s[:, hf * 6144:(hf + 1) * 6144], ARF[:, 0:6144]), "eV")
            b.wait("sync", e_cw)
        e_wo = b.dma_excl("sync", "dWl", wo_f_v.rearrange("p (c n) -> p c n", n=D),
                          wo_in[l * 512:(l + 1) * 512].rearrange("(c p) n -> p c n", p=128))
        b.wait("vector", e_wo)
        e_cwo = b.op("vector", lambda e: e.tensor_copy(wo_s[:, :], wo_f_v), "eV")
        b.wait("sync", e_cwo)
        b.dma("gpsimd", "dWc", w1c_d.rearrange("p c n -> p c n"),
              w1_in[l * D:(l + 1) * D].rearrange("(c p) n -> p c n", p=128))
        for k in range(22):
            rows = FFCH[k][1]
            b.dma("gpsimd", "dWc", w2c_d[0:rows, k],
                  w2_in[l * HID2 + k * 128: l * HID2 + k * 128 + rows])
        ev_wc = b.total("dWc")
        b.dma("sync", "dWb", biasg[:, 0:66],
              b1p_in[l].rearrange("c p -> p c"), allow_slow_non_contiguous=True)
        b.dma("sync", "dWb", biasg[:, 66:74],
              b2_in[l:l + 1].rearrange("o (c p) -> p (o c)", p=128), allow_slow_non_contiguous=True)
        b.dma("sync", "dWb", biasg[:, 74:82],
              lng_in[l:l + 1].rearrange("o (c p) -> p (o c)", p=128), allow_slow_non_contiguous=True)
        b.dma("sync", "dWb", biasg[:, 82:90],
              lnb_in[l:l + 1].rearrange("o (c p) -> p (o c)", p=128), allow_slow_non_contiguous=True)
        ev_bias = b.total("dWb")

        e_q1 = b.dma("sync", "dQs", tabs_f[:, 0:64], qs_in[l:l + 1, :].broadcast_to((128, 64)))
        e_q2 = b.dma("sync", "dQs", tabs_f[:, 64:96], qs_in[l:l + 1, 32:64].broadcast_to((128, 32)))
        e_q3 = b.dma("sync", "dQs", tabs_f[:, 96:128], qs_in[l:l + 1, 0:32].broadcast_to((128, 32)))
        ev_tt_prev = None
        for (w, qcol) in [(0, 0), (1, 64)]:
            for i in range(NT3):
                e_tl = b.dma_excl("sync", "dTl", sqh[:, 0:64], tabs_in[w, i * 128:(i + 1) * 128, :])
                b.wait("vector", e_tl, e_q1, e_q2, e_q3, ev_tt_prev)
                ev_tt_prev = b.op("vector", lambda e, w=w, i=i, qcol=qcol: e.tensor_mul(
                    tabs[:, (w * NT3 + i) * 64:(w * NT3 + i + 1) * 64],
                    sqh[:, 0:64], tabs_f[:, qcol:qcol + 64]), "eV")
        b.wait("sync", ev_tt_prev)
        e_q1 = b.dma("sync", "dQs", tabs_f[:, 0:64], ks_in[l:l + 1, :].broadcast_to((128, 64)))
        e_q2 = b.dma("sync", "dQs", tabs_f[:, 64:96], ks_in[l:l + 1, 32:64].broadcast_to((128, 32)))
        e_q3 = b.dma("sync", "dQs", tabs_f[:, 96:128], ks_in[l:l + 1, 0:32].broadcast_to((128, 32)))
        for (w, qcol) in [(2, 0), (3, 64)]:
            for i in range(NT3):
                e_tl = b.dma_excl("sync", "dTl", sqh[:, 0:64], tabs_in[w, i * 128:(i + 1) * 128, :])
                b.wait("vector", e_tl, e_q1, e_q2, e_q3, ev_tt_prev)
                ev_tt_prev = b.op("vector", lambda e, w=w, i=i, qcol=qcol: e.tensor_mul(
                    tabs[:, (w * NT3 + i) * 64:(w * NT3 + i + 1) * 64],
                    sqh[:, 0:64], tabs_f[:, qcol:qcol + 64]), "eV")
        b.barrier()

        ev_fm_rd = None
        e_qe = e_ke = e_ve = None
        e_qkd = e_kkd = e_vkd = None
        ev_rot = None
        ev_tr16_rd = None
        for i in range(QK0[l], NT3):
            b.wait("sync", ev_fm_rd)
            e_fm = b.dma("sync", "dFm", fm_st[:, :].rearrange("p (c t) -> p c t", t=129),
                         cur16[:, :, i * 128 - 1:(i + 1) * 128].rearrange("c p t -> p c t"))
            ems = {}
            for (ps, n0, nm) in ((ps_q, 0, "q"), (ps_k, 512, "k"), (ps_v, 1024, "v")):
                b.wait("tensor", e_fm,
                       e_qe if ps is ps_q else (e_ke if ps is ps_k else e_ve))
                em = None
                for c in range(8):
                    coff = 1 if c < 4 else 0
                    em = b.op("tensor", lambda e, c=c, coff=coff, n0=n0, ps=ps: e.matmul(
                        ps[:, 0:512],
                        fm_st[:, c * 129 + coff: c * 129 + coff + 128],
                        wqkv_s[:, c * 1536 + n0: c * 1536 + n0 + 512],
                        start=(c == 0), stop=(c == 7)), "eT")
                ems[nm] = em
            ev_fm_rd = ems["v"]
            b.wait("vector", ems["v"], e_vkd)
            e_ve = b.op("vector", lambda e: e.tensor_copy(v16[:, :], ps_v[:, 0:512]), "eV")
            b.wait("gpsimd", e_ve)
            e_vkd = b.dma("gpsimd", "dVd", vd[i], v16[:, :])
            for (ps, tw, in_q) in ((ps_q, 0, True), (ps_k, 2, False)):
                if in_q and i < QA0[l]:
                    e_qe = ems["q"]
                    continue
                b.wait("vector", ems["q" if in_q else "k"], ev_rot)
                e_qf = b.op("vector", lambda e, ps=ps: e.tensor_copy(qkv_f[:, 0:512], ps[:, 0:512]), "eV")
                if in_q:
                    e_qe = e_qf
                else:
                    e_ke = e_qf
                b.wait("scalar", e_qf)
                e_s3 = b.op("scalar", lambda e: e.activation(sqh[:, 0:512], qkv_f[:, 0:512], AF.Square), "eS")
                b.wait("vector", e_s3)
                e_rd = b.op("vector", lambda e: e.tensor_reduce(
                    ssh[:, 0:8].rearrange("p (h o) -> p h o", o=1),
                    sqh[:, 0:512].rearrange("p (h c) -> p h c", c=64), axis=AX.X, op=ALU.add), "eV")
                b.wait("scalar", e_rd)
                e_s4 = b.op("scalar", lambda e: e.activation(ssh[:, 8:16], ssh[:, 0:8], AF.Sqrt), "eS")
                b.wait("vector", e_s4)
                e_m4 = b.op("vector", lambda e: e.tensor_scalar_max(ssh[:, 8:16], ssh[:, 8:16], 1e-12), "eV")
                b.wait("vector", e_m4)
                e_r4 = b.op("vector", lambda e: e.reciprocal(ssh[:, 8:16], ssh[:, 8:16]), "eV")
                b.wait("vector", e_r4)
                e_qn = b.op("vector", lambda e: e.tensor_mul(
                    qkv_f[:, 0:512].rearrange("p (h c) -> p h c", c=64),
                    qkv_f[:, 0:512].rearrange("p (h c) -> p h c", c=64),
                    ssh[:, 8:16].rearrange("p (h o) -> p h o", o=1).broadcast_to((128, 8, 64))), "eV")
                tA = tabs[:, (tw * NT3 + i) * 64:(tw * NT3 + i + 1) * 64]
                tB = tabs[:, ((tw + 1) * NT3 + i) * 64:((tw + 1) * NT3 + i + 1) * 64]
                b.wait("vector", e_qn)
                e_r1 = b.op("vector", lambda e, tA=tA: e.tensor_mul(
                    sqh[:, 0:512].rearrange("p (h c) -> p h c", c=64),
                    qkv_f[:, 0:512].rearrange("p (h c) -> p h c", c=64),
                    tA.rearrange("p (o c) -> p o c", o=1).broadcast_to((128, 8, 64))), "eV")
                qv = qkv_f[:, 0:512].rearrange("p (h two c) -> p h two c", two=2, c=32)
                ov = o_t[:, 0:512].rearrange("p (h two c) -> p h two c", two=2, c=32)
                e_r2 = b.op("vector", lambda e, tB=tB, qv=qv, ov=ov: e.tensor_mul(
                    ov[:, :, 0, :], qv[:, :, 1, :],
                    tB.rearrange("p (o two c) -> p o two c", o=1, two=2)[:, :, 0, :]
                    .broadcast_to((128, 8, 32))), "eV")
                e_r3 = b.op("vector", lambda e, tB=tB, qv=qv, ov=ov: e.tensor_mul(
                    ov[:, :, 1, :], qv[:, :, 0, :],
                    tB.rearrange("p (o two c) -> p o two c", o=1, two=2)[:, :, 1, :]
                    .broadcast_to((128, 8, 32))), "eV")
                b.wait("vector", e_r1, e_r2, e_r3, ev_tr16_rd)
                ev_rot = b.op("vector", lambda e: e.tensor_add(qk16[:, :], sqh[:, 0:512], o_t[:, 0:512]), "eV")
                b.wait("tensor", ev_rot, e_qkd if in_q else e_kkd)
                e_t = None
                for h in range(8):
                    e_t = b.op("tensor", lambda e, h=h: e.transpose(
                        ps_tr[0:64, h * 128:(h + 1) * 128], qk16[:, h * 64:(h + 1) * 64],
                        identb[:, :]), "eT")
                ev_tr16_rd = e_t
                b.wait("vector", e_t)
                e_cpq = b.op("vector", lambda e: e.tensor_copy(o_fm[0:64, :], ps_tr[0:64, 0:1024]), "eV")
                b.wait("gpsimd", e_cpq)
                dst = qd if in_q else kd
                e_dd = b.dma("gpsimd", "dQd", dst[:, :, i * 128:(i + 1) * 128].rearrange("h d t -> d h t"),
                             o_fm[0:64, :].rearrange("d (h t) -> d h t", t=128))
                if in_q:
                    e_qkd = e_dd
                else:
                    e_kkd = e_dd
        b.barrier()
        _l3_attn(b, l, locals())
        b.barrier()
        _l3_stats_rows(b, l, locals())
        b.barrier()
        _l3_ffn(b, l, locals())
        b.barrier()
        if dbg and l == 0:
            b.dma("gpsimd", "dDb", dbg_xa0, xaf_d)
            b.dma("gpsimd", "dDb", dbg_xl1, nxtf)
            b.dma("gpsimd", "dDb", dbg_q0, qd)
            b.dma("gpsimd", "dDb", dbg_k0, kd)
            b.dma("gpsimd", "dDb", dbg_m0, mrow_d[:, :])
            b.dma("gpsimd", "dDb", dbg_st, strow_d)
            b.dma("gpsimd", "dDb", dbg_vd, vd)
            b.wait("gpsimd", b.total("dDb"))
        if dbg and l == 1:
            b.dma("gpsimd", "dDb", dbg_xa1, xaf_d)
            b.dma("gpsimd", "dDb", dbg_g, nxtf)
            b.wait("gpsimd", b.total("dDb"))
        if l == 1:
            _l3_combine(b, locals())

    b.wait("gpsimd", b.total("dXL"), b.total("dMC"))
    return b


def _l3_attn(b, l, g):
    V, SC, T, P = "vector", "scalar", "tensor", "gpsimd"
    identb = g["identb"]; identf = g["identf"]
    wm3 = g["wm3"]; kring = g["kring"]; vring = g["vring"]; qtile = g["qtile"]
    srow3 = g["srow3"]; p163 = g["p163"]; ptb = g["ptb"]
    mx3 = g["mx3"]; sm3 = g["sm3"]; rs3 = g["rs3"]
    o_t = g["o_t"]; o16 = g["o16"]; o_fm = g["o_fm"]
    xa_f = g["xa_f"]; xa_16 = g["xa_16"]; stats = g["stats"]
    wo_s = g["wo_s"]; sqh = g["sqh"]; qkv_f = g["qkv_f"]
    ps_tr = g["ps_tr"]; ps_ftr = g["ps_ftr"]
    ps_q = g["ps_q"]; ps_k = g["ps_k"]; ps_v = g["ps_v"]; ps_o = g["ps_o"]
    qd = g["qd"]; kd = g["kd"]; vd = g["vd"]
    curf = g["curf"]; xaf_d = g["xaf_d"]; xa16_d = g["xa16_d"]
    e_cwo = g["e_cwo"]
    A0 = QA0[l]

    ev_kr = {}
    ev_vr = {}
    ev_ring_rd = [None] * 6
    ev_sev = None
    e_exp3 = None
    ev_pt3 = None
    ev_pv3 = None
    e_oev = None
    ev_o16 = None
    ev_ofm = None
    e_xa = None
    ev_xa16 = None
    e_xad = e_xad2 = None
    ev_st_wr = None
    e_qload = None
    ev_qrd = None
    ev_strd = None
    for i in range(A0, NT3):
        need = list(range(i - 4, i + 1)) if i == A0 else [i]
        for kt in need:
            sl = kt % 6
            b.wait("sync", ev_ring_rd[sl])
            ev_kr[kt] = b.dma("sync", "dKr%d" % (sl % 2),
                              kring[0:64, sl * 1024:(sl + 1) * 1024].rearrange("d (h t) -> d h t", t=128),
                              kd[:, :, kt * 128:(kt + 1) * 128].rearrange("h d t -> d h t"))
            ev_vr[kt] = b.dma("sync", "dVr%d" % (sl % 2), vring[:, sl * 512:(sl + 1) * 512], vd[kt])
        b.wait("sync", ev_qrd)
        e_qload = b.dma("sync", "dQl", qtile[0:64, :].rearrange("d (h t) -> d h t", t=128),
                        qd[:, :, i * 128:(i + 1) * 128].rearrange("h d t -> d h t"))
        for h in range(8):
            b.wait(T, e_qload, ev_sev)
            em0 = em1 = None
            for j in range(4):
                kt = i - 4 + j
                sl = kt % 6
                b.wait(T, ev_kr[kt])
                em0 = b.op(T, lambda e, h=h, j=j, sl=sl: e.matmul(
                    ps_q[:, j * 128:(j + 1) * 128], qtile[0:64, h * 128:(h + 1) * 128],
                    kring[0:64, sl * 1024 + h * 128: sl * 1024 + (h + 1) * 128],
                    start=True, stop=True), "eT")
            sl = i % 6
            b.wait(T, ev_kr[i])
            em1 = b.op(T, lambda e, h=h, sl=sl: e.matmul(
                ps_k[:, 0:128], qtile[0:64, h * 128:(h + 1) * 128],
                kring[0:64, sl * 1024 + h * 128: sl * 1024 + (h + 1) * 128],
                start=True, stop=True), "eT")
            b.wait(V, em0, em1, e_exp3)
            ev_sev = b.op(V, lambda e, h=h: e.tensor_copy(
                srow3[:, h * 640:h * 640 + 512], ps_q[:, 0:512]), "eV")
            ev_sev = b.op(V, lambda e, h=h: e.tensor_copy(
                srow3[:, h * 640 + 512:h * 640 + 640], ps_k[:, 0:128]), "eV")
        slot = (i - 12) if 13 <= i <= 16 else 0
        b.wait(P, ev_sev)
        ev_msk3 = b.op(P, lambda e, slot=slot: e.tensor_add(
            srow3.rearrange("p (h n) -> p h n", n=640),
            srow3.rearrange("p (h n) -> p h n", n=640),
            wm3[:, slot * 640:(slot + 1) * 640].rearrange("p (o n) -> p o n", o=1)
            .broadcast_to((128, 8, 640))), "eP")
        if g.get("dbg") and l == 0 and i == 30:
            e_tap = b.dma(P, "dDs", g["dbg_sr"], srow3)
            b.wait(V, e_tap)
        b.wait(V, ev_msk3)
        e_mx3 = b.op(V, lambda e: e.tensor_reduce(
            mx3[:, :].rearrange("p (h o) -> p h o", o=1),
            srow3.rearrange("p (h n) -> p h n", n=640), axis=AX.X,
            op=ALU.max, negate=True), "eV")
        b.wait(SC, e_mx3, ev_pt3)
        for h in range(8):
            e_exp3 = b.op(SC, lambda e, h=h: e.activation(
                p163[:, h * 640:(h + 1) * 640], srow3[:, h * 640:(h + 1) * 640],
                AF.Exp, bias=mx3[:, h:h + 1], accum_out=sm3[:, h:h + 1]), "eS")
        if g.get("dbg") and l == 0 and i == 30:
            b.wait(P, e_exp3)
            e_tap2 = b.dma(P, "dDs", g["dbg_p16"], p163)
            b.dma(P, "dDs", g["dbg_ms"][:, 0:8], mx3[:, :])
            e_tap3 = b.dma(P, "dDs", g["dbg_ms"][:, 8:16], sm3[:, :])
            b.wait(SC, e_tap2, e_tap3)
        b.wait(V, e_exp3)
        e_rs3 = b.op(V, lambda e: e.reciprocal(rs3[:, :], sm3[:, :]), "eV")
        ev_ptrd3 = None
        for f in range(5):
            b.wait(T, e_exp3, ev_pv3 if f == 0 else None, ev_ptrd3)
            e_t = None
            for k8 in range(8):
                idx = f * 8 + k8
                hh, pp = idx // 5, idx % 5
                e_t = b.op(T, lambda e, hh=hh, pp=pp, k8=k8: e.transpose(
                    ps_tr[:, k8 * 128:(k8 + 1) * 128],
                    p163[:, hh * 640 + pp * 128: hh * 640 + (pp + 1) * 128], identb[:, :]), "eT")
            ev_pt3 = e_t
            b.wait(V, e_t)
            ev_ptrd3 = b.op(V, lambda e, f=f: e.tensor_copy(
                ptb[:, f * 1024:(f + 1) * 1024], ps_tr[:, 0:1024]), "eV")
        b.wait(T, ev_ptrd3, e_oev)
        for h in range(8):
            for j in range(5):
                kt = i - 4 + j
                sl = kt % 6
                idx = h * 5 + j
                b.wait(T, ev_vr[kt])
                ev_pv3 = b.op(T, lambda e, h=h, j=j, sl=sl, idx=idx: e.matmul(
                    ps_o[:, h * 64:(h + 1) * 64], ptb[:, idx * 128:(idx + 1) * 128],
                    vring[:, sl * 512 + h * 64: sl * 512 + (h + 1) * 64],
                    start=(j == 0), stop=(j == 4)), "eT")
        ev_ring_rd[(i - 4) % 6] = ev_pv3
        b.wait(V, ev_pv3, e_rs3, ev_o16,
               Ev("dDo", b.cnt.get("dDo", 0)) if g.get("dbg") and l == 0 else None)
        e_oev = b.op(V, lambda e: e.tensor_mul(
            o_t[:, :].rearrange("p (h c) -> p h c", c=64),
            ps_o[:, 0:512].rearrange("p (h c) -> p h c", c=64),
            rs3[:, :].rearrange("p (h o) -> p h o", o=1).broadcast_to((128, 8, 64))), "eV")
        if g.get("dbg") and l == 0:
            b.wait(P, e_oev)
            b.dma(P, "dDo", g["dbg_od"][i], o_t[:, :])
            b.dma(P, "dDo", g["dbg_rs"][i, :, 0:8], mx3[:, :])
            b.dma(P, "dDo", g["dbg_rs"][i, :, 8:16], sm3[:, :])
            b.dma(P, "dDo", g["dbg_rs"][i, :, 16:24], rs3[:, :])
        b.wait(V, e_oev, ev_ofm)
        ev_o16 = b.op(V, lambda e: e.tensor_copy(o16[:, :], o_t[:, :]), "eV")
        b.wait(T, ev_o16, ev_ptrd3)
        e_t = None
        for c in range(4):
            e_t = b.op(T, lambda e, c=c: e.transpose(
                ps_tr[:, c * 128:(c + 1) * 128], o16[:, c * 128:(c + 1) * 128], identb[:, :]), "eT")
        b.wait(V, e_t)
        ev_ofm = b.op(V, lambda e: e.tensor_copy(o_fm[:, 0:512], ps_tr[:, 0:512]), "eV")
        b.wait("sync", e_xa, ev_strd, e_xad)
        e_xcur = b.dma("sync", "dXc", xa_f[:, :].rearrange("p (c t) -> p c t", t=128),
                       curf[:, :, i * 128:(i + 1) * 128].rearrange("c p t -> p c t"))
        ev_wo_mm = None
        for mc in range(8):
            b.wait(T, ev_ofm, e_cwo, ev_wo_mm)
            em = None
            for kc in range(4):
                em = b.op(T, lambda e, mc=mc, kc=kc: e.matmul(
                    ps_v[:, 0:128], wo_s[:, kc * D + mc * 128: kc * D + (mc + 1) * 128],
                    o_fm[:, kc * 128:(kc + 1) * 128],
                    start=(kc == 0), stop=(kc == 3)), "eT")
            b.wait(V, em, e_xcur, ev_xa16, e_xa)
            e_xa = b.op(V, lambda e, mc=mc: e.tensor_add(
                xa_f[:, mc * 128:(mc + 1) * 128], xa_f[:, mc * 128:(mc + 1) * 128],
                ps_v[:, 0:128]), "eV")
            ev_wo_mm = e_xa
        if i == 12:
            b.wait(V, e_xa)
            for mc in range(8):
                e_xa = b.op(V, lambda e, mc=mc: e.tensor_mul(
                    xa_f[:, mc * 128 + 127: mc * 128 + 128],
                    xa_f[:, mc * 128 + 127: mc * 128 + 128], g["colm"][:, :]), "eV")
        b.wait(V, e_xa, ev_xa16)
        ev_xa16 = b.op(V, lambda e: e.tensor_copy(xa_16[:, :], xa_f[:, :]), "eV")
        b.wait(P, e_xa, ev_xa16, e_xad, e_xad2)
        e_xad = b.dma(P, "dXa", xaf_d[:, :, i * 128:(i + 1) * 128].rearrange("c p t -> p c t"),
                      xa_f[:, :].rearrange("p (c t) -> p c t", t=128))
        e_xad2 = b.dma(P, "dXa2", xa16_d[:, :, i * 128:(i + 1) * 128].rearrange("c p t -> p c t"),
                       xa_16[:, :].rearrange("p (c t) -> p c t", t=128))
        for g_ in range(2):
            b.wait(T, e_xa, ev_qrd)
            e_t = None
            for kk in range(4):
                c = g_ * 4 + kk
                e_t = b.op(T, lambda e, c=c, kk=kk: e.transpose(
                    ps_ftr[:, kk * 128:(kk + 1) * 128], xa_f[:, c * 128:(c + 1) * 128],
                    identf[:, :]), "eT")
            b.wait(V, e_t, ev_qrd, ev_st_wr)
            ev_qrd = b.op(V, lambda e, g_=g_: e.tensor_copy(
                qkv_f[:, g_ * 512:(g_ + 1) * 512], ps_ftr[:, 0:512]), "eV")
        b.wait(V, ev_qrd)
        e_a1 = b.op(V, lambda e, i=i: e.reduce_sum(stats[:, i * 4:i * 4 + 1], qkv_f[:, 0:512], axis=AX.X), "eV")
        e_b1 = b.op(V, lambda e, i=i: e.reduce_sum(stats[:, i * 4 + 1:i * 4 + 2], qkv_f[:, 512:1024], axis=AX.X), "eV")
        b.wait(SC, ev_qrd, e_a1, e_b1)
        e_q5 = b.op(SC, lambda e, i=i: e.activation(sqh[:, 0:512], qkv_f[:, 0:512], AF.Square,
                                                    accum_out=stats[:, i * 4 + 2:i * 4 + 3]), "eS")
        b.wait(SC, e_q5)
        e_q6 = b.op(SC, lambda e, i=i: e.activation(sqh[:, 0:512], qkv_f[:, 512:1024], AF.Square,
                                                    accum_out=stats[:, i * 4 + 3:i * 4 + 4]), "eS")
        ev_st_wr = e_q6
        ev_strd = e_q6
    g["_"] = None


def _l3_stats_rows(b, l, g):
    V, SC, P = "vector", "scalar", "gpsimd"
    stats = g["stats"]; strow_d = g["strow_d"]; mrow_d = g["mrow_d"]
    rowseg = g["rowseg"]
    r0 = FFG[l][0][0]
    for j in range(4):
        b.dma(P, "dSt", strow_d[j:j + 1, :].rearrange("o (t p) -> p t o", p=128),
              stats[:, :].rearrange("p (t j) -> p t j", j=4)[:, :, j:j + 1],
              allow_slow_non_contiguous=True)
    b.wait("sync", Ev("dSt", b.cnt["dSt"]))
    seglen = 1184
    e_md = None
    e6 = None
    for s0 in range(r0, R3, seglen):
        w = min(seglen, R3 - s0)
        b.wait("sync", e6)
        e_l0 = b.dma("sync", "dSr", rowseg[0][0:1, 0:w], strow_d[0:1, s0:s0 + w])
        e_l1 = b.dma("sync", "dSr2", rowseg[1][0:1, 0:w], strow_d[1:2, s0 - 1:s0 - 1 + w])
        e_l2 = b.dma("sync", "dSr3", rowseg[2][0:1, 0:w], strow_d[2:3, s0:s0 + w])
        e_l3 = b.dma("sync", "dSr4", rowseg[3][0:1, 0:w], strow_d[3:4, s0 - 1:s0 - 1 + w])
        b.wait(V, e_l0, e_l1, e_l2, e_l3, e_md)
        e1 = b.op(V, lambda e, w=w: e.tensor_add(rowseg[4][0:1, 0:w], rowseg[0][0:1, 0:w],
                                                 rowseg[1][0:1, 0:w]), "eV")
        b.wait(V, e1)
        e2 = b.op(V, lambda e, w=w: e.tensor_scalar_mul(rowseg[4][0:1, 0:w], rowseg[4][0:1, 0:w],
                                                        1.0 / D), "eV")
        e3 = b.op(V, lambda e, w=w: e.tensor_add(rowseg[5][0:1, 0:w], rowseg[2][0:1, 0:w],
                                                 rowseg[3][0:1, 0:w]), "eV")
        b.wait(V, e2, e3)
        e4 = b.op(V, lambda e, w=w: e.tensor_scalar_mul(rowseg[5][0:1, 0:w], rowseg[5][0:1, 0:w],
                                                        1.0 / D), "eV")
        e5 = b.op(V, lambda e, w=w: e.tensor_mul(rowseg[0][0:1, 0:w], rowseg[4][0:1, 0:w],
                                                 rowseg[4][0:1, 0:w]), "eV")
        b.wait(V, e4, e5)
        e6 = b.op(V, lambda e, w=w: e.tensor_sub(rowseg[5][0:1, 0:w], rowseg[5][0:1, 0:w],
                                                 rowseg[0][0:1, 0:w]), "eV")
        b.wait(V, e6)
        e7 = b.op(V, lambda e, w=w: e.tensor_scalar_add(rowseg[5][0:1, 0:w], rowseg[5][0:1, 0:w],
                                                        1e-5), "eV")
        b.wait(SC, e7)
        e8 = b.op(SC, lambda e, w=w: e.activation(rowseg[5][0:1, 0:w], rowseg[5][0:1, 0:w],
                                                  AF.Sqrt), "eS")
        b.wait(V, e8)
        e9 = b.op(V, lambda e, w=w: e.reciprocal(rowseg[5][0:1, 0:w], rowseg[5][0:1, 0:w]), "eV")
        b.wait(P, e2, e9)
        b.dma(P, "dMr", mrow_d[0:1, s0:s0 + w], rowseg[4][0:1, 0:w])
        e_md = b.dma(P, "dMr", mrow_d[1:2, s0:s0 + w], rowseg[5][0:1, 0:w])
    g["_"] = None


def _l3_ffn(b, l, g):
    V, SC, T, P = "vector", "scalar", "tensor", "gpsimd"
    xg_f = g["xg_f"]; og_f = g["og_f"]; hn16 = g["hn16"]; prod3 = g["prod3"]
    w1tb3 = g["w1tb3"]; w2tb3 = g["w2tb3"]; ac3 = g["ac3"]; cb16 = g["cb16"]
    sqh = g["sqh"]; mbc = g["mbc"]; rbc = g["rbc"]; biasg = g["biasg"]; colm = g["colm"]
    ps_q = g["ps_q"]; ps_k = g["ps_k"]; ps_v = g["ps_v"]
    xaf_d = g["xaf_d"]; xa16_d = g["xa16_d"]; mrow_d = g["mrow_d"]
    w1c_d = g["w1c_d"]; w2c_d = g["w2c_d"]
    nxtf = g["nxtf"]; nxt16 = g["nxt16"]
    ev_wc = g["ev_wc"]; ev_bias = g["ev_bias"]

    ev_w1l3 = [None, None]
    ev_w2l3 = [None, None]
    e_hn = None
    ev_hn_rd = None
    e_ac = None
    e_gg = None
    e_pr = None
    ev_pr_rd = None
    e_og = None
    ev_og16 = None
    e_ogd = e_ogd2 = None
    ev_xg_rd = None
    for gi, (t0, gw) in enumerate(FFG[l]):
        b.wait("sync", ev_xg_rd)
        e_xgl = b.dma("sync", "dXg", xg_f.rearrange("p (c t) -> p c t", t=SG + 1)[:, :, 0:gw + 1],
                      xaf_d[:, :, t0 - 1:t0 + gw].rearrange("c p t -> p c t"))
        e_mb = b.dma("sync", "dXg2", mbc[:, 0:gw], mrow_d[0:1, t0:t0 + gw].broadcast_to((128, gw)))
        e_rb = b.dma("sync", "dXg2", rbc[:, 0:gw], mrow_d[1:2, t0:t0 + gw].broadcast_to((128, gw)))
        for c in range(8):
            coff = 1 if c < 4 else 0
            b.wait(V, e_xgl, e_mb, e_rb,
                   ev_hn_rd if c == 0 else None, e_hn if c == 0 else None,
                   e_ogd if c == 0 else None)
            e1 = b.op(V, lambda e, c=c, coff=coff, gw=gw: e.tensor_sub(
                og_f[:, 0:gw], xg_f[:, c * (SG + 1) + coff: c * (SG + 1) + coff + gw], mbc[:, 0:gw]), "eV")
            b.wait(V, e1)
            e2 = b.op(V, lambda e, gw=gw: e.tensor_mul(og_f[:, 0:gw], og_f[:, 0:gw], rbc[:, 0:gw]), "eV")
            b.wait(V, e2)
            e_hn = b.op(V, lambda e, c=c, gw=gw: e.tensor_scalar(
                hn16[:, c * SG:c * SG + gw], og_f[:, 0:gw],
                biasg[:, 74 + c:74 + c + 1], biasg[:, 82 + c:82 + c + 1],
                ALU.mult, ALU.add), "eV")
        if g.get("dbg") and l == 0:
            b.wait(P, e_hn)
            e_tapH = b.dma(P, "dDg", g["dbg_hn"][gi], hn16[:, 0:4096])
            b.wait(V, e_tapH)
        for ci, (c0, w) in enumerate(FFCH):
            wb_ = ci % 2
            b.wait("sync", ev_w1l3[wb_])
            e_w1 = b.dma("sync", "dWf%d" % wb_,
                         w1tb3[wb_][:, 0:8 * w].rearrange("p (c m) -> p c m", m=w),
                         w1c_d[:, :, c0:c0 + w])
            e_w1g = b.dma("sync", "dWf%d" % wb_,
                          w1tb3[wb_][:, 1024:1024 + 8 * w].rearrange("p (c m) -> p c m", m=w),
                          w1c_d[:, :, HID2 + c0:HID2 + c0 + w])
            b.wait(T, e_w1, e_w1g, e_hn, ev_wc, e_ac, e_gg)
            ema = emg = None
            for kc in range(8):
                ema = b.op(T, lambda e, kc=kc, w=w, wb_=wb_, gw=gw: e.matmul(
                    ps_q[0:w, 0:gw], w1tb3[wb_][:, kc * w:(kc + 1) * w],
                    hn16[:, kc * SG:kc * SG + gw],
                    start=(kc == 0), stop=(kc == 7)), "eT")
            for kc in range(8):
                emg = b.op(T, lambda e, kc=kc, w=w, wb_=wb_, gw=gw: e.matmul(
                    ps_k[0:w, 0:gw], w1tb3[wb_][:, 1024 + kc * w: 1024 + (kc + 1) * w],
                    hn16[:, kc * SG:kc * SG + gw],
                    start=(kc == 0), stop=(kc == 7)), "eT")
            ev_w1l3[wb_] = emg
            b.wait(V, ema, e_pr, ev_bias)
            e_ac = b.op(V, lambda e, ci=ci, w=w, gw=gw: e.tensor_scalar(
                ac3[0:w, 0:gw], ps_q[0:w, 0:gw], biasg[0:w, ci:ci + 1], None, ALU.add), "eV")
            b.wait(SC, emg, e_pr)
            e_gg = b.op(SC, lambda e, ci=ci, w=w, gw=gw: e.activation(
                sqh[0:w, 0:gw], ps_k[0:w, 0:gw], AF.Gelu, bias=biasg[0:w, 44 + ci:44 + ci + 1]), "eS")
            b.wait(V, e_ac, e_gg, ev_pr_rd if ci == 0 else None)
            e_pr = b.op(V, lambda e, ci=ci, w=w, gw=gw: e.tensor_mul(
                prod3[0:w, ci * SG:ci * SG + gw], ac3[0:w, 0:gw], sqh[0:w, 0:gw]), "eV")
        for mc in range(8):
            wb_ = mc % 2
            b.wait("sync", ev_w2l3[wb_])
            e_w2 = b.dma("sync", "dWh%d" % wb_,
                         w2tb3[wb_][:, 0:2816].rearrange("p (c m) -> p c m", m=128),
                         w2c_d[:, :, mc * 128:(mc + 1) * 128])
            em = None
            b.wait(T, e_w2, e_pr, e_og)
            for ci, (c0, w) in enumerate(FFCH):
                em = b.op(T, lambda e, ci=ci, w=w, mc=mc, wb_=wb_, gw=gw: e.matmul(
                    ps_v[:, 0:gw], w2tb3[wb_][0:w, ci * 128:(ci + 1) * 128],
                    prod3[0:w, ci * SG:ci * SG + gw],
                    start=(ci == 0), stop=(ci == 21)), "eT")
            ev_w2l3[wb_] = em
            ev_pr_rd = em
            b.wait(V, em, e_og, ev_og16, e_ogd)
            e_og = b.op(V, lambda e, mc=mc, gw=gw: e.scalar_tensor_tensor(
                og_f[:, mc * SG:mc * SG + gw], ps_v[:, 0:gw],
                biasg[:, 66 + mc:66 + mc + 1],
                xg_f[:, mc * (SG + 1) + 1: mc * (SG + 1) + 1 + gw], ALU.add, ALU.add), "eV")
            ev_xg_rd = e_og
        if g.get("dbg") and l == 0:
            b.wait(P, e_og)
            e_tapG = b.dma(P, "dDg", g["dbg_og"][gi], og_f[:, 0:4096])
            b.wait(V, e_tapG)
        if l == 0 and t0 <= 1663 < t0 + gw:
            co = 1663 - t0
            b.wait(V, e_og)
            for mc in range(8):
                e_og = b.op(V, lambda e, mc=mc, co=co: e.tensor_mul(
                    og_f[:, mc * SG + co: mc * SG + co + 1],
                    og_f[:, mc * SG + co: mc * SG + co + 1], colm[:, :]), "eV")
        b.wait(V, e_og, ev_og16, Ev("dXo2", b.cnt.get("dXo2", 0)))
        ev_og16 = b.op(V, lambda e, gw=gw: e.tensor_copy(
            hn16[:, 0:8 * SG].rearrange("p (c t) -> p c t", t=SG)[:, :, 0:gw],
            og_f[:, 0:8 * SG].rearrange("p (c t) -> p c t", t=SG)[:, :, 0:gw]), "eV")
        b.wait(P, ev_og16, e_ogd2)
        if g.get("dbg") and l == 0:
            e_tapO = b.dma(P, "dDg", g["dbg_o16"][gi], hn16[:, 0:4096])
        e_ogd2 = b.dma(P, "dXo2", nxt16[:, :, t0:t0 + gw].rearrange("c p t -> p c t"),
                       hn16[:, 0:8 * SG].rearrange("p (c t) -> p c t", t=SG)[:, :, 0:gw])
        b.wait(P, e_og, e_ogd)
        e_ogd = b.dma(P, "dXo", nxtf[:, :, t0:t0 + gw].rearrange("c p t -> p c t"),
                      og_f.rearrange("p (c t) -> p c t", t=SG)[:, :, 0:gw])
        ev_hn_rd = e_ogd2
    g["_"] = None


def _l3_combine(b, g):
    V, SC, P = "vector", "scalar", "gpsimd"
    ARF = g["ARF"]
    nxtf = g["nxtf"]; s_in = g["s_in"]; xfm_in = g["xfm_in"]; xL1_out = g["xL1_out"]
    gbuf = ARF[:, 0:2048]
    sbuf_ = ARF[:, 2048:4096]
    xbuf = ARF[:, 4096:6144]
    SG2 = 256
    e_si = e_mu = e_ad = e_od = None
    for gi in range(16):
        t0 = 1664 + gi * SG2
        so = gi * SG2
        b.wait("sync", e_si, e_mu)
        e_gl = b.dma("sync", "dCg", gbuf.rearrange("p (c t) -> p c t", t=SG2),
                     nxtf[:, :, t0:t0 + SG2].rearrange("c p t -> p c t"))
        b.wait("sync", e_ad, e_od)
        e_sl = b.dma("sync", "dCs", sbuf_.rearrange("p (c t) -> p c t", t=SG2),
                     s_in[:, :, so:so + SG2].rearrange("c p t -> p c t"))
        b.wait("sync", e_ad, e_od)
        e_xl = b.dma("sync", "dCx", xbuf.rearrange("p (c t) -> p c t", t=SG2),
                     xfm_in[:, :, t0:t0 + SG2].rearrange("c p t -> p c t"))
        b.wait(SC, e_gl, e_mu)
        e_si = b.op(SC, lambda e: e.activation(gbuf, gbuf, AF.Silu), "eS")
        b.wait(V, e_si, e_sl)
        e_mu = b.op(V, lambda e: e.tensor_mul(sbuf_, sbuf_, gbuf), "eV")
        b.wait(V, e_mu, e_xl)
        e_ad = b.op(V, lambda e: e.tensor_add(sbuf_, sbuf_, xbuf), "eV")
        b.wait(P, e_ad)
        e_od = b.dma(P, "dXL", xL1_out[:, :, so:so + SG2].rearrange("c p t -> p c t"),
                     sbuf_.rearrange("p (c t) -> p c t", t=SG2))
    g["_"] = None



def _l3_tables(start_global):
    t = np.arange(start_global - 1664, start_global + OWN, dtype=np.float64)
    half_idx = np.arange(0, 64, 2, dtype=np.float64)
    inv_freq = 1.0 / 10000.0 ** (half_idx / 64.0)
    freqs = t[:, None] * inv_freq[None, :]
    freqs = np.concatenate([freqs, freqs], 1)
    base = (half_idx + 0.4 * 64.0) / (1.4 * 64.0)
    power = (t - 8192 // 2) / 256.0
    scale = base[None, :] ** power[:, None]
    scale = np.concatenate([scale, scale], 1)
    cos = np.cos(freqs)
    sin = np.sin(freqs)
    sgn = np.ones(64); sgn[:32] = -1.0
    cS8q = (cos * scale * 8.0).astype(np.float32)
    Bq8 = (np.concatenate([sin[:, :32] * scale[:, :32], sin[:, 32:] * scale[:, 32:]], 1)
           * sgn[None, :] * 8.0).astype(np.float32)
    cSk = (cos / scale).astype(np.float32)
    Bk = (np.concatenate([sin[:, :32] / scale[:, :32], sin[:, 32:] / scale[:, 32:]], 1)
          * sgn[None, :]).astype(np.float32)
    return np.stack([cS8q, Bq8, cSk, Bk]).astype(np.float32)


def _l3_masks(first_half):
    m = np.full((5, 128, 640), NEG, np.float32)
    r = np.arange(128)[:, None]
    k = np.arange(640)[None, :]
    band = (k >= r) & (k <= r + 512)
    m[0][band] = 0.0
    for s in range(1, 5):
        i = 12 + s
        if first_half:
            ok = band & (k >= 2176 - 128 * i)
        else:
            ok = band
        m[s][ok] = 0.0
    return m


def run_launch3(x_full, s_list, dbg=False):
    b = build_launch3(dbg)
    nc = b.finalize()
    G = _G
    wqkv = np.ascontiguousarray(G["L0_ffn_wqkv"].reshape(2 * D, 1536))
    wo = np.ascontiguousarray(G["L0_ffn_wo"].reshape(2 * 512, D))
    w1 = np.ascontiguousarray(G["L0_ffn_w1"].reshape(2 * D, FF2))
    w2 = np.ascontiguousarray(G["L0_ffn_w2"].reshape(2 * HID2, D))
    b1p = np.zeros((2, 66, 128), np.float32)
    for l in range(2):
        for ci, (c0, w) in enumerate(FFCH):
            b1p[l, ci, :w] = G["L0_ffn_b1"][l, c0:c0 + w]
            b1p[l, 44 + ci, :w] = G["L0_ffn_b1"][l, HID2 + c0:HID2 + c0 + w]
    in_maps = []
    for c in range(NCORES):
        bi, h = c // 2, c % 2
        start = h * OWN
        lo = start - 1664
        if lo < 0:
            xs = np.concatenate([np.zeros((-lo, D), np.float32), x_full[bi, :start + OWN]], 0)
        else:
            xs = x_full[bi, lo:start + OWN]
        xs = np.ascontiguousarray(xs)
        xfm = np.ascontiguousarray(xs.T.reshape(8, 128, R3))
        colm = np.full((128, 1), 0.0 if h == 0 else 1.0, np.float32)
        in_maps.append({
            "x": xs, "xfm": xfm, "sin": s_list[c],
            "wqkv": wqkv, "qs": G["L0_ffn_q_scale"], "ks": G["L0_ffn_k_scale"],
            "wo": wo, "lng": G["L0_ffn_ln_g"], "lnb": G["L0_ffn_ln_b"],
            "w1": w1, "b1p": b1p, "w2": w2, "b2": G["L0_ffn_b2"],
            "tabs_i": _l3_tables(start), "wm3_i": _l3_masks(h == 0),
            "identb_i": _identity_bf16(), "identf_i": np.eye(128, dtype=np.float32),
            "colm_i": colm,
            "w1f": np.ascontiguousarray(G["L1_ffn_W1"].reshape(8 * D, 4096)),
            "w2f": np.ascontiguousarray(G["L1_ffn_W2"].reshape(8 * 4096, D)),
        })
    res = run_bass_kernel_spmd(nc, in_maps, core_ids=list(range(NCORES)))
    run_launch3.res = res
    outs = []
    for c in range(NCORES):
        xf = res.results[c]["xL1"]
        rows = np.zeros((OWN, D), np.float32)
        for dc in range(8):
            rows[:, dc * 128:(dc + 1) * 128] = xf[dc].T
        outs.append(rows)
    return outs


_G = {}


def kernel(**inputs):
    _G.clear()
    for k, v in inputs.items():
        _G[k] = np.asarray(v)
    x_full = _G["L0_attn_x"]

    res1 = run_launch1(x_full)
    x1_list = [res1[c]["x1"] for c in range(NCORES)]
    hdd2f_list = [res1[c]["hdd2f"] for c in range(NCORES)]
    gradp_list = [res1[c]["gradp"] for c in range(NCORES)]
    s_list = run_launch2(x1_list, hdd2f_list, gradp_list)

    s_fm = [np.ascontiguousarray(s.T.reshape(8, 128, OWN)) for s in s_list]
    xL1_list = run_launch3(x_full, s_fm)
    r3 = run_launch3.res.results[0]
    w1b, w2b = r3["w1b"], r3["w2b"]

    xL1_full = np.zeros((B, S, D), np.float32)
    for c in range(NCORES):
        bi, h = c // 2, c % 2
        xL1_full[bi, h * OWN:(h + 1) * OWN] = xL1_list[c]
    run_launch4(xL1_full)
    s1f_list = [run_launch4.res.results[c]["s1"] for c in range(NCORES)]

    outs5 = run_launch5(xL1_list, s1f_list, w1b, w2b)
    out = np.zeros((B, S, D), np.float32)
    for c in range(NCORES):
        bi, h = c // 2, c % 2
        out[bi, h * OWN:(h + 1) * OWN] = outs5[c]
    return out
